# version 03
# baseline (speedup 1.0000x reference)
import contextlib
import numpy as np

import concourse.bass as bass
import concourse.mybir as mybir
from concourse.bass import IndirectOffsetOnAxis
from concourse.bass_utils import run_bass_kernel_spmd

F32 = mybir.dt.float32
BF16 = mybir.dt.bfloat16
U32 = mybir.dt.uint32
I32 = mybir.dt.int32
AF = mybir.ActivationFunctionType
ALU = mybir.AluOpType
AX = mybir.AxisListType

T = 32768
D = 1024
E = 64
KTOP = 8
CAP = 512
II = 512
TILE = 128
NT = T // TILE
DS = 64
NB = 2
TB = T // NB
NTB = TB // TILE
SCALE = 2.5
NSLOT = E * CAP
OOB_BIG = 10_000_000

STAGES = {"ln": 0, "route": 1, "expert": 2, "ffn1": 3, "attn": 4, "full": 5}


class Prog:

    def __init__(self, nc, cs):
        self.nc = nc
        self.cs = cs
        self._h = {}
        self.v = {}

    def sem(self, name):
        if name not in self._h:
            self._h[name] = self.cs.enter_context(self.nc.semaphore(name))
            self.v[name] = 0
        return self._h[name]

    def ti(self, instr, name, n=1):
        h = self.sem(name)
        instr.then_inc(h, n, skip_validation=True)
        self.v[name] += n
        return self.v[name]

    def val(self, name):
        self.sem(name)
        return self.v[name]

    def stepper(self, eng, name):
        def step(instr):
            self.ti(instr, name)
            eng.wait_ge(self.sem(name), self.v[name])
            return instr
        return step


def build_program(stage="full", debug=(), out_rows=(0, T), route_pass=3, expert_pass=9, combine_mode=3):
    stn = STAGES[stage]
    nc = bass.Bass("TRN2", target_bir_lowering=False, debug=False)
    cs = contextlib.ExitStack()
    P = Prog(nc, cs)

    def sbuf(name, shape, dt=F32):
        return cs.enter_context(nc.sbuf_tensor(name, shape, dt))

    def psum(name, shape, dt=F32):
        return cs.enter_context(nc.psum_tensor(name, shape, dt))

    _uid = [0]

    def uname(base):
        _uid[0] += 1
        return base + "_u" + str(_uid[0])

    x_in = nc.declare_dram_parameter("x", [T, D], F32, isOutput=False)
    router_in = nc.declare_dram_parameter("router_w", [D, E], F32, isOutput=False)
    if stn >= 2:
        gate_in = nc.declare_dram_parameter("gate_w", [E, D, II], F32, isOutput=False)
        up_in = nc.declare_dram_parameter("up_w", [E, D, II], F32, isOutput=False)
        down_in = nc.declare_dram_parameter("down_w", [E, II, D], F32, isOutput=False)
    if stn >= 3:
        sgate_in = nc.declare_dram_parameter("shared_gate_w", [D, II], F32, isOutput=False)
        sup_in = nc.declare_dram_parameter("shared_up_w", [D, II], F32, isOutput=False)
        sdown_in = nc.declare_dram_parameter("shared_down_w", [II, D], F32, isOutput=False)
    if stn >= 4:
        wq_in = nc.declare_dram_parameter("wq_t", [D, DS], F32, isOutput=False)
        wk_in = nc.declare_dram_parameter("wk_t", [D, DS], F32, isOutput=False)
        wv_in = nc.declare_dram_parameter("wv_t", [D, DS], F32, isOutput=False)
        wo_in = nc.declare_dram_parameter("wo_t", [DS, D], F32, isOutput=False)
        bqkv_in = nc.declare_dram_parameter("bqkv", [1, 3 * DS], F32, isOutput=False)
        bo_in = nc.declare_dram_parameter("bo", [1, D], F32, isOutput=False)
        decay_in = nc.declare_dram_parameter("decay", [1, DS], F32, isOutput=False)

    n_out = out_rows[1] - out_rows[0]
    y_out = nc.declare_dram_parameter("y", [n_out, D], F32, isOutput=True)

    dbg = {}
    for name, shape, dt in (
        ("hln", [T, D], BF16), ("routeS", [NT, TILE, 24], F32),
        ("cnt", [E, NT], F32), ("inv", [NSLOT, 2], F32), ("xd", [NSLOT, D], BF16),
        ("yd", [NSLOT, D], BF16), ("x1", [T, D], F32), ("x2", [T, D], F32),
    ):
        if name in debug:
            dbg[name] = nc.declare_dram_parameter("dbg_" + name, shape, dt, isOutput=True)

    hln_d = nc.dram_tensor("hln_d", [T + TILE, D], BF16)
    inv_d = nc.dram_tensor("inv_d", [NSLOT, 2], F32)
    xd_d = nc.dram_tensor("xd_d", [NSLOT, D], BF16)
    yd_d = nc.dram_tensor("yd_d", [NSLOT + TILE, D], BF16)
    x1_d = nc.dram_tensor("x1_d", [T, D], F32)
    x2_d = nc.dram_tensor("x2_d", [T, D], F32)
    if stn >= 2:
        gbf_d = nc.dram_tensor("gbf_d", [E, D, II], BF16)
        ubf_d = nc.dram_tensor("ubf_d", [E, D, II], BF16)
        dbf_d = nc.dram_tensor("dbf_d", [E, II, D], BF16)
    if stn >= 3:
        sgbf_d = nc.dram_tensor("sgbf_d", [D, II], BF16)
        subf_d = nc.dram_tensor("subf_d", [D, II], BF16)
        sdbf_d = nc.dram_tensor("sdbf_d", [II, D], BF16)
    if stn >= 4:
        r2_d = nc.dram_tensor("r2_d", [T, D], F32)
        misc_d = nc.dram_tensor("misc_d", [DS, 1], F32)
        wqkv_bf_d = nc.dram_tensor("wqkv_bf_d", [D, 3 * DS], BF16)
        wo_bf_d = nc.dram_tensor("wo_bf_d", [DS, D], BF16)

    ones1 = sbuf("ones1", [1, TILE])
    onescol_b = sbuf("onescol_b", [TILE, 1], BF16)
    su_bf = sbuf("su_bf", [TILE, TILE], BF16)
    iota_rev = sbuf("iota_rev", [TILE, E])
    iota_enc = sbuf("iota_enc", [TILE, E])
    iota_p = sbuf("iota_p", [TILE, 1])
    oob_tab = sbuf("oob_tab", [TILE, KTOP])
    dummy_tab = sbuf("dummy_tab", [TILE, 1])
    dummyT_tab = sbuf("dummyT_tab", [TILE, 1])
    fill_neg = sbuf("fill_neg", [TILE, KTOP])
    router_sb = sbuf("router_sb", [TILE, 8 * E], BF16)

    czero = sbuf("czero", [TILE, 1])
    ceps = sbuf("ceps", [TILE, 1])
    conef = sbuf("conef", [TILE, 1])
    chalf = sbuf("chalf", [TILE, 1])
    nc.const_aps.register(F32, 0.0, czero[:, :])
    nc.const_aps.register(F32, 1e-5, ceps[:, :])
    nc.const_aps.register(F32, 1.0, conef[:, :])
    nc.const_aps.register(F32, 0.5, chalf[:, :])

    xg = sbuf("xg", [TILE, 2 * D], BF16)
    xa = sbuf("xa", [TILE, D])
    xb = sbuf("xb", [TILE, D])
    hba = sbuf("hba", [TILE, D], BF16)
    hbb = sbuf("hbb", [TILE, D], BF16)
    stat = sbuf("stat", [TILE, 16])
    sqbuf = sbuf("sqbuf", [TILE, D])

    hlnT = sbuf("hlnT", [TILE, 2 * 4096], BF16)
    scr = sbuf("scr", [TILE, 2 * 256])
    msk = sbuf("msk", [TILE, E])
    gsr = sbuf("gsr", [TILE, 32])
    tk = sbuf("tk", [TILE, 64])
    a_all = sbuf("a_all", [TILE, NT * E], BF16)
    ew_all = sbuf("ew_all", [TILE, NT * 2 * KTOP])
    runrow = sbuf("runrow", [1, E])
    sidx_all = sbuf("sidx_all", [TILE, NT * KTOP], U32)
    pairs_all = sbuf("pairs_all", [TILE, NT * 2 * KTOP])
    cmb_all = sbuf("cmb_all", [TILE, NT * KTOP], U32)
    kflag = sbuf("kflag", [1, NT])
    kflag_i = sbuf("kflag_i", [1, NT], I32)
    rs_all = sbuf("rs_all", [TILE, NT * 24]) if "routeS" in debug else None
    vsum = sbuf("vsum", [TILE, NT])
    valid_i = sbuf("valid_i", [TILE, KTOP], I32)
    zer512 = sbuf("zer512", [TILE, 512])
    if STAGES[stage] >= 2:
        inv_sb = sbuf("inv_sb", [TILE, 2 * NT])
        idx_all = sbuf("idx_all", [TILE, NT], U32)
        wden = sbuf("wden", [TILE, NT])
        imsk = sbuf("imsk", [TILE, NT], I32)
        wg_sb = sbuf("wg_sb", [TILE, 2 * 4096], BF16)
        wu_sb = sbuf("wu_sb", [TILE, 2 * 4096], BF16)
        wd_sb = sbuf("wd_sb", [TILE, 2 * 4096], BF16)
        sact = sbuf("sact", [TILE, 512])
        act_sb = sbuf("act_sb", [TILE, 4 * 512], BF16)
        y_sb = sbuf("y_sb", [TILE, 2 * D], BF16)
    if STAGES[stage] >= 3:
        x1g = pairs_all
        stg = a_all
        ident_b = sbuf("ident_b", [TILE, TILE], BF16)
        identX = sbuf("identX", [TILE, 2 * TILE], BF16)
        flagcol = sbuf("flagcol", [TILE, NT])
        kflag01 = sbuf("kflag01", [1, NT])

    if stn >= 4:
        woT_sb = wg_sb[0:DS, 1024:2048]
        mask3 = wg_sb[:, 0:3 * TILE]
        obf = wg_sb[0:DS, 512:768]
        qT_bf = wu_sb[0:DS, 0:3 * TILE]
        kT_bf = wu_sb[0:DS, 512:512 + 3 * TILE]
        kvt = wu_sb[:, 1024:1024 + 3 * TILE]
        v4 = wd_sb[:, 0:3 * 4 * DS]
        mav = wd_sb[:, 1024:1024 + 6 * TILE]
        dcolx = sact[:, 0:4 * DS]
        lnd_rep = sact[:, 256:256 + DS]
        obuf = sact[0:DS, 320:448]
        dtile_rep = sact[0:DS, 448:512]
        drowt = xb[0:DS, 0:4 * TILE]
        exprow = xb[0:DS, 512:1024]
        bqkv_sb = sbuf("bqkv_sb", [1, 3 * DS])
        bo_sb = sbuf("bo_sb", [1, D])
        drow_sb = sbuf("drow_sb", [1, 4 * DS])
        lnd_col = sbuf("lnd_col", [DS, 1])
        st_f = sbuf("st_f", [DS, DS])
        st_b = sbuf("st_b", [DS, DS], BF16)
        expcol = sbuf("expcol", [TILE, 8])

    ps = [psum(f"ps{i}", [TILE, 512]) for i in range(8)]

    block = cs.enter_context(nc.Block())

    @block.vector
    def _(v):
        P.ti(v.memset(ones1[:, :], 1.0), "const")
        P.ti(v.memset(onescol_b[:, :], 1.0), "const")
        P.ti(v.memset(su_bf[:, :], 0.0), "const")
        P.ti(v.memset(fill_neg[:, :], -1e30), "const")
        P.ti(v.memset(zer512[:, :], 0.0), "const")
        if STAGES[stage] >= 3:
            P.ti(v.memset(ident_b[:, :], 0.0), "const")
        P.ti(v.memset(czero[:, :], 0.0), "const")
        P.ti(v.memset(ceps[:, :], 1e-5), "const")
        P.ti(v.memset(conef[:, :], 1.0), "const")
        P.ti(v.memset(chalf[:, :], 0.5), "const")

    @block.gpsimd
    def _(gp):
        gp.wait_ge(P.sem("const"), 9 + (1 if STAGES[stage] >= 3 else 0))
        P.ti(gp.affine_select(out=su_bf[:, :], in_=su_bf[:, :], pattern=[[-1, TILE]],
                              compare_op=ALU.is_ge, fill=1.0, base=0,
                              channel_multiplier=1), "const")
        if STAGES[stage] >= 3:
            gp.wait_ge(P.sem("const"), 10)
            P.ti(gp.affine_select(out=ident_b[:, :], in_=ident_b[:, :], pattern=[[-1, TILE]],
                                  compare_op=ALU.not_equal, fill=1.0, base=0,
                                  channel_multiplier=1), "const")
        P.ti(gp.iota(iota_rev[:, :], pattern=[[-1, E]], base=E, channel_multiplier=0,
                     allow_small_or_imprecise_dtypes=True), "const")
        P.ti(gp.iota(iota_enc[:, :], pattern=[[-32768, E]], base=E * 32768, channel_multiplier=0,
                     allow_small_or_imprecise_dtypes=True), "const")
        P.ti(gp.iota(oob_tab[:, :], pattern=[[1, KTOP]], base=OOB_BIG, channel_multiplier=KTOP,
                     allow_small_or_imprecise_dtypes=True), "const")
        P.ti(gp.iota(iota_p[:, :], pattern=[[0, 1]], base=0, channel_multiplier=1,
                     allow_small_or_imprecise_dtypes=True), "const")
        P.ti(gp.iota(dummy_tab[:, :], pattern=[[0, 1]], base=NSLOT, channel_multiplier=1,
                     allow_small_or_imprecise_dtypes=True), "const")
        P.ti(gp.iota(dummyT_tab[:, :], pattern=[[0, 1]], base=T, channel_multiplier=1,
                     allow_small_or_imprecise_dtypes=True), "const")
        P.ti(gp.dma_start(out=hln_d[T:T + TILE, 0:512], in_=zer512[:, :]), "pz", 16)
        P.ti(gp.dma_start(out=hln_d[T:T + TILE, 512:1024], in_=zer512[:, :]), "pz", 16)
        P.ti(gp.dma_start(out=inv_d.ap().rearrange("(a b) c -> a (b c)", a=TILE),
                          in_=zer512[:, :]), "pz", 16)
        if stn >= 2:
            P.ti(gp.dma_start(out=yd_d[NSLOT:NSLOT + TILE, 0:512], in_=zer512[:, :]), "pz", 16)
            P.ti(gp.dma_start(out=yd_d[NSLOT:NSLOT + TILE, 512:1024], in_=zer512[:, :]), "pz", 16)

    if stn >= 2:
        @block.gpsimd
        def _(gp):
            for e in range(E):
                P.ti(gp.dma_start(out=gbf_d[e, :, :], in_=gate_in[e, :, :]), "wcast", 16)
                P.ti(gp.dma_start(out=ubf_d[e, :, :], in_=up_in[e, :, :]), "wcast", 16)
                P.ti(gp.dma_start(out=dbf_d[e, :, :], in_=down_in[e, :, :]), "wcast", 16)
            if stn >= 3:
                P.ti(gp.dma_start(out=sgbf_d[:, :], in_=sgate_in[:, :]), "wcast", 16)
                P.ti(gp.dma_start(out=subf_d[:, :], in_=sup_in[:, :]), "wcast", 16)
                P.ti(gp.dma_start(out=sdbf_d[:, :], in_=sdown_in[:, :]), "wcast", 16)
            if stn >= 4:
                P.ti(gp.dma_start(out=wqkv_bf_d[:, 0:DS], in_=wq_in[:, :]), "wcast", 16)
                P.ti(gp.dma_start(out=wqkv_bf_d[:, DS:2 * DS], in_=wk_in[:, :]), "wcast", 16)
                P.ti(gp.dma_start(out=wqkv_bf_d[:, 2 * DS:3 * DS], in_=wv_in[:, :]), "wcast", 16)
                P.ti(gp.dma_start(out=wo_bf_d[:, :], in_=wo_in[:, :]), "wcast", 16)

    @block.sync
    def _(sy):
        for c in range(8):
            P.ti(sy.dma_start(out=scr[:, c * E:(c + 1) * E],
                              in_=router_in[c * TILE:(c + 1) * TILE, :]), "ld0", 16)

    @block.vector
    def _(v):
        v.wait_ge(P.sem("ld0"), P.val("ld0"))
        P.ti(v.tensor_copy(out=router_sb[:, :], in_=scr[:, 0:8 * E]), "rcp")

    CONST_TOTAL = P.val("const")

    def emit_ln(src_dram, dst_dram, tag, r2_dram=None):
        ld, cp, dv, st = tag + "ld", tag + "cp", tag + "dv", tag + "st"
        ld0, cp0, dv0, st0 = P.val(ld), P.val(cp), P.val(dv), P.val(st)
        r20 = P.val(tag + "r2")

        @block.sync
        def _(sy):
            for t in range(NT):
                buf = xa if t % 2 == 0 else xb
                if t >= 2:
                    sy.wait_ge(P.sem(cp), cp0 + t - 1)
                    if r2_dram is not None:
                        sy.wait_ge(P.sem(tag + "r2"), r20 + t - 1)
                P.ti(sy.dma_start(out=buf[:, :], in_=src_dram[t * TILE:(t + 1) * TILE, :]), ld, 16)

        @block.vector
        def _(v):
            step = P.stepper(v, tag + "vch")
            v.wait_ge(P.sem("const"), CONST_TOTAL)
            for t in range(NT):
                buf = xa if t % 2 == 0 else xb
                v.wait_ge(P.sem(ld), ld0 + (t + 1) * 16)
                if t >= 1:
                    v.wait_ge(P.sem(cp), cp0 + t)
                P.ti(v.tensor_reduce(out=stat[:, 0:1], in_=buf[:, :], axis=AX.X, op=ALU.add), dv)
                if r2_dram is not None:
                    hbuf = hba if t % 2 == 0 else hbb
                    v.wait_ge(P.sem(cp), cp0 + t + 1)
                    if t >= 2:
                        v.wait_ge(P.sem(tag + "r2s"), (t - 1) * 16)
                    step(v.tensor_tensor(out=ew_all[:, (t % 2) * D:(t % 2 + 1) * D],
                                         in0=buf[:, :], in1=hbuf[:, :], op=ALU.add))
                    P.ti(v.engine_nop(), tag + "r2")

        @block.scalar
        def _(sc):
            sc.wait_ge(P.sem("const"), CONST_TOTAL)
            step = P.stepper(sc, tag + "ch")
            for t in range(NT):
                buf = xa if t % 2 == 0 else xb
                hbuf = hba if t % 2 == 0 else hbb
                sc.wait_ge(P.sem(ld), ld0 + (t + 1) * 16)
                step(sc.activation(out=xg[:, 0:D], in_=buf[:, :], func=AF.Square,
                                   accum_out=stat[:, 2:3]))
                sc.wait_ge(P.sem(dv), dv0 + t + 1)
                step(sc.mul(out=stat[:, 1:2], in_=stat[:, 0:1], mul=1.0 / D))
                step(sc.mul(out=stat[:, 3:4], in_=stat[:, 1:2], mul=stat[:, 1:2]))
                step(sc.mul(out=stat[:, 4:5], in_=stat[:, 3:4], mul=-1.0))
                step(sc.activation(out=stat[:, 5:6], in_=stat[:, 2:3], func=AF.Identity,
                                   scale=1.0 / D, bias=stat[:, 4:5]))
                step(sc.activation(out=stat[:, 5:6], in_=stat[:, 5:6], func=AF.Ln, bias=1e-5))
                step(sc.activation(out=stat[:, 5:6], in_=stat[:, 5:6], func=AF.Exp, scale=-0.5))
                step(sc.mul(out=stat[:, 6:7], in_=stat[:, 1:2], mul=-1.0))
                step(sc.mul(out=stat[:, 6:7], in_=stat[:, 6:7], mul=stat[:, 5:6]))
                if t >= 2:
                    sc.wait_ge(P.sem(st), st0 + (t - 1) * 16)
                if r2_dram is not None and t >= 2:
                    sc.wait_ge(P.sem(tag + "r2"), r20 + t - 1)
                P.ti(sc.activation(out=hbuf[:, :], in_=buf[:, :], func=AF.Identity,
                                   bias=stat[:, 6:7], scale=stat[:, 5:6]), cp)
                sc.wait_ge(P.sem(cp), cp0 + t + 1)
                P.ti(sc.dma_start(out=dst_dram[t * TILE:(t + 1) * TILE, :], in_=hbuf[:, :]), st, 16)
                if r2_dram is not None:
                    sc.wait_ge(P.sem(tag + "r2"), r20 + t + 1)
                    P.ti(sc.dma_start(out=r2_dram[t * TILE:(t + 1) * TILE, :],
                                      in_=ew_all[:, (t % 2) * D:(t % 2 + 1) * D]), tag + "r2s", 16)

        return st

    def emit_routing(ln_st, tag, route_pass=3):
        rt = tag
        NGRP = NT // 4
        ln_st0 = P.val(ln_st) - NT * 16
        tp0 = P.val(rt + "tp")
        mm0 = P.val(rt + "mm")
        sg0 = P.val(rt + "sg")
        tk0 = P.val(rt + "tk")
        pm0 = P.val(rt + "pm")
        p20 = P.val(rt + "p2")
        vc0 = P.val(rt + "vc")

        @block.scalar
        def _(sc):
            for g in range(NGRP + 1):
                if g < NGRP:
                    hbuf = hlnT[:, (g % 2) * 4096:(g % 2 + 1) * 4096]
                    sc.wait_ge(P.sem(ln_st), ln_st0 + (g * 4 + 4) * 16)
                    if g >= 2:
                        sc.wait_ge(P.sem(rt + "mm"), mm0 + (g - 1) * 32)
                    for c in range(8):
                        P.ti(sc.dma_start(out=hbuf[:, c * 512:(c + 1) * 512],
                                          in_=hln_d[g * 512:(g + 1) * 512, c * TILE:(c + 1) * TILE],
                                          transpose=True), rt + "tp", 16)
                if g >= 1:
                    gg = g - 1
                    sb = scr[:, (gg % 2) * 256:(gg % 2 + 1) * 256]
                    if gg >= 2:
                        sc.wait_ge(P.sem(rt + "tk"), tk0 + (gg - 1) * 4)
                    for tt in range(4):
                        pb = ps[tt]
                        sc.wait_ge(P.sem(rt + "mm"), mm0 + gg * 32 + (tt + 1) * 8)
                        P.ti(sc.activation(out=sb[:, tt * 64:(tt + 1) * 64], in_=pb[:, 0:E],
                                           func=AF.Tanh, scale=0.5), rt + "sg")

        @block.tensor
        def _(te):
            te.wait_ge(P.sem("rcp"), 1)
            for g in range(NGRP):
                hbuf = hlnT[:, (g % 2) * 4096:(g % 2 + 1) * 4096]
                te.wait_ge(P.sem(rt + "tp"), tp0 + (g + 1) * 128)
                if g >= 1:
                    te.wait_ge(P.sem(rt + "sg"), sg0 + g * 4)
                for tt in range(4):
                    pb = ps[tt]
                    for c in range(8):
                        mm = te.matmul(pb[:, 0:E],
                                       hbuf[:, c * 512 + tt * TILE:c * 512 + (tt + 1) * TILE],
                                       router_sb[:, c * E:(c + 1) * E],
                                       start=(c == 0), stop=(c == 7))
                    P.ti(mm, rt + "mm", 8)


        @block.vector
        def _(v):
            step = P.stepper(v, rt + "vch")
            for g in range(NGRP):
                sb = scr[:, (g % 2) * 256:(g % 2 + 1) * 256]
                v.wait_ge(P.sem(rt + "sg"), sg0 + g * 4 + 4)
                step(v.tensor_scalar(out=sb, in0=sb, scalar1=0.5, scalar2=0.5,
                                     op0=ALU.mult, op1=ALU.add))
                step(v.tensor_reduce(out=gsr[:, :], in_=sb.rearrange("p (a b) -> p a b", b=8),
                                     axis=AX.X, op=ALU.max))
                P.ti(v.engine_nop(), rt + "gm")
                for tt in range(4):
                    t = g * 4 + tt
                    gs = gsr[:, tt * 8:(tt + 1) * 8]
                    step(v.max(out=tk[:, 0:8], in_=gs))
                    step(v.tensor_copy(out=fill_neg[:, 0:4], in_=tk[:, 0:4]))
                    step(v.match_replace(out=gs, in_to_replace=fill_neg[:, :], in_values=gs,
                                         imm_value=-1e30))
                    step(v.tensor_single_scalar(out=gs, in_=gs, scalar=-1e29, op=ALU.is_lt))
                    step(v.tensor_tensor(out=msk[:, :], in0=sb[:, tt * 64:(tt + 1) * 64],
                                         in1=gs.rearrange("p (a b) -> p a b", b=1).broadcast_to((TILE, 8, 8)),
                                         op=ALU.mult))
                    step(v.max(out=tk[:, 8:16], in_=msk[:, :]))
                    step(v.tensor_copy(out=sqbuf[:, 0:64], in_=msk[:, :]))
                    step(v.match_replace(out=sqbuf[:, 0:64], in_to_replace=tk[:, 8:16],
                                         in_values=sqbuf[:, 0:64], imm_value=-1e30))
                    step(v.tensor_single_scalar(out=sqbuf[:, 0:64], in_=sqbuf[:, 0:64],
                                                scalar=-1e29, op=ALU.is_lt))
                    step(v.tensor_copy(out=a_all[:, t * E:(t + 1) * E], in_=sqbuf[:, 0:64]))
                    step(v.scalar_tensor_tensor(out=msk[:, :], in0=msk[:, :], scalar=0.5,
                                                in1=iota_rev[:, :], op0=ALU.mult, op1=ALU.add))
                    step(v.tensor_tensor(out=msk[:, :], in0=msk[:, :], in1=sqbuf[:, 0:64],
                                         op=ALU.mult))
                    step(v.max(out=tk[:, 8:16], in_=msk[:, :]))
                    ew = ew_all[:, t * 16:t * 16 + 16]
                    step(v.max_index(out=sidx_all[:, t * 8:(t + 1) * 8], in_max=tk[:, 8:16],
                                     in_values=msk[:, :]))
                    step(v.tensor_copy(out=ew[:, 0:8], in_=sidx_all[:, t * 8:(t + 1) * 8]))
                    step(v.tensor_tensor(out=tk[:, 24:32], in0=tk[:, 8:16], in1=ew[:, 0:8],
                                         op=ALU.add))
                    step(v.tensor_single_scalar(out=tk[:, 24:32], in_=tk[:, 24:32], scalar=-64.0,
                                                op=ALU.add))
                    step(v.tensor_reduce(out=tk[:, 32:33], in_=tk[:, 24:32], axis=AX.X, op=ALU.add))
                    step(v.tensor_scalar_max(out=tk[:, 33:34], in0=tk[:, 32:33], scalar1=5e-10))
                    step(v.reciprocal(out=tk[:, 34:35], in_=tk[:, 33:34]))
                    P.ti(v.tensor_scalar(out=ew[:, 8:16], in0=tk[:, 24:32],
                                         scalar1=tk[:, 34:35], scalar2=SCALE,
                                         op0=ALU.mult, op1=ALU.mult), rt + "tk")
                    v.wait_ge(P.sem(rt + "tk"), P.val(rt + "tk"))


        def emit_dump():
            if "routeS" not in dbg:
                return
            @block.scalar
            def _(sc):
                sc.wait_ge(P.sem(rt + "tk"), tk0 + NT)
                if route_pass >= 2:
                    sc.wait_ge(P.sem(rt + "p2"), p20 + NT)
                for t in range(NT):
                    ew = ew_all[:, t * 16:t * 16 + 16]
                    P.ti(sc.dma_start(out=dbg["routeS"][t, :, 0:8], in_=ew[:, 0:8]), rt + "dbst", 16)
                    P.ti(sc.dma_start(out=dbg["routeS"][t, :, 8:16], in_=ew[:, 8:16]), rt + "dbst", 16)
                    if route_pass >= 2:
                        P.ti(sc.dma_start(out=dbg["routeS"][t, :, 16:24],
                                          in_=rs_all[:, t * 24:t * 24 + 8]), rt + "dbst", 16)

        if route_pass < 2:
            emit_dump()
            return rt
        @block.vector
        def _(v):
            v.wait_ge(P.sem("const"), CONST_TOTAL)
            P.ti(v.memset(runrow[:, :], 0.0), rt + "rr")

        rr0 = P.val(rt + "rr")

        @block.tensor
        def _(te):
            te.wait_ge(P.sem(rt + "rr"), rr0)
            for t in range(NT):
                pb = ps[4 + (t % 2)]
                te.wait_ge(P.sem(rt + "tk"), tk0 + t + 1)
                if t >= 2:
                    te.wait_ge(P.sem(rt + "p2"), p20 + t - 1)
                if t >= 1:
                    te.wait_ge(P.sem(rt + "rr"), rr0 + t)
                te.matmul(pb[:, 0:E], su_bf[:, :], a_all[:, t * E:(t + 1) * E],
                          start=True, stop=False)
                te.matmul(pb[:, 0:E], ones1[:, :], runrow[:, :], start=False, stop=True)
                mm3 = te.matmul(ps[6 + (t % 2)][0:1, (t % 4) * E:(t % 4) * E + E],
                                onescol_b[:, :], a_all[:, t * E:(t + 1) * E],
                                start=True, stop=True)
                P.ti(mm3, rt + "pm")

        @block.vector
        def _(v):
            step = P.stepper(v, rt + "vch")
            for t in range(NT):
                pb = ps[4 + (t % 2)]
                v.wait_ge(P.sem(rt + "pm"), pm0 + t + 1)
                step(v.tensor_tensor(out=runrow[:, :], in0=runrow[:, :],
                                     in1=ps[6 + (t % 2)][0:1, (t % 4) * E:(t % 4) * E + E],
                                     op=ALU.add))
                P.ti(v.engine_nop(), rt + "rr")
                step(v.tensor_tensor(out=sqbuf[:, 0:64], in0=pb[:, 0:E], in1=iota_enc[:, :],
                                     op=ALU.add))
                step(v.tensor_tensor(out=sqbuf[:, 0:64], in0=sqbuf[:, 0:64],
                                     in1=a_all[:, t * E:(t + 1) * E], op=ALU.mult))
                step(v.max(out=tk[:, 0:8], in_=sqbuf[:, 0:64]))
                ew = ew_all[:, t * 16:t * 16 + 16]
                step(v.tensor_single_scalar(out=tk[:, 8:16], in_=ew[:, 0:8], scalar=-64.0,
                                            op=ALU.add))
                step(v.tensor_scalar(out=tk[:, 8:16], in0=tk[:, 8:16], scalar1=32768.0,
                                     scalar2=None, op0=ALU.mult))
                step(v.tensor_tensor(out=tk[:, 8:16], in0=tk[:, 8:16], in1=tk[:, 0:8],
                                     op=ALU.add))
                step(v.tensor_single_scalar(out=tk[:, 16:24], in_=tk[:, 8:16], scalar=float(CAP),
                                            op=ALU.is_lt))
                step(v.tensor_copy(out=valid_i[:, :], in_=tk[:, 16:24]))
                step(v.tensor_scalar(out=tk[:, 24:32], in0=ew[:, 0:8], scalar1=512.0,
                                     scalar2=None, op0=ALU.mult))
                step(v.tensor_tensor(out=tk[:, 24:32], in0=tk[:, 24:32], in1=tk[:, 8:16],
                                     op=ALU.add))
                step(v.tensor_single_scalar(out=tk[:, 48:56], in_=tk[:, 8:16],
                                            scalar=128.0, op=ALU.is_ge))
                step(v.tensor_single_scalar(out=tk[:, 56:64], in_=tk[:, 8:16],
                                            scalar=256.0, op=ALU.is_ge))
                step(v.tensor_tensor(out=tk[:, 48:56], in0=tk[:, 48:56], in1=tk[:, 56:64],
                                     op=ALU.add))
                step(v.tensor_single_scalar(out=tk[:, 56:64], in_=tk[:, 8:16],
                                            scalar=384.0, op=ALU.is_ge))
                step(v.tensor_tensor(out=tk[:, 48:56], in0=tk[:, 48:56], in1=tk[:, 56:64],
                                     op=ALU.add))
                step(v.tensor_scalar(out=tk[:, 56:64], in0=tk[:, 48:56], scalar1=-128.0,
                                     scalar2=None, op0=ALU.mult))
                step(v.tensor_tensor(out=tk[:, 56:64], in0=tk[:, 56:64], in1=tk[:, 8:16],
                                     op=ALU.add))
                step(v.tensor_scalar(out=tk[:, 56:64], in0=tk[:, 56:64], scalar1=256.0,
                                     scalar2=None, op0=ALU.mult))
                step(v.tensor_scalar(out=tk[:, 48:56], in0=tk[:, 48:56], scalar1=1.0,
                                     scalar2=None, op0=ALU.mult))
                step(v.scalar_tensor_tensor(out=tk[:, 48:56], in0=ew[:, 0:8], scalar=4.0,
                                            in1=tk[:, 48:56], op0=ALU.mult, op1=ALU.add))
                step(v.tensor_tensor(out=tk[:, 48:56], in0=tk[:, 48:56], in1=tk[:, 56:64],
                                     op=ALU.add))
                step(v.tensor_single_scalar(out=tk[:, 32:40], in_=oob_tab[:, :],
                                            scalar=float(t * 1024), op=ALU.add))
                step(v.select(out=tk[:, 32:40], mask=valid_i[:, :], on_true=tk[:, 48:56],
                              on_false=tk[:, 32:40]))
                step(v.tensor_copy(out=sidx_all[:, t * 8:(t + 1) * 8], in_=tk[:, 32:40]))
                step(v.select(out=tk[:, 40:48], mask=valid_i[:, :], on_true=tk[:, 24:32],
                              on_false=dummy_tab[:, 0:1].broadcast_to((TILE, 8))))
                step(v.tensor_copy(out=cmb_all[:, t * 8:(t + 1) * 8], in_=tk[:, 40:48]))
                pr = pairs_all[:, t * 16:(t + 1) * 16].rearrange("p (a b) -> p a b", b=2)
                step(v.tensor_single_scalar(out=tk[:, 56:57], in_=iota_p[:, :],
                                            scalar=float(t * TILE + 1), op=ALU.add))
                step(v.tensor_copy(out=pr[:, :, 0:1],
                                   in_=tk[:, 56:57].rearrange("p (a b) -> p a b", b=1).broadcast_to((TILE, 8, 1))))
                step(v.tensor_copy(out=pr[:, :, 1:2], in_=ew[:, 8:16].rearrange("p (a b) -> p a b", b=1)))
                step(v.tensor_reduce(out=vsum[:, t:t + 1], in_=tk[:, 16:24], axis=AX.X,
                                     op=ALU.add))
                if "routeS" in dbg:
                    step(v.tensor_copy(out=rs_all[:, t * 24 + 0:t * 24 + 8], in_=tk[:, 8:16]))
                    step(v.tensor_copy(out=rs_all[:, t * 24 + 8:t * 24 + 16], in_=tk[:, 16:24]))
                    step(v.tensor_copy(out=rs_all[:, t * 24 + 16:t * 24 + 24], in_=tk[:, 24:32]))
                P.ti(v.engine_nop(), rt + "p2")

        @block.tensor
        def _(te):
            pb = ps[2]
            te.wait_ge(P.sem(rt + "sg"), sg0 + NT)
            for t in range(NT):
                te.wait_ge(P.sem(rt + "p2"), p20 + t + 1)
                mmv = te.matmul(pb[0:1, t:t + 1], conef[:, :], vsum[:, t:t + 1],
                                start=True, stop=True)
            P.ti(mmv, rt + "vc")

        @block.vector
        def _(v):
            step = P.stepper(v, rt + "vch")
            v.wait_ge(P.sem(rt + "vc"), P.val(rt + "vc"))
            pb = ps[2]
            step(v.tensor_copy(out=kflag[:, :], in_=pb[0:1, 0:NT]))
            P.ti(v.tensor_copy(out=kflag_i[:, :], in_=kflag[:, :]), rt + "kf")


        if route_pass < 3:
            emit_dump()
            return rt
        @block.gpsimd
        def _(gp):
            gp.wait_ge(P.sem("pz"), P.val("pz"))
            gp.wait_ge(P.sem(rt + "kf"), P.val(rt + "kf"))
            with gp.register(uname(rt + "bnd")) as rb, gp.register(uname(rt + "rf")) as rf:
                gp.reg_mov(rb, NSLOT - 1)
                for t in range(NT):
                    gp.reg_load(rf, kflag_i[0:1, t:t + 1])
                    with gp.If_eq(rf, 0):
                        gp.engine_nop().then_inc(P.sem(rt + "sc"), 16 * KTOP, skip_validation=True)
                    with gp.Else():
                        for k in range(KTOP):
                            gp.indirect_dma_start(
                                out=inv_d.ap(),
                                out_offset=IndirectOffsetOnAxis(sidx_all[:, t * 8 + k:t * 8 + k + 1], 0),
                                in_=pairs_all[:, t * 16 + 2 * k:t * 16 + 2 * k + 2],
                                in_offset=None,
                                bounds_check=rb, oob_is_err=False,
                            ).then_inc(P.sem(rt + "sc"), 16, skip_validation=True)
                    P.v[rt + "sc"] = P.val(rt + "sc") + 16 * KTOP
            P.ti(gp.engine_nop(), rt + "scdone")

        emit_dump()
        return rt

    def emit_expert(rt, hsrc_dram, expert_pass=9):
        ex = rt + "x"
        scd = P.val(rt + "scdone")
        xg0, xs0, xt0, wl0 = P.val(ex + "g"), P.val(ex + "s"), P.val(ex + "t"), P.val(ex + "w")
        ma0, si0, av0, mb0 = P.val(ex + "ma"), P.val(ex + "si"), P.val(ex + "av"), P.val(ex + "mb")
        ye0, ys0 = P.val(ex + "ye"), P.val(ex + "ys")

        @block.sync
        def _(sy):
            sy.wait_ge(P.sem(rt + "scdone"), scd)
            P.ti(sy.dma_start(out=inv_sb[:, :],
                              in_=inv_d.ap().rearrange("(p tt) c -> p (tt c)", p=TILE)), ex + "il", 16)

        @block.vector
        def _(v):
            step = P.stepper(v, ex + "vch")
            v.wait_ge(P.sem(ex + "il"), P.val(ex + "il"))
            iv = inv_sb[:, :].rearrange("p (tt c) -> p tt c", c=2)
            step(v.tensor_copy(out=wden[:, :], in_=iv[:, :, 1:2].rearrange("p a b -> p (a b)")))
            step(v.tensor_single_scalar(out=sqbuf[:, 0:NT], in_=iv[:, :, 0:1].rearrange("p a b -> p (a b)"),
                                        scalar=0.0, op=ALU.is_gt))
            step(v.tensor_copy(out=imsk[:, :], in_=sqbuf[:, 0:NT]))
            step(v.tensor_single_scalar(out=sqbuf[:, NT:2 * NT],
                                        in_=iv[:, :, 0:1].rearrange("p a b -> p (a b)"),
                                        scalar=-1.0, op=ALU.add))
            step(v.select(out=sqbuf[:, 2 * NT:3 * NT], mask=imsk[:, :],
                          on_true=sqbuf[:, NT:2 * NT],
                          on_false=dummyT_tab[:, 0:1].broadcast_to((TILE, NT))))
            P.ti(v.tensor_copy(out=idx_all[:, :], in_=sqbuf[:, 2 * NT:3 * NT]), ex + "ip")

        ip_done = P.val(ex + "ip")

        if expert_pass < 2:
            return ex
        @block.gpsimd
        def _(gp):
            gp.wait_ge(P.sem(ex + "ip"), ip_done)
            for e in range(E):
                for g in range(4):
                    st4 = e * 4 + g
                    if st4 >= 2:
                        gp.wait_ge(P.sem(ex + "s"), xs0 + (st4 - 1) * 16)
                    P.ti(gp.indirect_dma_start(
                        out=xg[:, (st4 % 2) * D:(st4 % 2) * D + D], out_offset=None,
                        in_=hsrc_dram.ap(),
                        in_offset=IndirectOffsetOnAxis(idx_all[:, st4:st4 + 1], 0)), ex + "g", 16)

        @block.sync
        def _(sy):
            for e in range(E):
                if e >= 2 and expert_pass >= 4:
                    sy.wait_ge(P.sem(ex + "ma"), ma0 + (e - 1) * 8)
                    if expert_pass >= 5:
                        sy.wait_ge(P.sem(ex + "mb"), mb0 + (e - 1) * 8)
                par = (e % 2) * 4096
                P.ti(sy.dma_start(out=wg_sb[:, par:par + 4096].rearrange("p (c i) -> p c i", i=II),
                                  in_=gbf_d[e].rearrange("(c p) i -> p c i", p=TILE)), ex + "w", 16)
                P.ti(sy.dma_start(out=wu_sb[:, par:par + 4096].rearrange("p (c i) -> p c i", i=II),
                                  in_=ubf_d[e].rearrange("(c p) i -> p c i", p=TILE)), ex + "w", 16)
                P.ti(sy.dma_start(out=wd_sb[:, par:par + 4096].rearrange("p (c i) -> p c i", i=D),
                                  in_=dbf_d[e].rearrange("(c p) i -> p c i", p=TILE)), ex + "w", 16)
                for g in range(4):
                    st4 = e * 4 + g
                    sy.wait_ge(P.sem(ex + "g"), xg0 + (st4 + 1) * 16)
                    P.ti(sy.dma_start(out=xd_d[st4 * TILE:(st4 + 1) * TILE, :],
                                      in_=xg[:, (st4 % 2) * D:(st4 % 2) * D + D]), ex + "s", 16)

        @block.scalar
        def _(sc):
            for e in range(E):
                par = (e % 2) * 4096
                sc.wait_ge(P.sem(ex + "s"), xs0 + (e + 1) * 64)
                if e >= 2 and expert_pass >= 4:
                    sc.wait_ge(P.sem(ex + "ma"), ma0 + (e - 1) * 8)
                for c in range(8):
                    P.ti(sc.dma_start(out=hlnT[:, par + c * 512:par + (c + 1) * 512],
                                      in_=xd_d[e * 512:(e + 1) * 512, c * TILE:(c + 1) * TILE],
                                      transpose=True), ex + "t", 16)
                for it in range(4 if expert_pass >= 4 else 0):
                    sc.wait_ge(P.sem(ex + "ma"), ma0 + e * 8 + 2 * it + 1)
                    if it > 0:
                        sc.wait_ge(P.sem(ex + "av"), av0 + e * 4 + it)
                    P.ti(sc.activation(out=sact[:, :], in_=ps[it][:, :], func=AF.Silu), ex + "si")
                for st_ in range(4 if expert_pass >= 5 else 0):
                    for half in range(2):
                        sc.wait_ge(P.sem(ex + "mb"), mb0 + e * 8 + st_ * 2 + half + 1)
                        if half == 0:
                            prev_store = e * 4 + st_ - 2
                            if prev_store >= 0:
                                sc.wait_ge(P.sem(ex + "ys"), ys0 + (prev_store + 1) * 16)
                        P.ti(sc.activation(out=y_sb[:, (st_ % 2) * D + half * 512:
                                                      (st_ % 2) * D + half * 512 + 512],
                                           in_=ps[st_ + 4 * half][:, :], func=AF.Identity,
                                           scale=wden[:, e * 4 + st_:e * 4 + st_ + 1]), ex + "ye")
                    P.ti(sc.dma_start(out=yd_d[(e * 4 + st_) * TILE:(e * 4 + st_ + 1) * TILE, :],
                                      in_=y_sb[:, (st_ % 2) * D:(st_ % 2) * D + D]), ex + "ys", 16)

        @block.vector
        def _(v):
            step = P.stepper(v, ex + "vch")
            for e in range(E if expert_pass >= 4 else 0):
                for it in range(4):
                    v.wait_ge(P.sem(ex + "si"), si0 + e * 4 + it + 1)
                    v.wait_ge(P.sem(ex + "ma"), ma0 + e * 8 + 2 * it + 2)
                    step(v.tensor_tensor(out=act_sb[:, it * 512:(it + 1) * 512],
                                         in0=sact[:, :], in1=ps[4 + it][:, :], op=ALU.mult))
                    P.ti(v.engine_nop(), ex + "av")

        @block.tensor
        def _(te):
            for e in range(E if expert_pass >= 4 else 0):
                par = (e % 2) * 4096
                te.wait_ge(P.sem(ex + "t"), xt0 + (e + 1) * 128)
                te.wait_ge(P.sem(ex + "w"), wl0 + (e + 1) * 48)
                if e >= 1:
                    if expert_pass >= 5:
                        te.wait_ge(P.sem(ex + "ye"), ye0 + e * 8)
                    te.wait_ge(P.sem(ex + "av"), av0 + e * 4)
                for it in range(4):
                    for c in range(8):
                        mmg = te.matmul(ps[it][:, :],
                                        wg_sb[:, par + c * 512 + it * TILE:par + c * 512 + it * TILE + TILE],
                                        hlnT[:, par + c * 512:par + (c + 1) * 512],
                                        start=(c == 0), stop=(c == 7))
                    P.ti(mmg, ex + "ma")
                    for c in range(8):
                        mmu = te.matmul(ps[4 + it][:, :],
                                        wu_sb[:, par + c * 512 + it * TILE:par + c * 512 + it * TILE + TILE],
                                        hlnT[:, par + c * 512:par + (c + 1) * 512],
                                        start=(c == 0), stop=(c == 7))
                    P.ti(mmu, ex + "ma")
                if expert_pass < 5:
                    continue
                te.wait_ge(P.sem(ex + "av"), av0 + e * 4 + 4)
                for st_ in range(4):
                    te.wait_ge(P.sem(ex + "si"), si0 + e * 4 + 4)
                    for half in range(2):
                        for ic in range(4):
                            mmd = te.matmul(ps[st_ + 4 * half][:, :],
                                            act_sb[:, ic * 512 + st_ * TILE:ic * 512 + st_ * TILE + TILE],
                                            wd_sb[:, par + ic * 1024 + half * 512:par + ic * 1024 + half * 512 + 512],
                                            start=(ic == 0), stop=(ic == 3))
                        P.ti(mmd, ex + "mb")
        return ex

    def emit_shared(rt, ex, xsrc_dram, xdst_dram):
        cmode = combine_mode
        if cmode >= 3:
            @block.vector
            def _(v):
                step = P.stepper(v, rt + "hvch")
                v.wait_ge(P.sem(rt + "kf"), P.val(rt + "kf"))
                step(v.tensor_single_scalar(out=kflag01[:, :], in_=kflag[:, :], scalar=0.0,
                                            op=ALU.is_gt))
                step(v.memset(stg[:, :], 0.0))
                P.ti(v.engine_nop(), rt + "hfg")

            @block.tensor
            def _(te):
                te.wait_ge(P.sem(rt + "hfg"), P.val(rt + "hfg"))
                te.wait_ge(P.sem(rt + "xye"), P.val(rt + "xye"))
                te.wait_ge(P.sem(rt + "xav"), P.val(rt + "xav"))
                mmf = te.matmul(ps[0][:, 0:NT], ones1[:, :], kflag01[:, :], start=True, stop=True)
                P.ti(mmf, rt + "hfm")

            @block.vector
            def _(v):
                step = P.stepper(v, rt + "hvch")
                v.wait_ge(P.sem(rt + "hfm"), P.val(rt + "hfm"))
                step(v.tensor_copy(out=flagcol[:, :], in_=ps[0][:, 0:NT]))
                P.ti(v.engine_nop(), rt + "hfc")
        sh = rt + "h"
        NGRP = NT // 4
        tp0, ma0, si0, av0 = P.val(sh + "tp"), P.val(sh + "ma"), P.val(sh + "si"), P.val(sh + "av")
        mb0, ev0, st0 = P.val(sh + "mb"), P.val(sh + "ev"), P.val(sh + "st")
        cg0, cq0 = P.val(sh + "cg"), P.val(sh + "cq")
        idf0 = P.val(sh + "idf")
        xl0 = P.val(sh + "xl")
        exma, exmb = P.val(ex + "ma"), P.val(ex + "mb")
        exys = P.val(ex + "ys")

        @block.sync
        def _(sy):
            sy.wait_ge(P.sem(ex + "ma"), exma)
            sy.wait_ge(P.sem(ex + "mb"), exmb)
            P.ti(sy.dma_start(out=wg_sb[:, 0:4096].rearrange("p (c i) -> p c i", i=II),
                              in_=sgbf_d.ap().rearrange("(c p) i -> p c i", p=TILE)), sh + "wl", 16)
            P.ti(sy.dma_start(out=wu_sb[:, 0:4096].rearrange("p (c i) -> p c i", i=II),
                              in_=subf_d.ap().rearrange("(c p) i -> p c i", p=TILE)), sh + "wl", 16)
            P.ti(sy.dma_start(out=wd_sb[:, 0:4096].rearrange("p (c i) -> p c i", i=D),
                              in_=sdbf_d.ap().rearrange("(c p) i -> p c i", p=TILE)), sh + "wl", 16)
            for t in range(NT + 4):
                if t < NT:
                    buf = xa if t % 2 == 0 else xb
                    if t >= 2:
                        sy.wait_ge(P.sem(sh + "ev"), ev0 + 2 * (t - 1))
                    P.ti(sy.dma_start(out=buf[:, :], in_=xsrc_dram[t * TILE:(t + 1) * TILE, :]),
                         sh + "xl", 16)
                if t >= 4:
                    ts_ = t - 4
                    tt = ts_ % 4
                    sy.wait_ge(P.sem(sh + "ev"), ev0 + 2 * (ts_ + 1))
                    P.ti(sy.dma_start(out=xdst_dram[ts_ * TILE:(ts_ + 1) * TILE, :],
                                      in_=x1g[:, tt * D:(tt + 1) * D]), sh + "st", 16)

        @block.scalar
        def _(sc):
            sc.wait_ge(P.sem(ex + "ma"), exma)
            for g in range(NGRP):
                par = (g % 2) * 4096
                if g >= 2:
                    sc.wait_ge(P.sem(sh + "ma"), ma0 + (g - 1) * 8)
                for c in range(8):
                    P.ti(sc.dma_start(out=hlnT[:, par + c * 512:par + (c + 1) * 512],
                                      in_=hln_d[g * 512:(g + 1) * 512, c * TILE:(c + 1) * TILE],
                                      transpose=True), sh + "tp", 16)
                for it in range(4):
                    sc.wait_ge(P.sem(sh + "ma"), ma0 + g * 8 + 2 * it + 1)
                    if it > 0:
                        sc.wait_ge(P.sem(sh + "av"), av0 + g * 4 + it)
                    P.ti(sc.activation(out=sact[:, :], in_=ps[it][:, :], func=AF.Silu), sh + "si")

        @block.vector
        def _(v):
            step = P.stepper(v, sh + "vch")
            for g in range(NGRP):
                for it in range(4):
                    v.wait_ge(P.sem(sh + "si"), si0 + g * 4 + it + 1)
                    v.wait_ge(P.sem(sh + "ma"), ma0 + g * 8 + 2 * it + 2)
                    step(v.tensor_tensor(out=act_sb[:, it * 512:(it + 1) * 512],
                                         in0=sact[:, :], in1=ps[4 + it][:, :], op=ALU.mult))
                    P.ti(v.engine_nop(), sh + "av")
                for tt in range(4):
                    t = g * 4 + tt
                    buf = xa if t % 2 == 0 else xb
                    if cmode >= 3:
                        tf = t
                        if tf < NT:
                            if tf >= 2:
                                v.wait_ge(P.sem(sh + "cq"), cq0 + tf - 1)
                            step(v.tensor_scalar_mul(out=identX[:, (tf % 2) * TILE:(tf % 2 + 1) * TILE],
                                                     in0=ident_b[:, :],
                                                     scalar1=flagcol[:, tf:tf + 1]))
                            P.ti(v.engine_nop(), sh + "idf")
                    v.wait_ge(P.sem(sh + "cq"), cq0 + t + 1)
                    for half in range(2):
                        v.wait_ge(P.sem(sh + "mb"), mb0 + g * 8 + tt * 2 + half + 1)
                        v.wait_ge(P.sem(sh + "xl"), xl0 + (t + 1) * 16)
                        if g >= 1:
                            v.wait_ge(P.sem(sh + "st"), st0 + (g - 1) * 64 + (tt + 1) * 16)
                        step(v.scalar_tensor_tensor(out=x1g[:, tt * D + half * 512:tt * D + half * 512 + 512],
                                                    in0=ps[tt + 4 * half][:, :], scalar=0.5,
                                                    in1=buf[:, half * 512:half * 512 + 512],
                                                    op0=ALU.mult, op1=ALU.add))
                        P.ti(v.engine_nop(), sh + "ev")

        @block.tensor
        def _(te):
            te.wait_ge(P.sem(sh + "wl"), P.val(sh + "wl"))
            te.wait_ge(P.sem(ex + "ye"), P.val(ex + "ye"))
            te.wait_ge(P.sem(ex + "av"), P.val(ex + "av"))
            for g in range(NGRP):
                par = (g % 2) * 4096
                te.wait_ge(P.sem(sh + "tp"), tp0 + (g + 1) * 128)
                if g >= 1:
                    te.wait_ge(P.sem(sh + "ev"), ev0 + g * 8)
                    te.wait_ge(P.sem(sh + "av"), av0 + g * 4)
                for it in range(4):
                    for c in range(8):
                        mmg = te.matmul(ps[it][:, :],
                                        wg_sb[:, c * 512 + it * TILE:c * 512 + it * TILE + TILE],
                                        hlnT[:, par + c * 512:par + (c + 1) * 512],
                                        start=(c == 0), stop=(c == 7))
                    P.ti(mmg, sh + "ma")
                    for c in range(8):
                        mmu = te.matmul(ps[4 + it][:, :],
                                        wu_sb[:, c * 512 + it * TILE:c * 512 + it * TILE + TILE],
                                        hlnT[:, par + c * 512:par + (c + 1) * 512],
                                        start=(c == 0), stop=(c == 7))
                    P.ti(mmu, sh + "ma")
                te.wait_ge(P.sem(sh + "av"), av0 + g * 4 + 4)
                te.wait_ge(P.sem(sh + "si"), si0 + g * 4 + 4)
                for tt in range(4):
                    t = g * 4 + tt
                    spar = (t % 2) * KTOP * D
                    for half in range(2):
                        for ic in range(4):
                            mmd = te.matmul(ps[tt + 4 * half][:, :],
                                            act_sb[:, ic * 512 + tt * TILE:ic * 512 + tt * TILE + TILE],
                                            wd_sb[:, ic * 1024 + half * 512:ic * 1024 + half * 512 + 512],
                                            start=(ic == 0), stop=(ic == 3))
                        P.ti(mmd, sh + "mb")
                    te.wait_ge(P.sem(sh + "cg"), cg0 + (t + 1) * 16 * KTOP)
                    if cmode == 0:
                        P.ti(te.nop(), sh + "cq")
                    elif cmode >= 3:
                        te.wait_ge(P.sem(sh + "idf"), P.val(sh + "idf") - NT + t + 1
                                   if False else idf0 + t + 1)
                        for k in range(KTOP):
                            for half in range(2):
                                mi = te.matmul(ps[tt + 4 * half][:, :],
                                               identX[:, (t % 2) * TILE:(t % 2 + 1) * TILE],
                                               stg[:, spar + k * D + half * 512:
                                                   spar + k * D + half * 512 + 512],
                                               start=False, stop=(k == KTOP - 1))
                        P.ti(mi, sh + "cq")
                    elif cmode == 1:
                        for k in range(KTOP):
                            for half in range(2):
                                mi = te.matmul(ps[tt + 4 * half][:, :], ident_b[:, :],
                                               stg[:, spar + k * D + half * 512:
                                                   spar + k * D + half * 512 + 512],
                                               start=False, stop=(k == KTOP - 1))
                        P.ti(mi, sh + "cq")
                    else:
                        with te.register(uname(sh + f"cr{t}")) as rt_:
                            te.reg_load(rt_, kflag_i[0:1, t:t + 1])
                            with te.If_eq(rt_, 0):
                                te.nop().then_inc(P.sem(sh + "cq"), 1, skip_validation=True)
                            with te.Else():
                                for k in range(KTOP):
                                    for half in range(2):
                                        mi = te.matmul(ps[tt + 4 * half][:, :], ident_b[:, :],
                                                       stg[:, spar + k * D + half * 512:
                                                           spar + k * D + half * 512 + 512],
                                                       start=False, stop=(k == KTOP - 1))
                                mi.then_inc(P.sem(sh + "cq"), 1, skip_validation=True)
                        P.v[sh + "cq"] = P.val(sh + "cq") + 1

        @block.gpsimd
        def _(gp):
            gp.wait_ge(P.sem(ex + "ys"), exys)
            if cmode >= 3:
                gp.wait_ge(P.sem(rt + "hfg"), P.val(rt + "hfg"))
            with gp.register(uname(sh + "rf")) as rf:
                for t in range(NT):
                    spar = (t % 2) * KTOP * D
                    if cmode == 0:
                        P.ti(gp.engine_nop(), sh + "cg", 16 * KTOP)
                        continue
                    if t >= 2:
                        gp.wait_ge(P.sem(sh + "cq"), cq0 + t - 1)
                    if cmode == 1:
                        for k in range(KTOP):
                            P.ti(gp.indirect_dma_start(
                                out=stg[:, spar + k * D:spar + (k + 1) * D], out_offset=None,
                                in_=yd_d.ap(),
                                in_offset=IndirectOffsetOnAxis(cmb_all[:, t * 8 + k:t * 8 + k + 1], 0)),
                                sh + "cg", 16)
                        continue
                    gp.reg_load(rf, kflag_i[0:1, t:t + 1])
                    with gp.If_eq(rf, 0):
                        gp.engine_nop().then_inc(P.sem(sh + "cg"), 16 * KTOP, skip_validation=True)
                    with gp.Else():
                        for k in range(KTOP):
                            gp.indirect_dma_start(
                                out=stg[:, spar + k * D:spar + (k + 1) * D], out_offset=None,
                                in_=yd_d.ap(),
                                in_offset=IndirectOffsetOnAxis(cmb_all[:, t * 8 + k:t * 8 + k + 1], 0),
                            ).then_inc(P.sem(sh + "cg"), 16, skip_validation=True)
                    P.v[sh + "cg"] = P.val(sh + "cg") + 16 * KTOP

        return sh


    def phase_barrier():
        snap = [(n, P.v[n]) for n in list(P.v.keys()) if P.v[n] > 0]
        for sec in (block.sync, block.scalar, block.vector, block.tensor, block.gpsimd):
            @sec
            def _(eng, snap=snap):
                for n, c in snap:
                    eng.wait_ge(P.sem(n), c)

    def emit_attn():
        NGRP = NT // 4
        NTB_ = NT // NB
        r2buf = pairs_all
        x2sb = ew_all
        ttab = scr
        wqkv_sb = xg
        tmpo = sqbuf

        @block.sync
        def _(sy):
            sy.wait_ge(P.sem("wcast"), P.val("wcast"))
            P.ti(sy.dma_start(out=wqkv_sb[:, 0:1536].rearrange("p (c w) -> p c w", w=192),
                              in_=wqkv_bf_d.ap().rearrange("(c p) w -> p c w", p=TILE)), "atwl", 16)
            P.ti(sy.dma_start(out=woT_sb[:, :], in_=wo_bf_d[:, :]), "atwl", 16)
            P.ti(sy.dma_start(out=bqkv_sb[:, :], in_=bqkv_in[:, :]), "atwl", 16)
            P.ti(sy.dma_start(out=bo_sb[:, :], in_=bo_in[:, :]), "atwl", 16)
            P.ti(sy.dma_start(out=drow_sb[0:1, 0:DS], in_=decay_in[:, :]), "atwl", 16)
            sy.wait_ge(P.sem("atprA"), 3)
            P.ti(sy.dma_start(out=misc_d.ap().rearrange("e one -> one e"),
                              in_=drow_sb[0:1, 2 * DS:3 * DS]), "atms", 16)
            sy.wait_ge(P.sem("atms"), 16)
            P.ti(sy.dma_start(out=lnd_col[:, :], in_=misc_d.ap()), "atms", 16)
            for t in range(NT + 2):
                if t < NT:
                    if t >= 4:
                        sy.wait_ge(P.sem("atx2"), t - 3)
                    P.ti(sy.dma_start(out=r2buf[:, (t % 4) * D:(t % 4 + 1) * D],
                                      in_=r2_d[t * TILE:(t + 1) * TILE, :]), "atr2l", 16)
                if t >= 2:
                    tau = t - 2
                    sy.wait_ge(P.sem("atx2h"), tau + 1)
                    sy.wait_ge(P.sem("atx2"), tau + 1)
                    P.ti(sy.dma_start(out=x2_d[tau * TILE:(tau + 1) * TILE, :],
                                      in_=x2sb[:, (tau % 2) * D:(tau % 2 + 1) * D]), "atst", 16)

        @block.gpsimd
        def _(gp):
            P.ti(gp.iota(exprow[:, 0:TILE], pattern=[[0, 4], [1, 32]], base=-16,
                         channel_multiplier=0, allow_small_or_imprecise_dtypes=True), "atpi")
            P.ti(gp.iota(exprow[:, TILE:2 * TILE], pattern=[[0, 4], [1, 32]], base=0,
                         channel_multiplier=0, allow_small_or_imprecise_dtypes=True), "atpi")
            P.ti(gp.iota(exprow[:, 2 * TILE:3 * TILE], pattern=[[1, TILE]], base=-64,
                         channel_multiplier=0, allow_small_or_imprecise_dtypes=True), "atpi")
            P.ti(gp.iota(exprow[:, 3 * TILE:4 * TILE], pattern=[[1, TILE]], base=1,
                         channel_multiplier=0, allow_small_or_imprecise_dtypes=True), "atpi")
            P.ti(gp.iota(ttab[:, 0:TILE], pattern=[[1, TILE]], base=0,
                         channel_multiplier=0, allow_small_or_imprecise_dtypes=True), "atpi")
            P.ti(gp.iota(ttab[:, TILE:2 * TILE], pattern=[[1, 4], [0, 32]], base=0,
                         channel_multiplier=0, allow_small_or_imprecise_dtypes=True), "atpi")
            P.ti(gp.iota(ttab[:, 2 * TILE:3 * TILE], pattern=[[1, 2], [0, 64]], base=0,
                         channel_multiplier=0, allow_small_or_imprecise_dtypes=True), "atpi")
            P.ti(gp.iota(ttab[:, 3 * TILE:4 * TILE], pattern=[[0, 2], [1, 64]], base=0,
                         channel_multiplier=0, allow_small_or_imprecise_dtypes=True), "atpi")

        @block.scalar
        def _(sc):
            stepA = P.stepper(sc, "atprA")
            sc.wait_ge(P.sem("atwl"), 16 * 5)
            stepA(sc.activation(out=drow_sb[0:1, DS:2 * DS], in_=drow_sb[0:1, 0:DS],
                                func=AF.Tanh, scale=0.5, bias=0.0))
            stepA(sc.activation(out=drow_sb[0:1, DS:2 * DS], in_=drow_sb[0:1, DS:2 * DS],
                                func=AF.Identity, scale=0.5, bias=0.5))
            stepA(sc.activation(out=drow_sb[0:1, 2 * DS:3 * DS], in_=drow_sb[0:1, DS:2 * DS],
                                func=AF.Ln, bias=0.0))
            stepA(sc.activation(out=drow_sb[0:1, 3 * DS:4 * DS], in_=drow_sb[0:1, 2 * DS:3 * DS],
                                func=AF.Exp, scale=float(TILE), bias=0.0))
            sc.wait_ge(P.sem("atdc"), 1)
            P.ti(sc.activation(out=dcolx[:, :], in_=dcolx[:, :], func=AF.Exp, bias=0.0), "attab")
            sc.wait_ge(P.sem("atdc"), 2)
            P.ti(sc.activation(out=drowt[:, :], in_=drowt[:, :], func=AF.Exp, bias=0.0), "attab")
            for c in range(8):
                P.ti(sc.dma_start(out=hlnT[:, c * 512:(c + 1) * 512],
                                  in_=hln_d[0:512, c * TILE:(c + 1) * TILE],
                                  transpose=True), "attp", 16)
            for t in range(NT + 2):
                if (t + 3) % 4 == 0:
                    g = (t + 3) // 4
                    if g < NGRP:
                        if g >= 2:
                            sc.wait_ge(P.sem("atqk"), 4 * g - 4)
                        gpar = (g % 2) * 4096
                        for c in range(8):
                            P.ti(sc.dma_start(out=hlnT[:, gpar + c * 512:gpar + (c + 1) * 512],
                                              in_=hln_d[g * 512:(g + 1) * 512, c * TILE:(c + 1) * TILE],
                                              transpose=True), "attp", 16)
                for tq in ([0, 1] if t == 0 else [t + 1]):
                    if tq < NT:
                        sc.wait_ge(P.sem("atqk"), tq + 1)
                        if tq >= 3:
                            sc.wait_ge(P.sem("atsa"), tq - 2)
                        pqq = ps[6 + tq % 2]
                        sl = (tq % 3) * TILE
                        P.ti(sc.activation(out=qT_bf[:, sl:sl + TILE], in_=pqq[0:DS, 0:TILE],
                                           func=AF.Identity, bias=0.0), "atev")
                        P.ti(sc.activation(out=kT_bf[:, sl:sl + TILE], in_=pqq[0:DS, TILE:2 * TILE],
                                           func=AF.Identity, bias=0.0), "atev")
                        P.ti(sc.activation(out=kvt[:, sl:sl + TILE], in_=pqq[:, 2 * TILE:3 * TILE],
                                           func=AF.Identity, bias=0.0), "atev")

        @block.vector
        def _(v):
            step = P.stepper(v, "atvch")
            v.wait_ge(P.sem("atprT"), 1)
            step(v.tensor_copy(out=lnd_rep[:, :], in_=ps[0][:, 0:DS]))
            v.wait_ge(P.sem("atprT"), 2)
            step(v.tensor_copy(out=dtile_rep[:, :], in_=ps[1][0:DS, 0:DS]))
            step(v.tensor_single_scalar(out=expcol[:, 0:1], in_=iota_p[:, :], scalar=31.5, op=ALU.is_gt))
            step(v.tensor_single_scalar(out=expcol[:, 1:2], in_=iota_p[:, :], scalar=63.5, op=ALU.is_gt))
            step(v.tensor_single_scalar(out=expcol[:, 2:3], in_=iota_p[:, :], scalar=95.5, op=ALU.is_gt))
            step(v.tensor_tensor(out=expcol[:, 0:1], in0=expcol[:, 0:1], in1=expcol[:, 1:2], op=ALU.add))
            step(v.tensor_tensor(out=expcol[:, 0:1], in0=expcol[:, 0:1], in1=expcol[:, 2:3], op=ALU.add))
            step(v.tensor_scalar(out=expcol[:, 3:4], in0=expcol[:, 0:1], scalar1=-32.0, scalar2=0.0, op0=ALU.mult))
            step(v.tensor_tensor(out=expcol[:, 3:4], in0=expcol[:, 3:4], in1=iota_p[:, :], op=ALU.add))
            step(v.tensor_scalar(out=expcol[:, 4:5], in0=expcol[:, 3:4], scalar1=-1.0, scalar2=16.0,
                                 op0=ALU.mult, op1=ALU.add))
            step(v.tensor_scalar(out=expcol[:, 5:6], in0=expcol[:, 3:4], scalar1=-1.0, scalar2=32.0,
                                 op0=ALU.mult, op1=ALU.add))
            step(v.tensor_scalar(out=expcol[:, 6:7], in0=iota_p[:, :], scalar1=-1.0, scalar2=64.0,
                                 op0=ALU.mult, op1=ALU.add))
            step(v.tensor_scalar(out=expcol[:, 7:8], in0=iota_p[:, :], scalar1=-1.0, scalar2=127.0,
                                 op0=ALU.mult, op1=ALU.add))
            for l, cc in enumerate((4, 5, 6, 7)):
                step(v.tensor_tensor(out=dcolx[:, l * DS:(l + 1) * DS], in0=lnd_rep[:, :],
                                     in1=expcol[:, cc:cc + 1].broadcast_to((TILE, DS)), op=ALU.mult))
            step(v.tensor_scalar(out=dcolx[:, :], in0=dcolx[:, :], scalar1=80.0, scalar2=0.0, op0=ALU.min))
            P.ti(v.engine_nop(), "atdc")
            v.wait_ge(P.sem("atpi"), 4)
            v.wait_ge(P.sem("atms"), 32)
            step(v.tensor_tensor(out=drowt[:, :], in0=exprow[:, :],
                                 in1=lnd_col[:, :].broadcast_to((DS, 4 * TILE)), op=ALU.mult))
            step(v.tensor_scalar(out=drowt[:, :], in0=drowt[:, :], scalar1=80.0, scalar2=0.0, op0=ALU.min))
            P.ti(v.engine_nop(), "atdc")
            v.wait_ge(P.sem("atpi"), 8)
            tA = xa[:, 0:TILE]
            tB = xa[:, TILE:2 * TILE]
            step(v.tensor_tensor(out=tA, in0=ttab[:, TILE:2 * TILE],
                                 in1=expcol[:, 0:1].broadcast_to((TILE, TILE)), op=ALU.is_equal))
            step(v.tensor_tensor(out=tB, in0=ttab[:, 0:TILE],
                                 in1=iota_p[:, :].broadcast_to((TILE, TILE)), op=ALU.is_ge))
            step(v.tensor_tensor(out=mask3[:, 0:TILE], in0=tA, in1=tB, op=ALU.mult))
            step(v.tensor_tensor(out=tA, in0=ttab[:, 2 * TILE:3 * TILE],
                                 in1=expcol[:, 1:2].broadcast_to((TILE, TILE)), op=ALU.is_equal))
            step(v.tensor_single_scalar(out=tB, in_=ttab[:, 3 * TILE:4 * TILE], scalar=31.5, op=ALU.is_gt))
            step(v.tensor_tensor(out=tA, in0=tA, in1=tB, op=ALU.mult))
            step(v.tensor_scalar(out=expcol[:, 3:4], in0=expcol[:, 1:2], scalar1=-64.0, scalar2=0.0, op0=ALU.mult))
            step(v.tensor_tensor(out=expcol[:, 3:4], in0=expcol[:, 3:4], in1=iota_p[:, :], op=ALU.add))
            step(v.tensor_single_scalar(out=expcol[:, 3:4], in_=expcol[:, 3:4], scalar=31.5, op=ALU.is_gt))
            step(v.tensor_scalar(out=expcol[:, 3:4], in0=expcol[:, 3:4], scalar1=-1.0, scalar2=1.0,
                                 op0=ALU.mult, op1=ALU.add))
            step(v.tensor_tensor(out=mask3[:, TILE:2 * TILE], in0=tA,
                                 in1=expcol[:, 3:4].broadcast_to((TILE, TILE)), op=ALU.mult))
            step(v.tensor_single_scalar(out=tA, in_=ttab[:, 0:TILE], scalar=63.5, op=ALU.is_gt))
            step(v.tensor_scalar(out=expcol[:, 4:5], in0=expcol[:, 1:2], scalar1=-1.0, scalar2=1.0,
                                 op0=ALU.mult, op1=ALU.add))
            step(v.tensor_tensor(out=mask3[:, 2 * TILE:3 * TILE], in0=tA,
                                 in1=expcol[:, 4:5].broadcast_to((TILE, TILE)), op=ALU.mult))
            step(v.memset(st_f[:, :], 0.0))
            step(v.memset(st_b[:, :], 0.0))
            P.ti(v.engine_nop(), "atsu")
            v.wait_ge(P.sem("attab"), 2)
            for t in range(NT + 2):
                for tq in ([0, 1] if t == 0 else [t + 1]):
                    if tq < NT:
                        v.wait_ge(P.sem("atev"), 3 * tq + 3)
                        if tq >= 3:
                            v.wait_ge(P.sem("atim"), tq - 2)
                        sl = (tq % 3) * TILE
                        vsl = (tq % 3) * 256
                        P.ti(v.tensor_tensor(
                            out=v4[:, vsl:vsl + 256].rearrange("p (l i) -> p l i", i=DS),
                            in0=kvt[:, sl + DS:sl + TILE].rearrange("p (o i) -> p o i", o=1)
                                .broadcast_to((TILE, 4, DS)),
                            in1=dcolx[:, :].rearrange("p (l i) -> p l i", i=DS),
                            op=ALU.mult), "atvv")
                if t < NT:
                    v.wait_ge(P.sem("atat"), t + 1)
                    if t >= 2:
                        v.wait_ge(P.sem("atim"), t - 1)
                    pt_ = ps[2 + t % 2]
                    msl = (t % 2) * 3 * TILE
                    for l in range(3):
                        P.ti(v.tensor_tensor(out=mav[:, msl + l * TILE:msl + (l + 1) * TILE],
                                             in0=pt_[:, 0:TILE], in1=mask3[:, l * TILE:(l + 1) * TILE],
                                             op=ALU.mult), "atmk")
                if 0 <= t - 2 < NT:
                    tau = t - 2
                    v.wait_ge(P.sem("atpjh"), 2 * tau + 1)
                    v.wait_ge(P.sem("atr2l"), (tau + 1) * 16)
                    if tau >= 2:
                        v.wait_ge(P.sem("atst"), (tau - 1) * 16)
                    P.ti(v.tensor_tensor(out=x2sb[:, (tau % 2) * D:(tau % 2) * D + 512],
                                         in0=ps[4 + tau % 2][:, :],
                                         in1=r2buf[:, (tau % 4) * D:(tau % 4) * D + 512],
                                         op=ALU.add), "atx2h")
                if 1 <= t <= NT - 1:
                    v.wait_ge(P.sem("atsa"), t)
                    if t == NTB_:
                        step(v.memset(st_f[:, :], 0.0))
                        step(v.memset(st_b[:, :], 0.0))
                    else:
                        v.wait_ge(P.sem("attt"), t)
                        ptm = ps[2 + (t - 1) % 2]
                        step(v.tensor_tensor(out=st_f[:, :], in0=st_f[:, :], in1=dtile_rep[:, :],
                                             op=ALU.mult))
                        step(v.tensor_tensor(out=st_f[:, :], in0=st_f[:, :],
                                             in1=ptm[0:DS, TILE:TILE + DS], op=ALU.add))
                        step(v.tensor_copy(out=st_b[:, :], in_=st_f[:, :]))
                    P.ti(v.engine_nop(), "atsu")
                if 0 <= t - 1 < NT:
                    tau = t - 1
                    v.wait_ge(P.sem("atim"), tau + 1)
                    v.wait_ge(P.sem("atsa"), tau + 1)
                    if tau >= 2:
                        v.wait_ge(P.sem("atpjh"), 2 * tau - 2)
                    pa_ = ps[0 + tau % 2]
                    step(v.tensor_tensor(out=tmpo[0:DS, 0:512], in0=pa_[0:DS, :],
                                         in1=drowt[0:DS, :], op=ALU.mult))
                    step(v.tensor_reduce(
                        out=obuf[0:DS, :].rearrange("p (x o) -> p x o", o=1),
                        in_=tmpo[0:DS, 0:512].rearrange("p (k x) -> p x k", k=4),
                        axis=AX.X, op=ALU.add))
                    P.ti(v.tensor_copy(out=obf[0:DS, (tau % 2) * TILE:(tau % 2 + 1) * TILE],
                                       in_=obuf[0:DS, :]), "atob")
                if 0 <= t - 2 < NT:
                    tau = t - 2
                    v.wait_ge(P.sem("atpjh"), 2 * tau + 2)
                    P.ti(v.tensor_tensor(out=x2sb[:, (tau % 2) * D + 512:(tau % 2) * D + 1024],
                                         in0=ps[4 + tau % 2][:, :],
                                         in1=r2buf[:, (tau % 4) * D + 512:(tau % 4) * D + 1024],
                                         op=ALU.add), "atx2")

        @block.tensor
        def _(te):
            te.wait_ge(P.sem("atwl"), 16 * 5)
            te.wait_ge(P.sem("atprA"), 3)
            P.ti(te.matmul(ps[0][:, 0:DS], ones1[0:1, 0:TILE], drow_sb[0:1, 2 * DS:3 * DS],
                           start=True, stop=True), "atprT")
            te.wait_ge(P.sem("atprA"), 4)
            P.ti(te.matmul(ps[1][0:DS, 0:DS], ones1[0:1, 0:DS], drow_sb[0:1, 3 * DS:4 * DS],
                           start=True, stop=True), "atprT")
            for t in range(NT + 2):
                if 0 <= t - 2 < NT:
                    tau = t - 2
                    pp = ps[4 + tau % 2]
                    te.wait_ge(P.sem("atob"), tau + 1)
                    if tau >= 2:
                        te.wait_ge(P.sem("atx2"), tau - 1)
                    osl = (tau % 2) * TILE
                    te.matmul(pp[:, :], obf[0:DS, osl:osl + TILE], woT_sb[0:DS, 0:512],
                              start=True, stop=False)
                    P.ti(te.matmul(pp[:, :], ones1[0:1, 0:TILE], bo_sb[0:1, 0:512],
                                   start=False, stop=True), "atpjh")
                for tq in ([0, 1] if t == 0 else [t + 1]):
                    if tq < NT:
                        g = tq // 4
                        te.wait_ge(P.sem("attp"), (g + 1) * 128)
                        if tq >= 2:
                            te.wait_ge(P.sem("atev"), 3 * tq)
                        pqq = ps[6 + tq % 2]
                        base = (g % 2) * 4096 + (tq % 4) * TILE
                        for c in range(8):
                            te.matmul(pqq[0:DS, 0:TILE], wqkv_sb[:, c * 192:c * 192 + DS],
                                      hlnT[:, base + c * 512:base + c * 512 + TILE],
                                      start=(c == 0), stop=False)
                        te.matmul(pqq[0:DS, 0:TILE], bqkv_sb[0:1, 0:DS], ones1[0:1, 0:TILE],
                                  start=False, stop=True)
                        for c in range(8):
                            te.matmul(pqq[0:DS, TILE:2 * TILE],
                                      wqkv_sb[:, c * 192 + DS:c * 192 + 2 * DS],
                                      hlnT[:, base + c * 512:base + c * 512 + TILE],
                                      start=(c == 0), stop=False)
                        te.matmul(pqq[0:DS, TILE:2 * TILE], bqkv_sb[0:1, DS:2 * DS],
                                  ones1[0:1, 0:TILE], start=False, stop=True)
                        for c in range(8):
                            te.matmul(pqq[:, 2 * TILE:3 * TILE],
                                      hlnT[:, base + c * 512:base + c * 512 + TILE],
                                      wqkv_sb[:, c * 192 + DS:c * 192 + 192],
                                      start=(c == 0), stop=False)
                        P.ti(te.matmul(pqq[:, 2 * TILE:3 * TILE], ones1[0:1, 0:TILE],
                                       bqkv_sb[0:1, DS:192], start=False, stop=True), "atqk")
                if t < NT:
                    te.wait_ge(P.sem("atev"), 3 * t + 3)
                    if t >= 2:
                        te.wait_ge(P.sem("atmk"), 3 * t - 3)
                        te.wait_ge(P.sem("atsu"), t)
                    pt_ = ps[2 + t % 2]
                    sl = (t % 3) * TILE
                    P.ti(te.matmul(pt_[:, 0:TILE], kT_bf[:, sl:sl + TILE],
                                   qT_bf[:, sl:sl + TILE], start=True, stop=True), "atat")
                    te.wait_ge(P.sem("atvv"), t + 1)
                    P.ti(te.matmul(pt_[0:DS, TILE:TILE + DS], kvt[:, sl:sl + DS],
                                   v4[:, (t % 3) * 256 + 192:(t % 3) * 256 + 256],
                                   start=True, stop=True), "attt")
                if 0 <= t - 1 < NT:
                    tau = t - 1
                    pa_ = ps[0 + tau % 2]
                    te.wait_ge(P.sem("atmk"), 3 * tau + 3)
                    if tau >= 2:
                        te.wait_ge(P.sem("atob"), tau - 1)
                    vsl = (tau % 3) * 256
                    msl = (tau % 2) * 3 * TILE
                    for l in range(3):
                        mi = te.matmul(pa_[0:DS, l * TILE:(l + 1) * TILE],
                                       v4[:, vsl + l * DS:vsl + (l + 1) * DS],
                                       mav[:, msl + l * TILE:msl + (l + 1) * TILE],
                                       start=True, stop=True)
                    P.ti(mi, "atim")
                    te.wait_ge(P.sem("atsu"), tau + 1)
                    sl1 = (tau % 3) * TILE
                    P.ti(te.matmul(pa_[0:DS, 3 * TILE:4 * TILE], st_b[:, :],
                                   qT_bf[:, sl1:sl1 + TILE], start=True, stop=True), "atsa")
                if 0 <= t - 2 < NT:
                    tau = t - 2
                    pp = ps[4 + tau % 2]
                    te.wait_ge(P.sem("atx2h"), tau + 1)
                    osl = (tau % 2) * TILE
                    te.matmul(pp[:, :], obf[0:DS, osl:osl + TILE], woT_sb[0:DS, 512:1024],
                              start=True, stop=False)
                    P.ti(te.matmul(pp[:, :], ones1[0:1, 0:TILE], bo_sb[0:1, 512:1024],
                                   start=False, stop=True), "atpjh")

    ln1_st = emit_ln(x_in, hln_d, "l1")
    if stn >= 1:
        emit_routing(ln1_st, "f1", route_pass)
    if stn >= 2:
        ex1 = emit_expert("f1", hln_d, expert_pass)
    if stn >= 3:
        sh1 = emit_shared("f1", "f1x", x_in, x1_d if stn >= 4 else y_out)
    if stn >= 4:
        phase_barrier()
        emit_ln(x1_d, hln_d, "l2", r2_dram=r2_d)
        phase_barrier()
        emit_attn()
    if stn >= 5:
        phase_barrier()
        ln3_st = emit_ln(x2_d, hln_d, "l1")
        emit_routing(ln3_st, "f1", route_pass)
        ex2 = emit_expert("f1", hln_d, expert_pass)
        sh2 = emit_shared("f1", "f1x", x2_d, y_out)

    @block.sync
    def _(sy):
        if "hln" in dbg:
            sy.wait_ge(P.sem(ln1_st), P.val(ln1_st))
            for t in range(0, NT, 8):
                P.ti(sy.dma_start(out=dbg["hln"][t * TILE:(t + 8) * TILE, :],
                                  in_=hln_d[t * TILE:(t + 8) * TILE, :]), "dbst", 16)
        if stn >= 1:
            if stn >= 3 and "x1" in dbg:
                sy.wait_ge(P.sem("f1hst"), P.val("f1hst"))
                for r in range(0, T, 4096):
                    P.ti(sy.dma_start(out=dbg["x1"][r:r + 4096, :], in_=x1_d[r:r + 4096, :]), "dbst", 16)
            if stn >= 2 and "xd" in dbg:
                sy.wait_ge(P.sem("f1xs"), P.val("f1xs"))
                for r in range(0, NSLOT, 4096):
                    P.ti(sy.dma_start(out=dbg["xd"][r:r + 4096, :], in_=xd_d[r:r + 4096, :]), "dbst", 16)
            if stn >= 2 and "yd" in dbg:
                sy.wait_ge(P.sem("f1xys"), P.val("f1xys"))
                for r in range(0, NSLOT, 4096):
                    P.ti(sy.dma_start(out=dbg["yd"][r:r + 4096, :], in_=yd_d[r:r + 4096, :]), "dbst", 16)
            if stn >= 4 and "x2" in dbg:
                sy.wait_ge(P.sem("atst"), P.val("atst"))
                for r in range(0, T, 4096):
                    P.ti(sy.dma_start(out=dbg["x2"][r:r + 4096, :], in_=x2_d[r:r + 4096, :]), "dbst", 16)
            if "inv" in dbg and P.val("f1scdone") > 0:
                sy.wait_ge(P.sem("f1scdone"), P.val("f1scdone"))
                for r in range(0, NSLOT, 4096):
                    P.ti(sy.dma_start(out=dbg["inv"][r:r + 4096, :],
                                      in_=inv_d[r:r + 4096, :]), "dbst", 16)

    if stn < 3:
        @block.gpsimd
        def _(gp):
            gp.wait_ge(P.sem(ln1_st), P.val(ln1_st))
            for r in range(0, n_out, 1024):
                P.ti(gp.dma_start(out=y_out[r:r + 1024, :],
                                  in_=hln_d[out_rows[0] + r:out_rows[0] + r + 1024, :]), "yst", 16)
    elif stn == 4:
        @block.gpsimd
        def _(gp):
            gp.wait_ge(P.sem("atst"), P.val("atst"))
            for r in range(0, n_out, 1024):
                P.ti(gp.dma_start(out=y_out[r:r + 1024, :],
                                  in_=x2_d[out_rows[0] + r:out_rows[0] + r + 1024, :]), "yst", 16)

    @block.sync
    def _(sy):
        for name in list(P.v.keys()):
            if name.endswith(("st", "sc", "scdone")) or name in ("wcast", "pz"):
                sy.wait_ge(P.sem(name), P.val(name))

    cs.close()
    return nc


def _make_in_map(inputs, stage):
    stn = STAGES[stage]
    m = {
        "x": np.ascontiguousarray(np.asarray(inputs["L0_attn_x"], np.float32).reshape(T, D)),
        "router_w": np.ascontiguousarray(np.asarray(inputs["L0_ffn_router_w"], np.float32)),
    }
    if stn >= 2:
        m["gate_w"] = np.ascontiguousarray(np.asarray(inputs["L0_ffn_gate_w"], np.float32))
        m["up_w"] = np.ascontiguousarray(np.asarray(inputs["L0_ffn_up_w"], np.float32))
        m["down_w"] = np.ascontiguousarray(np.asarray(inputs["L0_ffn_down_w"], np.float32))
    if stn >= 3:
        m["shared_gate_w"] = np.ascontiguousarray(np.asarray(inputs["L0_ffn_shared_gate_w"], np.float32))
        m["shared_up_w"] = np.ascontiguousarray(np.asarray(inputs["L0_ffn_shared_up_w"], np.float32))
        m["shared_down_w"] = np.ascontiguousarray(np.asarray(inputs["L0_ffn_shared_down_w"], np.float32))
    if stn >= 4:
        m["wq_t"] = np.ascontiguousarray(np.asarray(inputs["L0_attn_Wq"], np.float32).T)
        m["wk_t"] = np.ascontiguousarray(np.asarray(inputs["L0_attn_Wk"], np.float32).T)
        m["wv_t"] = np.ascontiguousarray(np.asarray(inputs["L0_attn_Wv"], np.float32).T)
        m["wo_t"] = np.ascontiguousarray(np.asarray(inputs["L0_attn_Wo"], np.float32).T)
        m["bqkv"] = np.ascontiguousarray(np.concatenate([
            np.asarray(inputs["L0_attn_bq"], np.float32),
            np.asarray(inputs["L0_attn_bk"], np.float32),
            np.asarray(inputs["L0_attn_bv"], np.float32)])[None, :])
        m["bo"] = np.ascontiguousarray(np.asarray(inputs["L0_attn_bo"], np.float32)[None, :])
        m["decay"] = np.ascontiguousarray(np.asarray(inputs["L0_attn_decay"], np.float32)[None, :])
    return m


def kernel(**inputs):
    stage = "full"
    nc = build_program(stage=stage)
    res = run_bass_kernel_spmd(nc, [_make_in_map(inputs, stage)], core_ids=[0])
    return res.results[0]["y"].reshape(NB, TB, D)


# version 04
# speedup vs baseline: 1.1340x; 1.1340x over previous
import contextlib
import numpy as np

import concourse.bass as bass
import concourse.mybir as mybir
from concourse.bass import IndirectOffsetOnAxis
from concourse.bass_utils import run_bass_kernel_spmd

F32 = mybir.dt.float32
BF16 = mybir.dt.bfloat16
U32 = mybir.dt.uint32
I32 = mybir.dt.int32
AF = mybir.ActivationFunctionType
ALU = mybir.AluOpType
AX = mybir.AxisListType

T = 32768
D = 1024
E = 64
KTOP = 8
CAP = 512
II = 512
TILE = 128
NT = T // TILE
DS = 64
NB = 2
TB = T // NB
NTB = TB // TILE
SCALE = 2.5
NSLOT = E * CAP
OOB_BIG = 10_000_000

STAGES = {"ln": 0, "route": 1, "expert": 2, "ffn1": 3, "attn": 4, "full": 5}


class Prog:

    def __init__(self, nc, cs):
        self.nc = nc
        self.cs = cs
        self._h = {}
        self.v = {}

    def sem(self, name):
        if name not in self._h:
            self._h[name] = self.cs.enter_context(self.nc.semaphore(name))
            self.v[name] = 0
        return self._h[name]

    def ti(self, instr, name, n=1):
        h = self.sem(name)
        instr.then_inc(h, n, skip_validation=True)
        self.v[name] += n
        return self.v[name]

    def val(self, name):
        self.sem(name)
        return self.v[name]

    def stepper(self, eng, name):
        def step(instr):
            self.ti(instr, name)
            eng.wait_ge(self.sem(name), self.v[name])
            return instr
        return step


def build_program(stage="full", debug=(), out_rows=(0, T), route_pass=3, expert_pass=9, combine_mode=3):
    stn = STAGES[stage]
    nc = bass.Bass("TRN2", target_bir_lowering=False, debug=False)
    cs = contextlib.ExitStack()
    P = Prog(nc, cs)

    def sbuf(name, shape, dt=F32):
        return cs.enter_context(nc.sbuf_tensor(name, shape, dt))

    def psum(name, shape, dt=F32):
        return cs.enter_context(nc.psum_tensor(name, shape, dt))

    _uid = [0]

    def uname(base):
        _uid[0] += 1
        return base + "_u" + str(_uid[0])

    def chn(eng, nm, thunk):
        eng.wait_ge(P.sem(nm), P.val(nm))
        P.ti(thunk(), nm)

    x_in = nc.declare_dram_parameter("x", [T, D], F32, isOutput=False)
    router_in = nc.declare_dram_parameter("router_w", [D, E], F32, isOutput=False)
    if stn >= 2:
        gate_in = nc.declare_dram_parameter("gate_w", [E, D, II], F32, isOutput=False)
        up_in = nc.declare_dram_parameter("up_w", [E, D, II], F32, isOutput=False)
        down_in = nc.declare_dram_parameter("down_w", [E, II, D], F32, isOutput=False)
    if stn >= 3:
        sgate_in = nc.declare_dram_parameter("shared_gate_w", [D, II], F32, isOutput=False)
        sup_in = nc.declare_dram_parameter("shared_up_w", [D, II], F32, isOutput=False)
        sdown_in = nc.declare_dram_parameter("shared_down_w", [II, D], F32, isOutput=False)
    if stn >= 4:
        wq_in = nc.declare_dram_parameter("wq_t", [D, DS], F32, isOutput=False)
        wk_in = nc.declare_dram_parameter("wk_t", [D, DS], F32, isOutput=False)
        wv_in = nc.declare_dram_parameter("wv_t", [D, DS], F32, isOutput=False)
        wo_in = nc.declare_dram_parameter("wo_t", [DS, D], F32, isOutput=False)
        bqkv_in = nc.declare_dram_parameter("bqkv", [1, 3 * DS], F32, isOutput=False)
        bo_in = nc.declare_dram_parameter("bo", [1, D], F32, isOutput=False)
        decay_in = nc.declare_dram_parameter("decay", [1, DS], F32, isOutput=False)

    n_out = out_rows[1] - out_rows[0]
    y_out = nc.declare_dram_parameter("y", [n_out, D], F32, isOutput=True)

    dbg = {}
    for name, shape, dt in (
        ("hln", [T, D], BF16), ("routeS", [NT, TILE, 24], F32),
        ("cnt", [E, NT], F32), ("inv", [NSLOT, 2], F32), ("xd", [NSLOT, D], BF16),
        ("yd", [NSLOT, D], BF16), ("x1", [T, D], F32), ("x2", [T, D], F32),
    ):
        if name in debug:
            dbg[name] = nc.declare_dram_parameter("dbg_" + name, shape, dt, isOutput=True)

    hln_d = nc.dram_tensor("hln_d", [T + TILE, D], BF16)
    inv_d = nc.dram_tensor("inv_d", [NSLOT, 2], F32)
    xd_d = nc.dram_tensor("xd_d", [NSLOT, D], BF16)
    yd_d = nc.dram_tensor("yd_d", [NSLOT + TILE, D], BF16)
    x1_d = nc.dram_tensor("x1_d", [T, D], F32)
    x2_d = nc.dram_tensor("x2_d", [T, D], F32)
    if stn >= 2:
        gbf_d = nc.dram_tensor("gbf_d", [E, D, II], BF16)
        ubf_d = nc.dram_tensor("ubf_d", [E, D, II], BF16)
        dbf_d = nc.dram_tensor("dbf_d", [E, II, D], BF16)
    if stn >= 3:
        sgbf_d = nc.dram_tensor("sgbf_d", [D, II], BF16)
        subf_d = nc.dram_tensor("subf_d", [D, II], BF16)
        sdbf_d = nc.dram_tensor("sdbf_d", [II, D], BF16)
    if stn >= 4:
        r2_d = nc.dram_tensor("r2_d", [T, D], F32)
        misc_d = nc.dram_tensor("misc_d", [DS, 1], F32)
        wqkv_bf_d = nc.dram_tensor("wqkv_bf_d", [D, 3 * DS], BF16)
        wo_bf_d = nc.dram_tensor("wo_bf_d", [DS, D], BF16)

    ones1 = sbuf("ones1", [1, TILE])
    onescol_b = sbuf("onescol_b", [TILE, 1], BF16)
    su_bf = sbuf("su_bf", [TILE, TILE], BF16)
    iota_rev = sbuf("iota_rev", [TILE, E])
    iota_enc = sbuf("iota_enc", [TILE, E])
    iota_p = sbuf("iota_p", [TILE, 1])
    oob_tab = sbuf("oob_tab", [TILE, KTOP])
    dummy_tab = sbuf("dummy_tab", [TILE, 1])
    dummyT_tab = sbuf("dummyT_tab", [TILE, 1])
    fill_neg = sbuf("fill_neg", [TILE, KTOP])
    router_sb = sbuf("router_sb", [TILE, 8 * E], BF16)

    czero = sbuf("czero", [TILE, 1])
    ceps = sbuf("ceps", [TILE, 1])
    conef = sbuf("conef", [TILE, 1])
    chalf = sbuf("chalf", [TILE, 1])
    nc.const_aps.register(F32, 0.0, czero[:, :])
    nc.const_aps.register(F32, 1e-5, ceps[:, :])
    nc.const_aps.register(F32, 1.0, conef[:, :])
    nc.const_aps.register(F32, 0.5, chalf[:, :])

    xg = sbuf("xg", [TILE, 2 * D], BF16)
    xa = sbuf("xa", [TILE, D])
    xb = sbuf("xb", [TILE, D])
    hba = sbuf("hba", [TILE, D], BF16)
    hbb = sbuf("hbb", [TILE, D], BF16)
    stat = sbuf("stat", [TILE, 16])
    sqbuf = sbuf("sqbuf", [TILE, D])

    hlnT = sbuf("hlnT", [TILE, 2 * 4096], BF16)
    scr = sbuf("scr", [TILE, 2 * 256])
    msk = sbuf("msk", [TILE, E])
    gsr = sbuf("gsr", [TILE, 32])
    tk = sbuf("tk", [TILE, 64])
    a_all = sbuf("a_all", [TILE, NT * E], BF16)
    ew_all = sbuf("ew_all", [TILE, NT * 2 * KTOP])
    runrow = sbuf("runrow", [1, E])
    sidx_all = sbuf("sidx_all", [TILE, NT * KTOP], U32)
    pairs_all = sbuf("pairs_all", [TILE, NT * 2 * KTOP])
    cmb_all = sbuf("cmb_all", [TILE, NT * KTOP], U32)
    kflag = sbuf("kflag", [1, NT])
    kflag_i = sbuf("kflag_i", [1, NT], I32)
    rs_all = sbuf("rs_all", [TILE, NT * 24]) if "routeS" in debug else None
    vsum = sbuf("vsum", [TILE, NT])
    valid_i = sbuf("valid_i", [TILE, 2 * KTOP], I32)
    zer512 = sbuf("zer512", [TILE, 512])
    if STAGES[stage] >= 2:
        inv_sb = sbuf("inv_sb", [TILE, 2 * NT])
        idx_all = sbuf("idx_all", [TILE, NT], U32)
        wden = sbuf("wden", [TILE, NT])
        imsk = sbuf("imsk", [TILE, NT], I32)
        wg_sb = sbuf("wg_sb", [TILE, 2 * 4096], BF16)
        wu_sb = sbuf("wu_sb", [TILE, 2 * 4096], BF16)
        wd_sb = sbuf("wd_sb", [TILE, 2 * 4096], BF16)
        sact = sbuf("sact", [TILE, 512])
        act_sb = sbuf("act_sb", [TILE, 4 * 512], BF16)
        y_sb = sbuf("y_sb", [TILE, 2 * D], BF16)
    if STAGES[stage] >= 3:
        x1g = pairs_all
        stg = a_all
        ident_b = sbuf("ident_b", [TILE, TILE], BF16)
        identX = sbuf("identX", [TILE, 2 * TILE], BF16)
        flagcol = sbuf("flagcol", [TILE, NT])
        kflag01 = sbuf("kflag01", [1, NT])

    if stn >= 4:
        woT_sb = wg_sb[0:DS, 1024:2048]
        mask3 = wg_sb[:, 0:3 * TILE]
        obf = wg_sb[0:DS, 512:768]
        qT_bf = wu_sb[0:DS, 0:3 * TILE]
        kT_bf = wu_sb[0:DS, 512:512 + 3 * TILE]
        kvt = wu_sb[:, 1024:1024 + 3 * TILE]
        v4 = wd_sb[:, 0:3 * 4 * DS]
        mav = wd_sb[:, 1024:1024 + 6 * TILE]
        dcolx = sact[:, 0:4 * DS]
        lnd_rep = sact[:, 256:256 + DS]
        obuf = sact[0:DS, 320:448]
        dtile_rep = sact[0:DS, 448:512]
        drowt = xb[0:DS, 0:4 * TILE]
        exprow = xb[0:DS, 512:1024]
        bqkv_sb = sbuf("bqkv_sb", [1, 3 * DS])
        bo_sb = sbuf("bo_sb", [1, D])
        drow_sb = sbuf("drow_sb", [1, 4 * DS])
        lnd_col = sbuf("lnd_col", [DS, 1])
        st_f = sbuf("st_f", [DS, DS])
        st_b = sbuf("st_b", [DS, DS], BF16)
        expcol = sbuf("expcol", [TILE, 8])

    ps = [psum(f"ps{i}", [TILE, 512]) for i in range(8)]

    block = cs.enter_context(nc.Block())

    @block.vector
    def _(v):
        P.ti(v.memset(ones1[:, :], 1.0), "const")
        P.ti(v.memset(onescol_b[:, :], 1.0), "const")
        P.ti(v.memset(su_bf[:, :], 0.0), "const")
        P.ti(v.memset(fill_neg[:, :], -1e30), "const")
        P.ti(v.memset(zer512[:, :], 0.0), "const")
        if STAGES[stage] >= 3:
            P.ti(v.memset(ident_b[:, :], 0.0), "const")
        P.ti(v.memset(czero[:, :], 0.0), "const")
        P.ti(v.memset(ceps[:, :], 1e-5), "const")
        P.ti(v.memset(conef[:, :], 1.0), "const")
        P.ti(v.memset(chalf[:, :], 0.5), "const")

    @block.gpsimd
    def _(gp):
        gp.wait_ge(P.sem("const"), 9 + (1 if STAGES[stage] >= 3 else 0))
        P.ti(gp.affine_select(out=su_bf[:, :], in_=su_bf[:, :], pattern=[[-1, TILE]],
                              compare_op=ALU.is_ge, fill=1.0, base=0,
                              channel_multiplier=1), "const")
        if STAGES[stage] >= 3:
            gp.wait_ge(P.sem("const"), 10)
            P.ti(gp.affine_select(out=ident_b[:, :], in_=ident_b[:, :], pattern=[[-1, TILE]],
                                  compare_op=ALU.not_equal, fill=1.0, base=0,
                                  channel_multiplier=1), "const")
        P.ti(gp.iota(iota_rev[:, :], pattern=[[-1, E]], base=E, channel_multiplier=0,
                     allow_small_or_imprecise_dtypes=True), "const")
        P.ti(gp.iota(iota_enc[:, :], pattern=[[-32768, E]], base=E * 32768, channel_multiplier=0,
                     allow_small_or_imprecise_dtypes=True), "const")
        P.ti(gp.iota(oob_tab[:, :], pattern=[[1, KTOP]], base=OOB_BIG, channel_multiplier=KTOP,
                     allow_small_or_imprecise_dtypes=True), "const")
        P.ti(gp.iota(iota_p[:, :], pattern=[[0, 1]], base=0, channel_multiplier=1,
                     allow_small_or_imprecise_dtypes=True), "const")
        P.ti(gp.iota(dummy_tab[:, :], pattern=[[0, 1]], base=NSLOT, channel_multiplier=1,
                     allow_small_or_imprecise_dtypes=True), "const")
        P.ti(gp.iota(dummyT_tab[:, :], pattern=[[0, 1]], base=T, channel_multiplier=1,
                     allow_small_or_imprecise_dtypes=True), "const")
        P.ti(gp.dma_start(out=hln_d[T:T + TILE, 0:512], in_=zer512[:, :]), "pz", 16)
        P.ti(gp.dma_start(out=hln_d[T:T + TILE, 512:1024], in_=zer512[:, :]), "pz", 16)
        P.ti(gp.dma_start(out=inv_d.ap().rearrange("(a b) c -> a (b c)", a=TILE),
                          in_=zer512[:, :]), "pz", 16)
        if stn >= 2:
            P.ti(gp.dma_start(out=yd_d[NSLOT:NSLOT + TILE, 0:512], in_=zer512[:, :]), "pz", 16)
            P.ti(gp.dma_start(out=yd_d[NSLOT:NSLOT + TILE, 512:1024], in_=zer512[:, :]), "pz", 16)

    if stn >= 2:
        @block.gpsimd
        def _(gp):
            for e in range(E):
                P.ti(gp.dma_start(out=gbf_d[e, :, :], in_=gate_in[e, :, :]), "wcast", 16)
                P.ti(gp.dma_start(out=ubf_d[e, :, :], in_=up_in[e, :, :]), "wcast", 16)
                P.ti(gp.dma_start(out=dbf_d[e, :, :], in_=down_in[e, :, :]), "wcast", 16)
            if stn >= 3:
                P.ti(gp.dma_start(out=sgbf_d[:, :], in_=sgate_in[:, :]), "wcast", 16)
                P.ti(gp.dma_start(out=subf_d[:, :], in_=sup_in[:, :]), "wcast", 16)
                P.ti(gp.dma_start(out=sdbf_d[:, :], in_=sdown_in[:, :]), "wcast", 16)
            if stn >= 4:
                P.ti(gp.dma_start(out=wqkv_bf_d[:, 0:DS], in_=wq_in[:, :]), "wcast", 16)
                P.ti(gp.dma_start(out=wqkv_bf_d[:, DS:2 * DS], in_=wk_in[:, :]), "wcast", 16)
                P.ti(gp.dma_start(out=wqkv_bf_d[:, 2 * DS:3 * DS], in_=wv_in[:, :]), "wcast", 16)
                P.ti(gp.dma_start(out=wo_bf_d[:, :], in_=wo_in[:, :]), "wcast", 16)

    @block.sync
    def _(sy):
        for c in range(8):
            P.ti(sy.dma_start(out=scr[:, c * E:(c + 1) * E],
                              in_=router_in[c * TILE:(c + 1) * TILE, :]), "ld0", 16)

    @block.vector
    def _(v):
        v.wait_ge(P.sem("ld0"), P.val("ld0"))
        P.ti(v.tensor_copy(out=router_sb[:, :], in_=scr[:, 0:8 * E]), "rcp")

    CONST_TOTAL = P.val("const")

    def emit_ln(src_dram, dst_dram, tag, r2_dram=None):
        ld, cp, dv, st = tag + "ld", tag + "cp", tag + "dv", tag + "st"
        ld0, cp0, dv0, st0 = P.val(ld), P.val(cp), P.val(dv), P.val(st)
        r20 = P.val(tag + "r2")

        @block.sync
        def _(sy):
            for t in range(NT):
                buf = xa if t % 2 == 0 else xb
                if t >= 2:
                    sy.wait_ge(P.sem(cp), cp0 + t - 1)
                    if r2_dram is not None:
                        sy.wait_ge(P.sem(tag + "r2"), r20 + t - 1)
                P.ti(sy.dma_start(out=buf[:, :], in_=src_dram[t * TILE:(t + 1) * TILE, :]), ld, 16)

        @block.vector
        def _(v):
            step = P.stepper(v, tag + "vch")
            v.wait_ge(P.sem("const"), CONST_TOTAL)
            for t in range(NT):
                buf = xa if t % 2 == 0 else xb
                v.wait_ge(P.sem(ld), ld0 + (t + 1) * 16)
                if t >= 1:
                    v.wait_ge(P.sem(cp), cp0 + t)
                P.ti(v.tensor_reduce(out=stat[:, 0:1], in_=buf[:, :], axis=AX.X, op=ALU.add), dv)
                if r2_dram is not None:
                    hbuf = hba if t % 2 == 0 else hbb
                    v.wait_ge(P.sem(cp), cp0 + t + 1)
                    if t >= 2:
                        v.wait_ge(P.sem(tag + "r2s"), (t - 1) * 16)
                    step(v.tensor_tensor(out=ew_all[:, (t % 2) * D:(t % 2 + 1) * D],
                                         in0=buf[:, :], in1=hbuf[:, :], op=ALU.add))
                    P.ti(v.engine_nop(), tag + "r2")

        @block.scalar
        def _(sc):
            sc.wait_ge(P.sem("const"), CONST_TOTAL)
            step = P.stepper(sc, tag + "ch")
            for t in range(NT):
                buf = xa if t % 2 == 0 else xb
                hbuf = hba if t % 2 == 0 else hbb
                sc.wait_ge(P.sem(ld), ld0 + (t + 1) * 16)
                step(sc.activation(out=xg[:, 0:D], in_=buf[:, :], func=AF.Square,
                                   accum_out=stat[:, 2:3]))
                sc.wait_ge(P.sem(dv), dv0 + t + 1)
                step(sc.mul(out=stat[:, 1:2], in_=stat[:, 0:1], mul=1.0 / D))
                step(sc.mul(out=stat[:, 3:4], in_=stat[:, 1:2], mul=stat[:, 1:2]))
                step(sc.mul(out=stat[:, 4:5], in_=stat[:, 3:4], mul=-1.0))
                step(sc.activation(out=stat[:, 5:6], in_=stat[:, 2:3], func=AF.Identity,
                                   scale=1.0 / D, bias=stat[:, 4:5]))
                step(sc.activation(out=stat[:, 5:6], in_=stat[:, 5:6], func=AF.Ln, bias=1e-5))
                step(sc.activation(out=stat[:, 5:6], in_=stat[:, 5:6], func=AF.Exp, scale=-0.5))
                step(sc.mul(out=stat[:, 6:7], in_=stat[:, 1:2], mul=-1.0))
                step(sc.mul(out=stat[:, 6:7], in_=stat[:, 6:7], mul=stat[:, 5:6]))
                if t >= 2:
                    sc.wait_ge(P.sem(st), st0 + (t - 1) * 16)
                if r2_dram is not None and t >= 2:
                    sc.wait_ge(P.sem(tag + "r2"), r20 + t - 1)
                P.ti(sc.activation(out=hbuf[:, :], in_=buf[:, :], func=AF.Identity,
                                   bias=stat[:, 6:7], scale=stat[:, 5:6]), cp)
                sc.wait_ge(P.sem(cp), cp0 + t + 1)
                P.ti(sc.dma_start(out=dst_dram[t * TILE:(t + 1) * TILE, :], in_=hbuf[:, :]), st, 16)
                if r2_dram is not None:
                    sc.wait_ge(P.sem(tag + "r2"), r20 + t + 1)
                    P.ti(sc.dma_start(out=r2_dram[t * TILE:(t + 1) * TILE, :],
                                      in_=ew_all[:, (t % 2) * D:(t % 2 + 1) * D]), tag + "r2s", 16)

        return st

    def emit_routing(ln_st, tag, route_pass=3):
        rt = tag
        NGRP = NT // 4
        ln_st0 = P.val(ln_st) - NT * 16
        tp0 = P.val(rt + "tp")
        mm0 = P.val(rt + "mm")
        sg0 = P.val(rt + "sg")
        tk0 = P.val(rt + "tk")
        pm0 = P.val(rt + "pm")
        p20E, p20O = P.val(rt + "p2E"), P.val(rt + "p2O")
        vc0 = P.val(rt + "vc")

        @block.scalar
        def _(sc):
            for g in range(NGRP + 1):
                if g < NGRP:
                    hbuf = hlnT[:, (g % 2) * 4096:(g % 2 + 1) * 4096]
                    sc.wait_ge(P.sem(ln_st), ln_st0 + (g * 4 + 4) * 16)
                    if g >= 2:
                        sc.wait_ge(P.sem(rt + "mm"), mm0 + (g - 1) * 32)
                    for c in range(8):
                        P.ti(sc.dma_start(out=hbuf[:, c * 512:(c + 1) * 512],
                                          in_=hln_d[g * 512:(g + 1) * 512, c * TILE:(c + 1) * TILE],
                                          transpose=True), rt + "tp", 16)
                if g >= 1:
                    gg = g - 1
                    sb = scr[:, (gg % 2) * 256:(gg % 2 + 1) * 256]
                    if gg >= 2:
                        sc.wait_ge(P.sem(rt + "tk"), tk0 + (gg - 1) * 4)
                    for tt in range(4):
                        pb = ps[tt]
                        sc.wait_ge(P.sem(rt + "mm"), mm0 + gg * 32 + (tt + 1) * 8)
                        P.ti(sc.activation(out=sb[:, tt * 64:(tt + 1) * 64], in_=pb[:, 0:E],
                                           func=AF.Tanh, scale=0.5), rt + "sg")

        @block.tensor
        def _(te):
            te.wait_ge(P.sem("rcp"), 1)
            for g in range(NGRP):
                hbuf = hlnT[:, (g % 2) * 4096:(g % 2 + 1) * 4096]
                te.wait_ge(P.sem(rt + "tp"), tp0 + (g + 1) * 128)
                if g >= 1:
                    te.wait_ge(P.sem(rt + "sg"), sg0 + g * 4)
                for tt in range(4):
                    pb = ps[tt]
                    for c in range(8):
                        mm = te.matmul(pb[:, 0:E],
                                       hbuf[:, c * 512 + tt * TILE:c * 512 + (tt + 1) * TILE],
                                       router_sb[:, c * E:(c + 1) * E],
                                       start=(c == 0), stop=(c == 7))
                    P.ti(mm, rt + "mm", 8)


        @block.vector
        def _(v):
            step = P.stepper(v, rt + "vch")
            step(v.memset(sqbuf[:, :], -1e30))
            for g in range(NGRP):
                sb = scr[:, (g % 2) * 256:(g % 2 + 1) * 256]
                v.wait_ge(P.sem(rt + "sg"), sg0 + g * 4 + 4)
                step(v.tensor_scalar(out=sb, in0=sb, scalar1=0.5, scalar2=0.5,
                                     op0=ALU.mult, op1=ALU.add))
                step(v.tensor_reduce(out=gsr[:, :], in_=sb.rearrange("p (a b) -> p a b", b=8),
                                     axis=AX.X, op=ALU.max))

                def mk_ops(tt):
                    t = g * 4 + tt
                    b = tt * 256
                    gs = gsr[:, tt * 8:(tt + 1) * 8]
                    A0 = sqbuf[:, b + 0:b + 8]
                    fill = sqbuf[:, b + 8:b + 16]
                    t8 = sqbuf[:, b + 16:b + 24]
                    s24 = sqbuf[:, b + 24:b + 32]
                    s32 = sqbuf[:, b + 32:b + 33]
                    s33 = sqbuf[:, b + 33:b + 34]
                    s34 = sqbuf[:, b + 34:b + 35]
                    mk = sqbuf[:, b + 64:b + 128]
                    cpy = sqbuf[:, b + 128:b + 192]
                    ew = ew_all[:, t * 16:t * 16 + 16]
                    return [
                        lambda: v.max(out=A0, in_=gs),
                        lambda: v.tensor_copy(out=fill[:, 0:4], in_=A0[:, 0:4]),
                        lambda: v.match_replace(out=gs, in_to_replace=fill, in_values=gs,
                                                imm_value=-1e30),
                        lambda: v.tensor_single_scalar(out=gs, in_=gs, scalar=-1e29, op=ALU.is_lt),
                        lambda: v.tensor_tensor(out=mk, in0=sb[:, tt * 64:(tt + 1) * 64],
                                                in1=gs.rearrange("p (a b) -> p a b", b=1)
                                                .broadcast_to((TILE, 8, 8)), op=ALU.mult),
                        lambda: v.max(out=t8, in_=mk),
                        lambda: v.tensor_copy(out=cpy, in_=mk),
                        lambda: v.match_replace(out=cpy, in_to_replace=t8, in_values=cpy,
                                                imm_value=-1e30),
                        lambda: v.tensor_single_scalar(out=cpy, in_=cpy, scalar=-1e29, op=ALU.is_lt),
                        lambda: v.tensor_copy(out=a_all[:, t * E:(t + 1) * E], in_=cpy),
                        lambda: v.scalar_tensor_tensor(out=mk, in0=mk, scalar=0.5,
                                                       in1=iota_rev[:, :], op0=ALU.mult, op1=ALU.add),
                        lambda: v.tensor_tensor(out=mk, in0=mk, in1=cpy, op=ALU.mult),
                        lambda: v.max(out=t8, in_=mk),
                        lambda: v.max_index(out=sidx_all[:, t * 8:(t + 1) * 8], in_max=t8,
                                            in_values=mk),
                        lambda: v.tensor_copy(out=ew[:, 0:8], in_=sidx_all[:, t * 8:(t + 1) * 8]),
                        lambda: v.tensor_tensor(out=s24, in0=t8, in1=ew[:, 0:8], op=ALU.add),
                        lambda: v.tensor_single_scalar(out=s24, in_=s24, scalar=-64.0, op=ALU.add),
                        lambda: v.tensor_reduce(out=s32, in_=s24, axis=AX.X, op=ALU.add),
                        lambda: v.tensor_scalar_max(out=s33, in0=s32, scalar1=5e-10),
                        lambda: v.reciprocal(out=s34, in_=s33),
                        lambda: v.tensor_scalar(out=ew[:, 8:16], in0=s24, scalar1=s34,
                                                scalar2=SCALE, op0=ALU.mult, op1=ALU.mult),
                    ]

                chains = [mk_ops(tt) for tt in range(4)]
                for k in range(len(chains[0])):
                    for tt in range(4):
                        chn(v, rt + "pc" + str(tt), chains[tt][k])
                for tt in range(4):
                    nm = rt + "pc" + str(tt)
                    v.wait_ge(P.sem(nm), P.val(nm))
                    P.ti(v.engine_nop(), rt + "tk")

        def emit_dump():
            if "routeS" not in dbg:
                return
            @block.scalar
            def _(sc):
                sc.wait_ge(P.sem(rt + "tk"), tk0 + NT)
                if route_pass >= 2:
                    sc.wait_ge(P.sem(rt + "p2E"), p20E + NT // 2)
                    sc.wait_ge(P.sem(rt + "p2O"), p20O + NT // 2)
                for t in range(NT):
                    ew = ew_all[:, t * 16:t * 16 + 16]
                    P.ti(sc.dma_start(out=dbg["routeS"][t, :, 0:8], in_=ew[:, 0:8]), rt + "dbst", 16)
                    P.ti(sc.dma_start(out=dbg["routeS"][t, :, 8:16], in_=ew[:, 8:16]), rt + "dbst", 16)
                    if route_pass >= 2:
                        P.ti(sc.dma_start(out=dbg["routeS"][t, :, 16:24],
                                          in_=rs_all[:, t * 24:t * 24 + 8]), rt + "dbst", 16)

        if route_pass < 2:
            emit_dump()
            return rt
        @block.vector
        def _(v):
            v.wait_ge(P.sem("const"), CONST_TOTAL)
            P.ti(v.memset(runrow[:, :], 0.0), rt + "rr")

        rr0 = P.val(rt + "rr")

        @block.tensor
        def _(te):
            te.wait_ge(P.sem(rt + "rr"), rr0)
            for t in range(NT):
                pb = ps[4 + (t % 2)]
                te.wait_ge(P.sem(rt + "tk"), tk0 + t + 1)
                if t >= 2:
                    te.wait_ge(P.sem(rt + "p2" + ("E" if t % 2 == 0 else "O")),
                               (p20E if t % 2 == 0 else p20O) + (t - 2) // 2 + 1)
                if t >= 1:
                    te.wait_ge(P.sem(rt + "rr"), rr0 + t)
                te.matmul(pb[:, 0:E], su_bf[:, :], a_all[:, t * E:(t + 1) * E],
                          start=True, stop=False)
                te.matmul(pb[:, 0:E], ones1[:, :], runrow[:, :], start=False, stop=True)
                mm3 = te.matmul(ps[6 + (t % 2)][0:1, (t % 4) * E:(t % 4) * E + E],
                                onescol_b[:, :], a_all[:, t * E:(t + 1) * E],
                                start=True, stop=True)
                P.ti(mm3, rt + "pm")

        @block.vector
        def _(v):
            step = P.stepper(v, rt + "vch")

            def mk2(t):
                pb = ps[4 + (t % 2)]
                b = (t % 2) * 256
                enc = sqbuf[:, b + 64:b + 128]
                v2 = sqbuf[:, b + 0:b + 8]
                pos8 = sqbuf[:, b + 8:b + 16]
                vld = sqbuf[:, b + 16:b + 24]
                slot = sqbuf[:, b + 24:b + 32]
                oob = sqbuf[:, b + 32:b + 40]
                sel = sqbuf[:, b + 40:b + 48]
                h1 = sqbuf[:, b + 48:b + 56]
                t56 = sqbuf[:, b + 56:b + 64]
                ic = sqbuf[:, b + 128:b + 129]
                vi = valid_i[:, (t % 2) * KTOP:(t % 2) * KTOP + KTOP]
                ew = ew_all[:, t * 16:t * 16 + 16]
                pr = pairs_all[:, t * 16:(t + 1) * 16].rearrange("p (a b) -> p a b", b=2)
                ops = [
                    lambda: v.tensor_tensor(out=enc, in0=pb[:, 0:E], in1=iota_enc[:, :],
                                            op=ALU.add),
                    lambda: v.tensor_tensor(out=enc, in0=enc,
                                            in1=a_all[:, t * E:(t + 1) * E], op=ALU.mult),
                    lambda: v.max(out=v2, in_=enc),
                    lambda: v.tensor_single_scalar(out=pos8, in_=ew[:, 0:8], scalar=-64.0,
                                                   op=ALU.add),
                    lambda: v.tensor_scalar(out=pos8, in0=pos8, scalar1=32768.0,
                                            scalar2=0.0, op0=ALU.mult),
                    lambda: v.tensor_tensor(out=pos8, in0=pos8, in1=v2, op=ALU.add),
                    lambda: v.tensor_single_scalar(out=vld, in_=pos8, scalar=float(CAP),
                                                   op=ALU.is_lt),
                    lambda: v.tensor_copy(out=vi, in_=vld),
                    lambda: v.tensor_scalar(out=slot, in0=ew[:, 0:8], scalar1=512.0,
                                            scalar2=0.0, op0=ALU.mult),
                    lambda: v.tensor_tensor(out=slot, in0=slot, in1=pos8, op=ALU.add),
                    lambda: v.tensor_single_scalar(out=h1, in_=pos8, scalar=128.0, op=ALU.is_ge),
                    lambda: v.tensor_single_scalar(out=t56, in_=pos8, scalar=256.0, op=ALU.is_ge),
                    lambda: v.tensor_tensor(out=h1, in0=h1, in1=t56, op=ALU.add),
                    lambda: v.tensor_single_scalar(out=t56, in_=pos8, scalar=384.0, op=ALU.is_ge),
                    lambda: v.tensor_tensor(out=h1, in0=h1, in1=t56, op=ALU.add),
                    lambda: v.tensor_scalar(out=t56, in0=h1, scalar1=-128.0,
                                            scalar2=0.0, op0=ALU.mult),
                    lambda: v.tensor_tensor(out=t56, in0=t56, in1=pos8, op=ALU.add),
                    lambda: v.tensor_scalar(out=t56, in0=t56, scalar1=256.0,
                                            scalar2=0.0, op0=ALU.mult),
                    lambda: v.scalar_tensor_tensor(out=h1, in0=ew[:, 0:8], scalar=4.0,
                                                   in1=h1, op0=ALU.mult, op1=ALU.add),
                    lambda: v.tensor_tensor(out=h1, in0=h1, in1=t56, op=ALU.add),
                    lambda: v.tensor_single_scalar(out=oob, in_=oob_tab[:, :],
                                                   scalar=float(t * 1024), op=ALU.add),
                    lambda: v.select(out=oob, mask=vi, on_true=h1, on_false=oob),
                    lambda: v.tensor_copy(out=sidx_all[:, t * 8:(t + 1) * 8], in_=oob),
                    lambda: v.select(out=sel, mask=vi, on_true=slot,
                                     on_false=dummy_tab[:, 0:1].broadcast_to((TILE, 8))),
                    lambda: v.tensor_copy(out=cmb_all[:, t * 8:(t + 1) * 8], in_=sel),
                    lambda: v.tensor_single_scalar(out=ic, in_=iota_p[:, :],
                                                   scalar=float(t * TILE + 1), op=ALU.add),
                    lambda: v.tensor_copy(out=pr[:, :, 0:1],
                                          in_=ic.rearrange("p (a b) -> p a b", b=1)
                                          .broadcast_to((TILE, 8, 1))),
                    lambda: v.tensor_copy(out=pr[:, :, 1:2],
                                          in_=ew[:, 8:16].rearrange("p (a b) -> p a b", b=1)),
                    lambda: v.tensor_reduce(out=vsum[:, t:t + 1], in_=vld, axis=AX.X,
                                            op=ALU.add),
                ]
                if "routeS" in dbg:
                    ops += [
                        lambda: v.tensor_copy(out=rs_all[:, t * 24 + 0:t * 24 + 8], in_=pos8),
                        lambda: v.tensor_copy(out=rs_all[:, t * 24 + 8:t * 24 + 16], in_=vld),
                        lambda: v.tensor_copy(out=rs_all[:, t * 24 + 16:t * 24 + 24], in_=slot),
                    ]
                return ops

            for j in range(NT // 2):
                pair = [2 * j, 2 * j + 1]
                for t in pair:
                    v.wait_ge(P.sem(rt + "pm"), pm0 + t + 1)
                    step(v.tensor_tensor(out=runrow[:, :], in0=runrow[:, :],
                                         in1=ps[6 + (t % 2)][0:1, (t % 4) * E:(t % 4) * E + E],
                                         op=ALU.add))
                    P.ti(v.engine_nop(), rt + "rr")
                chains = [mk2(t) for t in pair]
                for k in range(len(chains[0])):
                    for ci in range(2):
                        chn(v, rt + "q" + str(ci), chains[ci][k])
                for ci, t in enumerate(pair):
                    nm = rt + "q" + str(ci)
                    v.wait_ge(P.sem(nm), P.val(nm))
                    P.ti(v.engine_nop(), rt + "p2" + ("E" if t % 2 == 0 else "O"))

        @block.tensor
        def _(te):
            pb = ps[2]
            te.wait_ge(P.sem(rt + "sg"), sg0 + NT)
            for t in range(NT):
                te.wait_ge(P.sem(rt + "p2" + ("E" if t % 2 == 0 else "O")),
                           (p20E if t % 2 == 0 else p20O) + t // 2 + 1)
                mmv = te.matmul(pb[0:1, t:t + 1], conef[:, :], vsum[:, t:t + 1],
                                start=True, stop=True)
            P.ti(mmv, rt + "vc")

        @block.vector
        def _(v):
            step = P.stepper(v, rt + "vch")
            v.wait_ge(P.sem(rt + "vc"), P.val(rt + "vc"))
            pb = ps[2]
            step(v.tensor_copy(out=kflag[:, :], in_=pb[0:1, 0:NT]))
            P.ti(v.tensor_copy(out=kflag_i[:, :], in_=kflag[:, :]), rt + "kf")


        if route_pass < 3:
            emit_dump()
            return rt
        @block.gpsimd
        def _(gp):
            gp.wait_ge(P.sem("pz"), P.val("pz"))
            gp.wait_ge(P.sem(rt + "kf"), P.val(rt + "kf"))
            with gp.register(uname(rt + "bnd")) as rb, gp.register(uname(rt + "rf")) as rf:
                gp.reg_mov(rb, NSLOT - 1)
                for t in range(NT):
                    gp.reg_load(rf, kflag_i[0:1, t:t + 1])
                    with gp.If_eq(rf, 0):
                        gp.engine_nop().then_inc(P.sem(rt + "sc"), 16 * KTOP, skip_validation=True)
                    with gp.Else():
                        for k in range(KTOP):
                            gp.indirect_dma_start(
                                out=inv_d.ap(),
                                out_offset=IndirectOffsetOnAxis(sidx_all[:, t * 8 + k:t * 8 + k + 1], 0),
                                in_=pairs_all[:, t * 16 + 2 * k:t * 16 + 2 * k + 2],
                                in_offset=None,
                                bounds_check=rb, oob_is_err=False,
                            ).then_inc(P.sem(rt + "sc"), 16, skip_validation=True)
                    P.v[rt + "sc"] = P.val(rt + "sc") + 16 * KTOP
            P.ti(gp.engine_nop(), rt + "scdone")

        emit_dump()
        return rt

    def emit_expert(rt, hsrc_dram, expert_pass=9):
        ex = rt + "x"
        scd = P.val(rt + "scdone")
        xg0, xs0, xt0, wl0 = P.val(ex + "g"), P.val(ex + "s"), P.val(ex + "t"), P.val(ex + "w")
        ma0, si0, av0, mb0 = P.val(ex + "ma"), P.val(ex + "si"), P.val(ex + "av"), P.val(ex + "mb")
        ye0, ys0 = P.val(ex + "ye"), P.val(ex + "ys")

        @block.sync
        def _(sy):
            sy.wait_ge(P.sem(rt + "scdone"), scd)
            P.ti(sy.dma_start(out=inv_sb[:, :],
                              in_=inv_d.ap().rearrange("(p tt) c -> p (tt c)", p=TILE)), ex + "il", 16)

        @block.vector
        def _(v):
            step = P.stepper(v, ex + "vch")
            v.wait_ge(P.sem(ex + "il"), P.val(ex + "il"))
            iv = inv_sb[:, :].rearrange("p (tt c) -> p tt c", c=2)
            step(v.tensor_copy(out=wden[:, :], in_=iv[:, :, 1:2].rearrange("p a b -> p (a b)")))
            step(v.tensor_single_scalar(out=sqbuf[:, 0:NT], in_=iv[:, :, 0:1].rearrange("p a b -> p (a b)"),
                                        scalar=0.0, op=ALU.is_gt))
            step(v.tensor_copy(out=imsk[:, :], in_=sqbuf[:, 0:NT]))
            step(v.tensor_single_scalar(out=sqbuf[:, NT:2 * NT],
                                        in_=iv[:, :, 0:1].rearrange("p a b -> p (a b)"),
                                        scalar=-1.0, op=ALU.add))
            step(v.select(out=sqbuf[:, 2 * NT:3 * NT], mask=imsk[:, :],
                          on_true=sqbuf[:, NT:2 * NT],
                          on_false=dummyT_tab[:, 0:1].broadcast_to((TILE, NT))))
            P.ti(v.tensor_copy(out=idx_all[:, :], in_=sqbuf[:, 2 * NT:3 * NT]), ex + "ip")

        ip_done = P.val(ex + "ip")

        if expert_pass < 2:
            return ex
        @block.gpsimd
        def _(gp):
            gp.wait_ge(P.sem(ex + "ip"), ip_done)
            for e in range(E):
                for g in range(4):
                    st4 = e * 4 + g
                    if st4 >= 2:
                        gp.wait_ge(P.sem(ex + "s"), xs0 + (st4 - 1) * 16)
                    P.ti(gp.indirect_dma_start(
                        out=xg[:, (st4 % 2) * D:(st4 % 2) * D + D], out_offset=None,
                        in_=hsrc_dram.ap(),
                        in_offset=IndirectOffsetOnAxis(idx_all[:, st4:st4 + 1], 0)), ex + "g", 16)

        @block.sync
        def _(sy):
            for e in range(E):
                if e >= 2 and expert_pass >= 4:
                    sy.wait_ge(P.sem(ex + "ma"), ma0 + (e - 1) * 8)
                    if expert_pass >= 5:
                        sy.wait_ge(P.sem(ex + "mb"), mb0 + (e - 1) * 8)
                par = (e % 2) * 4096
                P.ti(sy.dma_start(out=wg_sb[:, par:par + 4096].rearrange("p (c i) -> p c i", i=II),
                                  in_=gbf_d[e].rearrange("(c p) i -> p c i", p=TILE)), ex + "w", 16)
                P.ti(sy.dma_start(out=wu_sb[:, par:par + 4096].rearrange("p (c i) -> p c i", i=II),
                                  in_=ubf_d[e].rearrange("(c p) i -> p c i", p=TILE)), ex + "w", 16)
                P.ti(sy.dma_start(out=wd_sb[:, par:par + 4096].rearrange("p (c i) -> p c i", i=D),
                                  in_=dbf_d[e].rearrange("(c p) i -> p c i", p=TILE)), ex + "w", 16)
                for g in range(4):
                    st4 = e * 4 + g
                    sy.wait_ge(P.sem(ex + "g"), xg0 + (st4 + 1) * 16)
                    P.ti(sy.dma_start(out=xd_d[st4 * TILE:(st4 + 1) * TILE, :],
                                      in_=xg[:, (st4 % 2) * D:(st4 % 2) * D + D]), ex + "s", 16)

        @block.scalar
        def _(sc):
            for e in range(E):
                par = (e % 2) * 4096
                sc.wait_ge(P.sem(ex + "s"), xs0 + (e + 1) * 64)
                if e >= 2 and expert_pass >= 4:
                    sc.wait_ge(P.sem(ex + "ma"), ma0 + (e - 1) * 8)
                for c in range(8):
                    P.ti(sc.dma_start(out=hlnT[:, par + c * 512:par + (c + 1) * 512],
                                      in_=xd_d[e * 512:(e + 1) * 512, c * TILE:(c + 1) * TILE],
                                      transpose=True), ex + "t", 16)
                for it in range(4 if expert_pass >= 4 else 0):
                    sc.wait_ge(P.sem(ex + "ma"), ma0 + e * 8 + 2 * it + 1)
                    if it > 0:
                        sc.wait_ge(P.sem(ex + "av"), av0 + e * 4 + it)
                    P.ti(sc.activation(out=sact[:, :], in_=ps[it][:, :], func=AF.Silu), ex + "si")
                for st_ in range(4 if expert_pass >= 5 else 0):
                    for half in range(2):
                        sc.wait_ge(P.sem(ex + "mb"), mb0 + e * 8 + st_ * 2 + half + 1)
                        if half == 0:
                            prev_store = e * 4 + st_ - 2
                            if prev_store >= 0:
                                sc.wait_ge(P.sem(ex + "ys"), ys0 + (prev_store + 1) * 16)
                        P.ti(sc.activation(out=y_sb[:, (st_ % 2) * D + half * 512:
                                                      (st_ % 2) * D + half * 512 + 512],
                                           in_=ps[st_ + 4 * half][:, :], func=AF.Identity,
                                           scale=wden[:, e * 4 + st_:e * 4 + st_ + 1]), ex + "ye")
                    P.ti(sc.dma_start(out=yd_d[(e * 4 + st_) * TILE:(e * 4 + st_ + 1) * TILE, :],
                                      in_=y_sb[:, (st_ % 2) * D:(st_ % 2) * D + D]), ex + "ys", 16)

        @block.vector
        def _(v):
            step = P.stepper(v, ex + "vch")
            for e in range(E if expert_pass >= 4 else 0):
                for it in range(4):
                    v.wait_ge(P.sem(ex + "si"), si0 + e * 4 + it + 1)
                    v.wait_ge(P.sem(ex + "ma"), ma0 + e * 8 + 2 * it + 2)
                    step(v.tensor_tensor(out=act_sb[:, it * 512:(it + 1) * 512],
                                         in0=sact[:, :], in1=ps[4 + it][:, :], op=ALU.mult))
                    P.ti(v.engine_nop(), ex + "av")

        @block.tensor
        def _(te):
            for e in range(E if expert_pass >= 4 else 0):
                par = (e % 2) * 4096
                te.wait_ge(P.sem(ex + "t"), xt0 + (e + 1) * 128)
                te.wait_ge(P.sem(ex + "w"), wl0 + (e + 1) * 48)
                if e >= 1:
                    if expert_pass >= 5:
                        te.wait_ge(P.sem(ex + "ye"), ye0 + e * 8)
                    te.wait_ge(P.sem(ex + "av"), av0 + e * 4)
                for it in range(4):
                    for c in range(8):
                        mmg = te.matmul(ps[it][:, :],
                                        wg_sb[:, par + c * 512 + it * TILE:par + c * 512 + it * TILE + TILE],
                                        hlnT[:, par + c * 512:par + (c + 1) * 512],
                                        start=(c == 0), stop=(c == 7))
                    P.ti(mmg, ex + "ma")
                    for c in range(8):
                        mmu = te.matmul(ps[4 + it][:, :],
                                        wu_sb[:, par + c * 512 + it * TILE:par + c * 512 + it * TILE + TILE],
                                        hlnT[:, par + c * 512:par + (c + 1) * 512],
                                        start=(c == 0), stop=(c == 7))
                    P.ti(mmu, ex + "ma")
                if expert_pass < 5:
                    continue
                te.wait_ge(P.sem(ex + "av"), av0 + e * 4 + 4)
                for st_ in range(4):
                    te.wait_ge(P.sem(ex + "si"), si0 + e * 4 + 4)
                    for half in range(2):
                        for ic in range(4):
                            mmd = te.matmul(ps[st_ + 4 * half][:, :],
                                            act_sb[:, ic * 512 + st_ * TILE:ic * 512 + st_ * TILE + TILE],
                                            wd_sb[:, par + ic * 1024 + half * 512:par + ic * 1024 + half * 512 + 512],
                                            start=(ic == 0), stop=(ic == 3))
                        P.ti(mmd, ex + "mb")
        return ex

    def emit_shared(rt, ex, xsrc_dram, xdst_dram):
        cmode = combine_mode
        if cmode >= 3:
            @block.vector
            def _(v):
                step = P.stepper(v, rt + "hvch")
                v.wait_ge(P.sem(rt + "kf"), P.val(rt + "kf"))
                step(v.tensor_single_scalar(out=kflag01[:, :], in_=kflag[:, :], scalar=0.0,
                                            op=ALU.is_gt))
                step(v.memset(stg[:, :], 0.0))
                P.ti(v.engine_nop(), rt + "hfg")

            @block.tensor
            def _(te):
                te.wait_ge(P.sem(rt + "hfg"), P.val(rt + "hfg"))
                te.wait_ge(P.sem(rt + "xye"), P.val(rt + "xye"))
                te.wait_ge(P.sem(rt + "xav"), P.val(rt + "xav"))
                mmf = te.matmul(ps[0][:, 0:NT], ones1[:, :], kflag01[:, :], start=True, stop=True)
                P.ti(mmf, rt + "hfm")

            @block.vector
            def _(v):
                step = P.stepper(v, rt + "hvch")
                v.wait_ge(P.sem(rt + "hfm"), P.val(rt + "hfm"))
                step(v.tensor_copy(out=flagcol[:, :], in_=ps[0][:, 0:NT]))
                P.ti(v.engine_nop(), rt + "hfc")
        sh = rt + "h"
        NGRP = NT // 4
        tp0, ma0, si0, av0 = P.val(sh + "tp"), P.val(sh + "ma"), P.val(sh + "si"), P.val(sh + "av")
        mb0, ev0, st0 = P.val(sh + "mb"), P.val(sh + "ev"), P.val(sh + "st")
        cg0, cq0 = P.val(sh + "cg"), P.val(sh + "cq")
        idf0 = P.val(sh + "idf")
        xl0 = P.val(sh + "xl")
        exma, exmb = P.val(ex + "ma"), P.val(ex + "mb")
        exys = P.val(ex + "ys")

        @block.sync
        def _(sy):
            sy.wait_ge(P.sem(ex + "ma"), exma)
            sy.wait_ge(P.sem(ex + "mb"), exmb)
            P.ti(sy.dma_start(out=wg_sb[:, 0:4096].rearrange("p (c i) -> p c i", i=II),
                              in_=sgbf_d.ap().rearrange("(c p) i -> p c i", p=TILE)), sh + "wl", 16)
            P.ti(sy.dma_start(out=wu_sb[:, 0:4096].rearrange("p (c i) -> p c i", i=II),
                              in_=subf_d.ap().rearrange("(c p) i -> p c i", p=TILE)), sh + "wl", 16)
            P.ti(sy.dma_start(out=wd_sb[:, 0:4096].rearrange("p (c i) -> p c i", i=D),
                              in_=sdbf_d.ap().rearrange("(c p) i -> p c i", p=TILE)), sh + "wl", 16)
            for t in range(NT + 4):
                if t < NT:
                    buf = xa if t % 2 == 0 else xb
                    if t >= 2:
                        sy.wait_ge(P.sem(sh + "ev"), ev0 + 2 * (t - 1))
                    P.ti(sy.dma_start(out=buf[:, :], in_=xsrc_dram[t * TILE:(t + 1) * TILE, :]),
                         sh + "xl", 16)
                if t >= 4:
                    ts_ = t - 4
                    tt = ts_ % 4
                    sy.wait_ge(P.sem(sh + "ev"), ev0 + 2 * (ts_ + 1))
                    P.ti(sy.dma_start(out=xdst_dram[ts_ * TILE:(ts_ + 1) * TILE, :],
                                      in_=x1g[:, tt * D:(tt + 1) * D]), sh + "st", 16)

        @block.scalar
        def _(sc):
            sc.wait_ge(P.sem(ex + "ma"), exma)
            for g in range(NGRP):
                par = (g % 2) * 4096
                if g >= 2:
                    sc.wait_ge(P.sem(sh + "ma"), ma0 + (g - 1) * 8)
                for c in range(8):
                    P.ti(sc.dma_start(out=hlnT[:, par + c * 512:par + (c + 1) * 512],
                                      in_=hln_d[g * 512:(g + 1) * 512, c * TILE:(c + 1) * TILE],
                                      transpose=True), sh + "tp", 16)
                for it in range(4):
                    sc.wait_ge(P.sem(sh + "ma"), ma0 + g * 8 + 2 * it + 1)
                    if it > 0:
                        sc.wait_ge(P.sem(sh + "av"), av0 + g * 4 + it)
                    P.ti(sc.activation(out=sact[:, :], in_=ps[it][:, :], func=AF.Silu), sh + "si")

        @block.vector
        def _(v):
            step = P.stepper(v, sh + "vch")
            for g in range(NGRP):
                for it in range(4):
                    v.wait_ge(P.sem(sh + "si"), si0 + g * 4 + it + 1)
                    v.wait_ge(P.sem(sh + "ma"), ma0 + g * 8 + 2 * it + 2)
                    step(v.tensor_tensor(out=act_sb[:, it * 512:(it + 1) * 512],
                                         in0=sact[:, :], in1=ps[4 + it][:, :], op=ALU.mult))
                    P.ti(v.engine_nop(), sh + "av")
                for tt in range(4):
                    t = g * 4 + tt
                    buf = xa if t % 2 == 0 else xb
                    if cmode >= 3:
                        tf = t
                        if tf < NT:
                            if tf >= 2:
                                v.wait_ge(P.sem(sh + "cq"), cq0 + tf - 1)
                            step(v.tensor_scalar_mul(out=identX[:, (tf % 2) * TILE:(tf % 2 + 1) * TILE],
                                                     in0=ident_b[:, :],
                                                     scalar1=flagcol[:, tf:tf + 1]))
                            P.ti(v.engine_nop(), sh + "idf")
                    v.wait_ge(P.sem(sh + "cq"), cq0 + t + 1)
                    for half in range(2):
                        v.wait_ge(P.sem(sh + "mb"), mb0 + g * 8 + tt * 2 + half + 1)
                        v.wait_ge(P.sem(sh + "xl"), xl0 + (t + 1) * 16)
                        if g >= 1:
                            v.wait_ge(P.sem(sh + "st"), st0 + (g - 1) * 64 + (tt + 1) * 16)
                        step(v.scalar_tensor_tensor(out=x1g[:, tt * D + half * 512:tt * D + half * 512 + 512],
                                                    in0=ps[tt + 4 * half][:, :], scalar=0.5,
                                                    in1=buf[:, half * 512:half * 512 + 512],
                                                    op0=ALU.mult, op1=ALU.add))
                        P.ti(v.engine_nop(), sh + "ev")

        @block.tensor
        def _(te):
            te.wait_ge(P.sem(sh + "wl"), P.val(sh + "wl"))
            te.wait_ge(P.sem(ex + "ye"), P.val(ex + "ye"))
            te.wait_ge(P.sem(ex + "av"), P.val(ex + "av"))
            for g in range(NGRP):
                par = (g % 2) * 4096
                te.wait_ge(P.sem(sh + "tp"), tp0 + (g + 1) * 128)
                if g >= 1:
                    te.wait_ge(P.sem(sh + "ev"), ev0 + g * 8)
                    te.wait_ge(P.sem(sh + "av"), av0 + g * 4)
                for it in range(4):
                    for c in range(8):
                        mmg = te.matmul(ps[it][:, :],
                                        wg_sb[:, c * 512 + it * TILE:c * 512 + it * TILE + TILE],
                                        hlnT[:, par + c * 512:par + (c + 1) * 512],
                                        start=(c == 0), stop=(c == 7))
                    P.ti(mmg, sh + "ma")
                    for c in range(8):
                        mmu = te.matmul(ps[4 + it][:, :],
                                        wu_sb[:, c * 512 + it * TILE:c * 512 + it * TILE + TILE],
                                        hlnT[:, par + c * 512:par + (c + 1) * 512],
                                        start=(c == 0), stop=(c == 7))
                    P.ti(mmu, sh + "ma")
                te.wait_ge(P.sem(sh + "av"), av0 + g * 4 + 4)
                te.wait_ge(P.sem(sh + "si"), si0 + g * 4 + 4)
                for tt in range(4):
                    t = g * 4 + tt
                    spar = (t % 2) * KTOP * D
                    for half in range(2):
                        for ic in range(4):
                            mmd = te.matmul(ps[tt + 4 * half][:, :],
                                            act_sb[:, ic * 512 + tt * TILE:ic * 512 + tt * TILE + TILE],
                                            wd_sb[:, ic * 1024 + half * 512:ic * 1024 + half * 512 + 512],
                                            start=(ic == 0), stop=(ic == 3))
                        P.ti(mmd, sh + "mb")
                    te.wait_ge(P.sem(sh + "cg"), cg0 + (t + 1) * 16 * KTOP)
                    if cmode == 0:
                        P.ti(te.nop(), sh + "cq")
                    elif cmode >= 3:
                        te.wait_ge(P.sem(sh + "idf"), P.val(sh + "idf") - NT + t + 1
                                   if False else idf0 + t + 1)
                        for k in range(KTOP):
                            for half in range(2):
                                mi = te.matmul(ps[tt + 4 * half][:, :],
                                               identX[:, (t % 2) * TILE:(t % 2 + 1) * TILE],
                                               stg[:, spar + k * D + half * 512:
                                                   spar + k * D + half * 512 + 512],
                                               start=False, stop=(k == KTOP - 1))
                        P.ti(mi, sh + "cq")
                    elif cmode == 1:
                        for k in range(KTOP):
                            for half in range(2):
                                mi = te.matmul(ps[tt + 4 * half][:, :], ident_b[:, :],
                                               stg[:, spar + k * D + half * 512:
                                                   spar + k * D + half * 512 + 512],
                                               start=False, stop=(k == KTOP - 1))
                        P.ti(mi, sh + "cq")
                    else:
                        with te.register(uname(sh + f"cr{t}")) as rt_:
                            te.reg_load(rt_, kflag_i[0:1, t:t + 1])
                            with te.If_eq(rt_, 0):
                                te.nop().then_inc(P.sem(sh + "cq"), 1, skip_validation=True)
                            with te.Else():
                                for k in range(KTOP):
                                    for half in range(2):
                                        mi = te.matmul(ps[tt + 4 * half][:, :], ident_b[:, :],
                                                       stg[:, spar + k * D + half * 512:
                                                           spar + k * D + half * 512 + 512],
                                                       start=False, stop=(k == KTOP - 1))
                                mi.then_inc(P.sem(sh + "cq"), 1, skip_validation=True)
                        P.v[sh + "cq"] = P.val(sh + "cq") + 1

        @block.gpsimd
        def _(gp):
            gp.wait_ge(P.sem(ex + "ys"), exys)
            if cmode >= 3:
                gp.wait_ge(P.sem(rt + "hfg"), P.val(rt + "hfg"))
            with gp.register(uname(sh + "rf")) as rf:
                for t in range(NT):
                    spar = (t % 2) * KTOP * D
                    if cmode == 0:
                        P.ti(gp.engine_nop(), sh + "cg", 16 * KTOP)
                        continue
                    if t >= 2:
                        gp.wait_ge(P.sem(sh + "cq"), cq0 + t - 1)
                    if cmode == 1:
                        for k in range(KTOP):
                            P.ti(gp.indirect_dma_start(
                                out=stg[:, spar + k * D:spar + (k + 1) * D], out_offset=None,
                                in_=yd_d.ap(),
                                in_offset=IndirectOffsetOnAxis(cmb_all[:, t * 8 + k:t * 8 + k + 1], 0)),
                                sh + "cg", 16)
                        continue
                    gp.reg_load(rf, kflag_i[0:1, t:t + 1])
                    with gp.If_eq(rf, 0):
                        gp.engine_nop().then_inc(P.sem(sh + "cg"), 16 * KTOP, skip_validation=True)
                    with gp.Else():
                        for k in range(KTOP):
                            gp.indirect_dma_start(
                                out=stg[:, spar + k * D:spar + (k + 1) * D], out_offset=None,
                                in_=yd_d.ap(),
                                in_offset=IndirectOffsetOnAxis(cmb_all[:, t * 8 + k:t * 8 + k + 1], 0),
                            ).then_inc(P.sem(sh + "cg"), 16, skip_validation=True)
                    P.v[sh + "cg"] = P.val(sh + "cg") + 16 * KTOP

        return sh


    def phase_barrier():
        snap = [(n, P.v[n]) for n in list(P.v.keys()) if P.v[n] > 0]
        for sec in (block.sync, block.scalar, block.vector, block.tensor, block.gpsimd):
            @sec
            def _(eng, snap=snap):
                for n, c in snap:
                    eng.wait_ge(P.sem(n), c)

    def emit_attn():
        NGRP = NT // 4
        NTB_ = NT // NB
        r2buf = pairs_all
        x2sb = ew_all
        ttab = scr
        wqkv_sb = xg
        tmpo = sqbuf

        @block.sync
        def _(sy):
            sy.wait_ge(P.sem("wcast"), P.val("wcast"))
            P.ti(sy.dma_start(out=wqkv_sb[:, 0:1536].rearrange("p (c w) -> p c w", w=192),
                              in_=wqkv_bf_d.ap().rearrange("(c p) w -> p c w", p=TILE)), "atwl", 16)
            P.ti(sy.dma_start(out=woT_sb[:, :], in_=wo_bf_d[:, :]), "atwl", 16)
            P.ti(sy.dma_start(out=bqkv_sb[:, :], in_=bqkv_in[:, :]), "atwl", 16)
            P.ti(sy.dma_start(out=bo_sb[:, :], in_=bo_in[:, :]), "atwl", 16)
            P.ti(sy.dma_start(out=drow_sb[0:1, 0:DS], in_=decay_in[:, :]), "atwl", 16)
            sy.wait_ge(P.sem("atprA"), 3)
            P.ti(sy.dma_start(out=misc_d.ap().rearrange("e one -> one e"),
                              in_=drow_sb[0:1, 2 * DS:3 * DS]), "atms", 16)
            sy.wait_ge(P.sem("atms"), 16)
            P.ti(sy.dma_start(out=lnd_col[:, :], in_=misc_d.ap()), "atms", 16)
            for t in range(NT + 2):
                if t < NT:
                    if t >= 4:
                        sy.wait_ge(P.sem("atx2"), t - 3)
                    P.ti(sy.dma_start(out=r2buf[:, (t % 4) * D:(t % 4 + 1) * D],
                                      in_=r2_d[t * TILE:(t + 1) * TILE, :]), "atr2l", 16)
                if t >= 2:
                    tau = t - 2
                    sy.wait_ge(P.sem("atx2h"), tau + 1)
                    sy.wait_ge(P.sem("atx2"), tau + 1)
                    P.ti(sy.dma_start(out=x2_d[tau * TILE:(tau + 1) * TILE, :],
                                      in_=x2sb[:, (tau % 2) * D:(tau % 2 + 1) * D]), "atst", 16)

        @block.gpsimd
        def _(gp):
            P.ti(gp.iota(exprow[:, 0:TILE], pattern=[[0, 4], [1, 32]], base=-16,
                         channel_multiplier=0, allow_small_or_imprecise_dtypes=True), "atpi")
            P.ti(gp.iota(exprow[:, TILE:2 * TILE], pattern=[[0, 4], [1, 32]], base=0,
                         channel_multiplier=0, allow_small_or_imprecise_dtypes=True), "atpi")
            P.ti(gp.iota(exprow[:, 2 * TILE:3 * TILE], pattern=[[1, TILE]], base=-64,
                         channel_multiplier=0, allow_small_or_imprecise_dtypes=True), "atpi")
            P.ti(gp.iota(exprow[:, 3 * TILE:4 * TILE], pattern=[[1, TILE]], base=1,
                         channel_multiplier=0, allow_small_or_imprecise_dtypes=True), "atpi")
            P.ti(gp.iota(ttab[:, 0:TILE], pattern=[[1, TILE]], base=0,
                         channel_multiplier=0, allow_small_or_imprecise_dtypes=True), "atpi")
            P.ti(gp.iota(ttab[:, TILE:2 * TILE], pattern=[[1, 4], [0, 32]], base=0,
                         channel_multiplier=0, allow_small_or_imprecise_dtypes=True), "atpi")
            P.ti(gp.iota(ttab[:, 2 * TILE:3 * TILE], pattern=[[1, 2], [0, 64]], base=0,
                         channel_multiplier=0, allow_small_or_imprecise_dtypes=True), "atpi")
            P.ti(gp.iota(ttab[:, 3 * TILE:4 * TILE], pattern=[[0, 2], [1, 64]], base=0,
                         channel_multiplier=0, allow_small_or_imprecise_dtypes=True), "atpi")

        @block.scalar
        def _(sc):
            stepA = P.stepper(sc, "atprA")
            sc.wait_ge(P.sem("atwl"), 16 * 5)
            stepA(sc.activation(out=drow_sb[0:1, DS:2 * DS], in_=drow_sb[0:1, 0:DS],
                                func=AF.Tanh, scale=0.5, bias=0.0))
            stepA(sc.activation(out=drow_sb[0:1, DS:2 * DS], in_=drow_sb[0:1, DS:2 * DS],
                                func=AF.Identity, scale=0.5, bias=0.5))
            stepA(sc.activation(out=drow_sb[0:1, 2 * DS:3 * DS], in_=drow_sb[0:1, DS:2 * DS],
                                func=AF.Ln, bias=0.0))
            stepA(sc.activation(out=drow_sb[0:1, 3 * DS:4 * DS], in_=drow_sb[0:1, 2 * DS:3 * DS],
                                func=AF.Exp, scale=float(TILE), bias=0.0))
            sc.wait_ge(P.sem("atdc"), 1)
            P.ti(sc.activation(out=dcolx[:, :], in_=dcolx[:, :], func=AF.Exp, bias=0.0), "attab")
            sc.wait_ge(P.sem("atdc"), 2)
            P.ti(sc.activation(out=drowt[:, :], in_=drowt[:, :], func=AF.Exp, bias=0.0), "attab")
            for c in range(8):
                P.ti(sc.dma_start(out=hlnT[:, c * 512:(c + 1) * 512],
                                  in_=hln_d[0:512, c * TILE:(c + 1) * TILE],
                                  transpose=True), "attp", 16)
            for t in range(NT + 2):
                if (t + 3) % 4 == 0:
                    g = (t + 3) // 4
                    if g < NGRP:
                        if g >= 2:
                            sc.wait_ge(P.sem("atqk"), 4 * g - 4)
                        gpar = (g % 2) * 4096
                        for c in range(8):
                            P.ti(sc.dma_start(out=hlnT[:, gpar + c * 512:gpar + (c + 1) * 512],
                                              in_=hln_d[g * 512:(g + 1) * 512, c * TILE:(c + 1) * TILE],
                                              transpose=True), "attp", 16)
                for tq in ([0, 1] if t == 0 else [t + 1]):
                    if tq < NT:
                        sc.wait_ge(P.sem("atqk"), tq + 1)
                        if tq >= 3:
                            sc.wait_ge(P.sem("atsa"), tq - 2)
                        pqq = ps[6 + tq % 2]
                        sl = (tq % 3) * TILE
                        P.ti(sc.activation(out=qT_bf[:, sl:sl + TILE], in_=pqq[0:DS, 0:TILE],
                                           func=AF.Identity, bias=0.0), "atev")
                        P.ti(sc.activation(out=kT_bf[:, sl:sl + TILE], in_=pqq[0:DS, TILE:2 * TILE],
                                           func=AF.Identity, bias=0.0), "atev")
                        P.ti(sc.activation(out=kvt[:, sl:sl + TILE], in_=pqq[:, 2 * TILE:3 * TILE],
                                           func=AF.Identity, bias=0.0), "atev")

        @block.vector
        def _(v):
            step = P.stepper(v, "atvch")
            v.wait_ge(P.sem("atprT"), 1)
            step(v.tensor_copy(out=lnd_rep[:, :], in_=ps[0][:, 0:DS]))
            v.wait_ge(P.sem("atprT"), 2)
            step(v.tensor_copy(out=dtile_rep[:, :], in_=ps[1][0:DS, 0:DS]))
            step(v.tensor_single_scalar(out=expcol[:, 0:1], in_=iota_p[:, :], scalar=31.5, op=ALU.is_gt))
            step(v.tensor_single_scalar(out=expcol[:, 1:2], in_=iota_p[:, :], scalar=63.5, op=ALU.is_gt))
            step(v.tensor_single_scalar(out=expcol[:, 2:3], in_=iota_p[:, :], scalar=95.5, op=ALU.is_gt))
            step(v.tensor_tensor(out=expcol[:, 0:1], in0=expcol[:, 0:1], in1=expcol[:, 1:2], op=ALU.add))
            step(v.tensor_tensor(out=expcol[:, 0:1], in0=expcol[:, 0:1], in1=expcol[:, 2:3], op=ALU.add))
            step(v.tensor_scalar(out=expcol[:, 3:4], in0=expcol[:, 0:1], scalar1=-32.0, scalar2=0.0, op0=ALU.mult))
            step(v.tensor_tensor(out=expcol[:, 3:4], in0=expcol[:, 3:4], in1=iota_p[:, :], op=ALU.add))
            step(v.tensor_scalar(out=expcol[:, 4:5], in0=expcol[:, 3:4], scalar1=-1.0, scalar2=16.0,
                                 op0=ALU.mult, op1=ALU.add))
            step(v.tensor_scalar(out=expcol[:, 5:6], in0=expcol[:, 3:4], scalar1=-1.0, scalar2=32.0,
                                 op0=ALU.mult, op1=ALU.add))
            step(v.tensor_scalar(out=expcol[:, 6:7], in0=iota_p[:, :], scalar1=-1.0, scalar2=64.0,
                                 op0=ALU.mult, op1=ALU.add))
            step(v.tensor_scalar(out=expcol[:, 7:8], in0=iota_p[:, :], scalar1=-1.0, scalar2=127.0,
                                 op0=ALU.mult, op1=ALU.add))
            for l, cc in enumerate((4, 5, 6, 7)):
                step(v.tensor_tensor(out=dcolx[:, l * DS:(l + 1) * DS], in0=lnd_rep[:, :],
                                     in1=expcol[:, cc:cc + 1].broadcast_to((TILE, DS)), op=ALU.mult))
            step(v.tensor_scalar(out=dcolx[:, :], in0=dcolx[:, :], scalar1=80.0, scalar2=0.0, op0=ALU.min))
            P.ti(v.engine_nop(), "atdc")
            v.wait_ge(P.sem("atpi"), 4)
            v.wait_ge(P.sem("atms"), 32)
            step(v.tensor_tensor(out=drowt[:, :], in0=exprow[:, :],
                                 in1=lnd_col[:, :].broadcast_to((DS, 4 * TILE)), op=ALU.mult))
            step(v.tensor_scalar(out=drowt[:, :], in0=drowt[:, :], scalar1=80.0, scalar2=0.0, op0=ALU.min))
            P.ti(v.engine_nop(), "atdc")
            v.wait_ge(P.sem("atpi"), 8)
            tA = xa[:, 0:TILE]
            tB = xa[:, TILE:2 * TILE]
            step(v.tensor_tensor(out=tA, in0=ttab[:, TILE:2 * TILE],
                                 in1=expcol[:, 0:1].broadcast_to((TILE, TILE)), op=ALU.is_equal))
            step(v.tensor_tensor(out=tB, in0=ttab[:, 0:TILE],
                                 in1=iota_p[:, :].broadcast_to((TILE, TILE)), op=ALU.is_ge))
            step(v.tensor_tensor(out=mask3[:, 0:TILE], in0=tA, in1=tB, op=ALU.mult))
            step(v.tensor_tensor(out=tA, in0=ttab[:, 2 * TILE:3 * TILE],
                                 in1=expcol[:, 1:2].broadcast_to((TILE, TILE)), op=ALU.is_equal))
            step(v.tensor_single_scalar(out=tB, in_=ttab[:, 3 * TILE:4 * TILE], scalar=31.5, op=ALU.is_gt))
            step(v.tensor_tensor(out=tA, in0=tA, in1=tB, op=ALU.mult))
            step(v.tensor_scalar(out=expcol[:, 3:4], in0=expcol[:, 1:2], scalar1=-64.0, scalar2=0.0, op0=ALU.mult))
            step(v.tensor_tensor(out=expcol[:, 3:4], in0=expcol[:, 3:4], in1=iota_p[:, :], op=ALU.add))
            step(v.tensor_single_scalar(out=expcol[:, 3:4], in_=expcol[:, 3:4], scalar=31.5, op=ALU.is_gt))
            step(v.tensor_scalar(out=expcol[:, 3:4], in0=expcol[:, 3:4], scalar1=-1.0, scalar2=1.0,
                                 op0=ALU.mult, op1=ALU.add))
            step(v.tensor_tensor(out=mask3[:, TILE:2 * TILE], in0=tA,
                                 in1=expcol[:, 3:4].broadcast_to((TILE, TILE)), op=ALU.mult))
            step(v.tensor_single_scalar(out=tA, in_=ttab[:, 0:TILE], scalar=63.5, op=ALU.is_gt))
            step(v.tensor_scalar(out=expcol[:, 4:5], in0=expcol[:, 1:2], scalar1=-1.0, scalar2=1.0,
                                 op0=ALU.mult, op1=ALU.add))
            step(v.tensor_tensor(out=mask3[:, 2 * TILE:3 * TILE], in0=tA,
                                 in1=expcol[:, 4:5].broadcast_to((TILE, TILE)), op=ALU.mult))
            step(v.memset(st_f[:, :], 0.0))
            step(v.memset(st_b[:, :], 0.0))
            P.ti(v.engine_nop(), "atsu")
            v.wait_ge(P.sem("attab"), 2)
            for t in range(NT + 2):
                for tq in ([0, 1] if t == 0 else [t + 1]):
                    if tq < NT:
                        v.wait_ge(P.sem("atev"), 3 * tq + 3)
                        if tq >= 3:
                            v.wait_ge(P.sem("atim"), tq - 2)
                        sl = (tq % 3) * TILE
                        vsl = (tq % 3) * 256
                        P.ti(v.tensor_tensor(
                            out=v4[:, vsl:vsl + 256].rearrange("p (l i) -> p l i", i=DS),
                            in0=kvt[:, sl + DS:sl + TILE].rearrange("p (o i) -> p o i", o=1)
                                .broadcast_to((TILE, 4, DS)),
                            in1=dcolx[:, :].rearrange("p (l i) -> p l i", i=DS),
                            op=ALU.mult), "atvv")
                if t < NT:
                    v.wait_ge(P.sem("atat"), t + 1)
                    if t >= 2:
                        v.wait_ge(P.sem("atim"), t - 1)
                    pt_ = ps[2 + t % 2]
                    msl = (t % 2) * 3 * TILE
                    for l in range(3):
                        P.ti(v.tensor_tensor(out=mav[:, msl + l * TILE:msl + (l + 1) * TILE],
                                             in0=pt_[:, 0:TILE], in1=mask3[:, l * TILE:(l + 1) * TILE],
                                             op=ALU.mult), "atmk")
                if 0 <= t - 2 < NT:
                    tau = t - 2
                    v.wait_ge(P.sem("atpjh"), 2 * tau + 1)
                    v.wait_ge(P.sem("atr2l"), (tau + 1) * 16)
                    if tau >= 2:
                        v.wait_ge(P.sem("atst"), (tau - 1) * 16)
                    P.ti(v.tensor_tensor(out=x2sb[:, (tau % 2) * D:(tau % 2) * D + 512],
                                         in0=ps[4 + tau % 2][:, :],
                                         in1=r2buf[:, (tau % 4) * D:(tau % 4) * D + 512],
                                         op=ALU.add), "atx2h")
                if 1 <= t <= NT - 1:
                    v.wait_ge(P.sem("atsa"), t)
                    if t == NTB_:
                        step(v.memset(st_f[:, :], 0.0))
                        step(v.memset(st_b[:, :], 0.0))
                    else:
                        v.wait_ge(P.sem("attt"), t)
                        ptm = ps[2 + (t - 1) % 2]
                        step(v.tensor_tensor(out=st_f[:, :], in0=st_f[:, :], in1=dtile_rep[:, :],
                                             op=ALU.mult))
                        step(v.tensor_tensor(out=st_f[:, :], in0=st_f[:, :],
                                             in1=ptm[0:DS, TILE:TILE + DS], op=ALU.add))
                        step(v.tensor_copy(out=st_b[:, :], in_=st_f[:, :]))
                    P.ti(v.engine_nop(), "atsu")
                if 0 <= t - 1 < NT:
                    tau = t - 1
                    v.wait_ge(P.sem("atim"), tau + 1)
                    v.wait_ge(P.sem("atsa"), tau + 1)
                    if tau >= 2:
                        v.wait_ge(P.sem("atpjh"), 2 * tau - 2)
                    pa_ = ps[0 + tau % 2]
                    step(v.tensor_tensor(out=tmpo[0:DS, 0:512], in0=pa_[0:DS, :],
                                         in1=drowt[0:DS, :], op=ALU.mult))
                    step(v.tensor_reduce(
                        out=obuf[0:DS, :].rearrange("p (x o) -> p x o", o=1),
                        in_=tmpo[0:DS, 0:512].rearrange("p (k x) -> p x k", k=4),
                        axis=AX.X, op=ALU.add))
                    P.ti(v.tensor_copy(out=obf[0:DS, (tau % 2) * TILE:(tau % 2 + 1) * TILE],
                                       in_=obuf[0:DS, :]), "atob")
                if 0 <= t - 2 < NT:
                    tau = t - 2
                    v.wait_ge(P.sem("atpjh"), 2 * tau + 2)
                    P.ti(v.tensor_tensor(out=x2sb[:, (tau % 2) * D + 512:(tau % 2) * D + 1024],
                                         in0=ps[4 + tau % 2][:, :],
                                         in1=r2buf[:, (tau % 4) * D + 512:(tau % 4) * D + 1024],
                                         op=ALU.add), "atx2")

        @block.tensor
        def _(te):
            te.wait_ge(P.sem("atwl"), 16 * 5)
            te.wait_ge(P.sem("atprA"), 3)
            P.ti(te.matmul(ps[0][:, 0:DS], ones1[0:1, 0:TILE], drow_sb[0:1, 2 * DS:3 * DS],
                           start=True, stop=True), "atprT")
            te.wait_ge(P.sem("atprA"), 4)
            P.ti(te.matmul(ps[1][0:DS, 0:DS], ones1[0:1, 0:DS], drow_sb[0:1, 3 * DS:4 * DS],
                           start=True, stop=True), "atprT")
            for t in range(NT + 2):
                if 0 <= t - 2 < NT:
                    tau = t - 2
                    pp = ps[4 + tau % 2]
                    te.wait_ge(P.sem("atob"), tau + 1)
                    if tau >= 2:
                        te.wait_ge(P.sem("atx2"), tau - 1)
                    osl = (tau % 2) * TILE
                    te.matmul(pp[:, :], obf[0:DS, osl:osl + TILE], woT_sb[0:DS, 0:512],
                              start=True, stop=False)
                    P.ti(te.matmul(pp[:, :], ones1[0:1, 0:TILE], bo_sb[0:1, 0:512],
                                   start=False, stop=True), "atpjh")
                for tq in ([0, 1] if t == 0 else [t + 1]):
                    if tq < NT:
                        g = tq // 4
                        te.wait_ge(P.sem("attp"), (g + 1) * 128)
                        if tq >= 2:
                            te.wait_ge(P.sem("atev"), 3 * tq)
                        pqq = ps[6 + tq % 2]
                        base = (g % 2) * 4096 + (tq % 4) * TILE
                        for c in range(8):
                            te.matmul(pqq[0:DS, 0:TILE], wqkv_sb[:, c * 192:c * 192 + DS],
                                      hlnT[:, base + c * 512:base + c * 512 + TILE],
                                      start=(c == 0), stop=False)
                        te.matmul(pqq[0:DS, 0:TILE], bqkv_sb[0:1, 0:DS], ones1[0:1, 0:TILE],
                                  start=False, stop=True)
                        for c in range(8):
                            te.matmul(pqq[0:DS, TILE:2 * TILE],
                                      wqkv_sb[:, c * 192 + DS:c * 192 + 2 * DS],
                                      hlnT[:, base + c * 512:base + c * 512 + TILE],
                                      start=(c == 0), stop=False)
                        te.matmul(pqq[0:DS, TILE:2 * TILE], bqkv_sb[0:1, DS:2 * DS],
                                  ones1[0:1, 0:TILE], start=False, stop=True)
                        for c in range(8):
                            te.matmul(pqq[:, 2 * TILE:3 * TILE],
                                      hlnT[:, base + c * 512:base + c * 512 + TILE],
                                      wqkv_sb[:, c * 192 + DS:c * 192 + 192],
                                      start=(c == 0), stop=False)
                        P.ti(te.matmul(pqq[:, 2 * TILE:3 * TILE], ones1[0:1, 0:TILE],
                                       bqkv_sb[0:1, DS:192], start=False, stop=True), "atqk")
                if t < NT:
                    te.wait_ge(P.sem("atev"), 3 * t + 3)
                    if t >= 2:
                        te.wait_ge(P.sem("atmk"), 3 * t - 3)
                        te.wait_ge(P.sem("atsu"), t)
                    pt_ = ps[2 + t % 2]
                    sl = (t % 3) * TILE
                    P.ti(te.matmul(pt_[:, 0:TILE], kT_bf[:, sl:sl + TILE],
                                   qT_bf[:, sl:sl + TILE], start=True, stop=True), "atat")
                    te.wait_ge(P.sem("atvv"), t + 1)
                    P.ti(te.matmul(pt_[0:DS, TILE:TILE + DS], kvt[:, sl:sl + DS],
                                   v4[:, (t % 3) * 256 + 192:(t % 3) * 256 + 256],
                                   start=True, stop=True), "attt")
                if 0 <= t - 1 < NT:
                    tau = t - 1
                    pa_ = ps[0 + tau % 2]
                    te.wait_ge(P.sem("atmk"), 3 * tau + 3)
                    if tau >= 2:
                        te.wait_ge(P.sem("atob"), tau - 1)
                    vsl = (tau % 3) * 256
                    msl = (tau % 2) * 3 * TILE
                    for l in range(3):
                        mi = te.matmul(pa_[0:DS, l * TILE:(l + 1) * TILE],
                                       v4[:, vsl + l * DS:vsl + (l + 1) * DS],
                                       mav[:, msl + l * TILE:msl + (l + 1) * TILE],
                                       start=True, stop=True)
                    P.ti(mi, "atim")
                    te.wait_ge(P.sem("atsu"), tau + 1)
                    sl1 = (tau % 3) * TILE
                    P.ti(te.matmul(pa_[0:DS, 3 * TILE:4 * TILE], st_b[:, :],
                                   qT_bf[:, sl1:sl1 + TILE], start=True, stop=True), "atsa")
                if 0 <= t - 2 < NT:
                    tau = t - 2
                    pp = ps[4 + tau % 2]
                    te.wait_ge(P.sem("atx2h"), tau + 1)
                    osl = (tau % 2) * TILE
                    te.matmul(pp[:, :], obf[0:DS, osl:osl + TILE], woT_sb[0:DS, 512:1024],
                              start=True, stop=False)
                    P.ti(te.matmul(pp[:, :], ones1[0:1, 0:TILE], bo_sb[0:1, 512:1024],
                                   start=False, stop=True), "atpjh")

    ln1_st = emit_ln(x_in, hln_d, "l1")
    if stn >= 1:
        emit_routing(ln1_st, "f1", route_pass)
    if stn >= 2:
        ex1 = emit_expert("f1", hln_d, expert_pass)
    if stn >= 3:
        sh1 = emit_shared("f1", "f1x", x_in, x1_d if stn >= 4 else y_out)
    if stn >= 4:
        phase_barrier()
        emit_ln(x1_d, hln_d, "l2", r2_dram=r2_d)
        phase_barrier()
        emit_attn()
    if stn >= 5:
        phase_barrier()
        ln3_st = emit_ln(x2_d, hln_d, "l1")
        emit_routing(ln3_st, "f1", route_pass)
        ex2 = emit_expert("f1", hln_d, expert_pass)
        sh2 = emit_shared("f1", "f1x", x2_d, y_out)

    @block.sync
    def _(sy):
        if "hln" in dbg:
            sy.wait_ge(P.sem(ln1_st), P.val(ln1_st))
            for t in range(0, NT, 8):
                P.ti(sy.dma_start(out=dbg["hln"][t * TILE:(t + 8) * TILE, :],
                                  in_=hln_d[t * TILE:(t + 8) * TILE, :]), "dbst", 16)
        if stn >= 1:
            if stn >= 3 and "x1" in dbg:
                sy.wait_ge(P.sem("f1hst"), P.val("f1hst"))
                for r in range(0, T, 4096):
                    P.ti(sy.dma_start(out=dbg["x1"][r:r + 4096, :], in_=x1_d[r:r + 4096, :]), "dbst", 16)
            if stn >= 2 and "xd" in dbg:
                sy.wait_ge(P.sem("f1xs"), P.val("f1xs"))
                for r in range(0, NSLOT, 4096):
                    P.ti(sy.dma_start(out=dbg["xd"][r:r + 4096, :], in_=xd_d[r:r + 4096, :]), "dbst", 16)
            if stn >= 2 and "yd" in dbg:
                sy.wait_ge(P.sem("f1xys"), P.val("f1xys"))
                for r in range(0, NSLOT, 4096):
                    P.ti(sy.dma_start(out=dbg["yd"][r:r + 4096, :], in_=yd_d[r:r + 4096, :]), "dbst", 16)
            if stn >= 4 and "x2" in dbg:
                sy.wait_ge(P.sem("atst"), P.val("atst"))
                for r in range(0, T, 4096):
                    P.ti(sy.dma_start(out=dbg["x2"][r:r + 4096, :], in_=x2_d[r:r + 4096, :]), "dbst", 16)
            if "inv" in dbg and P.val("f1scdone") > 0:
                sy.wait_ge(P.sem("f1scdone"), P.val("f1scdone"))
                for r in range(0, NSLOT, 4096):
                    P.ti(sy.dma_start(out=dbg["inv"][r:r + 4096, :],
                                      in_=inv_d[r:r + 4096, :]), "dbst", 16)

    if stn < 3:
        @block.gpsimd
        def _(gp):
            gp.wait_ge(P.sem(ln1_st), P.val(ln1_st))
            for r in range(0, n_out, 1024):
                P.ti(gp.dma_start(out=y_out[r:r + 1024, :],
                                  in_=hln_d[out_rows[0] + r:out_rows[0] + r + 1024, :]), "yst", 16)
    elif stn == 4:
        @block.gpsimd
        def _(gp):
            gp.wait_ge(P.sem("atst"), P.val("atst"))
            for r in range(0, n_out, 1024):
                P.ti(gp.dma_start(out=y_out[r:r + 1024, :],
                                  in_=x2_d[out_rows[0] + r:out_rows[0] + r + 1024, :]), "yst", 16)

    @block.sync
    def _(sy):
        for name in list(P.v.keys()):
            if name.endswith(("st", "sc", "scdone")) or name in ("wcast", "pz"):
                sy.wait_ge(P.sem(name), P.val(name))

    cs.close()
    return nc


def _make_in_map(inputs, stage):
    stn = STAGES[stage]
    m = {
        "x": np.ascontiguousarray(np.asarray(inputs["L0_attn_x"], np.float32).reshape(T, D)),
        "router_w": np.ascontiguousarray(np.asarray(inputs["L0_ffn_router_w"], np.float32)),
    }
    if stn >= 2:
        m["gate_w"] = np.ascontiguousarray(np.asarray(inputs["L0_ffn_gate_w"], np.float32))
        m["up_w"] = np.ascontiguousarray(np.asarray(inputs["L0_ffn_up_w"], np.float32))
        m["down_w"] = np.ascontiguousarray(np.asarray(inputs["L0_ffn_down_w"], np.float32))
    if stn >= 3:
        m["shared_gate_w"] = np.ascontiguousarray(np.asarray(inputs["L0_ffn_shared_gate_w"], np.float32))
        m["shared_up_w"] = np.ascontiguousarray(np.asarray(inputs["L0_ffn_shared_up_w"], np.float32))
        m["shared_down_w"] = np.ascontiguousarray(np.asarray(inputs["L0_ffn_shared_down_w"], np.float32))
    if stn >= 4:
        m["wq_t"] = np.ascontiguousarray(np.asarray(inputs["L0_attn_Wq"], np.float32).T)
        m["wk_t"] = np.ascontiguousarray(np.asarray(inputs["L0_attn_Wk"], np.float32).T)
        m["wv_t"] = np.ascontiguousarray(np.asarray(inputs["L0_attn_Wv"], np.float32).T)
        m["wo_t"] = np.ascontiguousarray(np.asarray(inputs["L0_attn_Wo"], np.float32).T)
        m["bqkv"] = np.ascontiguousarray(np.concatenate([
            np.asarray(inputs["L0_attn_bq"], np.float32),
            np.asarray(inputs["L0_attn_bk"], np.float32),
            np.asarray(inputs["L0_attn_bv"], np.float32)])[None, :])
        m["bo"] = np.ascontiguousarray(np.asarray(inputs["L0_attn_bo"], np.float32)[None, :])
        m["decay"] = np.ascontiguousarray(np.asarray(inputs["L0_attn_decay"], np.float32)[None, :])
    return m


def kernel(**inputs):
    stage = "full"
    nc = build_program(stage=stage)
    res = run_bass_kernel_spmd(nc, [_make_in_map(inputs, stage)], core_ids=[0])
    return res.results[0]["y"].reshape(NB, TB, D)


# version 08 (final)
# speedup vs baseline: 1.1913x; 1.0505x over previous
import contextlib
import numpy as np

import concourse.bass as bass
import concourse.mybir as mybir
from concourse.bass import IndirectOffsetOnAxis
from concourse.bass_utils import run_bass_kernel_spmd

F32 = mybir.dt.float32
BF16 = mybir.dt.bfloat16
U32 = mybir.dt.uint32
I32 = mybir.dt.int32
AF = mybir.ActivationFunctionType
ALU = mybir.AluOpType
AX = mybir.AxisListType

T = 32768
D = 1024
E = 64
KTOP = 8
CAP = 512
II = 512
TILE = 128
NT = T // TILE
DS = 64
NB = 2
TB = T // NB
NTB = TB // TILE
SCALE = 2.5
NSLOT = E * CAP
OOB_BIG = 10_000_000

STAGES = {"ln": 0, "route": 1, "expert": 2, "ffn1": 3, "attn": 4, "full": 5}


class Prog:

    def __init__(self, nc, cs):
        self.nc = nc
        self.cs = cs
        self._h = {}
        self.v = {}

    def sem(self, name):
        if name not in self._h:
            self._h[name] = self.cs.enter_context(self.nc.semaphore(name))
            self.v[name] = 0
        return self._h[name]

    def ti(self, instr, name, n=1):
        h = self.sem(name)
        instr.then_inc(h, n, skip_validation=True)
        self.v[name] += n
        return self.v[name]

    def val(self, name):
        self.sem(name)
        return self.v[name]

    def stepper(self, eng, name):
        def step(instr):
            self.ti(instr, name)
            eng.wait_ge(self.sem(name), self.v[name])
            return instr
        return step


def build_program(stage="full", debug=(), out_rows=(0, T), route_pass=3, expert_pass=9, combine_mode=3):
    stn = STAGES[stage]
    nc = bass.Bass("TRN2", target_bir_lowering=False, debug=False)
    cs = contextlib.ExitStack()
    P = Prog(nc, cs)

    def sbuf(name, shape, dt=F32):
        return cs.enter_context(nc.sbuf_tensor(name, shape, dt))

    def psum(name, shape, dt=F32):
        return cs.enter_context(nc.psum_tensor(name, shape, dt))

    _uid = [0]

    def uname(base):
        _uid[0] += 1
        return base + "_u" + str(_uid[0])

    def chn(eng, nm, thunk):
        eng.wait_ge(P.sem(nm), P.val(nm))
        P.ti(thunk(), nm)

    x_in = nc.declare_dram_parameter("x", [T, D], F32, isOutput=False)
    router_in = nc.declare_dram_parameter("router_w", [D, E], F32, isOutput=False)
    if stn >= 2:
        gate_in = nc.declare_dram_parameter("gate_w", [E, D, II], F32, isOutput=False)
        up_in = nc.declare_dram_parameter("up_w", [E, D, II], F32, isOutput=False)
        down_in = nc.declare_dram_parameter("down_w", [E, II, D], F32, isOutput=False)
    if stn >= 3:
        sgate_in = nc.declare_dram_parameter("shared_gate_w", [D, II], F32, isOutput=False)
        sup_in = nc.declare_dram_parameter("shared_up_w", [D, II], F32, isOutput=False)
        sdown_in = nc.declare_dram_parameter("shared_down_w", [II, D], F32, isOutput=False)
    if stn >= 4:
        wq_in = nc.declare_dram_parameter("wq_t", [D, DS], F32, isOutput=False)
        wk_in = nc.declare_dram_parameter("wk_t", [D, DS], F32, isOutput=False)
        wv_in = nc.declare_dram_parameter("wv_t", [D, DS], F32, isOutput=False)
        wo_in = nc.declare_dram_parameter("wo_t", [DS, D], F32, isOutput=False)
        bqkv_in = nc.declare_dram_parameter("bqkv", [1, 3 * DS], F32, isOutput=False)
        bo_in = nc.declare_dram_parameter("bo", [1, D], F32, isOutput=False)
        decay_in = nc.declare_dram_parameter("decay", [1, DS], F32, isOutput=False)

    n_out = out_rows[1] - out_rows[0]
    y_out = nc.declare_dram_parameter("y", [n_out, D], F32, isOutput=True)

    dbg = {}
    for name, shape, dt in (
        ("hln", [T, D], BF16), ("routeS", [NT, TILE, 24], F32),
        ("cnt", [E, NT], F32), ("inv", [NSLOT, 2], F32), ("xd", [NSLOT, D], BF16),
        ("yd", [NSLOT, D], BF16), ("x1", [T, D], F32), ("x2", [T, D], F32),
    ):
        if name in debug:
            dbg[name] = nc.declare_dram_parameter("dbg_" + name, shape, dt, isOutput=True)

    hln_d = nc.dram_tensor("hln_d", [T + TILE, D], BF16)
    inv_d = nc.dram_tensor("inv_d", [NSLOT, 2], F32)
    xd_d = nc.dram_tensor("xd_d", [NSLOT, D], BF16)
    yd_d = nc.dram_tensor("yd_d", [NSLOT + TILE, D], BF16)
    x1_d = nc.dram_tensor("x1_d", [T, D], F32)
    x2_d = nc.dram_tensor("x2_d", [T, D], F32)
    if stn >= 2:
        gbf_d = nc.dram_tensor("gbf_d", [E, D, II], BF16)
        ubf_d = nc.dram_tensor("ubf_d", [E, D, II], BF16)
        dbf_d = nc.dram_tensor("dbf_d", [E, II, D], BF16)
    if stn >= 3:
        sgbf_d = nc.dram_tensor("sgbf_d", [D, II], BF16)
        subf_d = nc.dram_tensor("subf_d", [D, II], BF16)
        sdbf_d = nc.dram_tensor("sdbf_d", [II, D], BF16)
    if stn >= 4:
        r2_d = nc.dram_tensor("r2_d", [T, D], F32)
        misc_d = nc.dram_tensor("misc_d", [DS, 1], F32)
        wqkv_bf_d = nc.dram_tensor("wqkv_bf_d", [D, 3 * DS], BF16)
        wo_bf_d = nc.dram_tensor("wo_bf_d", [DS, D], BF16)

    ones1 = sbuf("ones1", [1, TILE])
    onescol_b = sbuf("onescol_b", [TILE, 1], BF16)
    su_bf = sbuf("su_bf", [TILE, TILE], BF16)
    iota_rev = sbuf("iota_rev", [TILE, E])
    iota_enc = sbuf("iota_enc", [TILE, E])
    iota_p = sbuf("iota_p", [TILE, 1])
    oob_tab = sbuf("oob_tab", [TILE, KTOP])
    dummy_tab = sbuf("dummy_tab", [TILE, 1])
    dummyT_tab = sbuf("dummyT_tab", [TILE, 1])
    fill_neg = sbuf("fill_neg", [TILE, KTOP])
    router_sb = sbuf("router_sb", [TILE, 8 * E], BF16)

    czero = sbuf("czero", [TILE, 1])
    ceps = sbuf("ceps", [TILE, 1])
    conef = sbuf("conef", [TILE, 1])
    chalf = sbuf("chalf", [TILE, 1])
    nc.const_aps.register(F32, 0.0, czero[:, :])
    nc.const_aps.register(F32, 1e-5, ceps[:, :])
    nc.const_aps.register(F32, 1.0, conef[:, :])
    nc.const_aps.register(F32, 0.5, chalf[:, :])

    xg = sbuf("xg", [TILE, 2 * D], BF16)
    xa = sbuf("xa", [TILE, D])
    xb = sbuf("xb", [TILE, D])
    hba = sbuf("hba", [TILE, D], BF16)
    hbb = sbuf("hbb", [TILE, D], BF16)
    stat = sbuf("stat", [TILE, 16])
    sqbuf = sbuf("sqbuf", [TILE, D])

    hlnT = sbuf("hlnT", [TILE, 2 * 4096], BF16)
    scr = sbuf("scr", [TILE, 2 * 256])
    msk = sbuf("msk", [TILE, E])
    gsr = sbuf("gsr", [TILE, 32])
    tk = sbuf("tk", [TILE, 64])
    a_all = sbuf("a_all", [TILE, NT * E], BF16)
    ew_all = sbuf("ew_all", [TILE, NT * 2 * KTOP])
    runrow = sbuf("runrow", [1, E])
    sidx_all = sbuf("sidx_all", [TILE, NT * KTOP], U32)
    pairs_all = sbuf("pairs_all", [TILE, NT * 2 * KTOP])
    cmb_all = sbuf("cmb_all", [TILE, NT * KTOP], U32)
    kflag = sbuf("kflag", [1, NT])
    kflag_i = sbuf("kflag_i", [1, NT], I32)
    rs_all = sbuf("rs_all", [TILE, NT * 24]) if "routeS" in debug else None
    vsum = sbuf("vsum", [TILE, NT])
    valid_i = sbuf("valid_i", [TILE, 2 * KTOP], I32)
    zer512 = sbuf("zer512", [TILE, 512])
    if STAGES[stage] >= 2:
        inv_sb = sbuf("inv_sb", [TILE, 2 * NT])
        idx_all = sbuf("idx_all", [TILE, NT], U32)
        wden = sbuf("wden", [TILE, NT])
        imsk = sbuf("imsk", [TILE, NT], I32)
        wg_sb = sbuf("wg_sb", [TILE, 2 * 4096], BF16)
        wu_sb = sbuf("wu_sb", [TILE, 2 * 4096], BF16)
        wd_sb = sbuf("wd_sb", [TILE, 2 * 4096], BF16)
        sact = sbuf("sact", [TILE, 512])
        act_sb = sbuf("act_sb", [TILE, 4 * 512], BF16)
        y_sb = sbuf("y_sb", [TILE, 2 * D], BF16)
    if STAGES[stage] >= 3:
        x1g = pairs_all
        stg = a_all
        ident_b = sbuf("ident_b", [TILE, TILE], BF16)
        identX = sbuf("identX", [TILE, 2 * TILE], BF16)
        flagcol = sbuf("flagcol", [TILE, NT])
        kflag01 = sbuf("kflag01", [1, NT])

    if stn >= 4:
        woT_sb = wg_sb[0:DS, 1024:2048]
        mask3 = wg_sb[:, 0:3 * TILE]
        obf = wg_sb[0:DS, 512:768]
        qT_bf = wu_sb[0:DS, 0:3 * TILE]
        kT_bf = wu_sb[0:DS, 512:512 + 3 * TILE]
        kvt = wu_sb[:, 1024:1024 + 3 * TILE]
        v4 = wd_sb[:, 0:3 * 4 * DS]
        mav = wd_sb[:, 1024:1024 + 6 * TILE]
        dcolx = sact[:, 0:4 * DS]
        lnd_rep = sact[:, 256:256 + DS]
        obuf = sact[0:DS, 320:448]
        dtile_rep = sact[0:DS, 448:512]
        drowt = xb[0:DS, 0:4 * TILE]
        exprow = xb[0:DS, 512:1024]
        bqkv_sb = sbuf("bqkv_sb", [1, 3 * DS])
        bo_sb = sbuf("bo_sb", [1, D])
        drow_sb = sbuf("drow_sb", [1, 4 * DS])
        lnd_col = sbuf("lnd_col", [DS, 1])
        st_f = sbuf("st_f", [DS, DS])
        st_b = sbuf("st_b", [DS, DS], BF16)
        expcol = sbuf("expcol", [TILE, 8])

    ps = [psum(f"ps{i}", [TILE, 512]) for i in range(8)]

    block = cs.enter_context(nc.Block())

    @block.vector
    def _(v):
        P.ti(v.memset(ones1[:, :], 1.0), "const")
        P.ti(v.memset(onescol_b[:, :], 1.0), "const")
        P.ti(v.memset(su_bf[:, :], 0.0), "const")
        P.ti(v.memset(fill_neg[:, :], -1e30), "const")
        P.ti(v.memset(zer512[:, :], 0.0), "const")
        if STAGES[stage] >= 3:
            P.ti(v.memset(ident_b[:, :], 0.0), "const")
        P.ti(v.memset(czero[:, :], 0.0), "const")
        P.ti(v.memset(ceps[:, :], 1e-5), "const")
        P.ti(v.memset(conef[:, :], 1.0), "const")
        P.ti(v.memset(chalf[:, :], 0.5), "const")

    @block.gpsimd
    def _(gp):
        gp.wait_ge(P.sem("const"), 9 + (1 if STAGES[stage] >= 3 else 0))
        P.ti(gp.affine_select(out=su_bf[:, :], in_=su_bf[:, :], pattern=[[-1, TILE]],
                              compare_op=ALU.is_ge, fill=1.0, base=0,
                              channel_multiplier=1), "const")
        if STAGES[stage] >= 3:
            gp.wait_ge(P.sem("const"), 10)
            P.ti(gp.affine_select(out=ident_b[:, :], in_=ident_b[:, :], pattern=[[-1, TILE]],
                                  compare_op=ALU.not_equal, fill=1.0, base=0,
                                  channel_multiplier=1), "const")
        P.ti(gp.iota(iota_rev[:, :], pattern=[[-1, E]], base=E, channel_multiplier=0,
                     allow_small_or_imprecise_dtypes=True), "const")
        P.ti(gp.iota(iota_enc[:, :], pattern=[[-32768, E]], base=E * 32768, channel_multiplier=0,
                     allow_small_or_imprecise_dtypes=True), "const")
        P.ti(gp.iota(oob_tab[:, :], pattern=[[1, KTOP]], base=OOB_BIG, channel_multiplier=KTOP,
                     allow_small_or_imprecise_dtypes=True), "const")
        P.ti(gp.iota(iota_p[:, :], pattern=[[0, 1]], base=0, channel_multiplier=1,
                     allow_small_or_imprecise_dtypes=True), "const")
        P.ti(gp.iota(dummy_tab[:, :], pattern=[[0, 1]], base=NSLOT, channel_multiplier=1,
                     allow_small_or_imprecise_dtypes=True), "const")
        P.ti(gp.iota(dummyT_tab[:, :], pattern=[[0, 1]], base=T, channel_multiplier=1,
                     allow_small_or_imprecise_dtypes=True), "const")
        P.ti(gp.dma_start(out=hln_d[T:T + TILE, 0:512], in_=zer512[:, :]), "pz", 16)
        P.ti(gp.dma_start(out=hln_d[T:T + TILE, 512:1024], in_=zer512[:, :]), "pz", 16)
        P.ti(gp.dma_start(out=inv_d.ap().rearrange("(a b) c -> a (b c)", a=TILE),
                          in_=zer512[:, :]), "pz", 16)
        if stn >= 2:
            P.ti(gp.dma_start(out=yd_d[NSLOT:NSLOT + TILE, 0:512], in_=zer512[:, :]), "pz", 16)
            P.ti(gp.dma_start(out=yd_d[NSLOT:NSLOT + TILE, 512:1024], in_=zer512[:, :]), "pz", 16)

    if stn >= 2:
        @block.gpsimd
        def _(gp):
            for e in range(E):
                P.ti(gp.dma_start(out=gbf_d[e, :, :], in_=gate_in[e, :, :]), "wcast", 16)
                P.ti(gp.dma_start(out=ubf_d[e, :, :], in_=up_in[e, :, :]), "wcast", 16)
                P.ti(gp.dma_start(out=dbf_d[e, :, :], in_=down_in[e, :, :]), "wcast", 16)
            if stn >= 3:
                P.ti(gp.dma_start(out=sgbf_d[:, :], in_=sgate_in[:, :]), "wcast", 16)
                P.ti(gp.dma_start(out=subf_d[:, :], in_=sup_in[:, :]), "wcast", 16)
                P.ti(gp.dma_start(out=sdbf_d[:, :], in_=sdown_in[:, :]), "wcast", 16)
            if stn >= 4:
                P.ti(gp.dma_start(out=wqkv_bf_d[:, 0:DS], in_=wq_in[:, :]), "wcast", 16)
                P.ti(gp.dma_start(out=wqkv_bf_d[:, DS:2 * DS], in_=wk_in[:, :]), "wcast", 16)
                P.ti(gp.dma_start(out=wqkv_bf_d[:, 2 * DS:3 * DS], in_=wv_in[:, :]), "wcast", 16)
                P.ti(gp.dma_start(out=wo_bf_d[:, :], in_=wo_in[:, :]), "wcast", 16)

    @block.sync
    def _(sy):
        for c in range(8):
            P.ti(sy.dma_start(out=scr[:, c * E:(c + 1) * E],
                              in_=router_in[c * TILE:(c + 1) * TILE, :]), "ld0", 16)

    @block.vector
    def _(v):
        v.wait_ge(P.sem("ld0"), P.val("ld0"))
        P.ti(v.tensor_copy(out=router_sb[:, :], in_=scr[:, 0:8 * E]), "rcp")

    CONST_TOTAL = P.val("const")

    def emit_ln(src_dram, dst_dram, tag, r2_dram=None):
        ld, cp, dv, st = tag + "ld", tag + "cp", tag + "dv", tag + "st"
        ld0, cp0, dv0, st0 = P.val(ld), P.val(cp), P.val(dv), P.val(st)
        r20 = P.val(tag + "r2")

        @block.sync
        def _(sy):
            for t in range(NT):
                buf = xa if t % 2 == 0 else xb
                if t >= 2:
                    sy.wait_ge(P.sem(cp), cp0 + t - 1)
                    if r2_dram is not None:
                        sy.wait_ge(P.sem(tag + "r2"), r20 + t - 1)
                P.ti(sy.dma_start(out=buf[:, :], in_=src_dram[t * TILE:(t + 1) * TILE, :]), ld, 16)

        @block.vector
        def _(v):
            step = P.stepper(v, tag + "vch")
            v.wait_ge(P.sem("const"), CONST_TOTAL)
            for t in range(NT):
                buf = xa if t % 2 == 0 else xb
                v.wait_ge(P.sem(ld), ld0 + (t + 1) * 16)
                if t >= 1:
                    v.wait_ge(P.sem(cp), cp0 + t)
                step(v.tensor_reduce(out=stat[:, 0:1], in_=buf[:, :], axis=AX.X, op=ALU.add))
                P.ti(v.tensor_scalar(out=stat[:, 3:4], in0=stat[:, 0:1], scalar1=-1.0 / D,
                                     scalar2=0.0, op0=ALU.mult), dv)
                step(v.tensor_scalar(out=stat[:, 1:2], in0=stat[:, 0:1],
                                     scalar1=stat[:, 0:1], scalar2=-1.0 / (float(D) * D),
                                     op0=ALU.mult, op1=ALU.mult))
                P.ti(v.tensor_single_scalar(out=stat[:, 4:5], in_=stat[:, 1:2],
                                            scalar=1e-5, op=ALU.add), dv)
                if r2_dram is not None:
                    hbuf = hba if t % 2 == 0 else hbb
                    v.wait_ge(P.sem(cp), cp0 + t + 1)
                    if t >= 2:
                        v.wait_ge(P.sem(tag + "r2s"), (t - 1) * 16)
                    step(v.tensor_tensor(out=ew_all[:, (t % 2) * D:(t % 2 + 1) * D],
                                         in0=buf[:, :], in1=hbuf[:, :], op=ALU.add))
                    P.ti(v.engine_nop(), tag + "r2")

        @block.scalar
        def _(sc):
            sc.wait_ge(P.sem("const"), CONST_TOTAL)
            step = P.stepper(sc, tag + "ch")
            for t in range(NT):
                buf = xa if t % 2 == 0 else xb
                hbuf = hba if t % 2 == 0 else hbb
                sc.wait_ge(P.sem(ld), ld0 + (t + 1) * 16)
                step(sc.activation(out=xg[:, 0:D], in_=buf[:, :], func=AF.Square,
                                   accum_out=stat[:, 2:3]))
                sc.wait_ge(P.sem(dv), dv0 + 2 * (t + 1))
                step(sc.activation(out=stat[:, 5:6], in_=stat[:, 2:3], func=AF.Ln,
                                   scale=1.0 / D, bias=stat[:, 4:5]))
                step(sc.activation(out=stat[:, 5:6], in_=stat[:, 5:6], func=AF.Exp, scale=-0.5))
                step(sc.mul(out=stat[:, 6:7], in_=stat[:, 3:4], mul=stat[:, 5:6]))
                if t >= 2:
                    sc.wait_ge(P.sem(st), st0 + (t - 1) * 16)
                if r2_dram is not None and t >= 2:
                    sc.wait_ge(P.sem(tag + "r2"), r20 + t - 1)
                P.ti(sc.activation(out=hbuf[:, :], in_=buf[:, :], func=AF.Identity,
                                   bias=stat[:, 6:7], scale=stat[:, 5:6]), cp)
                sc.wait_ge(P.sem(cp), cp0 + t + 1)
                P.ti(sc.dma_start(out=dst_dram[t * TILE:(t + 1) * TILE, :], in_=hbuf[:, :]), st, 16)
                if r2_dram is not None:
                    sc.wait_ge(P.sem(tag + "r2"), r20 + t + 1)
                    P.ti(sc.dma_start(out=r2_dram[t * TILE:(t + 1) * TILE, :],
                                      in_=ew_all[:, (t % 2) * D:(t % 2 + 1) * D]), tag + "r2s", 16)

        return st

    def emit_routing(ln_st, tag, route_pass=3):
        rt = tag
        NGRP = NT // 4
        ln_st0 = P.val(ln_st) - NT * 16
        tp0 = P.val(rt + "tp")
        mm0 = P.val(rt + "mm")
        sg0 = P.val(rt + "sg")
        tk0 = P.val(rt + "tk")
        pm0 = P.val(rt + "pm")
        p20E, p20O = P.val(rt + "p2E"), P.val(rt + "p2O")
        vc0 = P.val(rt + "vc")

        @block.scalar
        def _(sc):
            for g in range(NGRP + 1):
                if g < NGRP:
                    hbuf = hlnT[:, (g % 2) * 4096:(g % 2 + 1) * 4096]
                    sc.wait_ge(P.sem(ln_st), ln_st0 + (g * 4 + 4) * 16)
                    if g >= 2:
                        sc.wait_ge(P.sem(rt + "mm"), mm0 + (g - 1) * 32)
                    for c in range(8):
                        P.ti(sc.dma_start(out=hbuf[:, c * 512:(c + 1) * 512],
                                          in_=hln_d[g * 512:(g + 1) * 512, c * TILE:(c + 1) * TILE],
                                          transpose=True), rt + "tp", 16)
                if g >= 1:
                    gg = g - 1
                    sb = scr[:, (gg % 2) * 256:(gg % 2 + 1) * 256]
                    if gg >= 2:
                        sc.wait_ge(P.sem(rt + "tk"), tk0 + (gg - 1) * 4)
                    for tt in range(4):
                        pb = ps[tt]
                        sc.wait_ge(P.sem(rt + "mm"), mm0 + gg * 32 + (tt + 1) * 8)
                        P.ti(sc.activation(out=sb[:, tt * 64:(tt + 1) * 64], in_=pb[:, 0:E],
                                           func=AF.Tanh, scale=0.5), rt + "sg")

        @block.tensor
        def _(te):
            te.wait_ge(P.sem("rcp"), 1)
            for g in range(NGRP):
                hbuf = hlnT[:, (g % 2) * 4096:(g % 2 + 1) * 4096]
                te.wait_ge(P.sem(rt + "tp"), tp0 + (g + 1) * 128)
                if g >= 1:
                    te.wait_ge(P.sem(rt + "sg"), sg0 + g * 4)
                for tt in range(4):
                    pb = ps[tt]
                    for c in range(8):
                        mm = te.matmul(pb[:, 0:E],
                                       hbuf[:, c * 512 + tt * TILE:c * 512 + (tt + 1) * TILE],
                                       router_sb[:, c * E:(c + 1) * E],
                                       start=(c == 0), stop=(c == 7))
                    P.ti(mm, rt + "mm", 8)


        @block.vector
        def _(v):
            step = P.stepper(v, rt + "vch")
            step(v.memset(sqbuf[:, :], -1e30))
            for g in range(NGRP):
                sb = scr[:, (g % 2) * 256:(g % 2 + 1) * 256]
                v.wait_ge(P.sem(rt + "sg"), sg0 + g * 4 + 4)
                step(v.tensor_scalar(out=sb, in0=sb, scalar1=0.5, scalar2=0.5,
                                     op0=ALU.mult, op1=ALU.add))
                step(v.tensor_reduce(out=gsr[:, :], in_=sb.rearrange("p (a b) -> p a b", b=8),
                                     axis=AX.X, op=ALU.max))

                def mk_ops(tt):
                    t = g * 4 + tt
                    b = tt * 256
                    gs = gsr[:, tt * 8:(tt + 1) * 8]
                    A0 = sqbuf[:, b + 0:b + 8]
                    fill = sqbuf[:, b + 8:b + 16]
                    t8 = sqbuf[:, b + 16:b + 24]
                    s24 = sqbuf[:, b + 24:b + 32]
                    s32 = sqbuf[:, b + 32:b + 33]
                    s33 = sqbuf[:, b + 33:b + 34]
                    s34 = sqbuf[:, b + 34:b + 35]
                    mk = sqbuf[:, b + 64:b + 128]
                    cpy = sqbuf[:, b + 128:b + 192]
                    ew = ew_all[:, t * 16:t * 16 + 16]
                    return [
                        lambda: v.max(out=A0, in_=gs),
                        lambda: v.tensor_copy(out=fill[:, 0:4], in_=A0[:, 0:4]),
                        lambda: v.match_replace(out=gs, in_to_replace=fill, in_values=gs,
                                                imm_value=-1e30),
                        lambda: v.tensor_single_scalar(out=gs, in_=gs, scalar=-1e29, op=ALU.is_lt),
                        lambda: v.tensor_tensor(out=mk, in0=sb[:, tt * 64:(tt + 1) * 64],
                                                in1=gs.rearrange("p (a b) -> p a b", b=1)
                                                .broadcast_to((TILE, 8, 8)), op=ALU.mult),
                        lambda: v.max(out=t8, in_=mk),
                        lambda: v.tensor_copy(out=cpy, in_=mk),
                        lambda: v.match_replace(out=cpy, in_to_replace=t8, in_values=cpy,
                                                imm_value=-1e30),
                        lambda: v.tensor_single_scalar(out=cpy, in_=cpy, scalar=-1e29, op=ALU.is_lt),
                        lambda: v.tensor_copy(out=a_all[:, t * E:(t + 1) * E], in_=cpy),
                        lambda: v.scalar_tensor_tensor(out=mk, in0=mk, scalar=0.5,
                                                       in1=iota_rev[:, :], op0=ALU.mult, op1=ALU.add),
                        lambda: v.tensor_tensor(out=mk, in0=mk, in1=cpy, op=ALU.mult),
                        lambda: v.max(out=t8, in_=mk),
                        lambda: v.max_index(out=sidx_all[:, t * 8:(t + 1) * 8], in_max=t8,
                                            in_values=mk),
                        lambda: v.tensor_copy(out=ew[:, 0:8], in_=sidx_all[:, t * 8:(t + 1) * 8]),
                        lambda: v.tensor_tensor(out=s24, in0=t8, in1=ew[:, 0:8], op=ALU.add),
                        lambda: v.tensor_single_scalar(out=s24, in_=s24, scalar=-64.0, op=ALU.add),
                        lambda: v.tensor_reduce(out=s32, in_=s24, axis=AX.X, op=ALU.add),
                        lambda: v.tensor_scalar_max(out=s33, in0=s32, scalar1=5e-10),
                        lambda: v.reciprocal(out=s34, in_=s33),
                        lambda: v.tensor_scalar(out=ew[:, 8:16], in0=s24, scalar1=s34,
                                                scalar2=SCALE, op0=ALU.mult, op1=ALU.mult),
                    ]

                chains = [mk_ops(tt) for tt in range(4)]
                for k in range(len(chains[0])):
                    for tt in range(4):
                        chn(v, rt + "pc" + str(tt), chains[tt][k])
                for tt in range(4):
                    nm = rt + "pc" + str(tt)
                    v.wait_ge(P.sem(nm), P.val(nm))
                    P.ti(v.engine_nop(), rt + "tk")

        def emit_dump():
            if "routeS" not in dbg:
                return
            @block.scalar
            def _(sc):
                sc.wait_ge(P.sem(rt + "tk"), tk0 + NT)
                if route_pass >= 2:
                    sc.wait_ge(P.sem(rt + "p2E"), p20E + NT // 2)
                    sc.wait_ge(P.sem(rt + "p2O"), p20O + NT // 2)
                for t in range(NT):
                    ew = ew_all[:, t * 16:t * 16 + 16]
                    P.ti(sc.dma_start(out=dbg["routeS"][t, :, 0:8], in_=ew[:, 0:8]), rt + "dbst", 16)
                    P.ti(sc.dma_start(out=dbg["routeS"][t, :, 8:16], in_=ew[:, 8:16]), rt + "dbst", 16)
                    if route_pass >= 2:
                        P.ti(sc.dma_start(out=dbg["routeS"][t, :, 16:24],
                                          in_=rs_all[:, t * 24:t * 24 + 8]), rt + "dbst", 16)

        if route_pass < 2:
            emit_dump()
            return rt
        @block.vector
        def _(v):
            v.wait_ge(P.sem("const"), CONST_TOTAL)
            P.ti(v.memset(runrow[:, :], 0.0), rt + "rr")

        rr0 = P.val(rt + "rr")

        @block.tensor
        def _(te):
            te.wait_ge(P.sem(rt + "rr"), rr0)
            for t in range(NT):
                pb = ps[4 + (t % 2)]
                te.wait_ge(P.sem(rt + "tk"), tk0 + t + 1)
                if t >= 2:
                    te.wait_ge(P.sem(rt + "p2" + ("E" if t % 2 == 0 else "O")),
                               (p20E if t % 2 == 0 else p20O) + (t - 2) // 2 + 1)
                if t >= 1:
                    te.wait_ge(P.sem(rt + "rr"), rr0 + t)
                te.matmul(pb[:, 0:E], su_bf[:, :], a_all[:, t * E:(t + 1) * E],
                          start=True, stop=False)
                te.matmul(pb[:, 0:E], ones1[:, :], runrow[:, :], start=False, stop=True)
                mm3 = te.matmul(ps[6 + (t % 2)][0:1, (t % 4) * E:(t % 4) * E + E],
                                onescol_b[:, :], a_all[:, t * E:(t + 1) * E],
                                start=True, stop=True)
                P.ti(mm3, rt + "pm")

        @block.vector
        def _(v):
            step = P.stepper(v, rt + "vch")

            def mk2(t):
                pb = ps[4 + (t % 2)]
                b = (t % 2) * 256
                enc = sqbuf[:, b + 64:b + 128]
                v2 = sqbuf[:, b + 0:b + 8]
                pos8 = sqbuf[:, b + 8:b + 16]
                vld = sqbuf[:, b + 16:b + 24]
                slot = sqbuf[:, b + 24:b + 32]
                oob = sqbuf[:, b + 32:b + 40]
                sel = sqbuf[:, b + 40:b + 48]
                h1 = sqbuf[:, b + 48:b + 56]
                t56 = sqbuf[:, b + 56:b + 64]
                ic = sqbuf[:, b + 128:b + 129]
                vi = valid_i[:, (t % 2) * KTOP:(t % 2) * KTOP + KTOP]
                ew = ew_all[:, t * 16:t * 16 + 16]
                pr = pairs_all[:, t * 16:(t + 1) * 16].rearrange("p (a b) -> p a b", b=2)
                ops = [
                    lambda: v.tensor_tensor(out=enc, in0=pb[:, 0:E], in1=iota_enc[:, :],
                                            op=ALU.add),
                    lambda: v.tensor_tensor(out=enc, in0=enc,
                                            in1=a_all[:, t * E:(t + 1) * E], op=ALU.mult),
                    lambda: v.max(out=v2, in_=enc),
                    lambda: v.tensor_single_scalar(out=pos8, in_=ew[:, 0:8], scalar=-64.0,
                                                   op=ALU.add),
                    lambda: v.tensor_scalar(out=pos8, in0=pos8, scalar1=32768.0,
                                            scalar2=0.0, op0=ALU.mult),
                    lambda: v.tensor_tensor(out=pos8, in0=pos8, in1=v2, op=ALU.add),
                    lambda: v.tensor_single_scalar(out=vld, in_=pos8, scalar=float(CAP),
                                                   op=ALU.is_lt),
                    lambda: v.tensor_copy(out=vi, in_=vld),
                    lambda: v.tensor_scalar(out=slot, in0=ew[:, 0:8], scalar1=512.0,
                                            scalar2=0.0, op0=ALU.mult),
                    lambda: v.tensor_tensor(out=slot, in0=slot, in1=pos8, op=ALU.add),
                    lambda: v.tensor_single_scalar(out=h1, in_=pos8, scalar=128.0, op=ALU.is_ge),
                    lambda: v.tensor_single_scalar(out=t56, in_=pos8, scalar=256.0, op=ALU.is_ge),
                    lambda: v.tensor_tensor(out=h1, in0=h1, in1=t56, op=ALU.add),
                    lambda: v.tensor_single_scalar(out=t56, in_=pos8, scalar=384.0, op=ALU.is_ge),
                    lambda: v.tensor_tensor(out=h1, in0=h1, in1=t56, op=ALU.add),
                    lambda: v.tensor_scalar(out=t56, in0=h1, scalar1=-128.0,
                                            scalar2=0.0, op0=ALU.mult),
                    lambda: v.tensor_tensor(out=t56, in0=t56, in1=pos8, op=ALU.add),
                    lambda: v.tensor_scalar(out=t56, in0=t56, scalar1=256.0,
                                            scalar2=0.0, op0=ALU.mult),
                    lambda: v.scalar_tensor_tensor(out=h1, in0=ew[:, 0:8], scalar=4.0,
                                                   in1=h1, op0=ALU.mult, op1=ALU.add),
                    lambda: v.tensor_tensor(out=h1, in0=h1, in1=t56, op=ALU.add),
                    lambda: v.tensor_single_scalar(out=oob, in_=oob_tab[:, :],
                                                   scalar=float(t * 1024), op=ALU.add),
                    lambda: v.select(out=oob, mask=vi, on_true=h1, on_false=oob),
                    lambda: v.tensor_copy(out=sidx_all[:, t * 8:(t + 1) * 8], in_=oob),
                    lambda: v.select(out=sel, mask=vi, on_true=slot,
                                     on_false=dummy_tab[:, 0:1].broadcast_to((TILE, 8))),
                    lambda: v.tensor_copy(out=cmb_all[:, t * 8:(t + 1) * 8], in_=sel),
                    lambda: v.tensor_single_scalar(out=ic, in_=iota_p[:, :],
                                                   scalar=float(t * TILE + 1), op=ALU.add),
                    lambda: v.tensor_copy(out=pr[:, :, 0:1],
                                          in_=ic.rearrange("p (a b) -> p a b", b=1)
                                          .broadcast_to((TILE, 8, 1))),
                    lambda: v.tensor_copy(out=pr[:, :, 1:2],
                                          in_=ew[:, 8:16].rearrange("p (a b) -> p a b", b=1)),
                    lambda: v.tensor_reduce(out=vsum[:, t:t + 1], in_=vld, axis=AX.X,
                                            op=ALU.add),
                ]
                if "routeS" in dbg:
                    ops += [
                        lambda: v.tensor_copy(out=rs_all[:, t * 24 + 0:t * 24 + 8], in_=pos8),
                        lambda: v.tensor_copy(out=rs_all[:, t * 24 + 8:t * 24 + 16], in_=vld),
                        lambda: v.tensor_copy(out=rs_all[:, t * 24 + 16:t * 24 + 24], in_=slot),
                    ]
                return ops

            for j in range(NT // 2):
                pair = [2 * j, 2 * j + 1]
                for t in pair:
                    v.wait_ge(P.sem(rt + "pm"), pm0 + t + 1)
                    step(v.tensor_tensor(out=runrow[:, :], in0=runrow[:, :],
                                         in1=ps[6 + (t % 2)][0:1, (t % 4) * E:(t % 4) * E + E],
                                         op=ALU.add))
                    P.ti(v.engine_nop(), rt + "rr")
                chains = [mk2(t) for t in pair]
                for k in range(len(chains[0])):
                    for ci in range(2):
                        chn(v, rt + "q" + str(ci), chains[ci][k])
                for ci, t in enumerate(pair):
                    nm = rt + "q" + str(ci)
                    v.wait_ge(P.sem(nm), P.val(nm))
                    P.ti(v.engine_nop(), rt + "p2" + ("E" if t % 2 == 0 else "O"))

        @block.tensor
        def _(te):
            pb = ps[2]
            te.wait_ge(P.sem(rt + "sg"), sg0 + NT)
            for t in range(NT):
                te.wait_ge(P.sem(rt + "p2" + ("E" if t % 2 == 0 else "O")),
                           (p20E if t % 2 == 0 else p20O) + t // 2 + 1)
                mmv = te.matmul(pb[0:1, t:t + 1], conef[:, :], vsum[:, t:t + 1],
                                start=True, stop=True)
            P.ti(mmv, rt + "vc")

        @block.vector
        def _(v):
            step = P.stepper(v, rt + "vch")
            v.wait_ge(P.sem(rt + "vc"), P.val(rt + "vc"))
            pb = ps[2]
            step(v.tensor_copy(out=kflag[:, :], in_=pb[0:1, 0:NT]))
            P.ti(v.tensor_copy(out=kflag_i[:, :], in_=kflag[:, :]), rt + "kf")


        if route_pass < 3:
            emit_dump()
            return rt
        @block.gpsimd
        def _(gp):
            gp.wait_ge(P.sem("pz"), P.val("pz"))
            gp.wait_ge(P.sem(rt + "kf"), P.val(rt + "kf"))
            with gp.register(uname(rt + "bnd")) as rb, gp.register(uname(rt + "rf")) as rf:
                gp.reg_mov(rb, NSLOT - 1)
                for t in range(NT):
                    gp.reg_load(rf, kflag_i[0:1, t:t + 1])
                    with gp.If_eq(rf, 0):
                        gp.engine_nop().then_inc(P.sem(rt + "sc"), 16 * KTOP, skip_validation=True)
                    with gp.Else():
                        for k in range(KTOP):
                            gp.indirect_dma_start(
                                out=inv_d.ap(),
                                out_offset=IndirectOffsetOnAxis(sidx_all[:, t * 8 + k:t * 8 + k + 1], 0),
                                in_=pairs_all[:, t * 16 + 2 * k:t * 16 + 2 * k + 2],
                                in_offset=None,
                                bounds_check=rb, oob_is_err=False,
                            ).then_inc(P.sem(rt + "sc"), 16, skip_validation=True)
                    P.v[rt + "sc"] = P.val(rt + "sc") + 16 * KTOP
            P.ti(gp.engine_nop(), rt + "scdone")

        emit_dump()
        return rt

    def emit_expert(rt, hsrc_dram, expert_pass=9):
        ex = rt + "x"
        scd = P.val(rt + "scdone")
        xg0, xs0, xt0, wl0 = P.val(ex + "g"), P.val(ex + "s"), P.val(ex + "t"), P.val(ex + "w")
        ma0, si0, av0, mb0 = P.val(ex + "ma"), P.val(ex + "si"), P.val(ex + "av"), P.val(ex + "mb")
        ye0, ys0 = P.val(ex + "ye"), P.val(ex + "ys")

        @block.sync
        def _(sy):
            sy.wait_ge(P.sem(rt + "scdone"), scd)
            P.ti(sy.dma_start(out=inv_sb[:, :],
                              in_=inv_d.ap().rearrange("(p tt) c -> p (tt c)", p=TILE)), ex + "il", 16)

        @block.vector
        def _(v):
            step = P.stepper(v, ex + "vch")
            v.wait_ge(P.sem(ex + "il"), P.val(ex + "il"))
            iv = inv_sb[:, :].rearrange("p (tt c) -> p tt c", c=2)
            step(v.tensor_copy(out=wden[:, :], in_=iv[:, :, 1:2].rearrange("p a b -> p (a b)")))
            step(v.tensor_single_scalar(out=sqbuf[:, 0:NT], in_=iv[:, :, 0:1].rearrange("p a b -> p (a b)"),
                                        scalar=0.0, op=ALU.is_gt))
            step(v.tensor_copy(out=imsk[:, :], in_=sqbuf[:, 0:NT]))
            step(v.tensor_single_scalar(out=sqbuf[:, NT:2 * NT],
                                        in_=iv[:, :, 0:1].rearrange("p a b -> p (a b)"),
                                        scalar=-1.0, op=ALU.add))
            step(v.select(out=sqbuf[:, 2 * NT:3 * NT], mask=imsk[:, :],
                          on_true=sqbuf[:, NT:2 * NT],
                          on_false=dummyT_tab[:, 0:1].broadcast_to((TILE, NT))))
            P.ti(v.tensor_copy(out=idx_all[:, :], in_=sqbuf[:, 2 * NT:3 * NT]), ex + "ip")

        ip_done = P.val(ex + "ip")

        if expert_pass < 2:
            return ex
        @block.gpsimd
        def _(gp):
            gp.wait_ge(P.sem(ex + "ip"), ip_done)
            for e in range(E):
                for g in range(4):
                    st4 = e * 4 + g
                    if st4 >= 2:
                        gp.wait_ge(P.sem(ex + "s"), xs0 + (st4 - 1) * 16)
                    P.ti(gp.indirect_dma_start(
                        out=xg[:, (st4 % 2) * D:(st4 % 2) * D + D], out_offset=None,
                        in_=hsrc_dram.ap(),
                        in_offset=IndirectOffsetOnAxis(idx_all[:, st4:st4 + 1], 0)), ex + "g", 16)

        @block.sync
        def _(sy):
            for e in range(E):
                if e >= 2 and expert_pass >= 4:
                    sy.wait_ge(P.sem(ex + "ma"), ma0 + (e - 1) * 8)
                    if expert_pass >= 5:
                        sy.wait_ge(P.sem(ex + "mb"), mb0 + (e - 1) * 8)
                par = (e % 2) * 4096
                P.ti(sy.dma_start(out=wg_sb[:, par:par + 4096].rearrange("p (c i) -> p c i", i=II),
                                  in_=gbf_d[e].rearrange("(c p) i -> p c i", p=TILE)), ex + "w", 16)
                P.ti(sy.dma_start(out=wu_sb[:, par:par + 4096].rearrange("p (c i) -> p c i", i=II),
                                  in_=ubf_d[e].rearrange("(c p) i -> p c i", p=TILE)), ex + "w", 16)
                P.ti(sy.dma_start(out=wd_sb[:, par:par + 4096].rearrange("p (c i) -> p c i", i=D),
                                  in_=dbf_d[e].rearrange("(c p) i -> p c i", p=TILE)), ex + "w", 16)
                for g in range(4):
                    st4 = e * 4 + g
                    sy.wait_ge(P.sem(ex + "g"), xg0 + (st4 + 1) * 16)
                    P.ti(sy.dma_start(out=xd_d[st4 * TILE:(st4 + 1) * TILE, :],
                                      in_=xg[:, (st4 % 2) * D:(st4 % 2) * D + D]), ex + "s", 16)

        @block.scalar
        def _(sc):
            for e in range(E):
                par = (e % 2) * 4096
                sc.wait_ge(P.sem(ex + "s"), xs0 + (e + 1) * 64)
                if e >= 2 and expert_pass >= 4:
                    sc.wait_ge(P.sem(ex + "ma"), ma0 + (e - 1) * 8)
                for c in range(8):
                    P.ti(sc.dma_start(out=hlnT[:, par + c * 512:par + (c + 1) * 512],
                                      in_=xd_d[e * 512:(e + 1) * 512, c * TILE:(c + 1) * TILE],
                                      transpose=True), ex + "t", 16)
                for it in range(4 if expert_pass >= 4 else 0):
                    sc.wait_ge(P.sem(ex + "ma"), ma0 + e * 8 + 2 * it + 1)
                    if it > 0:
                        sc.wait_ge(P.sem(ex + "av"), av0 + e * 4 + it)
                    P.ti(sc.activation(out=sact[:, :], in_=ps[it][:, :], func=AF.Silu), ex + "si")
                for st_ in range(4 if expert_pass >= 5 else 0):
                    for half in range(2):
                        sc.wait_ge(P.sem(ex + "mb"), mb0 + e * 8 + st_ * 2 + half + 1)
                        if half == 0:
                            prev_store = e * 4 + st_ - 2
                            if prev_store >= 0:
                                sc.wait_ge(P.sem(ex + "ys"), ys0 + (prev_store + 1) * 16)
                        P.ti(sc.activation(out=y_sb[:, (st_ % 2) * D + half * 512:
                                                      (st_ % 2) * D + half * 512 + 512],
                                           in_=ps[st_ + 4 * half][:, :], func=AF.Identity,
                                           scale=wden[:, e * 4 + st_:e * 4 + st_ + 1]), ex + "ye")
                    P.ti(sc.dma_start(out=yd_d[(e * 4 + st_) * TILE:(e * 4 + st_ + 1) * TILE, :],
                                      in_=y_sb[:, (st_ % 2) * D:(st_ % 2) * D + D]), ex + "ys", 16)

        @block.vector
        def _(v):
            step = P.stepper(v, ex + "vch")
            for e in range(E if expert_pass >= 4 else 0):
                for it in range(4):
                    v.wait_ge(P.sem(ex + "si"), si0 + e * 4 + it + 1)
                    v.wait_ge(P.sem(ex + "ma"), ma0 + e * 8 + 2 * it + 2)
                    step(v.tensor_tensor(out=act_sb[:, it * 512:(it + 1) * 512],
                                         in0=sact[:, :], in1=ps[4 + it][:, :], op=ALU.mult))
                    P.ti(v.engine_nop(), ex + "av")

        @block.tensor
        def _(te):
            for e in range(E if expert_pass >= 4 else 0):
                par = (e % 2) * 4096
                te.wait_ge(P.sem(ex + "t"), xt0 + (e + 1) * 128)
                te.wait_ge(P.sem(ex + "w"), wl0 + (e + 1) * 48)
                if e >= 1:
                    if expert_pass >= 5:
                        te.wait_ge(P.sem(ex + "ye"), ye0 + e * 8)
                    te.wait_ge(P.sem(ex + "av"), av0 + e * 4)
                for it in range(4):
                    for c in range(8):
                        mmg = te.matmul(ps[it][:, :],
                                        wg_sb[:, par + c * 512 + it * TILE:par + c * 512 + it * TILE + TILE],
                                        hlnT[:, par + c * 512:par + (c + 1) * 512],
                                        start=(c == 0), stop=(c == 7))
                    P.ti(mmg, ex + "ma")
                    for c in range(8):
                        mmu = te.matmul(ps[4 + it][:, :],
                                        wu_sb[:, par + c * 512 + it * TILE:par + c * 512 + it * TILE + TILE],
                                        hlnT[:, par + c * 512:par + (c + 1) * 512],
                                        start=(c == 0), stop=(c == 7))
                    P.ti(mmu, ex + "ma")
                if expert_pass < 5:
                    continue
                te.wait_ge(P.sem(ex + "av"), av0 + e * 4 + 4)
                for st_ in range(4):
                    te.wait_ge(P.sem(ex + "si"), si0 + e * 4 + 4)
                    for half in range(2):
                        for ic in range(4):
                            mmd = te.matmul(ps[st_ + 4 * half][:, :],
                                            act_sb[:, ic * 512 + st_ * TILE:ic * 512 + st_ * TILE + TILE],
                                            wd_sb[:, par + ic * 1024 + half * 512:par + ic * 1024 + half * 512 + 512],
                                            start=(ic == 0), stop=(ic == 3))
                        P.ti(mmd, ex + "mb")
        return ex

    def emit_shared(rt, ex, xsrc_dram, xdst_dram):
        cmode = combine_mode
        if cmode >= 3:
            @block.vector
            def _(v):
                step = P.stepper(v, rt + "hvch")
                v.wait_ge(P.sem(rt + "kf"), P.val(rt + "kf"))
                step(v.tensor_single_scalar(out=kflag01[:, :], in_=kflag[:, :], scalar=0.0,
                                            op=ALU.is_gt))
                step(v.memset(stg[:, :], 0.0))
                P.ti(v.engine_nop(), rt + "hfg")

            @block.tensor
            def _(te):
                te.wait_ge(P.sem(rt + "hfg"), P.val(rt + "hfg"))
                te.wait_ge(P.sem(rt + "xye"), P.val(rt + "xye"))
                te.wait_ge(P.sem(rt + "xav"), P.val(rt + "xav"))
                mmf = te.matmul(ps[0][:, 0:NT], ones1[:, :], kflag01[:, :], start=True, stop=True)
                P.ti(mmf, rt + "hfm")

            @block.vector
            def _(v):
                step = P.stepper(v, rt + "hvch")
                v.wait_ge(P.sem(rt + "hfm"), P.val(rt + "hfm"))
                step(v.tensor_copy(out=flagcol[:, :], in_=ps[0][:, 0:NT]))
                P.ti(v.engine_nop(), rt + "hfc")
        sh = rt + "h"
        NGRP = NT // 4
        tp0, ma0, si0, av0 = P.val(sh + "tp"), P.val(sh + "ma"), P.val(sh + "si"), P.val(sh + "av")
        mb0, ev0, st0 = P.val(sh + "mb"), P.val(sh + "ev"), P.val(sh + "st")
        cg0, cq0 = P.val(sh + "cg"), P.val(sh + "cq")
        idf0 = P.val(sh + "idf")
        xl0 = P.val(sh + "xl")
        exma, exmb = P.val(ex + "ma"), P.val(ex + "mb")
        exys = P.val(ex + "ys")

        @block.sync
        def _(sy):
            sy.wait_ge(P.sem(ex + "ma"), exma)
            sy.wait_ge(P.sem(ex + "mb"), exmb)
            P.ti(sy.dma_start(out=wg_sb[:, 0:4096].rearrange("p (c i) -> p c i", i=II),
                              in_=sgbf_d.ap().rearrange("(c p) i -> p c i", p=TILE)), sh + "wl", 16)
            P.ti(sy.dma_start(out=wu_sb[:, 0:4096].rearrange("p (c i) -> p c i", i=II),
                              in_=subf_d.ap().rearrange("(c p) i -> p c i", p=TILE)), sh + "wl", 16)
            P.ti(sy.dma_start(out=wd_sb[:, 0:4096].rearrange("p (c i) -> p c i", i=D),
                              in_=sdbf_d.ap().rearrange("(c p) i -> p c i", p=TILE)), sh + "wl", 16)
            for t in range(NT + 4):
                if t < NT:
                    buf = xa if t % 2 == 0 else xb
                    if t >= 2:
                        sy.wait_ge(P.sem(sh + "ev"), ev0 + 2 * (t - 1))
                    P.ti(sy.dma_start(out=buf[:, :], in_=xsrc_dram[t * TILE:(t + 1) * TILE, :]),
                         sh + "xl", 16)
                if t >= 4:
                    ts_ = t - 4
                    tt = ts_ % 4
                    sy.wait_ge(P.sem(sh + "ev"), ev0 + 2 * (ts_ + 1))
                    P.ti(sy.dma_start(out=xdst_dram[ts_ * TILE:(ts_ + 1) * TILE, :],
                                      in_=x1g[:, tt * D:(tt + 1) * D]), sh + "st", 16)

        @block.scalar
        def _(sc):
            sc.wait_ge(P.sem(ex + "ma"), exma)
            for g in range(NGRP):
                par = (g % 2) * 4096
                if g >= 2:
                    sc.wait_ge(P.sem(sh + "ma"), ma0 + (g - 1) * 8)
                for c in range(8):
                    P.ti(sc.dma_start(out=hlnT[:, par + c * 512:par + (c + 1) * 512],
                                      in_=hln_d[g * 512:(g + 1) * 512, c * TILE:(c + 1) * TILE],
                                      transpose=True), sh + "tp", 16)
                for it in range(4):
                    sc.wait_ge(P.sem(sh + "ma"), ma0 + g * 8 + 2 * it + 1)
                    if it > 0:
                        sc.wait_ge(P.sem(sh + "av"), av0 + g * 4 + it)
                    P.ti(sc.activation(out=sact[:, :], in_=ps[it][:, :], func=AF.Silu), sh + "si")

        @block.vector
        def _(v):
            step = P.stepper(v, sh + "vch")
            for g in range(NGRP):
                for it in range(4):
                    v.wait_ge(P.sem(sh + "si"), si0 + g * 4 + it + 1)
                    v.wait_ge(P.sem(sh + "ma"), ma0 + g * 8 + 2 * it + 2)
                    step(v.tensor_tensor(out=act_sb[:, it * 512:(it + 1) * 512],
                                         in0=sact[:, :], in1=ps[4 + it][:, :], op=ALU.mult))
                    P.ti(v.engine_nop(), sh + "av")
                for tt in range(4):
                    t = g * 4 + tt
                    buf = xa if t % 2 == 0 else xb
                    if cmode >= 3:
                        tf = t
                        if tf < NT:
                            if tf >= 2:
                                v.wait_ge(P.sem(sh + "cq"), cq0 + tf - 1)
                            step(v.tensor_scalar_mul(out=identX[:, (tf % 2) * TILE:(tf % 2 + 1) * TILE],
                                                     in0=ident_b[:, :],
                                                     scalar1=flagcol[:, tf:tf + 1]))
                            P.ti(v.engine_nop(), sh + "idf")
                    v.wait_ge(P.sem(sh + "cq"), cq0 + t + 1)
                    for half in range(2):
                        v.wait_ge(P.sem(sh + "mb"), mb0 + g * 8 + tt * 2 + half + 1)
                        v.wait_ge(P.sem(sh + "xl"), xl0 + (t + 1) * 16)
                        if g >= 1:
                            v.wait_ge(P.sem(sh + "st"), st0 + (g - 1) * 64 + (tt + 1) * 16)
                        step(v.scalar_tensor_tensor(out=x1g[:, tt * D + half * 512:tt * D + half * 512 + 512],
                                                    in0=ps[tt + 4 * half][:, :], scalar=0.5,
                                                    in1=buf[:, half * 512:half * 512 + 512],
                                                    op0=ALU.mult, op1=ALU.add))
                        P.ti(v.engine_nop(), sh + "ev")

        @block.tensor
        def _(te):
            te.wait_ge(P.sem(sh + "wl"), P.val(sh + "wl"))
            te.wait_ge(P.sem(ex + "ye"), P.val(ex + "ye"))
            te.wait_ge(P.sem(ex + "av"), P.val(ex + "av"))
            for g in range(NGRP):
                par = (g % 2) * 4096
                te.wait_ge(P.sem(sh + "tp"), tp0 + (g + 1) * 128)
                if g >= 1:
                    te.wait_ge(P.sem(sh + "ev"), ev0 + g * 8)
                    te.wait_ge(P.sem(sh + "av"), av0 + g * 4)
                for it in range(4):
                    for c in range(8):
                        mmg = te.matmul(ps[it][:, :],
                                        wg_sb[:, c * 512 + it * TILE:c * 512 + it * TILE + TILE],
                                        hlnT[:, par + c * 512:par + (c + 1) * 512],
                                        start=(c == 0), stop=(c == 7))
                    P.ti(mmg, sh + "ma")
                    for c in range(8):
                        mmu = te.matmul(ps[4 + it][:, :],
                                        wu_sb[:, c * 512 + it * TILE:c * 512 + it * TILE + TILE],
                                        hlnT[:, par + c * 512:par + (c + 1) * 512],
                                        start=(c == 0), stop=(c == 7))
                    P.ti(mmu, sh + "ma")
                te.wait_ge(P.sem(sh + "av"), av0 + g * 4 + 4)
                te.wait_ge(P.sem(sh + "si"), si0 + g * 4 + 4)
                for tt in range(4):
                    t = g * 4 + tt
                    spar = (t % 2) * KTOP * D
                    for half in range(2):
                        for ic in range(4):
                            mmd = te.matmul(ps[tt + 4 * half][:, :],
                                            act_sb[:, ic * 512 + tt * TILE:ic * 512 + tt * TILE + TILE],
                                            wd_sb[:, ic * 1024 + half * 512:ic * 1024 + half * 512 + 512],
                                            start=(ic == 0), stop=(ic == 3))
                        P.ti(mmd, sh + "mb")
                    te.wait_ge(P.sem(sh + "cg"), cg0 + (t + 1) * 16 * KTOP)
                    if cmode == 0:
                        P.ti(te.nop(), sh + "cq")
                    elif cmode >= 3:
                        te.wait_ge(P.sem(sh + "idf"), P.val(sh + "idf") - NT + t + 1
                                   if False else idf0 + t + 1)
                        for k in range(KTOP):
                            for half in range(2):
                                mi = te.matmul(ps[tt + 4 * half][:, :],
                                               identX[:, (t % 2) * TILE:(t % 2 + 1) * TILE],
                                               stg[:, spar + k * D + half * 512:
                                                   spar + k * D + half * 512 + 512],
                                               start=False, stop=(k == KTOP - 1))
                        P.ti(mi, sh + "cq")
                    elif cmode == 1:
                        for k in range(KTOP):
                            for half in range(2):
                                mi = te.matmul(ps[tt + 4 * half][:, :], ident_b[:, :],
                                               stg[:, spar + k * D + half * 512:
                                                   spar + k * D + half * 512 + 512],
                                               start=False, stop=(k == KTOP - 1))
                        P.ti(mi, sh + "cq")
                    else:
                        with te.register(uname(sh + f"cr{t}")) as rt_:
                            te.reg_load(rt_, kflag_i[0:1, t:t + 1])
                            with te.If_eq(rt_, 0):
                                te.nop().then_inc(P.sem(sh + "cq"), 1, skip_validation=True)
                            with te.Else():
                                for k in range(KTOP):
                                    for half in range(2):
                                        mi = te.matmul(ps[tt + 4 * half][:, :], ident_b[:, :],
                                                       stg[:, spar + k * D + half * 512:
                                                           spar + k * D + half * 512 + 512],
                                                       start=False, stop=(k == KTOP - 1))
                                mi.then_inc(P.sem(sh + "cq"), 1, skip_validation=True)
                        P.v[sh + "cq"] = P.val(sh + "cq") + 1

        @block.gpsimd
        def _(gp):
            gp.wait_ge(P.sem(ex + "ys"), exys)
            if cmode >= 3:
                gp.wait_ge(P.sem(rt + "hfg"), P.val(rt + "hfg"))
            with gp.register(uname(sh + "rf")) as rf:
                for t in range(NT):
                    spar = (t % 2) * KTOP * D
                    if cmode == 0:
                        P.ti(gp.engine_nop(), sh + "cg", 16 * KTOP)
                        continue
                    if t >= 2:
                        gp.wait_ge(P.sem(sh + "cq"), cq0 + t - 1)
                    if cmode == 1:
                        for k in range(KTOP):
                            P.ti(gp.indirect_dma_start(
                                out=stg[:, spar + k * D:spar + (k + 1) * D], out_offset=None,
                                in_=yd_d.ap(),
                                in_offset=IndirectOffsetOnAxis(cmb_all[:, t * 8 + k:t * 8 + k + 1], 0)),
                                sh + "cg", 16)
                        continue
                    gp.reg_load(rf, kflag_i[0:1, t:t + 1])
                    with gp.If_eq(rf, 0):
                        gp.engine_nop().then_inc(P.sem(sh + "cg"), 16 * KTOP, skip_validation=True)
                    with gp.Else():
                        for k in range(KTOP):
                            gp.indirect_dma_start(
                                out=stg[:, spar + k * D:spar + (k + 1) * D], out_offset=None,
                                in_=yd_d.ap(),
                                in_offset=IndirectOffsetOnAxis(cmb_all[:, t * 8 + k:t * 8 + k + 1], 0),
                            ).then_inc(P.sem(sh + "cg"), 16, skip_validation=True)
                    P.v[sh + "cg"] = P.val(sh + "cg") + 16 * KTOP

        return sh


    def phase_barrier():
        snap = [(n, P.v[n]) for n in list(P.v.keys()) if P.v[n] > 0]
        for sec in (block.sync, block.scalar, block.vector, block.tensor, block.gpsimd):
            @sec
            def _(eng, snap=snap):
                for n, c in snap:
                    eng.wait_ge(P.sem(n), c)

    def emit_attn():
        NGRP = NT // 4
        NTB_ = NT // NB
        r2buf = pairs_all
        x2sb = ew_all
        ttab = scr
        wqkv_sb = xg
        tmpo = sqbuf

        @block.sync
        def _(sy):
            sy.wait_ge(P.sem("wcast"), P.val("wcast"))
            P.ti(sy.dma_start(out=wqkv_sb[:, 0:1536].rearrange("p (c w) -> p c w", w=192),
                              in_=wqkv_bf_d.ap().rearrange("(c p) w -> p c w", p=TILE)), "atwl", 16)
            P.ti(sy.dma_start(out=woT_sb[:, :], in_=wo_bf_d[:, :]), "atwl", 16)
            P.ti(sy.dma_start(out=bqkv_sb[:, :], in_=bqkv_in[:, :]), "atwl", 16)
            P.ti(sy.dma_start(out=bo_sb[:, :], in_=bo_in[:, :]), "atwl", 16)
            P.ti(sy.dma_start(out=drow_sb[0:1, 0:DS], in_=decay_in[:, :]), "atwl", 16)
            sy.wait_ge(P.sem("atprA"), 3)
            P.ti(sy.dma_start(out=misc_d.ap().rearrange("e one -> one e"),
                              in_=drow_sb[0:1, 2 * DS:3 * DS]), "atms", 16)
            sy.wait_ge(P.sem("atms"), 16)
            P.ti(sy.dma_start(out=lnd_col[:, :], in_=misc_d.ap()), "atms", 16)
            for t in range(NT + 2):
                if t < NT:
                    if t >= 4:
                        sy.wait_ge(P.sem("atx2"), t - 3)
                    P.ti(sy.dma_start(out=r2buf[:, (t % 4) * D:(t % 4 + 1) * D],
                                      in_=r2_d[t * TILE:(t + 1) * TILE, :]), "atr2l", 16)
                if t >= 2:
                    tau = t - 2
                    sy.wait_ge(P.sem("atx2h"), tau + 1)
                    sy.wait_ge(P.sem("atx2"), tau + 1)
                    P.ti(sy.dma_start(out=x2_d[tau * TILE:(tau + 1) * TILE, :],
                                      in_=x2sb[:, (tau % 2) * D:(tau % 2 + 1) * D]), "atst", 16)

        @block.gpsimd
        def _(gp):
            P.ti(gp.iota(exprow[:, 0:TILE], pattern=[[0, 4], [1, 32]], base=-16,
                         channel_multiplier=0, allow_small_or_imprecise_dtypes=True), "atpi")
            P.ti(gp.iota(exprow[:, TILE:2 * TILE], pattern=[[0, 4], [1, 32]], base=0,
                         channel_multiplier=0, allow_small_or_imprecise_dtypes=True), "atpi")
            P.ti(gp.iota(exprow[:, 2 * TILE:3 * TILE], pattern=[[1, TILE]], base=-64,
                         channel_multiplier=0, allow_small_or_imprecise_dtypes=True), "atpi")
            P.ti(gp.iota(exprow[:, 3 * TILE:4 * TILE], pattern=[[1, TILE]], base=1,
                         channel_multiplier=0, allow_small_or_imprecise_dtypes=True), "atpi")
            P.ti(gp.iota(ttab[:, 0:TILE], pattern=[[1, TILE]], base=0,
                         channel_multiplier=0, allow_small_or_imprecise_dtypes=True), "atpi")
            P.ti(gp.iota(ttab[:, TILE:2 * TILE], pattern=[[1, 4], [0, 32]], base=0,
                         channel_multiplier=0, allow_small_or_imprecise_dtypes=True), "atpi")
            P.ti(gp.iota(ttab[:, 2 * TILE:3 * TILE], pattern=[[1, 2], [0, 64]], base=0,
                         channel_multiplier=0, allow_small_or_imprecise_dtypes=True), "atpi")
            P.ti(gp.iota(ttab[:, 3 * TILE:4 * TILE], pattern=[[0, 2], [1, 64]], base=0,
                         channel_multiplier=0, allow_small_or_imprecise_dtypes=True), "atpi")

        @block.scalar
        def _(sc):
            stepA = P.stepper(sc, "atprA")
            sc.wait_ge(P.sem("atwl"), 16 * 5)
            stepA(sc.activation(out=drow_sb[0:1, DS:2 * DS], in_=drow_sb[0:1, 0:DS],
                                func=AF.Tanh, scale=0.5, bias=0.0))
            stepA(sc.activation(out=drow_sb[0:1, DS:2 * DS], in_=drow_sb[0:1, DS:2 * DS],
                                func=AF.Identity, scale=0.5, bias=0.5))
            stepA(sc.activation(out=drow_sb[0:1, 2 * DS:3 * DS], in_=drow_sb[0:1, DS:2 * DS],
                                func=AF.Ln, bias=0.0))
            stepA(sc.activation(out=drow_sb[0:1, 3 * DS:4 * DS], in_=drow_sb[0:1, 2 * DS:3 * DS],
                                func=AF.Exp, scale=float(TILE), bias=0.0))
            sc.wait_ge(P.sem("atdc"), 1)
            P.ti(sc.activation(out=dcolx[:, :], in_=dcolx[:, :], func=AF.Exp, bias=0.0), "attab")
            sc.wait_ge(P.sem("atdc"), 2)
            P.ti(sc.activation(out=drowt[:, :], in_=drowt[:, :], func=AF.Exp, bias=0.0), "attab")
            for c in range(8):
                P.ti(sc.dma_start(out=hlnT[:, c * 512:(c + 1) * 512],
                                  in_=hln_d[0:512, c * TILE:(c + 1) * TILE],
                                  transpose=True), "attp", 16)
            for t in range(NT + 2):
                if (t + 3) % 4 == 0:
                    g = (t + 3) // 4
                    if g < NGRP:
                        if g >= 2:
                            sc.wait_ge(P.sem("atqk"), 4 * g - 4)
                        gpar = (g % 2) * 4096
                        for c in range(8):
                            P.ti(sc.dma_start(out=hlnT[:, gpar + c * 512:gpar + (c + 1) * 512],
                                              in_=hln_d[g * 512:(g + 1) * 512, c * TILE:(c + 1) * TILE],
                                              transpose=True), "attp", 16)
                for tq in ([0, 1] if t == 0 else [t + 1]):
                    if tq < NT:
                        sc.wait_ge(P.sem("atqk"), tq + 1)
                        if tq >= 3:
                            sc.wait_ge(P.sem("atsa"), tq - 2)
                        pqq = ps[6 + tq % 2]
                        sl = (tq % 3) * TILE
                        P.ti(sc.activation(out=qT_bf[:, sl:sl + TILE], in_=pqq[0:DS, 0:TILE],
                                           func=AF.Identity, bias=0.0), "atev")
                        P.ti(sc.activation(out=kT_bf[:, sl:sl + TILE], in_=pqq[0:DS, TILE:2 * TILE],
                                           func=AF.Identity, bias=0.0), "atev")
                        P.ti(sc.activation(out=kvt[:, sl:sl + TILE], in_=pqq[:, 2 * TILE:3 * TILE],
                                           func=AF.Identity, bias=0.0), "atev")

        @block.vector
        def _(v):
            step = P.stepper(v, "atvch")
            v.wait_ge(P.sem("atprT"), 1)
            step(v.tensor_copy(out=lnd_rep[:, :], in_=ps[0][:, 0:DS]))
            v.wait_ge(P.sem("atprT"), 2)
            step(v.tensor_copy(out=dtile_rep[:, :], in_=ps[1][0:DS, 0:DS]))
            step(v.tensor_single_scalar(out=expcol[:, 0:1], in_=iota_p[:, :], scalar=31.5, op=ALU.is_gt))
            step(v.tensor_single_scalar(out=expcol[:, 1:2], in_=iota_p[:, :], scalar=63.5, op=ALU.is_gt))
            step(v.tensor_single_scalar(out=expcol[:, 2:3], in_=iota_p[:, :], scalar=95.5, op=ALU.is_gt))
            step(v.tensor_tensor(out=expcol[:, 0:1], in0=expcol[:, 0:1], in1=expcol[:, 1:2], op=ALU.add))
            step(v.tensor_tensor(out=expcol[:, 0:1], in0=expcol[:, 0:1], in1=expcol[:, 2:3], op=ALU.add))
            step(v.tensor_scalar(out=expcol[:, 3:4], in0=expcol[:, 0:1], scalar1=-32.0, scalar2=0.0, op0=ALU.mult))
            step(v.tensor_tensor(out=expcol[:, 3:4], in0=expcol[:, 3:4], in1=iota_p[:, :], op=ALU.add))
            step(v.tensor_scalar(out=expcol[:, 4:5], in0=expcol[:, 3:4], scalar1=-1.0, scalar2=16.0,
                                 op0=ALU.mult, op1=ALU.add))
            step(v.tensor_scalar(out=expcol[:, 5:6], in0=expcol[:, 3:4], scalar1=-1.0, scalar2=32.0,
                                 op0=ALU.mult, op1=ALU.add))
            step(v.tensor_scalar(out=expcol[:, 6:7], in0=iota_p[:, :], scalar1=-1.0, scalar2=64.0,
                                 op0=ALU.mult, op1=ALU.add))
            step(v.tensor_scalar(out=expcol[:, 7:8], in0=iota_p[:, :], scalar1=-1.0, scalar2=127.0,
                                 op0=ALU.mult, op1=ALU.add))
            for l, cc in enumerate((4, 5, 6, 7)):
                step(v.tensor_tensor(out=dcolx[:, l * DS:(l + 1) * DS], in0=lnd_rep[:, :],
                                     in1=expcol[:, cc:cc + 1].broadcast_to((TILE, DS)), op=ALU.mult))
            step(v.tensor_scalar(out=dcolx[:, :], in0=dcolx[:, :], scalar1=80.0, scalar2=0.0, op0=ALU.min))
            P.ti(v.engine_nop(), "atdc")
            v.wait_ge(P.sem("atpi"), 4)
            v.wait_ge(P.sem("atms"), 32)
            step(v.tensor_tensor(out=drowt[:, :], in0=exprow[:, :],
                                 in1=lnd_col[:, :].broadcast_to((DS, 4 * TILE)), op=ALU.mult))
            step(v.tensor_scalar(out=drowt[:, :], in0=drowt[:, :], scalar1=80.0, scalar2=0.0, op0=ALU.min))
            P.ti(v.engine_nop(), "atdc")
            v.wait_ge(P.sem("atpi"), 8)
            tA = xa[:, 0:TILE]
            tB = xa[:, TILE:2 * TILE]
            step(v.tensor_tensor(out=tA, in0=ttab[:, TILE:2 * TILE],
                                 in1=expcol[:, 0:1].broadcast_to((TILE, TILE)), op=ALU.is_equal))
            step(v.tensor_tensor(out=tB, in0=ttab[:, 0:TILE],
                                 in1=iota_p[:, :].broadcast_to((TILE, TILE)), op=ALU.is_ge))
            step(v.tensor_tensor(out=mask3[:, 0:TILE], in0=tA, in1=tB, op=ALU.mult))
            step(v.tensor_tensor(out=tA, in0=ttab[:, 2 * TILE:3 * TILE],
                                 in1=expcol[:, 1:2].broadcast_to((TILE, TILE)), op=ALU.is_equal))
            step(v.tensor_single_scalar(out=tB, in_=ttab[:, 3 * TILE:4 * TILE], scalar=31.5, op=ALU.is_gt))
            step(v.tensor_tensor(out=tA, in0=tA, in1=tB, op=ALU.mult))
            step(v.tensor_scalar(out=expcol[:, 3:4], in0=expcol[:, 1:2], scalar1=-64.0, scalar2=0.0, op0=ALU.mult))
            step(v.tensor_tensor(out=expcol[:, 3:4], in0=expcol[:, 3:4], in1=iota_p[:, :], op=ALU.add))
            step(v.tensor_single_scalar(out=expcol[:, 3:4], in_=expcol[:, 3:4], scalar=31.5, op=ALU.is_gt))
            step(v.tensor_scalar(out=expcol[:, 3:4], in0=expcol[:, 3:4], scalar1=-1.0, scalar2=1.0,
                                 op0=ALU.mult, op1=ALU.add))
            step(v.tensor_tensor(out=mask3[:, TILE:2 * TILE], in0=tA,
                                 in1=expcol[:, 3:4].broadcast_to((TILE, TILE)), op=ALU.mult))
            step(v.tensor_single_scalar(out=tA, in_=ttab[:, 0:TILE], scalar=63.5, op=ALU.is_gt))
            step(v.tensor_scalar(out=expcol[:, 4:5], in0=expcol[:, 1:2], scalar1=-1.0, scalar2=1.0,
                                 op0=ALU.mult, op1=ALU.add))
            step(v.tensor_tensor(out=mask3[:, 2 * TILE:3 * TILE], in0=tA,
                                 in1=expcol[:, 4:5].broadcast_to((TILE, TILE)), op=ALU.mult))
            step(v.memset(st_f[:, :], 0.0))
            step(v.memset(st_b[:, :], 0.0))
            P.ti(v.engine_nop(), "atsu")
            v.wait_ge(P.sem("attab"), 2)
            for t in range(NT + 2):
                for tq in ([0, 1] if t == 0 else [t + 1]):
                    if tq < NT:
                        v.wait_ge(P.sem("atev"), 3 * tq + 3)
                        if tq >= 3:
                            v.wait_ge(P.sem("atim"), tq - 2)
                        sl = (tq % 3) * TILE
                        vsl = (tq % 3) * 256
                        P.ti(v.tensor_tensor(
                            out=v4[:, vsl:vsl + 256].rearrange("p (l i) -> p l i", i=DS),
                            in0=kvt[:, sl + DS:sl + TILE].rearrange("p (o i) -> p o i", o=1)
                                .broadcast_to((TILE, 4, DS)),
                            in1=dcolx[:, :].rearrange("p (l i) -> p l i", i=DS),
                            op=ALU.mult), "atvv")
                if t < NT:
                    v.wait_ge(P.sem("atat"), t + 1)
                    if t >= 2:
                        v.wait_ge(P.sem("atim"), t - 1)
                    pt_ = ps[2 + t % 2]
                    msl = (t % 2) * 3 * TILE
                    for l in range(3):
                        P.ti(v.tensor_tensor(out=mav[:, msl + l * TILE:msl + (l + 1) * TILE],
                                             in0=pt_[:, 0:TILE], in1=mask3[:, l * TILE:(l + 1) * TILE],
                                             op=ALU.mult), "atmk")
                if 0 <= t - 2 < NT:
                    tau = t - 2
                    v.wait_ge(P.sem("atpjh"), 2 * tau + 1)
                    v.wait_ge(P.sem("atr2l"), (tau + 1) * 16)
                    if tau >= 2:
                        v.wait_ge(P.sem("atst"), (tau - 1) * 16)
                    P.ti(v.tensor_tensor(out=x2sb[:, (tau % 2) * D:(tau % 2) * D + 512],
                                         in0=ps[4 + tau % 2][:, :],
                                         in1=r2buf[:, (tau % 4) * D:(tau % 4) * D + 512],
                                         op=ALU.add), "atx2h")
                if 1 <= t <= NT - 1:
                    v.wait_ge(P.sem("atsa"), t)
                    if t == NTB_:
                        step(v.memset(st_f[:, :], 0.0))
                        step(v.memset(st_b[:, :], 0.0))
                    else:
                        v.wait_ge(P.sem("attt"), t)
                        ptm = ps[2 + (t - 1) % 2]
                        step(v.tensor_tensor(out=st_f[:, :], in0=st_f[:, :], in1=dtile_rep[:, :],
                                             op=ALU.mult))
                        step(v.tensor_tensor(out=st_f[:, :], in0=st_f[:, :],
                                             in1=ptm[0:DS, TILE:TILE + DS], op=ALU.add))
                        step(v.tensor_copy(out=st_b[:, :], in_=st_f[:, :]))
                    P.ti(v.engine_nop(), "atsu")
                if 0 <= t - 1 < NT:
                    tau = t - 1
                    v.wait_ge(P.sem("atim"), tau + 1)
                    v.wait_ge(P.sem("atsa"), tau + 1)
                    if tau >= 2:
                        v.wait_ge(P.sem("atpjh"), 2 * tau - 2)
                    pa_ = ps[0 + tau % 2]
                    step(v.tensor_tensor(out=tmpo[0:DS, 0:512], in0=pa_[0:DS, :],
                                         in1=drowt[0:DS, :], op=ALU.mult))
                    step(v.tensor_reduce(
                        out=obuf[0:DS, :].rearrange("p (x o) -> p x o", o=1),
                        in_=tmpo[0:DS, 0:512].rearrange("p (k x) -> p x k", k=4),
                        axis=AX.X, op=ALU.add))
                    P.ti(v.tensor_copy(out=obf[0:DS, (tau % 2) * TILE:(tau % 2 + 1) * TILE],
                                       in_=obuf[0:DS, :]), "atob")
                if 0 <= t - 2 < NT:
                    tau = t - 2
                    v.wait_ge(P.sem("atpjh"), 2 * tau + 2)
                    P.ti(v.tensor_tensor(out=x2sb[:, (tau % 2) * D + 512:(tau % 2) * D + 1024],
                                         in0=ps[4 + tau % 2][:, :],
                                         in1=r2buf[:, (tau % 4) * D + 512:(tau % 4) * D + 1024],
                                         op=ALU.add), "atx2")

        @block.tensor
        def _(te):
            te.wait_ge(P.sem("atwl"), 16 * 5)
            te.wait_ge(P.sem("atprA"), 3)
            P.ti(te.matmul(ps[0][:, 0:DS], ones1[0:1, 0:TILE], drow_sb[0:1, 2 * DS:3 * DS],
                           start=True, stop=True), "atprT")
            te.wait_ge(P.sem("atprA"), 4)
            P.ti(te.matmul(ps[1][0:DS, 0:DS], ones1[0:1, 0:DS], drow_sb[0:1, 3 * DS:4 * DS],
                           start=True, stop=True), "atprT")
            for t in range(NT + 2):
                if 0 <= t - 2 < NT:
                    tau = t - 2
                    pp = ps[4 + tau % 2]
                    te.wait_ge(P.sem("atob"), tau + 1)
                    if tau >= 2:
                        te.wait_ge(P.sem("atx2"), tau - 1)
                    osl = (tau % 2) * TILE
                    te.matmul(pp[:, :], obf[0:DS, osl:osl + TILE], woT_sb[0:DS, 0:512],
                              start=True, stop=False)
                    P.ti(te.matmul(pp[:, :], ones1[0:1, 0:TILE], bo_sb[0:1, 0:512],
                                   start=False, stop=True), "atpjh")
                for tq in ([0, 1] if t == 0 else [t + 1]):
                    if tq < NT:
                        g = tq // 4
                        te.wait_ge(P.sem("attp"), (g + 1) * 128)
                        if tq >= 2:
                            te.wait_ge(P.sem("atev"), 3 * tq)
                        pqq = ps[6 + tq % 2]
                        base = (g % 2) * 4096 + (tq % 4) * TILE
                        for c in range(8):
                            te.matmul(pqq[0:DS, 0:TILE], wqkv_sb[:, c * 192:c * 192 + DS],
                                      hlnT[:, base + c * 512:base + c * 512 + TILE],
                                      start=(c == 0), stop=False)
                        te.matmul(pqq[0:DS, 0:TILE], bqkv_sb[0:1, 0:DS], ones1[0:1, 0:TILE],
                                  start=False, stop=True)
                        for c in range(8):
                            te.matmul(pqq[0:DS, TILE:2 * TILE],
                                      wqkv_sb[:, c * 192 + DS:c * 192 + 2 * DS],
                                      hlnT[:, base + c * 512:base + c * 512 + TILE],
                                      start=(c == 0), stop=False)
                        te.matmul(pqq[0:DS, TILE:2 * TILE], bqkv_sb[0:1, DS:2 * DS],
                                  ones1[0:1, 0:TILE], start=False, stop=True)
                        for c in range(8):
                            te.matmul(pqq[:, 2 * TILE:3 * TILE],
                                      hlnT[:, base + c * 512:base + c * 512 + TILE],
                                      wqkv_sb[:, c * 192 + DS:c * 192 + 192],
                                      start=(c == 0), stop=False)
                        P.ti(te.matmul(pqq[:, 2 * TILE:3 * TILE], ones1[0:1, 0:TILE],
                                       bqkv_sb[0:1, DS:192], start=False, stop=True), "atqk")
                if t < NT:
                    te.wait_ge(P.sem("atev"), 3 * t + 3)
                    if t >= 2:
                        te.wait_ge(P.sem("atmk"), 3 * t - 3)
                        te.wait_ge(P.sem("atsu"), t)
                    pt_ = ps[2 + t % 2]
                    sl = (t % 3) * TILE
                    P.ti(te.matmul(pt_[:, 0:TILE], kT_bf[:, sl:sl + TILE],
                                   qT_bf[:, sl:sl + TILE], start=True, stop=True), "atat")
                    te.wait_ge(P.sem("atvv"), t + 1)
                    P.ti(te.matmul(pt_[0:DS, TILE:TILE + DS], kvt[:, sl:sl + DS],
                                   v4[:, (t % 3) * 256 + 192:(t % 3) * 256 + 256],
                                   start=True, stop=True), "attt")
                if 0 <= t - 1 < NT:
                    tau = t - 1
                    pa_ = ps[0 + tau % 2]
                    te.wait_ge(P.sem("atmk"), 3 * tau + 3)
                    if tau >= 2:
                        te.wait_ge(P.sem("atob"), tau - 1)
                    vsl = (tau % 3) * 256
                    msl = (tau % 2) * 3 * TILE
                    for l in range(3):
                        mi = te.matmul(pa_[0:DS, l * TILE:(l + 1) * TILE],
                                       v4[:, vsl + l * DS:vsl + (l + 1) * DS],
                                       mav[:, msl + l * TILE:msl + (l + 1) * TILE],
                                       start=True, stop=True)
                    P.ti(mi, "atim")
                    te.wait_ge(P.sem("atsu"), tau + 1)
                    sl1 = (tau % 3) * TILE
                    P.ti(te.matmul(pa_[0:DS, 3 * TILE:4 * TILE], st_b[:, :],
                                   qT_bf[:, sl1:sl1 + TILE], start=True, stop=True), "atsa")
                if 0 <= t - 2 < NT:
                    tau = t - 2
                    pp = ps[4 + tau % 2]
                    te.wait_ge(P.sem("atx2h"), tau + 1)
                    osl = (tau % 2) * TILE
                    te.matmul(pp[:, :], obf[0:DS, osl:osl + TILE], woT_sb[0:DS, 512:1024],
                              start=True, stop=False)
                    P.ti(te.matmul(pp[:, :], ones1[0:1, 0:TILE], bo_sb[0:1, 512:1024],
                                   start=False, stop=True), "atpjh")

    ln1_st = emit_ln(x_in, hln_d, "l1")
    if stn >= 1:
        emit_routing(ln1_st, "f1", route_pass)
    if stn >= 2:
        ex1 = emit_expert("f1", hln_d, expert_pass)
    if stn >= 3:
        sh1 = emit_shared("f1", "f1x", x_in, x1_d if stn >= 4 else y_out)
    if stn >= 4:
        phase_barrier()
        emit_ln(x1_d, hln_d, "l2", r2_dram=r2_d)
        phase_barrier()
        emit_attn()
    if stn >= 5:
        phase_barrier()
        ln3_st = emit_ln(x2_d, hln_d, "l1")
        emit_routing(ln3_st, "f1", route_pass)
        ex2 = emit_expert("f1", hln_d, expert_pass)
        sh2 = emit_shared("f1", "f1x", x2_d, y_out)

    @block.sync
    def _(sy):
        if "hln" in dbg:
            sy.wait_ge(P.sem(ln1_st), P.val(ln1_st))
            for t in range(0, NT, 8):
                P.ti(sy.dma_start(out=dbg["hln"][t * TILE:(t + 8) * TILE, :],
                                  in_=hln_d[t * TILE:(t + 8) * TILE, :]), "dbst", 16)
        if stn >= 1:
            if stn >= 3 and "x1" in dbg:
                sy.wait_ge(P.sem("f1hst"), P.val("f1hst"))
                for r in range(0, T, 4096):
                    P.ti(sy.dma_start(out=dbg["x1"][r:r + 4096, :], in_=x1_d[r:r + 4096, :]), "dbst", 16)
            if stn >= 2 and "xd" in dbg:
                sy.wait_ge(P.sem("f1xs"), P.val("f1xs"))
                for r in range(0, NSLOT, 4096):
                    P.ti(sy.dma_start(out=dbg["xd"][r:r + 4096, :], in_=xd_d[r:r + 4096, :]), "dbst", 16)
            if stn >= 2 and "yd" in dbg:
                sy.wait_ge(P.sem("f1xys"), P.val("f1xys"))
                for r in range(0, NSLOT, 4096):
                    P.ti(sy.dma_start(out=dbg["yd"][r:r + 4096, :], in_=yd_d[r:r + 4096, :]), "dbst", 16)
            if stn >= 4 and "x2" in dbg:
                sy.wait_ge(P.sem("atst"), P.val("atst"))
                for r in range(0, T, 4096):
                    P.ti(sy.dma_start(out=dbg["x2"][r:r + 4096, :], in_=x2_d[r:r + 4096, :]), "dbst", 16)
            if "inv" in dbg and P.val("f1scdone") > 0:
                sy.wait_ge(P.sem("f1scdone"), P.val("f1scdone"))
                for r in range(0, NSLOT, 4096):
                    P.ti(sy.dma_start(out=dbg["inv"][r:r + 4096, :],
                                      in_=inv_d[r:r + 4096, :]), "dbst", 16)

    if stn < 3:
        @block.gpsimd
        def _(gp):
            gp.wait_ge(P.sem(ln1_st), P.val(ln1_st))
            for r in range(0, n_out, 1024):
                P.ti(gp.dma_start(out=y_out[r:r + 1024, :],
                                  in_=hln_d[out_rows[0] + r:out_rows[0] + r + 1024, :]), "yst", 16)
    elif stn == 4:
        @block.gpsimd
        def _(gp):
            gp.wait_ge(P.sem("atst"), P.val("atst"))
            for r in range(0, n_out, 1024):
                P.ti(gp.dma_start(out=y_out[r:r + 1024, :],
                                  in_=x2_d[out_rows[0] + r:out_rows[0] + r + 1024, :]), "yst", 16)

    @block.sync
    def _(sy):
        for name in list(P.v.keys()):
            if name.endswith(("st", "sc", "scdone")) or name in ("wcast", "pz"):
                sy.wait_ge(P.sem(name), P.val(name))

    cs.close()
    return nc


def _make_in_map(inputs, stage):
    stn = STAGES[stage]
    m = {
        "x": np.ascontiguousarray(np.asarray(inputs["L0_attn_x"], np.float32).reshape(T, D)),
        "router_w": np.ascontiguousarray(np.asarray(inputs["L0_ffn_router_w"], np.float32)),
    }
    if stn >= 2:
        m["gate_w"] = np.ascontiguousarray(np.asarray(inputs["L0_ffn_gate_w"], np.float32))
        m["up_w"] = np.ascontiguousarray(np.asarray(inputs["L0_ffn_up_w"], np.float32))
        m["down_w"] = np.ascontiguousarray(np.asarray(inputs["L0_ffn_down_w"], np.float32))
    if stn >= 3:
        m["shared_gate_w"] = np.ascontiguousarray(np.asarray(inputs["L0_ffn_shared_gate_w"], np.float32))
        m["shared_up_w"] = np.ascontiguousarray(np.asarray(inputs["L0_ffn_shared_up_w"], np.float32))
        m["shared_down_w"] = np.ascontiguousarray(np.asarray(inputs["L0_ffn_shared_down_w"], np.float32))
    if stn >= 4:
        m["wq_t"] = np.ascontiguousarray(np.asarray(inputs["L0_attn_Wq"], np.float32).T)
        m["wk_t"] = np.ascontiguousarray(np.asarray(inputs["L0_attn_Wk"], np.float32).T)
        m["wv_t"] = np.ascontiguousarray(np.asarray(inputs["L0_attn_Wv"], np.float32).T)
        m["wo_t"] = np.ascontiguousarray(np.asarray(inputs["L0_attn_Wo"], np.float32).T)
        m["bqkv"] = np.ascontiguousarray(np.concatenate([
            np.asarray(inputs["L0_attn_bq"], np.float32),
            np.asarray(inputs["L0_attn_bk"], np.float32),
            np.asarray(inputs["L0_attn_bv"], np.float32)])[None, :])
        m["bo"] = np.ascontiguousarray(np.asarray(inputs["L0_attn_bo"], np.float32)[None, :])
        m["decay"] = np.ascontiguousarray(np.asarray(inputs["L0_attn_decay"], np.float32)[None, :])
    return m


def kernel(**inputs):
    stage = "full"
    nc = build_program(stage=stage)
    res = run_bass_kernel_spmd(nc, [_make_in_map(inputs, stage)], core_ids=[0])
    return res.results[0]["y"].reshape(NB, TB, D)


# version 10
# speedup vs baseline: 1.1996x; 1.0069x over previous
import contextlib
import numpy as np

import concourse.bass as bass
import concourse.mybir as mybir
from concourse.bass import IndirectOffsetOnAxis
from concourse.bass_utils import run_bass_kernel_spmd

F32 = mybir.dt.float32
BF16 = mybir.dt.bfloat16
U32 = mybir.dt.uint32
I32 = mybir.dt.int32
AF = mybir.ActivationFunctionType
ALU = mybir.AluOpType
AX = mybir.AxisListType

T = 32768
D = 1024
E = 64
KTOP = 8
CAP = 512
II = 512
TILE = 128
NT = T // TILE
DS = 64
NB = 2
TB = T // NB
NTB = TB // TILE
SCALE = 2.5
NSLOT = E * CAP
OOB_BIG = 10_000_000

STAGES = {"ln": 0, "route": 1, "expert": 2, "ffn1": 3, "attn": 4, "full": 5}


class Prog:

    def __init__(self, nc, cs):
        self.nc = nc
        self.cs = cs
        self._h = {}
        self.v = {}

    def sem(self, name):
        if name not in self._h:
            self._h[name] = self.cs.enter_context(self.nc.semaphore(name))
            self.v[name] = 0
        return self._h[name]

    def ti(self, instr, name, n=1):
        h = self.sem(name)
        instr.then_inc(h, n, skip_validation=True)
        self.v[name] += n
        return self.v[name]

    def val(self, name):
        self.sem(name)
        return self.v[name]

    def stepper(self, eng, name):
        def step(instr):
            self.ti(instr, name)
            eng.wait_ge(self.sem(name), self.v[name])
            return instr
        return step


def build_program(stage="full", debug=(), out_rows=(0, T), route_pass=3, expert_pass=9, combine_mode=3):
    stn = STAGES[stage]
    nc = bass.Bass("TRN2", target_bir_lowering=False, debug=False)
    cs = contextlib.ExitStack()
    P = Prog(nc, cs)

    def sbuf(name, shape, dt=F32):
        return cs.enter_context(nc.sbuf_tensor(name, shape, dt))

    def psum(name, shape, dt=F32):
        return cs.enter_context(nc.psum_tensor(name, shape, dt))

    _uid = [0]

    def uname(base):
        _uid[0] += 1
        return base + "_u" + str(_uid[0])

    def chn(eng, nm, thunk):
        eng.wait_ge(P.sem(nm), P.val(nm))
        P.ti(thunk(), nm)

    x_in = nc.declare_dram_parameter("x", [T, D], F32, isOutput=False)
    router_in = nc.declare_dram_parameter("router_w", [D, E], F32, isOutput=False)
    if stn >= 2:
        gate_in = nc.declare_dram_parameter("gate_w", [E, D, II], F32, isOutput=False)
        up_in = nc.declare_dram_parameter("up_w", [E, D, II], F32, isOutput=False)
        down_in = nc.declare_dram_parameter("down_w", [E, II, D], F32, isOutput=False)
    if stn >= 3:
        sgate_in = nc.declare_dram_parameter("shared_gate_w", [D, II], F32, isOutput=False)
        sup_in = nc.declare_dram_parameter("shared_up_w", [D, II], F32, isOutput=False)
        sdown_in = nc.declare_dram_parameter("shared_down_w", [II, D], F32, isOutput=False)
    if stn >= 4:
        wq_in = nc.declare_dram_parameter("wq_t", [D, DS], F32, isOutput=False)
        wk_in = nc.declare_dram_parameter("wk_t", [D, DS], F32, isOutput=False)
        wv_in = nc.declare_dram_parameter("wv_t", [D, DS], F32, isOutput=False)
        wo_in = nc.declare_dram_parameter("wo_t", [DS, D], F32, isOutput=False)
        bqkv_in = nc.declare_dram_parameter("bqkv", [1, 3 * DS], F32, isOutput=False)
        bo_in = nc.declare_dram_parameter("bo", [1, D], F32, isOutput=False)
        decay_in = nc.declare_dram_parameter("decay", [1, DS], F32, isOutput=False)

    n_out = out_rows[1] - out_rows[0]
    y_out = nc.declare_dram_parameter("y", [n_out, D], F32, isOutput=True)

    dbg = {}
    for name, shape, dt in (
        ("hln", [T, D], BF16), ("routeS", [NT, TILE, 24], F32),
        ("cnt", [E, NT], F32), ("inv", [NSLOT, 2], F32), ("xd", [NSLOT, D], BF16),
        ("yd", [NSLOT, D], BF16), ("x1", [T, D], F32), ("x2", [T, D], F32),
    ):
        if name in debug:
            dbg[name] = nc.declare_dram_parameter("dbg_" + name, shape, dt, isOutput=True)

    hln_d = nc.dram_tensor("hln_d", [T + TILE, D], BF16)
    inv_d = nc.dram_tensor("inv_d", [NSLOT, 2], F32)
    xd_d = nc.dram_tensor("xd_d", [NSLOT, D], BF16)
    yd_d = nc.dram_tensor("yd_d", [NSLOT + TILE, D], BF16)
    x1_d = nc.dram_tensor("x1_d", [T, D], F32)
    x2_d = nc.dram_tensor("x2_d", [T, D], F32)
    if stn >= 2:
        gbf_d = nc.dram_tensor("gbf_d", [E, D, II], BF16)
        ubf_d = nc.dram_tensor("ubf_d", [E, D, II], BF16)
        dbf_d = nc.dram_tensor("dbf_d", [E, II, D], BF16)
    if stn >= 3:
        sgbf_d = nc.dram_tensor("sgbf_d", [D, II], BF16)
        subf_d = nc.dram_tensor("subf_d", [D, II], BF16)
        sdbf_d = nc.dram_tensor("sdbf_d", [II, D], BF16)
    if stn >= 4:
        r2_d = nc.dram_tensor("r2_d", [T, D], F32)
        misc_d = nc.dram_tensor("misc_d", [DS, 1], F32)
        wqkv_bf_d = nc.dram_tensor("wqkv_bf_d", [D, 3 * DS], BF16)
        wo_bf_d = nc.dram_tensor("wo_bf_d", [DS, D], BF16)

    ones1 = sbuf("ones1", [1, TILE])
    onescol_b = sbuf("onescol_b", [TILE, 1], BF16)
    su_bf = sbuf("su_bf", [TILE, TILE], BF16)
    iota_rev = sbuf("iota_rev", [TILE, E])
    iota_enc = sbuf("iota_enc", [TILE, E])
    iota_p = sbuf("iota_p", [TILE, 1])
    oob_tab = sbuf("oob_tab", [TILE, KTOP])
    dummy_tab = sbuf("dummy_tab", [TILE, 1])
    dummyT_tab = sbuf("dummyT_tab", [TILE, 1])
    fill_neg = sbuf("fill_neg", [TILE, KTOP])
    router_sb = sbuf("router_sb", [TILE, 8 * E], BF16)

    czero = sbuf("czero", [TILE, 1])
    ceps = sbuf("ceps", [TILE, 1])
    conef = sbuf("conef", [TILE, 1])
    chalf = sbuf("chalf", [TILE, 1])
    nc.const_aps.register(F32, 0.0, czero[:, :])
    nc.const_aps.register(F32, 1e-5, ceps[:, :])
    nc.const_aps.register(F32, 1.0, conef[:, :])
    nc.const_aps.register(F32, 0.5, chalf[:, :])

    xg = sbuf("xg", [TILE, 2 * D], BF16)
    xa = sbuf("xa", [TILE, D])
    xb = sbuf("xb", [TILE, D])
    hba = sbuf("hba", [TILE, D], BF16)
    hbb = sbuf("hbb", [TILE, D], BF16)
    stat = sbuf("stat", [TILE, 16])
    sqbuf = sbuf("sqbuf", [TILE, D])

    hlnT = sbuf("hlnT", [TILE, 2 * 4096], BF16)
    scr = sbuf("scr", [TILE, 2 * 256])
    msk = sbuf("msk", [TILE, E])
    gsr = sbuf("gsr", [TILE, 32])
    tk = sbuf("tk", [TILE, 64])
    a_all = sbuf("a_all", [TILE, NT * E], BF16)
    ew_all = sbuf("ew_all", [TILE, NT * 2 * KTOP])
    runrow = sbuf("runrow", [1, E])
    sidx_all = sbuf("sidx_all", [TILE, NT * KTOP], U32)
    pairs_all = sbuf("pairs_all", [TILE, NT * 2 * KTOP])
    cmb_all = sbuf("cmb_all", [TILE, NT * KTOP], U32)
    kflag = sbuf("kflag", [1, NT])
    kflag_i = sbuf("kflag_i", [1, NT], I32)
    rs_all = sbuf("rs_all", [TILE, NT * 24]) if "routeS" in debug else None
    vsum = sbuf("vsum", [TILE, NT])
    valid_i = sbuf("valid_i", [TILE, 2 * KTOP], I32)
    zer512 = sbuf("zer512", [TILE, 512])
    if STAGES[stage] >= 2:
        inv_sb = sbuf("inv_sb", [TILE, 2 * NT])
        idx_all = sbuf("idx_all", [TILE, NT], U32)
        wden = sbuf("wden", [TILE, NT])
        imsk = sbuf("imsk", [TILE, NT], I32)
        wg_sb = sbuf("wg_sb", [TILE, 2 * 4096], BF16)
        wu_sb = sbuf("wu_sb", [TILE, 2 * 4096], BF16)
        wd_sb = sbuf("wd_sb", [TILE, 2 * 4096], BF16)
        sact = sbuf("sact", [TILE, 512])
        act_sb = sbuf("act_sb", [TILE, 4 * 512], BF16)
        y_sb = sbuf("y_sb", [TILE, 2 * D], BF16)
    if STAGES[stage] >= 3:
        x1g = pairs_all
        stg = a_all
        ident_b = sbuf("ident_b", [TILE, TILE], BF16)
        identX = sbuf("identX", [TILE, 2 * TILE], BF16)
        flagcol = sbuf("flagcol", [TILE, NT])
        kflag01 = sbuf("kflag01", [1, NT])

    if stn >= 4:
        woT_sb = wg_sb[0:DS, 1024:2048]
        mask3 = wg_sb[:, 0:3 * TILE]
        obf = wg_sb[0:DS, 512:768]
        qT_bf = wu_sb[0:DS, 0:3 * TILE]
        kT_bf = wu_sb[0:DS, 512:512 + 3 * TILE]
        kvt = wu_sb[:, 1024:1024 + 3 * TILE]
        v4 = wd_sb[:, 0:3 * 4 * DS]
        mav = wd_sb[:, 1024:1024 + 6 * TILE]
        dcolx = sact[:, 0:4 * DS]
        lnd_rep = sact[:, 256:256 + DS]
        obuf = sact[0:DS, 320:448]
        dtile_rep = sact[0:DS, 448:512]
        drowt = xb[0:DS, 0:4 * TILE]
        exprow = xb[0:DS, 512:1024]
        bqkv_sb = sbuf("bqkv_sb", [1, 3 * DS])
        bo_sb = sbuf("bo_sb", [1, D])
        drow_sb = sbuf("drow_sb", [1, 4 * DS])
        lnd_col = sbuf("lnd_col", [DS, 1])
        st_f = sbuf("st_f", [DS, DS])
        st_b = sbuf("st_b", [DS, DS], BF16)
        expcol = sbuf("expcol", [TILE, 8])

    ps = [psum(f"ps{i}", [TILE, 512]) for i in range(8)]

    block = cs.enter_context(nc.Block())

    @block.vector
    def _(v):
        P.ti(v.memset(ones1[:, :], 1.0), "const")
        P.ti(v.memset(onescol_b[:, :], 1.0), "const")
        P.ti(v.memset(su_bf[:, :], 0.0), "const")
        P.ti(v.memset(fill_neg[:, :], -1e30), "const")
        P.ti(v.memset(zer512[:, :], 0.0), "const")
        if STAGES[stage] >= 3:
            P.ti(v.memset(ident_b[:, :], 0.0), "const")
        P.ti(v.memset(czero[:, :], 0.0), "const")
        P.ti(v.memset(ceps[:, :], 1e-5), "const")
        P.ti(v.memset(conef[:, :], 1.0), "const")
        P.ti(v.memset(chalf[:, :], 0.5), "const")

    @block.gpsimd
    def _(gp):
        gp.wait_ge(P.sem("const"), 9 + (1 if STAGES[stage] >= 3 else 0))
        P.ti(gp.affine_select(out=su_bf[:, :], in_=su_bf[:, :], pattern=[[-1, TILE]],
                              compare_op=ALU.is_ge, fill=1.0, base=0,
                              channel_multiplier=1), "const")
        if STAGES[stage] >= 3:
            gp.wait_ge(P.sem("const"), 10)
            P.ti(gp.affine_select(out=ident_b[:, :], in_=ident_b[:, :], pattern=[[-1, TILE]],
                                  compare_op=ALU.not_equal, fill=1.0, base=0,
                                  channel_multiplier=1), "const")
        P.ti(gp.iota(iota_rev[:, :], pattern=[[-1, E]], base=E, channel_multiplier=0,
                     allow_small_or_imprecise_dtypes=True), "const")
        P.ti(gp.iota(iota_enc[:, :], pattern=[[-32768, E]], base=E * 32768, channel_multiplier=0,
                     allow_small_or_imprecise_dtypes=True), "const")
        P.ti(gp.iota(oob_tab[:, :], pattern=[[1, KTOP]], base=OOB_BIG, channel_multiplier=KTOP,
                     allow_small_or_imprecise_dtypes=True), "const")
        P.ti(gp.iota(iota_p[:, :], pattern=[[0, 1]], base=0, channel_multiplier=1,
                     allow_small_or_imprecise_dtypes=True), "const")
        P.ti(gp.iota(dummy_tab[:, :], pattern=[[0, 1]], base=NSLOT, channel_multiplier=1,
                     allow_small_or_imprecise_dtypes=True), "const")
        P.ti(gp.iota(dummyT_tab[:, :], pattern=[[0, 1]], base=T, channel_multiplier=1,
                     allow_small_or_imprecise_dtypes=True), "const")
        P.ti(gp.dma_start(out=hln_d[T:T + TILE, 0:512], in_=zer512[:, :]), "pz", 16)
        P.ti(gp.dma_start(out=hln_d[T:T + TILE, 512:1024], in_=zer512[:, :]), "pz", 16)
        P.ti(gp.dma_start(out=inv_d.ap().rearrange("(a b) c -> a (b c)", a=TILE),
                          in_=zer512[:, :]), "pz", 16)
        if stn >= 2:
            P.ti(gp.dma_start(out=yd_d[NSLOT:NSLOT + TILE, 0:512], in_=zer512[:, :]), "pz", 16)
            P.ti(gp.dma_start(out=yd_d[NSLOT:NSLOT + TILE, 512:1024], in_=zer512[:, :]), "pz", 16)

    if stn >= 2:
        @block.gpsimd
        def _(gp):
            for e in range(E):
                P.ti(gp.dma_start(out=gbf_d[e, :, :], in_=gate_in[e, :, :]), "wcast", 16)
                P.ti(gp.dma_start(out=ubf_d[e, :, :], in_=up_in[e, :, :]), "wcast", 16)
                P.ti(gp.dma_start(out=dbf_d[e, :, :], in_=down_in[e, :, :]), "wcast", 16)
            if stn >= 3:
                P.ti(gp.dma_start(out=sgbf_d[:, :], in_=sgate_in[:, :]), "wcast", 16)
                P.ti(gp.dma_start(out=subf_d[:, :], in_=sup_in[:, :]), "wcast", 16)
                P.ti(gp.dma_start(out=sdbf_d[:, :], in_=sdown_in[:, :]), "wcast", 16)
            if stn >= 4:
                P.ti(gp.dma_start(out=wqkv_bf_d[:, 0:DS], in_=wq_in[:, :]), "wcast", 16)
                P.ti(gp.dma_start(out=wqkv_bf_d[:, DS:2 * DS], in_=wk_in[:, :]), "wcast", 16)
                P.ti(gp.dma_start(out=wqkv_bf_d[:, 2 * DS:3 * DS], in_=wv_in[:, :]), "wcast", 16)
                P.ti(gp.dma_start(out=wo_bf_d[:, :], in_=wo_in[:, :]), "wcast", 16)

    @block.sync
    def _(sy):
        for c in range(8):
            P.ti(sy.dma_start(out=scr[:, c * E:(c + 1) * E],
                              in_=router_in[c * TILE:(c + 1) * TILE, :]), "ld0", 16)

    @block.vector
    def _(v):
        v.wait_ge(P.sem("ld0"), P.val("ld0"))
        P.ti(v.tensor_copy(out=router_sb[:, :], in_=scr[:, 0:8 * E]), "rcp")

    CONST_TOTAL = P.val("const")

    def emit_ln(src_dram, dst_dram, tag, r2_dram=None):
        ld, cp, dv, st = tag + "ld", tag + "cp", tag + "dv", tag + "st"
        ld0, cp0, dv0, st0 = P.val(ld), P.val(cp), P.val(dv), P.val(st)
        r20 = P.val(tag + "r2")

        @block.sync
        def _(sy):
            for t in range(NT):
                buf = xa if t % 2 == 0 else xb
                if t >= 2:
                    sy.wait_ge(P.sem(cp), cp0 + t - 1)
                    if r2_dram is not None:
                        sy.wait_ge(P.sem(tag + "r2"), r20 + t - 1)
                P.ti(sy.dma_start(out=buf[:, :], in_=src_dram[t * TILE:(t + 1) * TILE, :]), ld, 16)

        @block.vector
        def _(v):
            step = P.stepper(v, tag + "vch")
            v.wait_ge(P.sem("const"), CONST_TOTAL)
            for t in range(NT):
                buf = xa if t % 2 == 0 else xb
                v.wait_ge(P.sem(ld), ld0 + (t + 1) * 16)
                if t >= 1:
                    v.wait_ge(P.sem(cp), cp0 + t)
                step(v.tensor_reduce(out=stat[:, 0:1], in_=buf[:, :], axis=AX.X, op=ALU.add))
                P.ti(v.tensor_scalar(out=stat[:, 3:4], in0=stat[:, 0:1], scalar1=-1.0 / D,
                                     scalar2=0.0, op0=ALU.mult), dv)
                step(v.tensor_scalar(out=stat[:, 1:2], in0=stat[:, 0:1],
                                     scalar1=stat[:, 0:1], scalar2=-1.0 / (float(D) * D),
                                     op0=ALU.mult, op1=ALU.mult))
                P.ti(v.tensor_single_scalar(out=stat[:, 4:5], in_=stat[:, 1:2],
                                            scalar=1e-5, op=ALU.add), dv)
                if r2_dram is not None:
                    hbuf = hba if t % 2 == 0 else hbb
                    v.wait_ge(P.sem(cp), cp0 + t + 1)
                    if t >= 2:
                        v.wait_ge(P.sem(tag + "r2s"), (t - 1) * 16)
                    step(v.tensor_tensor(out=ew_all[:, (t % 2) * D:(t % 2 + 1) * D],
                                         in0=buf[:, :], in1=hbuf[:, :], op=ALU.add))
                    P.ti(v.engine_nop(), tag + "r2")

        @block.scalar
        def _(sc):
            sc.wait_ge(P.sem("const"), CONST_TOTAL)
            step = P.stepper(sc, tag + "ch")
            for t in range(NT):
                buf = xa if t % 2 == 0 else xb
                hbuf = hba if t % 2 == 0 else hbb
                sc.wait_ge(P.sem(ld), ld0 + (t + 1) * 16)
                step(sc.activation(out=xg[:, 0:D], in_=buf[:, :], func=AF.Square,
                                   accum_out=stat[:, 2:3]))
                sc.wait_ge(P.sem(dv), dv0 + 2 * (t + 1))
                step(sc.activation(out=stat[:, 5:6], in_=stat[:, 2:3], func=AF.Ln,
                                   scale=1.0 / D, bias=stat[:, 4:5]))
                step(sc.activation(out=stat[:, 5:6], in_=stat[:, 5:6], func=AF.Exp, scale=-0.5))
                step(sc.mul(out=stat[:, 6:7], in_=stat[:, 3:4], mul=stat[:, 5:6]))
                if t >= 2:
                    sc.wait_ge(P.sem(st), st0 + (t - 1) * 16)
                if r2_dram is not None and t >= 2:
                    sc.wait_ge(P.sem(tag + "r2"), r20 + t - 1)
                P.ti(sc.activation(out=hbuf[:, :], in_=buf[:, :], func=AF.Identity,
                                   bias=stat[:, 6:7], scale=stat[:, 5:6]), cp)
                sc.wait_ge(P.sem(cp), cp0 + t + 1)
                P.ti(sc.dma_start(out=dst_dram[t * TILE:(t + 1) * TILE, :], in_=hbuf[:, :]), st, 16)
                if r2_dram is not None:
                    sc.wait_ge(P.sem(tag + "r2"), r20 + t + 1)
                    P.ti(sc.dma_start(out=r2_dram[t * TILE:(t + 1) * TILE, :],
                                      in_=ew_all[:, (t % 2) * D:(t % 2 + 1) * D]), tag + "r2s", 16)

        return st

    def emit_routing(ln_st, tag, route_pass=3):
        rt = tag
        NGRP = NT // 4
        ln_st0 = P.val(ln_st) - NT * 16
        tp0 = P.val(rt + "tp")
        mm0 = P.val(rt + "mm")
        sg0 = P.val(rt + "sg")
        tk0 = P.val(rt + "tk")
        pm0 = P.val(rt + "pm")
        p20E, p20O = P.val(rt + "p2E"), P.val(rt + "p2O")
        vc0 = P.val(rt + "vc")

        @block.scalar
        def _(sc):
            for g in range(NGRP + 1):
                if g < NGRP:
                    hbuf = hlnT[:, (g % 2) * 4096:(g % 2 + 1) * 4096]
                    sc.wait_ge(P.sem(ln_st), ln_st0 + (g * 4 + 4) * 16)
                    if g >= 2:
                        sc.wait_ge(P.sem(rt + "mm"), mm0 + (g - 1) * 32)
                    for c in range(8):
                        P.ti(sc.dma_start(out=hbuf[:, c * 512:(c + 1) * 512],
                                          in_=hln_d[g * 512:(g + 1) * 512, c * TILE:(c + 1) * TILE],
                                          transpose=True), rt + "tp", 16)
                if g >= 1:
                    gg = g - 1
                    sb = scr[:, (gg % 2) * 256:(gg % 2 + 1) * 256]
                    if gg >= 2:
                        sc.wait_ge(P.sem(rt + "tk"), tk0 + (gg - 1) * 4)
                    for tt in range(4):
                        pb = ps[tt]
                        sc.wait_ge(P.sem(rt + "mm"), mm0 + gg * 32 + (tt + 1) * 8)
                        P.ti(sc.activation(out=sb[:, tt * 64:(tt + 1) * 64], in_=pb[:, 0:E],
                                           func=AF.Tanh, scale=0.5), rt + "sg")

        @block.tensor
        def _(te):
            te.wait_ge(P.sem("rcp"), 1)
            for g in range(NGRP):
                hbuf = hlnT[:, (g % 2) * 4096:(g % 2 + 1) * 4096]
                te.wait_ge(P.sem(rt + "tp"), tp0 + (g + 1) * 128)
                if g >= 1:
                    te.wait_ge(P.sem(rt + "sg"), sg0 + g * 4)
                for tt in range(4):
                    pb = ps[tt]
                    for c in range(8):
                        mm = te.matmul(pb[:, 0:E],
                                       hbuf[:, c * 512 + tt * TILE:c * 512 + (tt + 1) * TILE],
                                       router_sb[:, c * E:(c + 1) * E],
                                       start=(c == 0), stop=(c == 7))
                    P.ti(mm, rt + "mm", 8)


        @block.vector
        def _(v):
            step = P.stepper(v, rt + "vch")
            step(v.memset(sqbuf[:, :], -1e30))
            for g in range(NGRP):
                sb = scr[:, (g % 2) * 256:(g % 2 + 1) * 256]
                v.wait_ge(P.sem(rt + "sg"), sg0 + g * 4 + 4)
                step(v.tensor_scalar(out=sb, in0=sb, scalar1=0.5, scalar2=0.5,
                                     op0=ALU.mult, op1=ALU.add))
                step(v.tensor_reduce(out=gsr[:, :], in_=sb.rearrange("p (a b) -> p a b", b=8),
                                     axis=AX.X, op=ALU.max))

                def mk_ops(tt):
                    t = g * 4 + tt
                    b = tt * 256
                    gs = gsr[:, tt * 8:(tt + 1) * 8]
                    A0 = sqbuf[:, b + 0:b + 8]
                    fill = sqbuf[:, b + 8:b + 16]
                    t8 = sqbuf[:, b + 16:b + 24]
                    s24 = sqbuf[:, b + 24:b + 32]
                    s32 = sqbuf[:, b + 32:b + 33]
                    s33 = sqbuf[:, b + 33:b + 34]
                    s34 = sqbuf[:, b + 34:b + 35]
                    mk = sqbuf[:, b + 64:b + 128]
                    cpy = sqbuf[:, b + 128:b + 192]
                    ew = ew_all[:, t * 16:t * 16 + 16]
                    return [
                        lambda: v.max(out=A0, in_=gs),
                        lambda: v.tensor_copy(out=fill[:, 0:4], in_=A0[:, 0:4]),
                        lambda: v.match_replace(out=gs, in_to_replace=fill, in_values=gs,
                                                imm_value=-1e30),
                        lambda: v.tensor_single_scalar(out=gs, in_=gs, scalar=-1e29, op=ALU.is_lt),
                        lambda: v.tensor_tensor(out=mk, in0=sb[:, tt * 64:(tt + 1) * 64],
                                                in1=gs.rearrange("p (a b) -> p a b", b=1)
                                                .broadcast_to((TILE, 8, 8)), op=ALU.mult),
                        lambda: v.max(out=t8, in_=mk),
                        lambda: v.tensor_copy(out=cpy, in_=mk),
                        lambda: v.match_replace(out=cpy, in_to_replace=t8, in_values=cpy,
                                                imm_value=-1e30),
                        lambda: v.tensor_single_scalar(out=cpy, in_=cpy, scalar=-1e29, op=ALU.is_lt),
                        lambda: v.tensor_copy(out=a_all[:, t * E:(t + 1) * E], in_=cpy),
                        lambda: v.scalar_tensor_tensor(out=mk, in0=mk, scalar=0.5,
                                                       in1=iota_rev[:, :], op0=ALU.mult, op1=ALU.add),
                        lambda: v.tensor_tensor(out=mk, in0=mk, in1=cpy, op=ALU.mult),
                        lambda: v.max(out=t8, in_=mk),
                        lambda: v.max_index(out=sidx_all[:, t * 8:(t + 1) * 8], in_max=t8,
                                            in_values=mk),
                        lambda: v.tensor_copy(out=ew[:, 0:8], in_=sidx_all[:, t * 8:(t + 1) * 8]),
                        lambda: v.tensor_tensor(out=s24, in0=t8, in1=ew[:, 0:8], op=ALU.add),
                        lambda: v.tensor_single_scalar(out=s24, in_=s24, scalar=-64.0, op=ALU.add),
                        lambda: v.tensor_reduce(out=s32, in_=s24, axis=AX.X, op=ALU.add),
                        lambda: v.tensor_scalar_max(out=s33, in0=s32, scalar1=5e-10),
                        lambda: v.reciprocal(out=s34, in_=s33),
                        lambda: v.tensor_scalar(out=ew[:, 8:16], in0=s24, scalar1=s34,
                                                scalar2=SCALE, op0=ALU.mult, op1=ALU.mult),
                    ]

                chains = [mk_ops(tt) for tt in range(4)]
                for k in range(len(chains[0])):
                    for tt in range(4):
                        chn(v, rt + "pc" + str(tt), chains[tt][k])
                for tt in range(4):
                    nm = rt + "pc" + str(tt)
                    v.wait_ge(P.sem(nm), P.val(nm))
                    P.ti(v.engine_nop(), rt + "tk")

        def emit_dump():
            if "routeS" not in dbg:
                return
            @block.scalar
            def _(sc):
                sc.wait_ge(P.sem(rt + "tk"), tk0 + NT)
                if route_pass >= 2:
                    sc.wait_ge(P.sem(rt + "p2E"), p20E + NT // 2)
                    sc.wait_ge(P.sem(rt + "p2O"), p20O + NT // 2)
                for t in range(NT):
                    ew = ew_all[:, t * 16:t * 16 + 16]
                    P.ti(sc.dma_start(out=dbg["routeS"][t, :, 0:8], in_=ew[:, 0:8]), rt + "dbst", 16)
                    P.ti(sc.dma_start(out=dbg["routeS"][t, :, 8:16], in_=ew[:, 8:16]), rt + "dbst", 16)
                    if route_pass >= 2:
                        P.ti(sc.dma_start(out=dbg["routeS"][t, :, 16:24],
                                          in_=rs_all[:, t * 24:t * 24 + 8]), rt + "dbst", 16)

        if route_pass < 2:
            emit_dump()
            return rt
        @block.vector
        def _(v):
            v.wait_ge(P.sem("const"), CONST_TOTAL)
            P.ti(v.memset(runrow[:, :], 0.0), rt + "rr")

        rr0 = P.val(rt + "rr")

        @block.tensor
        def _(te):
            te.wait_ge(P.sem(rt + "rr"), rr0)
            for t in range(NT):
                pb = ps[4 + (t % 2)]
                te.wait_ge(P.sem(rt + "tk"), tk0 + t + 1)
                if t >= 2:
                    te.wait_ge(P.sem(rt + "p2" + ("E" if t % 2 == 0 else "O")),
                               (p20E if t % 2 == 0 else p20O) + (t - 2) // 2 + 1)
                if t >= 1:
                    te.wait_ge(P.sem(rt + "rr"), rr0 + t)
                te.matmul(pb[:, 0:E], su_bf[:, :], a_all[:, t * E:(t + 1) * E],
                          start=True, stop=False)
                te.matmul(pb[:, 0:E], ones1[:, :], runrow[:, :], start=False, stop=True)
                mm3 = te.matmul(ps[6 + (t % 2)][0:1, (t % 4) * E:(t % 4) * E + E],
                                onescol_b[:, :], a_all[:, t * E:(t + 1) * E],
                                start=True, stop=True)
                P.ti(mm3, rt + "pm")

        @block.vector
        def _(v):
            step = P.stepper(v, rt + "vch")

            def mk2(t):
                pb = ps[4 + (t % 2)]
                b = (t % 2) * 256
                enc = sqbuf[:, b + 64:b + 128]
                v2 = sqbuf[:, b + 0:b + 8]
                pos8 = sqbuf[:, b + 8:b + 16]
                vld = sqbuf[:, b + 16:b + 24]
                slot = sqbuf[:, b + 24:b + 32]
                oob = sqbuf[:, b + 32:b + 40]
                sel = sqbuf[:, b + 40:b + 48]
                h1 = sqbuf[:, b + 48:b + 56]
                t56 = sqbuf[:, b + 56:b + 64]
                ic = sqbuf[:, b + 128:b + 129]
                vi = valid_i[:, (t % 2) * KTOP:(t % 2) * KTOP + KTOP]
                ew = ew_all[:, t * 16:t * 16 + 16]
                pr = pairs_all[:, t * 16:(t + 1) * 16].rearrange("p (a b) -> p a b", b=2)
                ops = [
                    lambda: v.tensor_tensor(out=enc, in0=pb[:, 0:E], in1=iota_enc[:, :],
                                            op=ALU.add),
                    lambda: v.tensor_tensor(out=enc, in0=enc,
                                            in1=a_all[:, t * E:(t + 1) * E], op=ALU.mult),
                    lambda: v.max(out=v2, in_=enc),
                    lambda: v.tensor_single_scalar(out=pos8, in_=ew[:, 0:8], scalar=-64.0,
                                                   op=ALU.add),
                    lambda: v.tensor_scalar(out=pos8, in0=pos8, scalar1=32768.0,
                                            scalar2=0.0, op0=ALU.mult),
                    lambda: v.tensor_tensor(out=pos8, in0=pos8, in1=v2, op=ALU.add),
                    lambda: v.tensor_single_scalar(out=vld, in_=pos8, scalar=float(CAP),
                                                   op=ALU.is_lt),
                    lambda: v.tensor_copy(out=vi, in_=vld),
                    lambda: v.tensor_scalar(out=slot, in0=ew[:, 0:8], scalar1=512.0,
                                            scalar2=0.0, op0=ALU.mult),
                    lambda: v.tensor_tensor(out=slot, in0=slot, in1=pos8, op=ALU.add),
                    lambda: v.tensor_single_scalar(out=h1, in_=pos8, scalar=128.0, op=ALU.is_ge),
                    lambda: v.tensor_single_scalar(out=t56, in_=pos8, scalar=256.0, op=ALU.is_ge),
                    lambda: v.tensor_tensor(out=h1, in0=h1, in1=t56, op=ALU.add),
                    lambda: v.tensor_single_scalar(out=t56, in_=pos8, scalar=384.0, op=ALU.is_ge),
                    lambda: v.tensor_tensor(out=h1, in0=h1, in1=t56, op=ALU.add),
                    lambda: v.tensor_scalar(out=t56, in0=h1, scalar1=-128.0,
                                            scalar2=0.0, op0=ALU.mult),
                    lambda: v.tensor_tensor(out=t56, in0=t56, in1=pos8, op=ALU.add),
                    lambda: v.tensor_scalar(out=t56, in0=t56, scalar1=256.0,
                                            scalar2=0.0, op0=ALU.mult),
                    lambda: v.scalar_tensor_tensor(out=h1, in0=ew[:, 0:8], scalar=4.0,
                                                   in1=h1, op0=ALU.mult, op1=ALU.add),
                    lambda: v.tensor_tensor(out=h1, in0=h1, in1=t56, op=ALU.add),
                    lambda: v.tensor_single_scalar(out=oob, in_=oob_tab[:, :],
                                                   scalar=float(t * 1024), op=ALU.add),
                    lambda: v.select(out=oob, mask=vi, on_true=h1, on_false=oob),
                    lambda: v.tensor_copy(out=sidx_all[:, t * 8:(t + 1) * 8], in_=oob),
                    lambda: v.select(out=sel, mask=vi, on_true=slot,
                                     on_false=dummy_tab[:, 0:1].broadcast_to((TILE, 8))),
                    lambda: v.tensor_copy(out=cmb_all[:, t * 8:(t + 1) * 8], in_=sel),
                    lambda: v.tensor_single_scalar(out=ic, in_=iota_p[:, :],
                                                   scalar=float(t * TILE + 1), op=ALU.add),
                    lambda: v.tensor_copy(out=pr[:, :, 0:1],
                                          in_=ic.rearrange("p (a b) -> p a b", b=1)
                                          .broadcast_to((TILE, 8, 1))),
                    lambda: v.tensor_copy(out=pr[:, :, 1:2],
                                          in_=ew[:, 8:16].rearrange("p (a b) -> p a b", b=1)),
                    lambda: v.tensor_reduce(out=vsum[:, t:t + 1], in_=vld, axis=AX.X,
                                            op=ALU.add),
                ]
                if "routeS" in dbg:
                    ops += [
                        lambda: v.tensor_copy(out=rs_all[:, t * 24 + 0:t * 24 + 8], in_=pos8),
                        lambda: v.tensor_copy(out=rs_all[:, t * 24 + 8:t * 24 + 16], in_=vld),
                        lambda: v.tensor_copy(out=rs_all[:, t * 24 + 16:t * 24 + 24], in_=slot),
                    ]
                return ops

            for j in range(NT // 2):
                pair = [2 * j, 2 * j + 1]
                for t in pair:
                    v.wait_ge(P.sem(rt + "pm"), pm0 + t + 1)
                    step(v.tensor_tensor(out=runrow[:, :], in0=runrow[:, :],
                                         in1=ps[6 + (t % 2)][0:1, (t % 4) * E:(t % 4) * E + E],
                                         op=ALU.add))
                    P.ti(v.engine_nop(), rt + "rr")
                chains = [mk2(t) for t in pair]
                for k in range(len(chains[0])):
                    for ci in range(2):
                        chn(v, rt + "q" + str(ci), chains[ci][k])
                for ci, t in enumerate(pair):
                    nm = rt + "q" + str(ci)
                    v.wait_ge(P.sem(nm), P.val(nm))
                    P.ti(v.engine_nop(), rt + "p2" + ("E" if t % 2 == 0 else "O"))

        @block.tensor
        def _(te):
            pb = ps[2]
            te.wait_ge(P.sem(rt + "sg"), sg0 + NT)
            for t in range(NT):
                te.wait_ge(P.sem(rt + "p2" + ("E" if t % 2 == 0 else "O")),
                           (p20E if t % 2 == 0 else p20O) + t // 2 + 1)
                mmv = te.matmul(pb[0:1, t:t + 1], conef[:, :], vsum[:, t:t + 1],
                                start=True, stop=True)
            P.ti(mmv, rt + "vc")

        @block.vector
        def _(v):
            step = P.stepper(v, rt + "vch")
            v.wait_ge(P.sem(rt + "vc"), P.val(rt + "vc"))
            pb = ps[2]
            step(v.tensor_copy(out=kflag[:, :], in_=pb[0:1, 0:NT]))
            P.ti(v.tensor_copy(out=kflag_i[:, :], in_=kflag[:, :]), rt + "kf")


        if route_pass < 3:
            emit_dump()
            return rt
        @block.gpsimd
        def _(gp):
            gp.wait_ge(P.sem("pz"), P.val("pz"))
            gp.wait_ge(P.sem(rt + "kf"), P.val(rt + "kf"))
            with gp.register(uname(rt + "bnd")) as rb, gp.register(uname(rt + "rf")) as rf:
                gp.reg_mov(rb, NSLOT - 1)
                for t in range(NT):
                    gp.reg_load(rf, kflag_i[0:1, t:t + 1])
                    with gp.If_eq(rf, 0):
                        gp.engine_nop().then_inc(P.sem(rt + "sc"), 16 * KTOP, skip_validation=True)
                    with gp.Else():
                        for k in range(KTOP):
                            gp.indirect_dma_start(
                                out=inv_d.ap(),
                                out_offset=IndirectOffsetOnAxis(sidx_all[:, t * 8 + k:t * 8 + k + 1], 0),
                                in_=pairs_all[:, t * 16 + 2 * k:t * 16 + 2 * k + 2],
                                in_offset=None,
                                bounds_check=rb, oob_is_err=False,
                            ).then_inc(P.sem(rt + "sc"), 16, skip_validation=True)
                    P.v[rt + "sc"] = P.val(rt + "sc") + 16 * KTOP
            P.ti(gp.engine_nop(), rt + "scdone")

        emit_dump()
        return rt

    def emit_expert(rt, hsrc_dram, expert_pass=9):
        ex = rt + "x"
        scd = P.val(rt + "scdone")
        xg0, xs0, xt0, wl0 = P.val(ex + "g"), P.val(ex + "s"), P.val(ex + "t"), P.val(ex + "w")
        ma0, si0, av0, mb0 = P.val(ex + "ma"), P.val(ex + "si"), P.val(ex + "av"), P.val(ex + "mb")
        ye0, ys0 = P.val(ex + "ye"), P.val(ex + "ys")

        @block.sync
        def _(sy):
            sy.wait_ge(P.sem(rt + "scdone"), scd)
            P.ti(sy.dma_start(out=inv_sb[:, :],
                              in_=inv_d.ap().rearrange("(p tt) c -> p (tt c)", p=TILE)), ex + "il", 16)

        @block.vector
        def _(v):
            step = P.stepper(v, ex + "vch")
            v.wait_ge(P.sem(ex + "il"), P.val(ex + "il"))
            iv = inv_sb[:, :].rearrange("p (tt c) -> p tt c", c=2)
            step(v.tensor_copy(out=wden[:, :], in_=iv[:, :, 1:2].rearrange("p a b -> p (a b)")))
            step(v.tensor_single_scalar(out=sqbuf[:, 0:NT], in_=iv[:, :, 0:1].rearrange("p a b -> p (a b)"),
                                        scalar=0.0, op=ALU.is_gt))
            step(v.tensor_copy(out=imsk[:, :], in_=sqbuf[:, 0:NT]))
            step(v.tensor_single_scalar(out=sqbuf[:, NT:2 * NT],
                                        in_=iv[:, :, 0:1].rearrange("p a b -> p (a b)"),
                                        scalar=-1.0, op=ALU.add))
            step(v.select(out=sqbuf[:, 2 * NT:3 * NT], mask=imsk[:, :],
                          on_true=sqbuf[:, NT:2 * NT],
                          on_false=dummyT_tab[:, 0:1].broadcast_to((TILE, NT))))
            P.ti(v.tensor_copy(out=idx_all[:, :], in_=sqbuf[:, 2 * NT:3 * NT]), ex + "ip")

        ip_done = P.val(ex + "ip")

        if expert_pass < 2:
            return ex
        @block.gpsimd
        def _(gp):
            gp.wait_ge(P.sem(ex + "ip"), ip_done)
            for e in range(E):
                for g in range(4):
                    st4 = e * 4 + g
                    if st4 >= 2:
                        gp.wait_ge(P.sem(ex + "s"), xs0 + (st4 - 1) * 16)
                    P.ti(gp.indirect_dma_start(
                        out=xg[:, (st4 % 2) * D:(st4 % 2) * D + D], out_offset=None,
                        in_=hsrc_dram.ap(),
                        in_offset=IndirectOffsetOnAxis(idx_all[:, st4:st4 + 1], 0)), ex + "g", 16)

        @block.sync
        def _(sy):
            for e in range(E):
                if e >= 2 and expert_pass >= 4:
                    sy.wait_ge(P.sem(ex + "ma"), ma0 + (e - 1) * 8)
                    if expert_pass >= 5:
                        sy.wait_ge(P.sem(ex + "mb"), mb0 + (e - 1) * 8)
                par = (e % 2) * 4096
                P.ti(sy.dma_start(out=wg_sb[:, par:par + 4096].rearrange("p (c i) -> p c i", i=II),
                                  in_=gbf_d[e].rearrange("(c p) i -> p c i", p=TILE)), ex + "w", 16)
                P.ti(sy.dma_start(out=wu_sb[:, par:par + 4096].rearrange("p (c i) -> p c i", i=II),
                                  in_=ubf_d[e].rearrange("(c p) i -> p c i", p=TILE)), ex + "w", 16)
                P.ti(sy.dma_start(out=wd_sb[:, par:par + 4096].rearrange("p (c i) -> p c i", i=D),
                                  in_=dbf_d[e].rearrange("(c p) i -> p c i", p=TILE)), ex + "w", 16)
                for g in range(4):
                    st4 = e * 4 + g
                    sy.wait_ge(P.sem(ex + "g"), xg0 + (st4 + 1) * 16)
                    P.ti(sy.dma_start(out=xd_d[st4 * TILE:(st4 + 1) * TILE, :],
                                      in_=xg[:, (st4 % 2) * D:(st4 % 2) * D + D]), ex + "s", 16)

        @block.scalar
        def _(sc):
            for e in range(E):
                par = (e % 2) * 4096
                sc.wait_ge(P.sem(ex + "s"), xs0 + (e + 1) * 64)
                if e >= 2 and expert_pass >= 4:
                    sc.wait_ge(P.sem(ex + "ma"), ma0 + (e - 1) * 8)
                for c in range(8):
                    P.ti(sc.dma_start(out=hlnT[:, par + c * 512:par + (c + 1) * 512],
                                      in_=xd_d[e * 512:(e + 1) * 512, c * TILE:(c + 1) * TILE],
                                      transpose=True), ex + "t", 16)
                for it in range(4 if expert_pass >= 4 else 0):
                    sc.wait_ge(P.sem(ex + "ma"), ma0 + e * 8 + 2 * it + 1)
                    if it > 0:
                        sc.wait_ge(P.sem(ex + "av"), av0 + e * 4 + it)
                    P.ti(sc.activation(out=sact[:, :], in_=ps[it][:, :], func=AF.Silu), ex + "si")
                for st_ in range(4 if expert_pass >= 5 else 0):
                    for half in range(2):
                        sc.wait_ge(P.sem(ex + "mb"), mb0 + e * 8 + st_ * 2 + half + 1)
                        if half == 0:
                            prev_store = e * 4 + st_ - 2
                            if prev_store >= 0:
                                sc.wait_ge(P.sem(ex + "ys"), ys0 + (prev_store + 1) * 16)
                        P.ti(sc.activation(out=y_sb[:, (st_ % 2) * D + half * 512:
                                                      (st_ % 2) * D + half * 512 + 512],
                                           in_=ps[st_ + 4 * half][:, :], func=AF.Identity,
                                           scale=wden[:, e * 4 + st_:e * 4 + st_ + 1]), ex + "ye")
                    P.ti(sc.dma_start(out=yd_d[(e * 4 + st_) * TILE:(e * 4 + st_ + 1) * TILE, :],
                                      in_=y_sb[:, (st_ % 2) * D:(st_ % 2) * D + D]), ex + "ys", 16)

        @block.vector
        def _(v):
            step = P.stepper(v, ex + "vch")
            for e in range(E if expert_pass >= 4 else 0):
                for it in range(4):
                    v.wait_ge(P.sem(ex + "si"), si0 + e * 4 + it + 1)
                    v.wait_ge(P.sem(ex + "ma"), ma0 + e * 8 + 2 * it + 2)
                    step(v.tensor_tensor(out=act_sb[:, it * 512:(it + 1) * 512],
                                         in0=sact[:, :], in1=ps[4 + it][:, :], op=ALU.mult))
                    P.ti(v.engine_nop(), ex + "av")

        @block.tensor
        def _(te):
            for e in range(E if expert_pass >= 4 else 0):
                par = (e % 2) * 4096
                te.wait_ge(P.sem(ex + "t"), xt0 + (e + 1) * 128)
                te.wait_ge(P.sem(ex + "w"), wl0 + (e + 1) * 48)
                if e >= 1:
                    if expert_pass >= 5:
                        te.wait_ge(P.sem(ex + "ye"), ye0 + e * 8)
                    te.wait_ge(P.sem(ex + "av"), av0 + e * 4)
                for it in range(4):
                    for c in range(8):
                        mmg = te.matmul(ps[it][:, :],
                                        wg_sb[:, par + c * 512 + it * TILE:par + c * 512 + it * TILE + TILE],
                                        hlnT[:, par + c * 512:par + (c + 1) * 512],
                                        start=(c == 0), stop=(c == 7))
                    P.ti(mmg, ex + "ma")
                    for c in range(8):
                        mmu = te.matmul(ps[4 + it][:, :],
                                        wu_sb[:, par + c * 512 + it * TILE:par + c * 512 + it * TILE + TILE],
                                        hlnT[:, par + c * 512:par + (c + 1) * 512],
                                        start=(c == 0), stop=(c == 7))
                    P.ti(mmu, ex + "ma")
                if expert_pass < 5:
                    continue
                te.wait_ge(P.sem(ex + "av"), av0 + e * 4 + 4)
                for st_ in range(4):
                    te.wait_ge(P.sem(ex + "si"), si0 + e * 4 + 4)
                    for half in range(2):
                        for ic in range(4):
                            mmd = te.matmul(ps[st_ + 4 * half][:, :],
                                            act_sb[:, ic * 512 + st_ * TILE:ic * 512 + st_ * TILE + TILE],
                                            wd_sb[:, par + ic * 1024 + half * 512:par + ic * 1024 + half * 512 + 512],
                                            start=(ic == 0), stop=(ic == 3))
                        P.ti(mmd, ex + "mb")
        return ex

    def emit_shared(rt, ex, xsrc_dram, xdst_dram):
        cmode = combine_mode
        if cmode >= 3:
            @block.vector
            def _(v):
                step = P.stepper(v, rt + "hvch")
                v.wait_ge(P.sem(rt + "kf"), P.val(rt + "kf"))
                step(v.tensor_single_scalar(out=kflag01[:, :], in_=kflag[:, :], scalar=0.0,
                                            op=ALU.is_gt))
                step(v.memset(stg[:, :], 0.0))
                P.ti(v.engine_nop(), rt + "hfg")

            @block.tensor
            def _(te):
                te.wait_ge(P.sem(rt + "hfg"), P.val(rt + "hfg"))
                te.wait_ge(P.sem(rt + "xye"), P.val(rt + "xye"))
                te.wait_ge(P.sem(rt + "xav"), P.val(rt + "xav"))
                mmf = te.matmul(ps[0][:, 0:NT], ones1[:, :], kflag01[:, :], start=True, stop=True)
                P.ti(mmf, rt + "hfm")

            @block.vector
            def _(v):
                step = P.stepper(v, rt + "hvch")
                v.wait_ge(P.sem(rt + "hfm"), P.val(rt + "hfm"))
                step(v.tensor_copy(out=flagcol[:, :], in_=ps[0][:, 0:NT]))
                P.ti(v.engine_nop(), rt + "hfc")
        sh = rt + "h"
        NGRP = NT // 4
        tp0, ma0, si0, av0 = P.val(sh + "tp"), P.val(sh + "ma"), P.val(sh + "si"), P.val(sh + "av")
        mb0, ev0, st0 = P.val(sh + "mb"), P.val(sh + "ev"), P.val(sh + "st")
        cg0, cq0 = P.val(sh + "cg"), P.val(sh + "cq")
        idf0 = P.val(sh + "idf")
        xl0 = P.val(sh + "xl")
        exma, exmb = P.val(ex + "ma"), P.val(ex + "mb")
        exys = P.val(ex + "ys")

        @block.sync
        def _(sy):
            sy.wait_ge(P.sem(ex + "ma"), exma)
            sy.wait_ge(P.sem(ex + "mb"), exmb)
            P.ti(sy.dma_start(out=wg_sb[:, 0:4096].rearrange("p (c i) -> p c i", i=II),
                              in_=sgbf_d.ap().rearrange("(c p) i -> p c i", p=TILE)), sh + "wl", 16)
            P.ti(sy.dma_start(out=wu_sb[:, 0:4096].rearrange("p (c i) -> p c i", i=II),
                              in_=subf_d.ap().rearrange("(c p) i -> p c i", p=TILE)), sh + "wl", 16)
            P.ti(sy.dma_start(out=wd_sb[:, 0:4096].rearrange("p (c i) -> p c i", i=D),
                              in_=sdbf_d.ap().rearrange("(c p) i -> p c i", p=TILE)), sh + "wl", 16)
            for t in range(NT + 4):
                if t < NT:
                    buf = xa if t % 2 == 0 else xb
                    if t >= 2:
                        sy.wait_ge(P.sem(sh + "ev"), ev0 + 2 * (t - 1))
                    P.ti(sy.dma_start(out=buf[:, :], in_=xsrc_dram[t * TILE:(t + 1) * TILE, :]),
                         sh + "xl", 16)
                if t >= 4:
                    ts_ = t - 4
                    tt = ts_ % 4
                    sy.wait_ge(P.sem(sh + "ev"), ev0 + 2 * (ts_ + 1))
                    P.ti(sy.dma_start(out=xdst_dram[ts_ * TILE:(ts_ + 1) * TILE, :],
                                      in_=x1g[:, tt * D:(tt + 1) * D]), sh + "st", 16)

        @block.scalar
        def _(sc):
            sc.wait_ge(P.sem(ex + "ma"), exma)
            for g in range(NGRP):
                par = (g % 2) * 4096
                if g >= 2:
                    sc.wait_ge(P.sem(sh + "ma"), ma0 + (g - 1) * 8)
                for c in range(8):
                    P.ti(sc.dma_start(out=hlnT[:, par + c * 512:par + (c + 1) * 512],
                                      in_=hln_d[g * 512:(g + 1) * 512, c * TILE:(c + 1) * TILE],
                                      transpose=True), sh + "tp", 16)
                for it in range(4):
                    sc.wait_ge(P.sem(sh + "ma"), ma0 + g * 8 + 2 * it + 1)
                    if it > 0:
                        sc.wait_ge(P.sem(sh + "av"), av0 + g * 4 + it)
                    P.ti(sc.activation(out=sact[:, :], in_=ps[it][:, :], func=AF.Silu), sh + "si")

        @block.vector
        def _(v):
            step = P.stepper(v, sh + "vch")
            for g in range(NGRP):
                for it in range(4):
                    v.wait_ge(P.sem(sh + "si"), si0 + g * 4 + it + 1)
                    v.wait_ge(P.sem(sh + "ma"), ma0 + g * 8 + 2 * it + 2)
                    step(v.tensor_tensor(out=act_sb[:, it * 512:(it + 1) * 512],
                                         in0=sact[:, :], in1=ps[4 + it][:, :], op=ALU.mult))
                    P.ti(v.engine_nop(), sh + "av")
                for tt in range(4):
                    t = g * 4 + tt
                    buf = xa if t % 2 == 0 else xb
                    if cmode >= 3:
                        tf = t
                        if tf < NT:
                            if tf >= 2:
                                v.wait_ge(P.sem(sh + "cq"), cq0 + tf - 1)
                            step(v.tensor_scalar_mul(out=identX[:, (tf % 2) * TILE:(tf % 2 + 1) * TILE],
                                                     in0=ident_b[:, :],
                                                     scalar1=flagcol[:, tf:tf + 1]))
                            P.ti(v.engine_nop(), sh + "idf")
                    v.wait_ge(P.sem(sh + "cq"), cq0 + t + 1)
                    for half in range(2):
                        v.wait_ge(P.sem(sh + "mb"), mb0 + g * 8 + tt * 2 + half + 1)
                        v.wait_ge(P.sem(sh + "xl"), xl0 + (t + 1) * 16)
                        if g >= 1:
                            v.wait_ge(P.sem(sh + "st"), st0 + (g - 1) * 64 + (tt + 1) * 16)
                        step(v.scalar_tensor_tensor(out=x1g[:, tt * D + half * 512:tt * D + half * 512 + 512],
                                                    in0=ps[tt + 4 * half][:, :], scalar=0.5,
                                                    in1=buf[:, half * 512:half * 512 + 512],
                                                    op0=ALU.mult, op1=ALU.add))
                        P.ti(v.engine_nop(), sh + "ev")

        @block.tensor
        def _(te):
            te.wait_ge(P.sem(sh + "wl"), P.val(sh + "wl"))
            te.wait_ge(P.sem(ex + "ye"), P.val(ex + "ye"))
            te.wait_ge(P.sem(ex + "av"), P.val(ex + "av"))
            for g in range(NGRP):
                par = (g % 2) * 4096
                te.wait_ge(P.sem(sh + "tp"), tp0 + (g + 1) * 128)
                if g >= 1:
                    te.wait_ge(P.sem(sh + "ev"), ev0 + g * 8)
                    te.wait_ge(P.sem(sh + "av"), av0 + g * 4)
                for it in range(4):
                    for c in range(8):
                        mmg = te.matmul(ps[it][:, :],
                                        wg_sb[:, c * 512 + it * TILE:c * 512 + it * TILE + TILE],
                                        hlnT[:, par + c * 512:par + (c + 1) * 512],
                                        start=(c == 0), stop=(c == 7))
                    P.ti(mmg, sh + "ma")
                    for c in range(8):
                        mmu = te.matmul(ps[4 + it][:, :],
                                        wu_sb[:, c * 512 + it * TILE:c * 512 + it * TILE + TILE],
                                        hlnT[:, par + c * 512:par + (c + 1) * 512],
                                        start=(c == 0), stop=(c == 7))
                    P.ti(mmu, sh + "ma")
                te.wait_ge(P.sem(sh + "av"), av0 + g * 4 + 4)
                te.wait_ge(P.sem(sh + "si"), si0 + g * 4 + 4)
                for tt in range(4):
                    t = g * 4 + tt
                    spar = (t % 2) * KTOP * D
                    for half in range(2):
                        for ic in range(4):
                            mmd = te.matmul(ps[tt + 4 * half][:, :],
                                            act_sb[:, ic * 512 + tt * TILE:ic * 512 + tt * TILE + TILE],
                                            wd_sb[:, ic * 1024 + half * 512:ic * 1024 + half * 512 + 512],
                                            start=(ic == 0), stop=(ic == 3))
                        P.ti(mmd, sh + "mb")
                    te.wait_ge(P.sem(sh + "cg"), cg0 + (t + 1) * 16 * KTOP)
                    if cmode == 0:
                        P.ti(te.nop(), sh + "cq")
                    elif cmode >= 3:
                        te.wait_ge(P.sem(sh + "idf"), P.val(sh + "idf") - NT + t + 1
                                   if False else idf0 + t + 1)
                        for k in range(KTOP):
                            for half in range(2):
                                mi = te.matmul(ps[tt + 4 * half][:, :],
                                               identX[:, (t % 2) * TILE:(t % 2 + 1) * TILE],
                                               stg[:, spar + k * D + half * 512:
                                                   spar + k * D + half * 512 + 512],
                                               start=False, stop=(k == KTOP - 1))
                        P.ti(mi, sh + "cq")
                    elif cmode == 1:
                        for k in range(KTOP):
                            for half in range(2):
                                mi = te.matmul(ps[tt + 4 * half][:, :], ident_b[:, :],
                                               stg[:, spar + k * D + half * 512:
                                                   spar + k * D + half * 512 + 512],
                                               start=False, stop=(k == KTOP - 1))
                        P.ti(mi, sh + "cq")
                    else:
                        with te.register(uname(sh + f"cr{t}")) as rt_:
                            te.reg_load(rt_, kflag_i[0:1, t:t + 1])
                            with te.If_eq(rt_, 0):
                                te.nop().then_inc(P.sem(sh + "cq"), 1, skip_validation=True)
                            with te.Else():
                                for k in range(KTOP):
                                    for half in range(2):
                                        mi = te.matmul(ps[tt + 4 * half][:, :], ident_b[:, :],
                                                       stg[:, spar + k * D + half * 512:
                                                           spar + k * D + half * 512 + 512],
                                                       start=False, stop=(k == KTOP - 1))
                                mi.then_inc(P.sem(sh + "cq"), 1, skip_validation=True)
                        P.v[sh + "cq"] = P.val(sh + "cq") + 1

        @block.gpsimd
        def _(gp):
            gp.wait_ge(P.sem(ex + "ys"), exys)
            if cmode >= 3:
                gp.wait_ge(P.sem(rt + "hfg"), P.val(rt + "hfg"))
            with gp.register(uname(sh + "rf")) as rf:
                for t in range(NT):
                    spar = (t % 2) * KTOP * D
                    if cmode == 0:
                        P.ti(gp.engine_nop(), sh + "cg", 16 * KTOP)
                        continue
                    if t >= 2:
                        gp.wait_ge(P.sem(sh + "cq"), cq0 + t - 1)
                    if cmode == 1:
                        for k in range(KTOP):
                            P.ti(gp.indirect_dma_start(
                                out=stg[:, spar + k * D:spar + (k + 1) * D], out_offset=None,
                                in_=yd_d.ap(),
                                in_offset=IndirectOffsetOnAxis(cmb_all[:, t * 8 + k:t * 8 + k + 1], 0)),
                                sh + "cg", 16)
                        continue
                    gp.reg_load(rf, kflag_i[0:1, t:t + 1])
                    with gp.If_eq(rf, 0):
                        gp.engine_nop().then_inc(P.sem(sh + "cg"), 16 * KTOP, skip_validation=True)
                    with gp.Else():
                        for k in range(KTOP):
                            gp.indirect_dma_start(
                                out=stg[:, spar + k * D:spar + (k + 1) * D], out_offset=None,
                                in_=yd_d.ap(),
                                in_offset=IndirectOffsetOnAxis(cmb_all[:, t * 8 + k:t * 8 + k + 1], 0),
                            ).then_inc(P.sem(sh + "cg"), 16, skip_validation=True)
                    P.v[sh + "cg"] = P.val(sh + "cg") + 16 * KTOP

        return sh


    def phase_barrier():
        snap = [(n, P.v[n]) for n in list(P.v.keys()) if P.v[n] > 0]
        for sec in (block.sync, block.scalar, block.vector, block.tensor, block.gpsimd):
            @sec
            def _(eng, snap=snap):
                for n, c in snap:
                    eng.wait_ge(P.sem(n), c)

    def emit_attn():
        NGRP = NT // 4
        NTB_ = NT // NB
        r2buf = pairs_all
        h2buf = wd_sb[:, 2048:2048 + 4 * D]
        x2sb = ew_all
        ttab = scr
        wqkv_sb = xg
        tmpo = sqbuf

        @block.sync
        def _(sy):
            sy.wait_ge(P.sem("wcast"), P.val("wcast"))
            P.ti(sy.dma_start(out=wqkv_sb[:, 0:1536].rearrange("p (c w) -> p c w", w=192),
                              in_=wqkv_bf_d.ap().rearrange("(c p) w -> p c w", p=TILE)), "atwl", 16)
            P.ti(sy.dma_start(out=woT_sb[:, :], in_=wo_bf_d[:, :]), "atwl", 16)
            P.ti(sy.dma_start(out=bqkv_sb[:, :], in_=bqkv_in[:, :]), "atwl", 16)
            P.ti(sy.dma_start(out=bo_sb[:, :], in_=bo_in[:, :]), "atwl", 16)
            P.ti(sy.dma_start(out=drow_sb[0:1, 0:DS], in_=decay_in[:, :]), "atwl", 16)
            sy.wait_ge(P.sem("atprA"), 3)
            P.ti(sy.dma_start(out=misc_d.ap().rearrange("e one -> one e"),
                              in_=drow_sb[0:1, 2 * DS:3 * DS]), "atms", 16)
            sy.wait_ge(P.sem("atms"), 16)
            P.ti(sy.dma_start(out=lnd_col[:, :], in_=misc_d.ap()), "atms", 16)
            for t in range(NT + 2):
                if t < NT:
                    if t >= 4:
                        sy.wait_ge(P.sem("atx2"), t - 3)
                    P.ti(sy.dma_start(out=r2buf[:, (t % 4) * D:(t % 4 + 1) * D],
                                      in_=x1_d[t * TILE:(t + 1) * TILE, :]), "atr2l", 16)
                    P.ti(sy.dma_start(out=h2buf[:, (t % 4) * D:(t % 4 + 1) * D],
                                      in_=hln_d[t * TILE:(t + 1) * TILE, :]), "ath2l", 16)
                if t >= 2:
                    tau = t - 2
                    sy.wait_ge(P.sem("atx2h"), tau + 1)
                    sy.wait_ge(P.sem("atx2"), tau + 1)
                    P.ti(sy.dma_start(out=x2_d[tau * TILE:(tau + 1) * TILE, :],
                                      in_=x2sb[:, (tau % 2) * D:(tau % 2 + 1) * D]), "atst", 16)

        @block.gpsimd
        def _(gp):
            P.ti(gp.iota(exprow[:, 0:TILE], pattern=[[0, 4], [1, 32]], base=-16,
                         channel_multiplier=0, allow_small_or_imprecise_dtypes=True), "atpi")
            P.ti(gp.iota(exprow[:, TILE:2 * TILE], pattern=[[0, 4], [1, 32]], base=0,
                         channel_multiplier=0, allow_small_or_imprecise_dtypes=True), "atpi")
            P.ti(gp.iota(exprow[:, 2 * TILE:3 * TILE], pattern=[[1, TILE]], base=-64,
                         channel_multiplier=0, allow_small_or_imprecise_dtypes=True), "atpi")
            P.ti(gp.iota(exprow[:, 3 * TILE:4 * TILE], pattern=[[1, TILE]], base=1,
                         channel_multiplier=0, allow_small_or_imprecise_dtypes=True), "atpi")
            P.ti(gp.iota(ttab[:, 0:TILE], pattern=[[1, TILE]], base=0,
                         channel_multiplier=0, allow_small_or_imprecise_dtypes=True), "atpi")
            P.ti(gp.iota(ttab[:, TILE:2 * TILE], pattern=[[1, 4], [0, 32]], base=0,
                         channel_multiplier=0, allow_small_or_imprecise_dtypes=True), "atpi")
            P.ti(gp.iota(ttab[:, 2 * TILE:3 * TILE], pattern=[[1, 2], [0, 64]], base=0,
                         channel_multiplier=0, allow_small_or_imprecise_dtypes=True), "atpi")
            P.ti(gp.iota(ttab[:, 3 * TILE:4 * TILE], pattern=[[0, 2], [1, 64]], base=0,
                         channel_multiplier=0, allow_small_or_imprecise_dtypes=True), "atpi")

        @block.scalar
        def _(sc):
            stepA = P.stepper(sc, "atprA")
            sc.wait_ge(P.sem("atwl"), 16 * 5)
            stepA(sc.activation(out=drow_sb[0:1, DS:2 * DS], in_=drow_sb[0:1, 0:DS],
                                func=AF.Tanh, scale=0.5, bias=0.0))
            stepA(sc.activation(out=drow_sb[0:1, DS:2 * DS], in_=drow_sb[0:1, DS:2 * DS],
                                func=AF.Identity, scale=0.5, bias=0.5))
            stepA(sc.activation(out=drow_sb[0:1, 2 * DS:3 * DS], in_=drow_sb[0:1, DS:2 * DS],
                                func=AF.Ln, bias=0.0))
            stepA(sc.activation(out=drow_sb[0:1, 3 * DS:4 * DS], in_=drow_sb[0:1, 2 * DS:3 * DS],
                                func=AF.Exp, scale=float(TILE), bias=0.0))
            sc.wait_ge(P.sem("atdc"), 1)
            P.ti(sc.activation(out=dcolx[:, :], in_=dcolx[:, :], func=AF.Exp, bias=0.0), "attab")
            sc.wait_ge(P.sem("atdc"), 2)
            P.ti(sc.activation(out=drowt[:, :], in_=drowt[:, :], func=AF.Exp, bias=0.0), "attab")
            for c in range(8):
                P.ti(sc.dma_start(out=hlnT[:, c * 512:(c + 1) * 512],
                                  in_=hln_d[0:512, c * TILE:(c + 1) * TILE],
                                  transpose=True), "attp", 16)
            for t in range(NT + 2):
                if (t + 3) % 4 == 0:
                    g = (t + 3) // 4
                    if g < NGRP:
                        if g >= 2:
                            sc.wait_ge(P.sem("atqk"), 4 * g - 4)
                        gpar = (g % 2) * 4096
                        for c in range(8):
                            P.ti(sc.dma_start(out=hlnT[:, gpar + c * 512:gpar + (c + 1) * 512],
                                              in_=hln_d[g * 512:(g + 1) * 512, c * TILE:(c + 1) * TILE],
                                              transpose=True), "attp", 16)
                for tq in ([0, 1] if t == 0 else [t + 1]):
                    if tq < NT:
                        sc.wait_ge(P.sem("atqk"), tq + 1)
                        if tq >= 3:
                            sc.wait_ge(P.sem("atsa"), tq - 2)
                        pqq = ps[6 + tq % 2]
                        sl = (tq % 3) * TILE
                        P.ti(sc.activation(out=qT_bf[:, sl:sl + TILE], in_=pqq[0:DS, 0:TILE],
                                           func=AF.Identity, bias=0.0), "atev")
                        P.ti(sc.activation(out=kT_bf[:, sl:sl + TILE], in_=pqq[0:DS, TILE:2 * TILE],
                                           func=AF.Identity, bias=0.0), "atev")
                        P.ti(sc.activation(out=kvt[:, sl:sl + TILE], in_=pqq[:, 2 * TILE:3 * TILE],
                                           func=AF.Identity, bias=0.0), "atev")

        @block.vector
        def _(v):
            step = P.stepper(v, "atvch")
            v.wait_ge(P.sem("atprT"), 1)
            step(v.tensor_copy(out=lnd_rep[:, :], in_=ps[0][:, 0:DS]))
            v.wait_ge(P.sem("atprT"), 2)
            step(v.tensor_copy(out=dtile_rep[:, :], in_=ps[1][0:DS, 0:DS]))
            step(v.tensor_single_scalar(out=expcol[:, 0:1], in_=iota_p[:, :], scalar=31.5, op=ALU.is_gt))
            step(v.tensor_single_scalar(out=expcol[:, 1:2], in_=iota_p[:, :], scalar=63.5, op=ALU.is_gt))
            step(v.tensor_single_scalar(out=expcol[:, 2:3], in_=iota_p[:, :], scalar=95.5, op=ALU.is_gt))
            step(v.tensor_tensor(out=expcol[:, 0:1], in0=expcol[:, 0:1], in1=expcol[:, 1:2], op=ALU.add))
            step(v.tensor_tensor(out=expcol[:, 0:1], in0=expcol[:, 0:1], in1=expcol[:, 2:3], op=ALU.add))
            step(v.tensor_scalar(out=expcol[:, 3:4], in0=expcol[:, 0:1], scalar1=-32.0, scalar2=0.0, op0=ALU.mult))
            step(v.tensor_tensor(out=expcol[:, 3:4], in0=expcol[:, 3:4], in1=iota_p[:, :], op=ALU.add))
            step(v.tensor_scalar(out=expcol[:, 4:5], in0=expcol[:, 3:4], scalar1=-1.0, scalar2=16.0,
                                 op0=ALU.mult, op1=ALU.add))
            step(v.tensor_scalar(out=expcol[:, 5:6], in0=expcol[:, 3:4], scalar1=-1.0, scalar2=32.0,
                                 op0=ALU.mult, op1=ALU.add))
            step(v.tensor_scalar(out=expcol[:, 6:7], in0=iota_p[:, :], scalar1=-1.0, scalar2=64.0,
                                 op0=ALU.mult, op1=ALU.add))
            step(v.tensor_scalar(out=expcol[:, 7:8], in0=iota_p[:, :], scalar1=-1.0, scalar2=127.0,
                                 op0=ALU.mult, op1=ALU.add))
            for l, cc in enumerate((4, 5, 6, 7)):
                step(v.tensor_tensor(out=dcolx[:, l * DS:(l + 1) * DS], in0=lnd_rep[:, :],
                                     in1=expcol[:, cc:cc + 1].broadcast_to((TILE, DS)), op=ALU.mult))
            step(v.tensor_scalar(out=dcolx[:, :], in0=dcolx[:, :], scalar1=80.0, scalar2=0.0, op0=ALU.min))
            P.ti(v.engine_nop(), "atdc")
            v.wait_ge(P.sem("atpi"), 4)
            v.wait_ge(P.sem("atms"), 32)
            step(v.tensor_tensor(out=drowt[:, :], in0=exprow[:, :],
                                 in1=lnd_col[:, :].broadcast_to((DS, 4 * TILE)), op=ALU.mult))
            step(v.tensor_scalar(out=drowt[:, :], in0=drowt[:, :], scalar1=80.0, scalar2=0.0, op0=ALU.min))
            P.ti(v.engine_nop(), "atdc")
            v.wait_ge(P.sem("atpi"), 8)
            tA = xa[:, 0:TILE]
            tB = xa[:, TILE:2 * TILE]
            step(v.tensor_tensor(out=tA, in0=ttab[:, TILE:2 * TILE],
                                 in1=expcol[:, 0:1].broadcast_to((TILE, TILE)), op=ALU.is_equal))
            step(v.tensor_tensor(out=tB, in0=ttab[:, 0:TILE],
                                 in1=iota_p[:, :].broadcast_to((TILE, TILE)), op=ALU.is_ge))
            step(v.tensor_tensor(out=mask3[:, 0:TILE], in0=tA, in1=tB, op=ALU.mult))
            step(v.tensor_tensor(out=tA, in0=ttab[:, 2 * TILE:3 * TILE],
                                 in1=expcol[:, 1:2].broadcast_to((TILE, TILE)), op=ALU.is_equal))
            step(v.tensor_single_scalar(out=tB, in_=ttab[:, 3 * TILE:4 * TILE], scalar=31.5, op=ALU.is_gt))
            step(v.tensor_tensor(out=tA, in0=tA, in1=tB, op=ALU.mult))
            step(v.tensor_scalar(out=expcol[:, 3:4], in0=expcol[:, 1:2], scalar1=-64.0, scalar2=0.0, op0=ALU.mult))
            step(v.tensor_tensor(out=expcol[:, 3:4], in0=expcol[:, 3:4], in1=iota_p[:, :], op=ALU.add))
            step(v.tensor_single_scalar(out=expcol[:, 3:4], in_=expcol[:, 3:4], scalar=31.5, op=ALU.is_gt))
            step(v.tensor_scalar(out=expcol[:, 3:4], in0=expcol[:, 3:4], scalar1=-1.0, scalar2=1.0,
                                 op0=ALU.mult, op1=ALU.add))
            step(v.tensor_tensor(out=mask3[:, TILE:2 * TILE], in0=tA,
                                 in1=expcol[:, 3:4].broadcast_to((TILE, TILE)), op=ALU.mult))
            step(v.tensor_single_scalar(out=tA, in_=ttab[:, 0:TILE], scalar=63.5, op=ALU.is_gt))
            step(v.tensor_scalar(out=expcol[:, 4:5], in0=expcol[:, 1:2], scalar1=-1.0, scalar2=1.0,
                                 op0=ALU.mult, op1=ALU.add))
            step(v.tensor_tensor(out=mask3[:, 2 * TILE:3 * TILE], in0=tA,
                                 in1=expcol[:, 4:5].broadcast_to((TILE, TILE)), op=ALU.mult))
            step(v.memset(st_f[:, :], 0.0))
            step(v.memset(st_b[:, :], 0.0))
            P.ti(v.engine_nop(), "atsu")
            v.wait_ge(P.sem("attab"), 2)
            for t in range(NT + 2):
                for tq in ([0, 1] if t == 0 else [t + 1]):
                    if tq < NT:
                        v.wait_ge(P.sem("atev"), 3 * tq + 3)
                        if tq >= 3:
                            v.wait_ge(P.sem("atim"), tq - 2)
                        sl = (tq % 3) * TILE
                        vsl = (tq % 3) * 256
                        P.ti(v.tensor_tensor(
                            out=v4[:, vsl:vsl + 256].rearrange("p (l i) -> p l i", i=DS),
                            in0=kvt[:, sl + DS:sl + TILE].rearrange("p (o i) -> p o i", o=1)
                                .broadcast_to((TILE, 4, DS)),
                            in1=dcolx[:, :].rearrange("p (l i) -> p l i", i=DS),
                            op=ALU.mult), "atvv")
                if t < NT:
                    v.wait_ge(P.sem("atat"), t + 1)
                    if t >= 2:
                        v.wait_ge(P.sem("atim"), t - 1)
                    pt_ = ps[2 + t % 2]
                    msl = (t % 2) * 3 * TILE
                    for l in range(3):
                        P.ti(v.tensor_tensor(out=mav[:, msl + l * TILE:msl + (l + 1) * TILE],
                                             in0=pt_[:, 0:TILE], in1=mask3[:, l * TILE:(l + 1) * TILE],
                                             op=ALU.mult), "atmk")
                if 0 <= t - 2 < NT:
                    tau = t - 2
                    v.wait_ge(P.sem("atpjh"), 2 * tau + 1)
                    v.wait_ge(P.sem("atr2l"), (tau + 1) * 16)
                    v.wait_ge(P.sem("ath2l"), (tau + 1) * 16)
                    if tau >= 2:
                        v.wait_ge(P.sem("atst"), (tau - 1) * 16)
                    step(v.tensor_tensor(out=x2sb[:, (tau % 2) * D:(tau % 2) * D + 512],
                                         in0=ps[4 + tau % 2][:, :],
                                         in1=r2buf[:, (tau % 4) * D:(tau % 4) * D + 512],
                                         op=ALU.add))
                    P.ti(v.tensor_tensor(out=x2sb[:, (tau % 2) * D:(tau % 2) * D + 512],
                                         in0=x2sb[:, (tau % 2) * D:(tau % 2) * D + 512],
                                         in1=h2buf[:, (tau % 4) * D:(tau % 4) * D + 512],
                                         op=ALU.add), "atx2h")
                if 1 <= t <= NT - 1:
                    v.wait_ge(P.sem("atsa"), t)
                    if t == NTB_:
                        step(v.memset(st_f[:, :], 0.0))
                        step(v.memset(st_b[:, :], 0.0))
                    else:
                        v.wait_ge(P.sem("attt"), t)
                        ptm = ps[2 + (t - 1) % 2]
                        step(v.tensor_tensor(out=st_f[:, :], in0=st_f[:, :], in1=dtile_rep[:, :],
                                             op=ALU.mult))
                        step(v.tensor_tensor(out=st_f[:, :], in0=st_f[:, :],
                                             in1=ptm[0:DS, TILE:TILE + DS], op=ALU.add))
                        step(v.tensor_copy(out=st_b[:, :], in_=st_f[:, :]))
                    P.ti(v.engine_nop(), "atsu")
                if 0 <= t - 1 < NT:
                    tau = t - 1
                    v.wait_ge(P.sem("atim"), tau + 1)
                    v.wait_ge(P.sem("atsa"), tau + 1)
                    if tau >= 2:
                        v.wait_ge(P.sem("atpjh"), 2 * tau - 2)
                    pa_ = ps[0 + tau % 2]
                    step(v.tensor_tensor(out=tmpo[0:DS, 0:512], in0=pa_[0:DS, :],
                                         in1=drowt[0:DS, :], op=ALU.mult))
                    step(v.tensor_reduce(
                        out=obuf[0:DS, :].rearrange("p (x o) -> p x o", o=1),
                        in_=tmpo[0:DS, 0:512].rearrange("p (k x) -> p x k", k=4),
                        axis=AX.X, op=ALU.add))
                    P.ti(v.tensor_copy(out=obf[0:DS, (tau % 2) * TILE:(tau % 2 + 1) * TILE],
                                       in_=obuf[0:DS, :]), "atob")
                if 0 <= t - 2 < NT:
                    tau = t - 2
                    v.wait_ge(P.sem("atpjh"), 2 * tau + 2)
                    step(v.tensor_tensor(out=x2sb[:, (tau % 2) * D + 512:(tau % 2) * D + 1024],
                                         in0=ps[4 + tau % 2][:, :],
                                         in1=r2buf[:, (tau % 4) * D + 512:(tau % 4) * D + 1024],
                                         op=ALU.add))
                    P.ti(v.tensor_tensor(out=x2sb[:, (tau % 2) * D + 512:(tau % 2) * D + 1024],
                                         in0=x2sb[:, (tau % 2) * D + 512:(tau % 2) * D + 1024],
                                         in1=h2buf[:, (tau % 4) * D + 512:(tau % 4) * D + 1024],
                                         op=ALU.add), "atx2")

        @block.tensor
        def _(te):
            te.wait_ge(P.sem("atwl"), 16 * 5)
            te.wait_ge(P.sem("atprA"), 3)
            P.ti(te.matmul(ps[0][:, 0:DS], ones1[0:1, 0:TILE], drow_sb[0:1, 2 * DS:3 * DS],
                           start=True, stop=True), "atprT")
            te.wait_ge(P.sem("atprA"), 4)
            P.ti(te.matmul(ps[1][0:DS, 0:DS], ones1[0:1, 0:DS], drow_sb[0:1, 3 * DS:4 * DS],
                           start=True, stop=True), "atprT")
            for t in range(NT + 2):
                if 0 <= t - 2 < NT:
                    tau = t - 2
                    pp = ps[4 + tau % 2]
                    te.wait_ge(P.sem("atob"), tau + 1)
                    if tau >= 2:
                        te.wait_ge(P.sem("atx2"), tau - 1)
                    osl = (tau % 2) * TILE
                    te.matmul(pp[:, :], obf[0:DS, osl:osl + TILE], woT_sb[0:DS, 0:512],
                              start=True, stop=False)
                    P.ti(te.matmul(pp[:, :], ones1[0:1, 0:TILE], bo_sb[0:1, 0:512],
                                   start=False, stop=True), "atpjh")
                for tq in ([0, 1] if t == 0 else [t + 1]):
                    if tq < NT:
                        g = tq // 4
                        te.wait_ge(P.sem("attp"), (g + 1) * 128)
                        if tq >= 2:
                            te.wait_ge(P.sem("atev"), 3 * tq)
                        pqq = ps[6 + tq % 2]
                        base = (g % 2) * 4096 + (tq % 4) * TILE
                        for c in range(8):
                            te.matmul(pqq[0:DS, 0:TILE], wqkv_sb[:, c * 192:c * 192 + DS],
                                      hlnT[:, base + c * 512:base + c * 512 + TILE],
                                      start=(c == 0), stop=False)
                        te.matmul(pqq[0:DS, 0:TILE], bqkv_sb[0:1, 0:DS], ones1[0:1, 0:TILE],
                                  start=False, stop=True)
                        for c in range(8):
                            te.matmul(pqq[0:DS, TILE:2 * TILE],
                                      wqkv_sb[:, c * 192 + DS:c * 192 + 2 * DS],
                                      hlnT[:, base + c * 512:base + c * 512 + TILE],
                                      start=(c == 0), stop=False)
                        te.matmul(pqq[0:DS, TILE:2 * TILE], bqkv_sb[0:1, DS:2 * DS],
                                  ones1[0:1, 0:TILE], start=False, stop=True)
                        for c in range(8):
                            te.matmul(pqq[:, 2 * TILE:3 * TILE],
                                      hlnT[:, base + c * 512:base + c * 512 + TILE],
                                      wqkv_sb[:, c * 192 + DS:c * 192 + 192],
                                      start=(c == 0), stop=False)
                        P.ti(te.matmul(pqq[:, 2 * TILE:3 * TILE], ones1[0:1, 0:TILE],
                                       bqkv_sb[0:1, DS:192], start=False, stop=True), "atqk")
                if t < NT:
                    te.wait_ge(P.sem("atev"), 3 * t + 3)
                    if t >= 2:
                        te.wait_ge(P.sem("atmk"), 3 * t - 3)
                        te.wait_ge(P.sem("atsu"), t)
                    pt_ = ps[2 + t % 2]
                    sl = (t % 3) * TILE
                    P.ti(te.matmul(pt_[:, 0:TILE], kT_bf[:, sl:sl + TILE],
                                   qT_bf[:, sl:sl + TILE], start=True, stop=True), "atat")
                    te.wait_ge(P.sem("atvv"), t + 1)
                    P.ti(te.matmul(pt_[0:DS, TILE:TILE + DS], kvt[:, sl:sl + DS],
                                   v4[:, (t % 3) * 256 + 192:(t % 3) * 256 + 256],
                                   start=True, stop=True), "attt")
                if 0 <= t - 1 < NT:
                    tau = t - 1
                    pa_ = ps[0 + tau % 2]
                    te.wait_ge(P.sem("atmk"), 3 * tau + 3)
                    if tau >= 2:
                        te.wait_ge(P.sem("atob"), tau - 1)
                    vsl = (tau % 3) * 256
                    msl = (tau % 2) * 3 * TILE
                    for l in range(3):
                        mi = te.matmul(pa_[0:DS, l * TILE:(l + 1) * TILE],
                                       v4[:, vsl + l * DS:vsl + (l + 1) * DS],
                                       mav[:, msl + l * TILE:msl + (l + 1) * TILE],
                                       start=True, stop=True)
                    P.ti(mi, "atim")
                    te.wait_ge(P.sem("atsu"), tau + 1)
                    sl1 = (tau % 3) * TILE
                    P.ti(te.matmul(pa_[0:DS, 3 * TILE:4 * TILE], st_b[:, :],
                                   qT_bf[:, sl1:sl1 + TILE], start=True, stop=True), "atsa")
                if 0 <= t - 2 < NT:
                    tau = t - 2
                    pp = ps[4 + tau % 2]
                    te.wait_ge(P.sem("atx2h"), tau + 1)
                    osl = (tau % 2) * TILE
                    te.matmul(pp[:, :], obf[0:DS, osl:osl + TILE], woT_sb[0:DS, 512:1024],
                              start=True, stop=False)
                    P.ti(te.matmul(pp[:, :], ones1[0:1, 0:TILE], bo_sb[0:1, 512:1024],
                                   start=False, stop=True), "atpjh")

    ln1_st = emit_ln(x_in, hln_d, "l1")
    if stn >= 1:
        emit_routing(ln1_st, "f1", route_pass)
    if stn >= 2:
        ex1 = emit_expert("f1", hln_d, expert_pass)
    if stn >= 3:
        sh1 = emit_shared("f1", "f1x", x_in, x1_d if stn >= 4 else y_out)
    if stn >= 4:
        phase_barrier()
        emit_ln(x1_d, hln_d, "l2")
        phase_barrier()
        emit_attn()
    if stn >= 5:
        phase_barrier()
        ln3_st = emit_ln(x2_d, hln_d, "l1")
        emit_routing(ln3_st, "f1", route_pass)
        ex2 = emit_expert("f1", hln_d, expert_pass)
        sh2 = emit_shared("f1", "f1x", x2_d, y_out)

    @block.sync
    def _(sy):
        if "hln" in dbg:
            sy.wait_ge(P.sem(ln1_st), P.val(ln1_st))
            for t in range(0, NT, 8):
                P.ti(sy.dma_start(out=dbg["hln"][t * TILE:(t + 8) * TILE, :],
                                  in_=hln_d[t * TILE:(t + 8) * TILE, :]), "dbst", 16)
        if stn >= 1:
            if stn >= 3 and "x1" in dbg:
                sy.wait_ge(P.sem("f1hst"), P.val("f1hst"))
                for r in range(0, T, 4096):
                    P.ti(sy.dma_start(out=dbg["x1"][r:r + 4096, :], in_=x1_d[r:r + 4096, :]), "dbst", 16)
            if stn >= 2 and "xd" in dbg:
                sy.wait_ge(P.sem("f1xs"), P.val("f1xs"))
                for r in range(0, NSLOT, 4096):
                    P.ti(sy.dma_start(out=dbg["xd"][r:r + 4096, :], in_=xd_d[r:r + 4096, :]), "dbst", 16)
            if stn >= 2 and "yd" in dbg:
                sy.wait_ge(P.sem("f1xys"), P.val("f1xys"))
                for r in range(0, NSLOT, 4096):
                    P.ti(sy.dma_start(out=dbg["yd"][r:r + 4096, :], in_=yd_d[r:r + 4096, :]), "dbst", 16)
            if stn >= 4 and "x2" in dbg:
                sy.wait_ge(P.sem("atst"), P.val("atst"))
                for r in range(0, T, 4096):
                    P.ti(sy.dma_start(out=dbg["x2"][r:r + 4096, :], in_=x2_d[r:r + 4096, :]), "dbst", 16)
            if "inv" in dbg and P.val("f1scdone") > 0:
                sy.wait_ge(P.sem("f1scdone"), P.val("f1scdone"))
                for r in range(0, NSLOT, 4096):
                    P.ti(sy.dma_start(out=dbg["inv"][r:r + 4096, :],
                                      in_=inv_d[r:r + 4096, :]), "dbst", 16)

    if stn < 3:
        @block.gpsimd
        def _(gp):
            gp.wait_ge(P.sem(ln1_st), P.val(ln1_st))
            for r in range(0, n_out, 1024):
                P.ti(gp.dma_start(out=y_out[r:r + 1024, :],
                                  in_=hln_d[out_rows[0] + r:out_rows[0] + r + 1024, :]), "yst", 16)
    elif stn == 4:
        @block.gpsimd
        def _(gp):
            gp.wait_ge(P.sem("atst"), P.val("atst"))
            for r in range(0, n_out, 1024):
                P.ti(gp.dma_start(out=y_out[r:r + 1024, :],
                                  in_=x2_d[out_rows[0] + r:out_rows[0] + r + 1024, :]), "yst", 16)

    @block.sync
    def _(sy):
        for name in list(P.v.keys()):
            if name.endswith(("st", "sc", "scdone")) or name in ("wcast", "pz"):
                sy.wait_ge(P.sem(name), P.val(name))

    cs.close()
    return nc


def _make_in_map(inputs, stage):
    stn = STAGES[stage]
    m = {
        "x": np.ascontiguousarray(np.asarray(inputs["L0_attn_x"], np.float32).reshape(T, D)),
        "router_w": np.ascontiguousarray(np.asarray(inputs["L0_ffn_router_w"], np.float32)),
    }
    if stn >= 2:
        m["gate_w"] = np.ascontiguousarray(np.asarray(inputs["L0_ffn_gate_w"], np.float32))
        m["up_w"] = np.ascontiguousarray(np.asarray(inputs["L0_ffn_up_w"], np.float32))
        m["down_w"] = np.ascontiguousarray(np.asarray(inputs["L0_ffn_down_w"], np.float32))
    if stn >= 3:
        m["shared_gate_w"] = np.ascontiguousarray(np.asarray(inputs["L0_ffn_shared_gate_w"], np.float32))
        m["shared_up_w"] = np.ascontiguousarray(np.asarray(inputs["L0_ffn_shared_up_w"], np.float32))
        m["shared_down_w"] = np.ascontiguousarray(np.asarray(inputs["L0_ffn_shared_down_w"], np.float32))
    if stn >= 4:
        m["wq_t"] = np.ascontiguousarray(np.asarray(inputs["L0_attn_Wq"], np.float32).T)
        m["wk_t"] = np.ascontiguousarray(np.asarray(inputs["L0_attn_Wk"], np.float32).T)
        m["wv_t"] = np.ascontiguousarray(np.asarray(inputs["L0_attn_Wv"], np.float32).T)
        m["wo_t"] = np.ascontiguousarray(np.asarray(inputs["L0_attn_Wo"], np.float32).T)
        m["bqkv"] = np.ascontiguousarray(np.concatenate([
            np.asarray(inputs["L0_attn_bq"], np.float32),
            np.asarray(inputs["L0_attn_bk"], np.float32),
            np.asarray(inputs["L0_attn_bv"], np.float32)])[None, :])
        m["bo"] = np.ascontiguousarray(np.asarray(inputs["L0_attn_bo"], np.float32)[None, :])
        m["decay"] = np.ascontiguousarray(np.asarray(inputs["L0_attn_decay"], np.float32)[None, :])
    return m


def kernel(**inputs):
    stage = "full"
    nc = build_program(stage=stage)
    res = run_bass_kernel_spmd(nc, [_make_in_map(inputs, stage)], core_ids=[0])
    return res.results[0]["y"].reshape(NB, TB, D)


# version 12
# speedup vs baseline: 1.2569x; 1.0477x over previous
import contextlib
import numpy as np

import concourse.bass as bass
import concourse.mybir as mybir
from concourse.bass import IndirectOffsetOnAxis
from concourse.bass_utils import run_bass_kernel_spmd

F32 = mybir.dt.float32
BF16 = mybir.dt.bfloat16
U32 = mybir.dt.uint32
I32 = mybir.dt.int32
AF = mybir.ActivationFunctionType
ALU = mybir.AluOpType
AX = mybir.AxisListType

T = 32768
D = 1024
E = 64
KTOP = 8
CAP = 512
II = 512
TILE = 128
NT = T // TILE
DS = 64
NB = 2
TB = T // NB
NTB = TB // TILE
SCALE = 2.5
NSLOT = E * CAP
OOB_BIG = 10_000_000

STAGES = {"ln": 0, "route": 1, "expert": 2, "ffn1": 3, "attn": 4, "full": 5}


class Prog:

    def __init__(self, nc, cs):
        self.nc = nc
        self.cs = cs
        self._h = {}
        self.v = {}

    def sem(self, name):
        if name not in self._h:
            self._h[name] = self.cs.enter_context(self.nc.semaphore(name))
            self.v[name] = 0
        return self._h[name]

    def ti(self, instr, name, n=1):
        h = self.sem(name)
        instr.then_inc(h, n, skip_validation=True)
        self.v[name] += n
        return self.v[name]

    def val(self, name):
        self.sem(name)
        return self.v[name]

    def stepper(self, eng, name):
        def step(instr):
            self.ti(instr, name)
            eng.wait_ge(self.sem(name), self.v[name])
            return instr
        return step


def build_program(stage="full", debug=(), out_rows=(0, T), route_pass=3, expert_pass=9, combine_mode=3):
    stn = STAGES[stage]
    nc = bass.Bass("TRN2", target_bir_lowering=False, debug=False)
    cs = contextlib.ExitStack()
    P = Prog(nc, cs)

    def sbuf(name, shape, dt=F32):
        return cs.enter_context(nc.sbuf_tensor(name, shape, dt))

    def psum(name, shape, dt=F32):
        return cs.enter_context(nc.psum_tensor(name, shape, dt))

    _uid = [0]

    def uname(base):
        _uid[0] += 1
        return base + "_u" + str(_uid[0])

    def chn(eng, nm, thunk):
        eng.wait_ge(P.sem(nm), P.val(nm))
        P.ti(thunk(), nm)

    x_in = nc.declare_dram_parameter("x", [T, D], F32, isOutput=False)
    router_in = nc.declare_dram_parameter("router_w", [D, E], F32, isOutput=False)
    if stn >= 2:
        gate_in = nc.declare_dram_parameter("gate_w", [E, D, II], F32, isOutput=False)
        up_in = nc.declare_dram_parameter("up_w", [E, D, II], F32, isOutput=False)
        down_in = nc.declare_dram_parameter("down_w", [E, II, D], F32, isOutput=False)
    if stn >= 3:
        sgate_in = nc.declare_dram_parameter("shared_gate_w", [D, II], F32, isOutput=False)
        sup_in = nc.declare_dram_parameter("shared_up_w", [D, II], F32, isOutput=False)
        sdown_in = nc.declare_dram_parameter("shared_down_w", [II, D], F32, isOutput=False)
    if stn >= 4:
        wq_in = nc.declare_dram_parameter("wq_t", [D, DS], F32, isOutput=False)
        wk_in = nc.declare_dram_parameter("wk_t", [D, DS], F32, isOutput=False)
        wv_in = nc.declare_dram_parameter("wv_t", [D, DS], F32, isOutput=False)
        wo_in = nc.declare_dram_parameter("wo_t", [DS, D], F32, isOutput=False)
        bqkv_in = nc.declare_dram_parameter("bqkv", [1, 3 * DS], F32, isOutput=False)
        bo_in = nc.declare_dram_parameter("bo", [1, D], F32, isOutput=False)
        decay_in = nc.declare_dram_parameter("decay", [1, DS], F32, isOutput=False)

    n_out = out_rows[1] - out_rows[0]
    y_out = nc.declare_dram_parameter("y", [n_out, D], F32, isOutput=True)

    dbg = {}
    for name, shape, dt in (
        ("hln", [T, D], BF16), ("routeS", [NT, TILE, 24], F32),
        ("cnt", [E, NT], F32), ("inv", [NSLOT, 2], F32), ("xd", [NSLOT, D], BF16),
        ("yd", [NSLOT, D], BF16), ("x1", [T, D], F32), ("x2", [T, D], F32),
    ):
        if name in debug:
            dbg[name] = nc.declare_dram_parameter("dbg_" + name, shape, dt, isOutput=True)

    hln_d = nc.dram_tensor("hln_d", [T + TILE, D], BF16)
    inv_d = nc.dram_tensor("inv_d", [NSLOT, 2], F32)
    xd_d = nc.dram_tensor("xd_d", [NSLOT, D], BF16)
    yd_d = nc.dram_tensor("yd_d", [NSLOT + TILE, D], BF16)
    x1_d = nc.dram_tensor("x1_d", [T, D], F32)
    x2_d = nc.dram_tensor("x2_d", [T, D], F32)
    if stn >= 2:
        gbf_d = nc.dram_tensor("gbf_d", [E, D, II], BF16)
        ubf_d = nc.dram_tensor("ubf_d", [E, D, II], BF16)
        dbf_d = nc.dram_tensor("dbf_d", [E, II, D], BF16)
    if stn >= 3:
        sgbf_d = nc.dram_tensor("sgbf_d", [D, II], BF16)
        subf_d = nc.dram_tensor("subf_d", [D, II], BF16)
        sdbf_d = nc.dram_tensor("sdbf_d", [II, D], BF16)
    if stn >= 4:
        r2_d = nc.dram_tensor("r2_d", [T, D], F32)
        misc_d = nc.dram_tensor("misc_d", [DS, 1], F32)
        wqkv_bf_d = nc.dram_tensor("wqkv_bf_d", [D, 3 * DS], BF16)
        wo_bf_d = nc.dram_tensor("wo_bf_d", [DS, D], BF16)

    ones1 = sbuf("ones1", [1, TILE])
    onescol_b = sbuf("onescol_b", [TILE, 1], BF16)
    su_bf = sbuf("su_bf", [TILE, TILE], BF16)
    iota_rev = sbuf("iota_rev", [TILE, E])
    iota_enc = sbuf("iota_enc", [TILE, E])
    iota_p = sbuf("iota_p", [TILE, 1])
    oob_tab = sbuf("oob_tab", [TILE, KTOP])
    dummy_tab = sbuf("dummy_tab", [TILE, 1])
    dummyT_tab = sbuf("dummyT_tab", [TILE, 1])
    fill_neg = sbuf("fill_neg", [TILE, KTOP])
    router_sb = sbuf("router_sb", [TILE, 8 * E], BF16)

    czero = sbuf("czero", [TILE, 1])
    ceps = sbuf("ceps", [TILE, 1])
    conef = sbuf("conef", [TILE, 1])
    chalf = sbuf("chalf", [TILE, 1])
    nc.const_aps.register(F32, 0.0, czero[:, :])
    nc.const_aps.register(F32, 1e-5, ceps[:, :])
    nc.const_aps.register(F32, 1.0, conef[:, :])
    nc.const_aps.register(F32, 0.5, chalf[:, :])

    xg = sbuf("xg", [TILE, 2 * D], BF16)
    xa = sbuf("xa", [TILE, D])
    xb = sbuf("xb", [TILE, D])
    hba = sbuf("hba", [TILE, D], BF16)
    hbb = sbuf("hbb", [TILE, D], BF16)
    stat = sbuf("stat", [TILE, 16])
    sqbuf = sbuf("sqbuf", [TILE, D])

    hlnT = sbuf("hlnT", [TILE, 2 * 4096], BF16)
    scr = sbuf("scr", [TILE, 2 * 256])
    msk = sbuf("msk", [TILE, E])
    gsr = sbuf("gsr", [TILE, 32])
    tk = sbuf("tk", [TILE, 64])
    a_all = sbuf("a_all", [TILE, NT * E], BF16)
    ew_all = sbuf("ew_all", [TILE, NT * 2 * KTOP])
    runrow = sbuf("runrow", [1, E])
    sidx_all = sbuf("sidx_all", [TILE, NT * KTOP], U32)
    pairs_all = sbuf("pairs_all", [TILE, NT * 2 * KTOP])
    cmb_all = sbuf("cmb_all", [TILE, NT * KTOP], U32)
    kflag = sbuf("kflag", [1, NT])
    kflag_i = sbuf("kflag_i", [1, NT], I32)
    rs_all = sbuf("rs_all", [TILE, NT * 24]) if "routeS" in debug else None
    vsum = sbuf("vsum", [TILE, NT])
    valid_i = sbuf("valid_i", [TILE, 2 * KTOP], I32)
    zer512 = sbuf("zer512", [TILE, 512])
    if STAGES[stage] >= 2:
        inv_sb = sbuf("inv_sb", [TILE, 2 * NT])
        idx_all = sbuf("idx_all", [TILE, NT], U32)
        wden = sbuf("wden", [TILE, NT])
        imsk = sbuf("imsk", [TILE, NT], I32)
        wg_sb = sbuf("wg_sb", [TILE, 2 * 4096], BF16)
        wu_sb = sbuf("wu_sb", [TILE, 2 * 4096], BF16)
        wd_sb = sbuf("wd_sb", [TILE, 2 * 4096], BF16)
        sact = sbuf("sact", [TILE, 512])
        act_sb = sbuf("act_sb", [TILE, 4 * 512], BF16)
        y_sb = sbuf("y_sb", [TILE, 2 * D], BF16)
    if STAGES[stage] >= 3:
        x1g = pairs_all
        stg = a_all
        ident_b = sbuf("ident_b", [TILE, TILE], BF16)
        identX = sbuf("identX", [TILE, 2 * TILE], BF16)
        flagcol = sbuf("flagcol", [TILE, NT])
        kflag01 = sbuf("kflag01", [1, NT])

    if stn >= 4:
        woT_sb = wg_sb[0:DS, 1024:2048]
        mask3 = wg_sb[:, 0:3 * TILE]
        obf = wg_sb[0:DS, 512:768]
        qT_bf = wu_sb[0:DS, 0:3 * TILE]
        kT_bf = wu_sb[0:DS, 512:512 + 3 * TILE]
        kvt = wu_sb[:, 1024:1024 + 3 * TILE]
        v4 = wd_sb[:, 0:3 * 4 * DS]
        mav = wd_sb[:, 1024:1024 + 6 * TILE]
        dcolx = sact[:, 0:4 * DS]
        lnd_rep = sact[:, 256:256 + DS]
        obuf = sact[0:DS, 320:448]
        dtile_rep = sact[0:DS, 448:512]
        drowt = xb[0:DS, 0:4 * TILE]
        exprow = xb[0:DS, 512:1024]
        bqkv_sb = sbuf("bqkv_sb", [1, 3 * DS])
        bo_sb = sbuf("bo_sb", [1, D])
        drow_sb = sbuf("drow_sb", [1, 4 * DS])
        lnd_col = sbuf("lnd_col", [DS, 1])
        st_f = sbuf("st_f", [DS, DS])
        st_b = sbuf("st_b", [DS, DS], BF16)
        expcol = sbuf("expcol", [TILE, 8])

    ps = [psum(f"ps{i}", [TILE, 512]) for i in range(8)]

    block = cs.enter_context(nc.Block())

    @block.vector
    def _(v):
        P.ti(v.memset(ones1[:, :], 1.0), "const")
        P.ti(v.memset(onescol_b[:, :], 1.0), "const")
        P.ti(v.memset(su_bf[:, :], 0.0), "const")
        P.ti(v.memset(fill_neg[:, :], -1e30), "const")
        P.ti(v.memset(zer512[:, :], 0.0), "const")
        if STAGES[stage] >= 3:
            P.ti(v.memset(ident_b[:, :], 0.0), "const")
        P.ti(v.memset(czero[:, :], 0.0), "const")
        P.ti(v.memset(ceps[:, :], 1e-5), "const")
        P.ti(v.memset(conef[:, :], 1.0), "const")
        P.ti(v.memset(chalf[:, :], 0.5), "const")

    @block.gpsimd
    def _(gp):
        gp.wait_ge(P.sem("const"), 9 + (1 if STAGES[stage] >= 3 else 0))
        P.ti(gp.affine_select(out=su_bf[:, :], in_=su_bf[:, :], pattern=[[-1, TILE]],
                              compare_op=ALU.is_ge, fill=1.0, base=0,
                              channel_multiplier=1), "const")
        if STAGES[stage] >= 3:
            gp.wait_ge(P.sem("const"), 10)
            P.ti(gp.affine_select(out=ident_b[:, :], in_=ident_b[:, :], pattern=[[-1, TILE]],
                                  compare_op=ALU.not_equal, fill=1.0, base=0,
                                  channel_multiplier=1), "const")
        P.ti(gp.iota(iota_rev[:, :], pattern=[[-1, E]], base=E, channel_multiplier=0,
                     allow_small_or_imprecise_dtypes=True), "const")
        P.ti(gp.iota(iota_enc[:, :], pattern=[[-32768, E]], base=E * 32768, channel_multiplier=0,
                     allow_small_or_imprecise_dtypes=True), "const")
        P.ti(gp.iota(oob_tab[:, :], pattern=[[1, KTOP]], base=OOB_BIG, channel_multiplier=KTOP,
                     allow_small_or_imprecise_dtypes=True), "const")
        P.ti(gp.iota(iota_p[:, :], pattern=[[0, 1]], base=0, channel_multiplier=1,
                     allow_small_or_imprecise_dtypes=True), "const")
        P.ti(gp.iota(dummy_tab[:, :], pattern=[[0, 1]], base=NSLOT, channel_multiplier=1,
                     allow_small_or_imprecise_dtypes=True), "const")
        P.ti(gp.iota(dummyT_tab[:, :], pattern=[[0, 1]], base=T, channel_multiplier=1,
                     allow_small_or_imprecise_dtypes=True), "const")
        P.ti(gp.dma_start(out=hln_d[T:T + TILE, 0:512], in_=zer512[:, :]), "pz", 16)
        P.ti(gp.dma_start(out=hln_d[T:T + TILE, 512:1024], in_=zer512[:, :]), "pz", 16)
        P.ti(gp.dma_start(out=inv_d.ap().rearrange("(a b) c -> a (b c)", a=TILE),
                          in_=zer512[:, :]), "pz", 16)
        if stn >= 2:
            P.ti(gp.dma_start(out=yd_d[NSLOT:NSLOT + TILE, 0:512], in_=zer512[:, :]), "pz", 16)
            P.ti(gp.dma_start(out=yd_d[NSLOT:NSLOT + TILE, 512:1024], in_=zer512[:, :]), "pz", 16)

    if stn >= 2:
        @block.gpsimd
        def _(gp):
            for e in range(E):
                P.ti(gp.dma_start(out=gbf_d[e, :, :], in_=gate_in[e, :, :]), "wcast", 16)
                P.ti(gp.dma_start(out=ubf_d[e, :, :], in_=up_in[e, :, :]), "wcast", 16)
                P.ti(gp.dma_start(out=dbf_d[e, :, :], in_=down_in[e, :, :]), "wcast", 16)
            if stn >= 3:
                P.ti(gp.dma_start(out=sgbf_d[:, :], in_=sgate_in[:, :]), "wcast", 16)
                P.ti(gp.dma_start(out=subf_d[:, :], in_=sup_in[:, :]), "wcast", 16)
                P.ti(gp.dma_start(out=sdbf_d[:, :], in_=sdown_in[:, :]), "wcast", 16)
            if stn >= 4:
                P.ti(gp.dma_start(out=wqkv_bf_d[:, 0:DS], in_=wq_in[:, :]), "wcast", 16)
                P.ti(gp.dma_start(out=wqkv_bf_d[:, DS:2 * DS], in_=wk_in[:, :]), "wcast", 16)
                P.ti(gp.dma_start(out=wqkv_bf_d[:, 2 * DS:3 * DS], in_=wv_in[:, :]), "wcast", 16)
                P.ti(gp.dma_start(out=wo_bf_d[:, :], in_=wo_in[:, :]), "wcast", 16)

    @block.sync
    def _(sy):
        for c in range(8):
            P.ti(sy.dma_start(out=scr[:, c * E:(c + 1) * E],
                              in_=router_in[c * TILE:(c + 1) * TILE, :]), "ld0", 16)

    @block.vector
    def _(v):
        v.wait_ge(P.sem("ld0"), P.val("ld0"))
        P.ti(v.tensor_copy(out=router_sb[:, :], in_=scr[:, 0:8 * E]), "rcp")

    CONST_TOTAL = P.val("const")

    def emit_ln(src_dram, dst_dram, tag, r2_dram=None):
        ld, cp, dv, st = tag + "ld", tag + "cp", tag + "dv", tag + "st"
        ld0, cp0, dv0, st0 = P.val(ld), P.val(cp), P.val(dv), P.val(st)
        r20 = P.val(tag + "r2")

        @block.sync
        def _(sy):
            for t in range(NT):
                buf = xa if t % 2 == 0 else xb
                if t >= 2:
                    sy.wait_ge(P.sem(cp), cp0 + t - 1)
                    if r2_dram is not None:
                        sy.wait_ge(P.sem(tag + "r2"), r20 + t - 1)
                P.ti(sy.dma_start(out=buf[:, :], in_=src_dram[t * TILE:(t + 1) * TILE, :]), ld, 16)

        @block.vector
        def _(v):
            step = P.stepper(v, tag + "vch")
            v.wait_ge(P.sem("const"), CONST_TOTAL)
            for t in range(NT):
                buf = xa if t % 2 == 0 else xb
                v.wait_ge(P.sem(ld), ld0 + (t + 1) * 16)
                if t >= 1:
                    v.wait_ge(P.sem(cp), cp0 + t)
                step(v.tensor_reduce(out=stat[:, 0:1], in_=buf[:, :], axis=AX.X, op=ALU.add))
                P.ti(v.tensor_scalar(out=stat[:, 3:4], in0=stat[:, 0:1], scalar1=-1.0 / D,
                                     scalar2=0.0, op0=ALU.mult), dv)
                step(v.tensor_scalar(out=stat[:, 1:2], in0=stat[:, 0:1],
                                     scalar1=stat[:, 0:1], scalar2=-1.0 / (float(D) * D),
                                     op0=ALU.mult, op1=ALU.mult))
                P.ti(v.tensor_single_scalar(out=stat[:, 4:5], in_=stat[:, 1:2],
                                            scalar=1e-5, op=ALU.add), dv)
                if r2_dram is not None:
                    hbuf = hba if t % 2 == 0 else hbb
                    v.wait_ge(P.sem(cp), cp0 + t + 1)
                    if t >= 2:
                        v.wait_ge(P.sem(tag + "r2s"), (t - 1) * 16)
                    step(v.tensor_tensor(out=ew_all[:, (t % 2) * D:(t % 2 + 1) * D],
                                         in0=buf[:, :], in1=hbuf[:, :], op=ALU.add))
                    P.ti(v.engine_nop(), tag + "r2")

        @block.scalar
        def _(sc):
            sc.wait_ge(P.sem("const"), CONST_TOTAL)
            step = P.stepper(sc, tag + "ch")
            for t in range(NT):
                buf = xa if t % 2 == 0 else xb
                hbuf = hba if t % 2 == 0 else hbb
                sc.wait_ge(P.sem(ld), ld0 + (t + 1) * 16)
                step(sc.activation(out=xg[:, 0:D], in_=buf[:, :], func=AF.Square,
                                   accum_out=stat[:, 2:3]))
                sc.wait_ge(P.sem(dv), dv0 + 2 * (t + 1))
                step(sc.activation(out=stat[:, 5:6], in_=stat[:, 2:3], func=AF.Ln,
                                   scale=1.0 / D, bias=stat[:, 4:5]))
                step(sc.activation(out=stat[:, 5:6], in_=stat[:, 5:6], func=AF.Exp, scale=-0.5))
                step(sc.mul(out=stat[:, 6:7], in_=stat[:, 3:4], mul=stat[:, 5:6]))
                if t >= 2:
                    sc.wait_ge(P.sem(st), st0 + (t - 1) * 16)
                if r2_dram is not None and t >= 2:
                    sc.wait_ge(P.sem(tag + "r2"), r20 + t - 1)
                P.ti(sc.activation(out=hbuf[:, :], in_=buf[:, :], func=AF.Identity,
                                   bias=stat[:, 6:7], scale=stat[:, 5:6]), cp)
                sc.wait_ge(P.sem(cp), cp0 + t + 1)
                P.ti(sc.dma_start(out=dst_dram[t * TILE:(t + 1) * TILE, :], in_=hbuf[:, :]), st, 16)
                if r2_dram is not None:
                    sc.wait_ge(P.sem(tag + "r2"), r20 + t + 1)
                    P.ti(sc.dma_start(out=r2_dram[t * TILE:(t + 1) * TILE, :],
                                      in_=ew_all[:, (t % 2) * D:(t % 2 + 1) * D]), tag + "r2s", 16)

        return st

    def emit_routing(ln_st, tag, route_pass=3):
        rt = tag
        NGRP = NT // 4
        ln_st0 = P.val(ln_st) - NT * 16
        tp0 = P.val(rt + "tp")
        mm0 = P.val(rt + "mm")
        sg0 = P.val(rt + "sg")
        tk0 = P.val(rt + "tk")
        pm0 = P.val(rt + "pm")
        p20E, p20O = P.val(rt + "p2E"), P.val(rt + "p2O")
        vc0 = P.val(rt + "vc")

        @block.scalar
        def _(sc):
            for g in range(NGRP + 1):
                if g < NGRP:
                    hbuf = hlnT[:, (g % 2) * 4096:(g % 2 + 1) * 4096]
                    sc.wait_ge(P.sem(ln_st), ln_st0 + (g * 4 + 4) * 16)
                    if g >= 2:
                        sc.wait_ge(P.sem(rt + "mm"), mm0 + (g - 1) * 32)
                    for c in range(8):
                        P.ti(sc.dma_start(out=hbuf[:, c * 512:(c + 1) * 512],
                                          in_=hln_d[g * 512:(g + 1) * 512, c * TILE:(c + 1) * TILE],
                                          transpose=True), rt + "tp", 16)
                if g >= 1:
                    gg = g - 1
                    sb = scr[:, (gg % 2) * 256:(gg % 2 + 1) * 256]
                    if gg >= 2:
                        sc.wait_ge(P.sem(rt + "tk"), tk0 + (gg - 1) * 4)
                    for tt in range(4):
                        pb = ps[tt]
                        sc.wait_ge(P.sem(rt + "mm"), mm0 + gg * 32 + (tt + 1) * 8)
                        P.ti(sc.activation(out=sb[:, tt * 64:(tt + 1) * 64], in_=pb[:, 0:E],
                                           func=AF.Tanh, scale=0.5), rt + "sg")

        @block.tensor
        def _(te):
            te.wait_ge(P.sem("rcp"), 1)
            for g in range(NGRP):
                hbuf = hlnT[:, (g % 2) * 4096:(g % 2 + 1) * 4096]
                te.wait_ge(P.sem(rt + "tp"), tp0 + (g + 1) * 128)
                if g >= 1:
                    te.wait_ge(P.sem(rt + "sg"), sg0 + g * 4)
                for tt in range(4):
                    pb = ps[tt]
                    for c in range(8):
                        mm = te.matmul(pb[:, 0:E],
                                       hbuf[:, c * 512 + tt * TILE:c * 512 + (tt + 1) * TILE],
                                       router_sb[:, c * E:(c + 1) * E],
                                       start=(c == 0), stop=(c == 7))
                    P.ti(mm, rt + "mm", 8)


        @block.vector
        def _(v):
            step = P.stepper(v, rt + "vch")
            step(v.memset(sqbuf[:, :], -1e30))
            for g in range(NGRP):
                sb = scr[:, (g % 2) * 256:(g % 2 + 1) * 256]
                v.wait_ge(P.sem(rt + "sg"), sg0 + g * 4 + 4)
                step(v.tensor_scalar(out=sb, in0=sb, scalar1=0.5, scalar2=0.5,
                                     op0=ALU.mult, op1=ALU.add))
                step(v.tensor_reduce(out=gsr[:, :], in_=sb.rearrange("p (a b) -> p a b", b=8),
                                     axis=AX.X, op=ALU.max))

                def mk_ops(tt):
                    t = g * 4 + tt
                    b = tt * 256
                    gs = gsr[:, tt * 8:(tt + 1) * 8]
                    A0 = sqbuf[:, b + 0:b + 8]
                    fill = sqbuf[:, b + 8:b + 16]
                    t8 = sqbuf[:, b + 16:b + 24]
                    s24 = sqbuf[:, b + 24:b + 32]
                    s32 = sqbuf[:, b + 32:b + 33]
                    s33 = sqbuf[:, b + 33:b + 34]
                    s34 = sqbuf[:, b + 34:b + 35]
                    mk = sqbuf[:, b + 64:b + 128]
                    cpy = sqbuf[:, b + 128:b + 192]
                    ew = ew_all[:, t * 16:t * 16 + 16]
                    return [
                        lambda: v.max(out=A0, in_=gs),
                        lambda: v.tensor_copy(out=fill[:, 0:4], in_=A0[:, 0:4]),
                        lambda: v.match_replace(out=gs, in_to_replace=fill, in_values=gs,
                                                imm_value=-1e30),
                        lambda: v.tensor_single_scalar(out=gs, in_=gs, scalar=-1e29, op=ALU.is_lt),
                        lambda: v.tensor_tensor(out=mk, in0=sb[:, tt * 64:(tt + 1) * 64],
                                                in1=gs.rearrange("p (a b) -> p a b", b=1)
                                                .broadcast_to((TILE, 8, 8)), op=ALU.mult),
                        lambda: v.max(out=t8, in_=mk),
                        lambda: v.tensor_copy(out=cpy, in_=mk),
                        lambda: v.match_replace(out=cpy, in_to_replace=t8, in_values=cpy,
                                                imm_value=-1e30),
                        lambda: v.tensor_single_scalar(out=cpy, in_=cpy, scalar=-1e29, op=ALU.is_lt),
                        lambda: v.tensor_copy(out=a_all[:, t * E:(t + 1) * E], in_=cpy),
                        lambda: v.scalar_tensor_tensor(out=mk, in0=mk, scalar=0.5,
                                                       in1=iota_rev[:, :], op0=ALU.mult, op1=ALU.add),
                        lambda: v.tensor_tensor(out=mk, in0=mk, in1=cpy, op=ALU.mult),
                        lambda: v.max(out=t8, in_=mk),
                        lambda: v.max_index(out=sidx_all[:, t * 8:(t + 1) * 8], in_max=t8,
                                            in_values=mk),
                        lambda: v.tensor_copy(out=ew[:, 0:8], in_=sidx_all[:, t * 8:(t + 1) * 8]),
                        lambda: v.tensor_tensor(out=s24, in0=t8, in1=ew[:, 0:8], op=ALU.add),
                        lambda: v.tensor_single_scalar(out=s24, in_=s24, scalar=-64.0, op=ALU.add),
                        lambda: v.tensor_reduce(out=s32, in_=s24, axis=AX.X, op=ALU.add),
                        lambda: v.tensor_scalar_max(out=s33, in0=s32, scalar1=5e-10),
                        lambda: v.reciprocal(out=s34, in_=s33),
                        lambda: v.tensor_scalar(out=ew[:, 8:16], in0=s24, scalar1=s34,
                                                scalar2=SCALE, op0=ALU.mult, op1=ALU.mult),
                    ]

                chains = [mk_ops(tt) for tt in range(4)]
                for k in range(len(chains[0])):
                    for tt in range(4):
                        chn(v, rt + "pc" + str(tt), chains[tt][k])
                for tt in range(4):
                    nm = rt + "pc" + str(tt)
                    v.wait_ge(P.sem(nm), P.val(nm))
                    P.ti(v.engine_nop(), rt + "tk")

        def emit_dump():
            if "routeS" not in dbg:
                return
            @block.scalar
            def _(sc):
                sc.wait_ge(P.sem(rt + "tk"), tk0 + NT)
                if route_pass >= 2:
                    sc.wait_ge(P.sem(rt + "p2E"), p20E + NT // 2)
                    sc.wait_ge(P.sem(rt + "p2O"), p20O + NT // 2)
                for t in range(NT):
                    ew = ew_all[:, t * 16:t * 16 + 16]
                    P.ti(sc.dma_start(out=dbg["routeS"][t, :, 0:8], in_=ew[:, 0:8]), rt + "dbst", 16)
                    P.ti(sc.dma_start(out=dbg["routeS"][t, :, 8:16], in_=ew[:, 8:16]), rt + "dbst", 16)
                    if route_pass >= 2:
                        P.ti(sc.dma_start(out=dbg["routeS"][t, :, 16:24],
                                          in_=rs_all[:, t * 24:t * 24 + 8]), rt + "dbst", 16)

        if route_pass < 2:
            emit_dump()
            return rt
        @block.vector
        def _(v):
            v.wait_ge(P.sem("const"), CONST_TOTAL)
            P.ti(v.memset(runrow[:, :], 0.0), rt + "rr")

        rr0 = P.val(rt + "rr")

        @block.tensor
        def _(te):
            te.wait_ge(P.sem(rt + "rr"), rr0)
            for t in range(NT):
                pb = ps[4 + (t % 2)]
                te.wait_ge(P.sem(rt + "tk"), tk0 + t + 1)
                if t >= 2:
                    te.wait_ge(P.sem(rt + "p2" + ("E" if t % 2 == 0 else "O")),
                               (p20E if t % 2 == 0 else p20O) + (t - 2) // 2 + 1)
                if t >= 1:
                    te.wait_ge(P.sem(rt + "rr"), rr0 + t)
                te.matmul(pb[:, 0:E], su_bf[:, :], a_all[:, t * E:(t + 1) * E],
                          start=True, stop=False)
                te.matmul(pb[:, 0:E], ones1[:, :], runrow[:, :], start=False, stop=True)
                mm3 = te.matmul(ps[6 + (t % 2)][0:1, (t % 4) * E:(t % 4) * E + E],
                                onescol_b[:, :], a_all[:, t * E:(t + 1) * E],
                                start=True, stop=True)
                P.ti(mm3, rt + "pm")

        @block.vector
        def _(v):
            step = P.stepper(v, rt + "vch")

            def mk2(t):
                pb = ps[4 + (t % 2)]
                b = (t % 2) * 256
                enc = sqbuf[:, b + 64:b + 128]
                v2 = sqbuf[:, b + 0:b + 8]
                pos8 = sqbuf[:, b + 8:b + 16]
                vld = sqbuf[:, b + 16:b + 24]
                slot = sqbuf[:, b + 24:b + 32]
                oob = sqbuf[:, b + 32:b + 40]
                sel = sqbuf[:, b + 40:b + 48]
                h1 = sqbuf[:, b + 48:b + 56]
                t56 = sqbuf[:, b + 56:b + 64]
                ic = sqbuf[:, b + 128:b + 129]
                vi = valid_i[:, (t % 2) * KTOP:(t % 2) * KTOP + KTOP]
                ew = ew_all[:, t * 16:t * 16 + 16]
                pr = pairs_all[:, t * 16:(t + 1) * 16].rearrange("p (a b) -> p a b", b=2)
                ops = [
                    lambda: v.tensor_tensor(out=enc, in0=pb[:, 0:E], in1=iota_enc[:, :],
                                            op=ALU.add),
                    lambda: v.tensor_tensor(out=enc, in0=enc,
                                            in1=a_all[:, t * E:(t + 1) * E], op=ALU.mult),
                    lambda: v.max(out=v2, in_=enc),
                    lambda: v.tensor_single_scalar(out=pos8, in_=ew[:, 0:8], scalar=-64.0,
                                                   op=ALU.add),
                    lambda: v.tensor_scalar(out=pos8, in0=pos8, scalar1=32768.0,
                                            scalar2=0.0, op0=ALU.mult),
                    lambda: v.tensor_tensor(out=pos8, in0=pos8, in1=v2, op=ALU.add),
                    lambda: v.tensor_single_scalar(out=vld, in_=pos8, scalar=float(CAP),
                                                   op=ALU.is_lt),
                    lambda: v.tensor_copy(out=vi, in_=vld),
                    lambda: v.tensor_scalar(out=slot, in0=ew[:, 0:8], scalar1=512.0,
                                            scalar2=0.0, op0=ALU.mult),
                    lambda: v.tensor_tensor(out=slot, in0=slot, in1=pos8, op=ALU.add),
                    lambda: v.tensor_single_scalar(out=h1, in_=pos8, scalar=128.0, op=ALU.is_ge),
                    lambda: v.tensor_single_scalar(out=t56, in_=pos8, scalar=256.0, op=ALU.is_ge),
                    lambda: v.tensor_tensor(out=h1, in0=h1, in1=t56, op=ALU.add),
                    lambda: v.tensor_single_scalar(out=t56, in_=pos8, scalar=384.0, op=ALU.is_ge),
                    lambda: v.tensor_tensor(out=h1, in0=h1, in1=t56, op=ALU.add),
                    lambda: v.tensor_scalar(out=t56, in0=h1, scalar1=-128.0,
                                            scalar2=0.0, op0=ALU.mult),
                    lambda: v.tensor_tensor(out=t56, in0=t56, in1=pos8, op=ALU.add),
                    lambda: v.tensor_scalar(out=t56, in0=t56, scalar1=256.0,
                                            scalar2=0.0, op0=ALU.mult),
                    lambda: v.scalar_tensor_tensor(out=h1, in0=ew[:, 0:8], scalar=4.0,
                                                   in1=h1, op0=ALU.mult, op1=ALU.add),
                    lambda: v.tensor_tensor(out=h1, in0=h1, in1=t56, op=ALU.add),
                    lambda: v.tensor_single_scalar(out=oob, in_=oob_tab[:, :],
                                                   scalar=float(t * 1024), op=ALU.add),
                    lambda: v.select(out=oob, mask=vi, on_true=h1, on_false=oob),
                    lambda: v.tensor_copy(out=sidx_all[:, t * 8:(t + 1) * 8], in_=oob),
                    lambda: v.select(out=sel, mask=vi, on_true=slot,
                                     on_false=dummy_tab[:, 0:1].broadcast_to((TILE, 8))),
                    lambda: v.tensor_copy(out=cmb_all[:, t * 8:(t + 1) * 8], in_=sel),
                    lambda: v.tensor_single_scalar(out=ic, in_=iota_p[:, :],
                                                   scalar=float(t * TILE + 1), op=ALU.add),
                    lambda: v.tensor_copy(out=pr[:, :, 0:1],
                                          in_=ic.rearrange("p (a b) -> p a b", b=1)
                                          .broadcast_to((TILE, 8, 1))),
                    lambda: v.tensor_copy(out=pr[:, :, 1:2],
                                          in_=ew[:, 8:16].rearrange("p (a b) -> p a b", b=1)),
                    lambda: v.tensor_reduce(out=vsum[:, t:t + 1], in_=vld, axis=AX.X,
                                            op=ALU.add),
                ]
                if "routeS" in dbg:
                    ops += [
                        lambda: v.tensor_copy(out=rs_all[:, t * 24 + 0:t * 24 + 8], in_=pos8),
                        lambda: v.tensor_copy(out=rs_all[:, t * 24 + 8:t * 24 + 16], in_=vld),
                        lambda: v.tensor_copy(out=rs_all[:, t * 24 + 16:t * 24 + 24], in_=slot),
                    ]
                return ops

            for j in range(NT // 2):
                pair = [2 * j, 2 * j + 1]
                for t in pair:
                    v.wait_ge(P.sem(rt + "pm"), pm0 + t + 1)
                    step(v.tensor_tensor(out=runrow[:, :], in0=runrow[:, :],
                                         in1=ps[6 + (t % 2)][0:1, (t % 4) * E:(t % 4) * E + E],
                                         op=ALU.add))
                    P.ti(v.engine_nop(), rt + "rr")
                chains = [mk2(t) for t in pair]
                for k in range(len(chains[0])):
                    for ci in range(2):
                        chn(v, rt + "q" + str(ci), chains[ci][k])
                for ci, t in enumerate(pair):
                    nm = rt + "q" + str(ci)
                    v.wait_ge(P.sem(nm), P.val(nm))
                    P.ti(v.engine_nop(), rt + "p2" + ("E" if t % 2 == 0 else "O"))

        @block.tensor
        def _(te):
            pb = ps[2]
            te.wait_ge(P.sem(rt + "sg"), sg0 + NT)
            for t in range(NT):
                te.wait_ge(P.sem(rt + "p2" + ("E" if t % 2 == 0 else "O")),
                           (p20E if t % 2 == 0 else p20O) + t // 2 + 1)
                mmv = te.matmul(pb[0:1, t:t + 1], conef[:, :], vsum[:, t:t + 1],
                                start=True, stop=True)
            P.ti(mmv, rt + "vc")

        @block.vector
        def _(v):
            step = P.stepper(v, rt + "vch")
            v.wait_ge(P.sem(rt + "vc"), P.val(rt + "vc"))
            pb = ps[2]
            step(v.tensor_copy(out=kflag[:, :], in_=pb[0:1, 0:NT]))
            P.ti(v.tensor_copy(out=kflag_i[:, :], in_=kflag[:, :]), rt + "kf")


        if route_pass < 3:
            emit_dump()
            return rt
        @block.gpsimd
        def _(gp):
            gp.wait_ge(P.sem("pz"), P.val("pz"))
            gp.wait_ge(P.sem(rt + "kf"), P.val(rt + "kf"))
            with gp.register(uname(rt + "bnd")) as rb, gp.register(uname(rt + "rf")) as rf:
                gp.reg_mov(rb, NSLOT - 1)
                for t in range(NT):
                    gp.reg_load(rf, kflag_i[0:1, t:t + 1])
                    with gp.If_eq(rf, 0):
                        gp.engine_nop().then_inc(P.sem(rt + "sc"), 16 * KTOP, skip_validation=True)
                    with gp.Else():
                        for k in range(KTOP):
                            gp.indirect_dma_start(
                                out=inv_d.ap(),
                                out_offset=IndirectOffsetOnAxis(sidx_all[:, t * 8 + k:t * 8 + k + 1], 0),
                                in_=pairs_all[:, t * 16 + 2 * k:t * 16 + 2 * k + 2],
                                in_offset=None,
                                bounds_check=rb, oob_is_err=False,
                            ).then_inc(P.sem(rt + "sc"), 16, skip_validation=True)
                    P.v[rt + "sc"] = P.val(rt + "sc") + 16 * KTOP
            P.ti(gp.engine_nop(), rt + "scdone")

        emit_dump()
        return rt

    def emit_expert(rt, hsrc_dram, expert_pass=9):
        ex = rt + "x"
        scd = P.val(rt + "scdone")
        xg0, xs0, xt0, wl0 = P.val(ex + "g"), P.val(ex + "s"), P.val(ex + "t"), P.val(ex + "w")
        ma0, si0, av0, mb0 = P.val(ex + "ma"), P.val(ex + "si"), P.val(ex + "av"), P.val(ex + "mb")
        ye0, ys0 = P.val(ex + "ye"), P.val(ex + "ys")

        @block.sync
        def _(sy):
            sy.wait_ge(P.sem(rt + "scdone"), scd)
            P.ti(sy.dma_start(out=inv_sb[:, :],
                              in_=inv_d.ap().rearrange("(p tt) c -> p (tt c)", p=TILE)), ex + "il", 16)

        @block.vector
        def _(v):
            step = P.stepper(v, ex + "vch")
            v.wait_ge(P.sem(ex + "il"), P.val(ex + "il"))
            iv = inv_sb[:, :].rearrange("p (tt c) -> p tt c", c=2)
            step(v.tensor_copy(out=wden[:, :], in_=iv[:, :, 1:2].rearrange("p a b -> p (a b)")))
            step(v.tensor_single_scalar(out=sqbuf[:, 0:NT], in_=iv[:, :, 0:1].rearrange("p a b -> p (a b)"),
                                        scalar=0.0, op=ALU.is_gt))
            step(v.tensor_copy(out=imsk[:, :], in_=sqbuf[:, 0:NT]))
            step(v.tensor_single_scalar(out=sqbuf[:, NT:2 * NT],
                                        in_=iv[:, :, 0:1].rearrange("p a b -> p (a b)"),
                                        scalar=-1.0, op=ALU.add))
            step(v.select(out=sqbuf[:, 2 * NT:3 * NT], mask=imsk[:, :],
                          on_true=sqbuf[:, NT:2 * NT],
                          on_false=dummyT_tab[:, 0:1].broadcast_to((TILE, NT))))
            P.ti(v.tensor_copy(out=idx_all[:, :], in_=sqbuf[:, 2 * NT:3 * NT]), ex + "ip")

        ip_done = P.val(ex + "ip")

        if expert_pass < 2:
            return ex
        @block.gpsimd
        def _(gp):
            gp.wait_ge(P.sem(ex + "ip"), ip_done)
            for e in range(E):
                for g in range(4):
                    st4 = e * 4 + g
                    if st4 >= 2:
                        gp.wait_ge(P.sem(ex + "s"), xs0 + (st4 - 1) * 16)
                    P.ti(gp.indirect_dma_start(
                        out=xg[:, (st4 % 2) * D:(st4 % 2) * D + D], out_offset=None,
                        in_=hsrc_dram.ap(),
                        in_offset=IndirectOffsetOnAxis(idx_all[:, st4:st4 + 1], 0)), ex + "g", 16)

        @block.sync
        def _(sy):
            for e in range(E):
                if e >= 2 and expert_pass >= 4:
                    sy.wait_ge(P.sem(ex + "ma"), ma0 + (e - 1) * 8)
                    if expert_pass >= 5:
                        sy.wait_ge(P.sem(ex + "mb"), mb0 + (e - 1) * 8)
                par = (e % 2) * 4096
                P.ti(sy.dma_start(out=wg_sb[:, par:par + 4096].rearrange("p (c i) -> p c i", i=II),
                                  in_=gbf_d[e].rearrange("(c p) i -> p c i", p=TILE)), ex + "w", 16)
                P.ti(sy.dma_start(out=wu_sb[:, par:par + 4096].rearrange("p (c i) -> p c i", i=II),
                                  in_=ubf_d[e].rearrange("(c p) i -> p c i", p=TILE)), ex + "w", 16)
                P.ti(sy.dma_start(out=wd_sb[:, par:par + 4096].rearrange("p (c i) -> p c i", i=D),
                                  in_=dbf_d[e].rearrange("(c p) i -> p c i", p=TILE)), ex + "w", 16)
                for g in range(4):
                    st4 = e * 4 + g
                    sy.wait_ge(P.sem(ex + "g"), xg0 + (st4 + 1) * 16)
                    P.ti(sy.dma_start(out=xd_d[st4 * TILE:(st4 + 1) * TILE, :],
                                      in_=xg[:, (st4 % 2) * D:(st4 % 2) * D + D]), ex + "s", 16)
                sy.wait_ge(P.sem(ex + "s"), xs0 + (e + 1) * 64)
                for c in range(8):
                    P.ti(sy.dma_start(out=hlnT[:, par + c * 512:par + (c + 1) * 512],
                                      in_=xd_d[e * 512:(e + 1) * 512, c * TILE:(c + 1) * TILE],
                                      transpose=True), ex + "t", 16)

        @block.scalar
        def _(sc):
            for e in range(E):
                for it in range(4 if expert_pass >= 4 else 0):
                    sc.wait_ge(P.sem(ex + "ma"), ma0 + e * 8 + 2 * it + 1)
                    if it > 0:
                        sc.wait_ge(P.sem(ex + "av"), av0 + e * 4 + it)
                    P.ti(sc.activation(out=sact[:, :], in_=ps[it][:, :], func=AF.Silu), ex + "si")
                for st_ in range(4 if expert_pass >= 5 else 0):
                    for half in range(2):
                        sc.wait_ge(P.sem(ex + "mb"), mb0 + e * 8 + st_ * 2 + half + 1)
                        if half == 0:
                            prev_store = e * 4 + st_ - 2
                            if prev_store >= 0:
                                sc.wait_ge(P.sem(ex + "ys"), ys0 + (prev_store + 1) * 16)
                        P.ti(sc.activation(out=y_sb[:, (st_ % 2) * D + half * 512:
                                                      (st_ % 2) * D + half * 512 + 512],
                                           in_=ps[st_ + 4 * half][:, :], func=AF.Identity,
                                           scale=wden[:, e * 4 + st_:e * 4 + st_ + 1]), ex + "ye")
                    P.ti(sc.dma_start(out=yd_d[(e * 4 + st_) * TILE:(e * 4 + st_ + 1) * TILE, :],
                                      in_=y_sb[:, (st_ % 2) * D:(st_ % 2) * D + D]), ex + "ys", 16)

        @block.vector
        def _(v):
            step = P.stepper(v, ex + "vch")
            for e in range(E if expert_pass >= 4 else 0):
                for it in range(4):
                    v.wait_ge(P.sem(ex + "si"), si0 + e * 4 + it + 1)
                    v.wait_ge(P.sem(ex + "ma"), ma0 + e * 8 + 2 * it + 2)
                    step(v.tensor_tensor(out=act_sb[:, it * 512:(it + 1) * 512],
                                         in0=sact[:, :], in1=ps[4 + it][:, :], op=ALU.mult))
                    P.ti(v.engine_nop(), ex + "av")

        @block.tensor
        def _(te):
            for e in range(E if expert_pass >= 4 else 0):
                par = (e % 2) * 4096
                te.wait_ge(P.sem(ex + "t"), xt0 + (e + 1) * 128)
                te.wait_ge(P.sem(ex + "w"), wl0 + (e + 1) * 48)
                if e >= 1:
                    if expert_pass >= 5:
                        te.wait_ge(P.sem(ex + "ye"), ye0 + e * 8)
                    te.wait_ge(P.sem(ex + "av"), av0 + e * 4)
                for it in range(4):
                    for c in range(8):
                        mmg = te.matmul(ps[it][:, :],
                                        wg_sb[:, par + c * 512 + it * TILE:par + c * 512 + it * TILE + TILE],
                                        hlnT[:, par + c * 512:par + (c + 1) * 512],
                                        start=(c == 0), stop=(c == 7))
                    P.ti(mmg, ex + "ma")
                    for c in range(8):
                        mmu = te.matmul(ps[4 + it][:, :],
                                        wu_sb[:, par + c * 512 + it * TILE:par + c * 512 + it * TILE + TILE],
                                        hlnT[:, par + c * 512:par + (c + 1) * 512],
                                        start=(c == 0), stop=(c == 7))
                    P.ti(mmu, ex + "ma")
                if expert_pass < 5:
                    continue
                te.wait_ge(P.sem(ex + "av"), av0 + e * 4 + 4)
                for st_ in range(4):
                    te.wait_ge(P.sem(ex + "si"), si0 + e * 4 + 4)
                    for half in range(2):
                        for ic in range(4):
                            mmd = te.matmul(ps[st_ + 4 * half][:, :],
                                            act_sb[:, ic * 512 + st_ * TILE:ic * 512 + st_ * TILE + TILE],
                                            wd_sb[:, par + ic * 1024 + half * 512:par + ic * 1024 + half * 512 + 512],
                                            start=(ic == 0), stop=(ic == 3))
                        P.ti(mmd, ex + "mb")
        return ex

    def emit_shared(rt, ex, xsrc_dram, xdst_dram):
        cmode = combine_mode
        if cmode >= 3:
            @block.vector
            def _(v):
                step = P.stepper(v, rt + "hvch")
                v.wait_ge(P.sem(rt + "kf"), P.val(rt + "kf"))
                step(v.tensor_single_scalar(out=kflag01[:, :], in_=kflag[:, :], scalar=0.0,
                                            op=ALU.is_gt))
                step(v.memset(stg[:, :], 0.0))
                P.ti(v.engine_nop(), rt + "hfg")

            @block.tensor
            def _(te):
                te.wait_ge(P.sem(rt + "hfg"), P.val(rt + "hfg"))
                te.wait_ge(P.sem(rt + "xye"), P.val(rt + "xye"))
                te.wait_ge(P.sem(rt + "xav"), P.val(rt + "xav"))
                mmf = te.matmul(ps[0][:, 0:NT], ones1[:, :], kflag01[:, :], start=True, stop=True)
                P.ti(mmf, rt + "hfm")

            @block.vector
            def _(v):
                step = P.stepper(v, rt + "hvch")
                v.wait_ge(P.sem(rt + "hfm"), P.val(rt + "hfm"))
                step(v.tensor_copy(out=flagcol[:, :], in_=ps[0][:, 0:NT]))
                P.ti(v.engine_nop(), rt + "hfc")
        sh = rt + "h"
        NGRP = NT // 4
        tp0, ma0, si0, av0 = P.val(sh + "tp"), P.val(sh + "ma"), P.val(sh + "si"), P.val(sh + "av")
        mb0, ev0, st0 = P.val(sh + "mb"), P.val(sh + "ev"), P.val(sh + "st")
        cg0, cq0 = P.val(sh + "cg"), P.val(sh + "cq")
        idf0 = P.val(sh + "idf")
        xl0 = P.val(sh + "xl")
        exma, exmb = P.val(ex + "ma"), P.val(ex + "mb")
        exys = P.val(ex + "ys")

        @block.sync
        def _(sy):
            sy.wait_ge(P.sem(ex + "ma"), exma)
            sy.wait_ge(P.sem(ex + "mb"), exmb)
            P.ti(sy.dma_start(out=wg_sb[:, 0:4096].rearrange("p (c i) -> p c i", i=II),
                              in_=sgbf_d.ap().rearrange("(c p) i -> p c i", p=TILE)), sh + "wl", 16)
            P.ti(sy.dma_start(out=wu_sb[:, 0:4096].rearrange("p (c i) -> p c i", i=II),
                              in_=subf_d.ap().rearrange("(c p) i -> p c i", p=TILE)), sh + "wl", 16)
            P.ti(sy.dma_start(out=wd_sb[:, 0:4096].rearrange("p (c i) -> p c i", i=D),
                              in_=sdbf_d.ap().rearrange("(c p) i -> p c i", p=TILE)), sh + "wl", 16)
            for t in range(NT + 4):
                if t < NT:
                    buf = xa if t % 2 == 0 else xb
                    if t >= 2:
                        sy.wait_ge(P.sem(sh + "ev"), ev0 + 2 * (t - 1))
                    P.ti(sy.dma_start(out=buf[:, :], in_=xsrc_dram[t * TILE:(t + 1) * TILE, :]),
                         sh + "xl", 16)
                if t >= 4:
                    ts_ = t - 4
                    tt = ts_ % 4
                    sy.wait_ge(P.sem(sh + "ev"), ev0 + 2 * (ts_ + 1))
                    P.ti(sy.dma_start(out=xdst_dram[ts_ * TILE:(ts_ + 1) * TILE, :],
                                      in_=x1g[:, tt * D:(tt + 1) * D]), sh + "st", 16)

        @block.scalar
        def _(sc):
            sc.wait_ge(P.sem(ex + "ma"), exma)
            for g in range(NGRP):
                par = (g % 2) * 4096
                if g >= 2:
                    sc.wait_ge(P.sem(sh + "ma"), ma0 + (g - 1) * 8)
                for c in range(8):
                    P.ti(sc.dma_start(out=hlnT[:, par + c * 512:par + (c + 1) * 512],
                                      in_=hln_d[g * 512:(g + 1) * 512, c * TILE:(c + 1) * TILE],
                                      transpose=True), sh + "tp", 16)
                for it in range(4):
                    sc.wait_ge(P.sem(sh + "ma"), ma0 + g * 8 + 2 * it + 1)
                    if it > 0:
                        sc.wait_ge(P.sem(sh + "av"), av0 + g * 4 + it)
                    P.ti(sc.activation(out=sact[:, :], in_=ps[it][:, :], func=AF.Silu), sh + "si")

        @block.vector
        def _(v):
            step = P.stepper(v, sh + "vch")
            for g in range(NGRP):
                for it in range(4):
                    v.wait_ge(P.sem(sh + "si"), si0 + g * 4 + it + 1)
                    v.wait_ge(P.sem(sh + "ma"), ma0 + g * 8 + 2 * it + 2)
                    step(v.tensor_tensor(out=act_sb[:, it * 512:(it + 1) * 512],
                                         in0=sact[:, :], in1=ps[4 + it][:, :], op=ALU.mult))
                    P.ti(v.engine_nop(), sh + "av")
                for tt in range(4):
                    t = g * 4 + tt
                    buf = xa if t % 2 == 0 else xb
                    if cmode >= 3:
                        tf = t
                        if tf < NT:
                            if tf >= 2:
                                v.wait_ge(P.sem(sh + "cq"), cq0 + tf - 1)
                            step(v.tensor_scalar_mul(out=identX[:, (tf % 2) * TILE:(tf % 2 + 1) * TILE],
                                                     in0=ident_b[:, :],
                                                     scalar1=flagcol[:, tf:tf + 1]))
                            P.ti(v.engine_nop(), sh + "idf")
                    v.wait_ge(P.sem(sh + "cq"), cq0 + t + 1)
                    for half in range(2):
                        v.wait_ge(P.sem(sh + "mb"), mb0 + g * 8 + tt * 2 + half + 1)
                        v.wait_ge(P.sem(sh + "xl"), xl0 + (t + 1) * 16)
                        if g >= 1:
                            v.wait_ge(P.sem(sh + "st"), st0 + (g - 1) * 64 + (tt + 1) * 16)
                        step(v.scalar_tensor_tensor(out=x1g[:, tt * D + half * 512:tt * D + half * 512 + 512],
                                                    in0=ps[tt + 4 * half][:, :], scalar=0.5,
                                                    in1=buf[:, half * 512:half * 512 + 512],
                                                    op0=ALU.mult, op1=ALU.add))
                        P.ti(v.engine_nop(), sh + "ev")

        @block.tensor
        def _(te):
            te.wait_ge(P.sem(sh + "wl"), P.val(sh + "wl"))
            te.wait_ge(P.sem(ex + "ye"), P.val(ex + "ye"))
            te.wait_ge(P.sem(ex + "av"), P.val(ex + "av"))
            for g in range(NGRP):
                par = (g % 2) * 4096
                te.wait_ge(P.sem(sh + "tp"), tp0 + (g + 1) * 128)
                if g >= 1:
                    te.wait_ge(P.sem(sh + "ev"), ev0 + g * 8)
                    te.wait_ge(P.sem(sh + "av"), av0 + g * 4)
                for it in range(4):
                    for c in range(8):
                        mmg = te.matmul(ps[it][:, :],
                                        wg_sb[:, c * 512 + it * TILE:c * 512 + it * TILE + TILE],
                                        hlnT[:, par + c * 512:par + (c + 1) * 512],
                                        start=(c == 0), stop=(c == 7))
                    P.ti(mmg, sh + "ma")
                    for c in range(8):
                        mmu = te.matmul(ps[4 + it][:, :],
                                        wu_sb[:, c * 512 + it * TILE:c * 512 + it * TILE + TILE],
                                        hlnT[:, par + c * 512:par + (c + 1) * 512],
                                        start=(c == 0), stop=(c == 7))
                    P.ti(mmu, sh + "ma")
                te.wait_ge(P.sem(sh + "av"), av0 + g * 4 + 4)
                te.wait_ge(P.sem(sh + "si"), si0 + g * 4 + 4)
                for tt in range(4):
                    t = g * 4 + tt
                    spar = (t % 2) * KTOP * D
                    for half in range(2):
                        for ic in range(4):
                            mmd = te.matmul(ps[tt + 4 * half][:, :],
                                            act_sb[:, ic * 512 + tt * TILE:ic * 512 + tt * TILE + TILE],
                                            wd_sb[:, ic * 1024 + half * 512:ic * 1024 + half * 512 + 512],
                                            start=(ic == 0), stop=(ic == 3))
                        P.ti(mmd, sh + "mb")
                    te.wait_ge(P.sem(sh + "cg"), cg0 + (t + 1) * 16 * KTOP)
                    if cmode == 0:
                        P.ti(te.nop(), sh + "cq")
                    elif cmode >= 3:
                        te.wait_ge(P.sem(sh + "idf"), P.val(sh + "idf") - NT + t + 1
                                   if False else idf0 + t + 1)
                        for k in range(KTOP):
                            for half in range(2):
                                mi = te.matmul(ps[tt + 4 * half][:, :],
                                               identX[:, (t % 2) * TILE:(t % 2 + 1) * TILE],
                                               stg[:, spar + k * D + half * 512:
                                                   spar + k * D + half * 512 + 512],
                                               start=False, stop=(k == KTOP - 1))
                        P.ti(mi, sh + "cq")
                    elif cmode == 1:
                        for k in range(KTOP):
                            for half in range(2):
                                mi = te.matmul(ps[tt + 4 * half][:, :], ident_b[:, :],
                                               stg[:, spar + k * D + half * 512:
                                                   spar + k * D + half * 512 + 512],
                                               start=False, stop=(k == KTOP - 1))
                        P.ti(mi, sh + "cq")
                    else:
                        with te.register(uname(sh + f"cr{t}")) as rt_:
                            te.reg_load(rt_, kflag_i[0:1, t:t + 1])
                            with te.If_eq(rt_, 0):
                                te.nop().then_inc(P.sem(sh + "cq"), 1, skip_validation=True)
                            with te.Else():
                                for k in range(KTOP):
                                    for half in range(2):
                                        mi = te.matmul(ps[tt + 4 * half][:, :], ident_b[:, :],
                                                       stg[:, spar + k * D + half * 512:
                                                           spar + k * D + half * 512 + 512],
                                                       start=False, stop=(k == KTOP - 1))
                                mi.then_inc(P.sem(sh + "cq"), 1, skip_validation=True)
                        P.v[sh + "cq"] = P.val(sh + "cq") + 1

        @block.gpsimd
        def _(gp):
            gp.wait_ge(P.sem(ex + "ys"), exys)
            if cmode >= 3:
                gp.wait_ge(P.sem(rt + "hfg"), P.val(rt + "hfg"))
            with gp.register(uname(sh + "rf")) as rf:
                for t in range(NT):
                    spar = (t % 2) * KTOP * D
                    if cmode == 0:
                        P.ti(gp.engine_nop(), sh + "cg", 16 * KTOP)
                        continue
                    if t >= 2:
                        gp.wait_ge(P.sem(sh + "cq"), cq0 + t - 1)
                    if cmode == 1:
                        for k in range(KTOP):
                            P.ti(gp.indirect_dma_start(
                                out=stg[:, spar + k * D:spar + (k + 1) * D], out_offset=None,
                                in_=yd_d.ap(),
                                in_offset=IndirectOffsetOnAxis(cmb_all[:, t * 8 + k:t * 8 + k + 1], 0)),
                                sh + "cg", 16)
                        continue
                    gp.reg_load(rf, kflag_i[0:1, t:t + 1])
                    with gp.If_eq(rf, 0):
                        gp.engine_nop().then_inc(P.sem(sh + "cg"), 16 * KTOP, skip_validation=True)
                    with gp.Else():
                        for k in range(KTOP):
                            gp.indirect_dma_start(
                                out=stg[:, spar + k * D:spar + (k + 1) * D], out_offset=None,
                                in_=yd_d.ap(),
                                in_offset=IndirectOffsetOnAxis(cmb_all[:, t * 8 + k:t * 8 + k + 1], 0),
                            ).then_inc(P.sem(sh + "cg"), 16, skip_validation=True)
                    P.v[sh + "cg"] = P.val(sh + "cg") + 16 * KTOP

        return sh


    def phase_barrier():
        snap = [(n, P.v[n]) for n in list(P.v.keys()) if P.v[n] > 0]
        for sec in (block.sync, block.scalar, block.vector, block.tensor, block.gpsimd):
            @sec
            def _(eng, snap=snap):
                for n, c in snap:
                    eng.wait_ge(P.sem(n), c)

    def emit_attn():
        NGRP = NT // 4
        NTB_ = NT // NB
        r2buf = pairs_all
        x2sb = ew_all
        ttab = scr
        wqkv_sb = xg
        tmpo = sqbuf

        @block.sync
        def _(sy):
            sy.wait_ge(P.sem("wcast"), P.val("wcast"))
            P.ti(sy.dma_start(out=wqkv_sb[:, 0:1536].rearrange("p (c w) -> p c w", w=192),
                              in_=wqkv_bf_d.ap().rearrange("(c p) w -> p c w", p=TILE)), "atwl", 16)
            P.ti(sy.dma_start(out=woT_sb[:, :], in_=wo_bf_d[:, :]), "atwl", 16)
            P.ti(sy.dma_start(out=bqkv_sb[:, :], in_=bqkv_in[:, :]), "atwl", 16)
            P.ti(sy.dma_start(out=bo_sb[:, :], in_=bo_in[:, :]), "atwl", 16)
            P.ti(sy.dma_start(out=drow_sb[0:1, 0:DS], in_=decay_in[:, :]), "atwl", 16)
            sy.wait_ge(P.sem("atprA"), 3)
            P.ti(sy.dma_start(out=misc_d.ap().rearrange("e one -> one e"),
                              in_=drow_sb[0:1, 2 * DS:3 * DS]), "atms", 16)
            sy.wait_ge(P.sem("atms"), 16)
            P.ti(sy.dma_start(out=lnd_col[:, :], in_=misc_d.ap()), "atms", 16)
            for t in range(NT + 2):
                if t < NT:
                    if t >= 4:
                        sy.wait_ge(P.sem("atx2"), t - 3)
                    P.ti(sy.dma_start(out=r2buf[:, (t % 4) * D:(t % 4 + 1) * D],
                                      in_=r2_d[t * TILE:(t + 1) * TILE, :]), "atr2l", 16)
                if t >= 2:
                    tau = t - 2
                    sy.wait_ge(P.sem("atx2h"), tau + 1)
                    sy.wait_ge(P.sem("atx2"), tau + 1)
                    P.ti(sy.dma_start(out=x2_d[tau * TILE:(tau + 1) * TILE, :],
                                      in_=x2sb[:, (tau % 2) * D:(tau % 2 + 1) * D]), "atst", 16)

        @block.gpsimd
        def _(gp):
            P.ti(gp.iota(exprow[:, 0:TILE], pattern=[[0, 4], [1, 32]], base=-16,
                         channel_multiplier=0, allow_small_or_imprecise_dtypes=True), "atpi")
            P.ti(gp.iota(exprow[:, TILE:2 * TILE], pattern=[[0, 4], [1, 32]], base=0,
                         channel_multiplier=0, allow_small_or_imprecise_dtypes=True), "atpi")
            P.ti(gp.iota(exprow[:, 2 * TILE:3 * TILE], pattern=[[1, TILE]], base=-64,
                         channel_multiplier=0, allow_small_or_imprecise_dtypes=True), "atpi")
            P.ti(gp.iota(exprow[:, 3 * TILE:4 * TILE], pattern=[[1, TILE]], base=1,
                         channel_multiplier=0, allow_small_or_imprecise_dtypes=True), "atpi")
            P.ti(gp.iota(ttab[:, 0:TILE], pattern=[[1, TILE]], base=0,
                         channel_multiplier=0, allow_small_or_imprecise_dtypes=True), "atpi")
            P.ti(gp.iota(ttab[:, TILE:2 * TILE], pattern=[[1, 4], [0, 32]], base=0,
                         channel_multiplier=0, allow_small_or_imprecise_dtypes=True), "atpi")
            P.ti(gp.iota(ttab[:, 2 * TILE:3 * TILE], pattern=[[1, 2], [0, 64]], base=0,
                         channel_multiplier=0, allow_small_or_imprecise_dtypes=True), "atpi")
            P.ti(gp.iota(ttab[:, 3 * TILE:4 * TILE], pattern=[[0, 2], [1, 64]], base=0,
                         channel_multiplier=0, allow_small_or_imprecise_dtypes=True), "atpi")

        @block.scalar
        def _(sc):
            stepA = P.stepper(sc, "atprA")
            sc.wait_ge(P.sem("atwl"), 16 * 5)
            stepA(sc.activation(out=drow_sb[0:1, DS:2 * DS], in_=drow_sb[0:1, 0:DS],
                                func=AF.Tanh, scale=0.5, bias=0.0))
            stepA(sc.activation(out=drow_sb[0:1, DS:2 * DS], in_=drow_sb[0:1, DS:2 * DS],
                                func=AF.Identity, scale=0.5, bias=0.5))
            stepA(sc.activation(out=drow_sb[0:1, 2 * DS:3 * DS], in_=drow_sb[0:1, DS:2 * DS],
                                func=AF.Ln, bias=0.0))
            stepA(sc.activation(out=drow_sb[0:1, 3 * DS:4 * DS], in_=drow_sb[0:1, 2 * DS:3 * DS],
                                func=AF.Exp, scale=float(TILE), bias=0.0))
            sc.wait_ge(P.sem("atdc"), 1)
            P.ti(sc.activation(out=dcolx[:, :], in_=dcolx[:, :], func=AF.Exp, bias=0.0), "attab")
            sc.wait_ge(P.sem("atdc"), 2)
            P.ti(sc.activation(out=drowt[:, :], in_=drowt[:, :], func=AF.Exp, bias=0.0), "attab")
            for c in range(8):
                P.ti(sc.dma_start(out=hlnT[:, c * 512:(c + 1) * 512],
                                  in_=hln_d[0:512, c * TILE:(c + 1) * TILE],
                                  transpose=True), "attp", 16)
            for t in range(NT + 2):
                if (t + 3) % 4 == 0:
                    g = (t + 3) // 4
                    if g < NGRP:
                        if g >= 2:
                            sc.wait_ge(P.sem("atqk"), 4 * g - 4)
                        gpar = (g % 2) * 4096
                        for c in range(8):
                            P.ti(sc.dma_start(out=hlnT[:, gpar + c * 512:gpar + (c + 1) * 512],
                                              in_=hln_d[g * 512:(g + 1) * 512, c * TILE:(c + 1) * TILE],
                                              transpose=True), "attp", 16)
                for tq in ([0, 1] if t == 0 else [t + 1]):
                    if tq < NT:
                        sc.wait_ge(P.sem("atqk"), tq + 1)
                        if tq >= 3:
                            sc.wait_ge(P.sem("atsa"), tq - 2)
                        pqq = ps[6 + tq % 2]
                        sl = (tq % 3) * TILE
                        P.ti(sc.activation(out=qT_bf[:, sl:sl + TILE], in_=pqq[0:DS, 0:TILE],
                                           func=AF.Identity, bias=0.0), "atev")
                        P.ti(sc.activation(out=kT_bf[:, sl:sl + TILE], in_=pqq[0:DS, TILE:2 * TILE],
                                           func=AF.Identity, bias=0.0), "atev")
                        P.ti(sc.activation(out=kvt[:, sl:sl + TILE], in_=pqq[:, 2 * TILE:3 * TILE],
                                           func=AF.Identity, bias=0.0), "atev")

        @block.vector
        def _(v):
            step = P.stepper(v, "atvch")
            v.wait_ge(P.sem("atprT"), 1)
            step(v.tensor_copy(out=lnd_rep[:, :], in_=ps[0][:, 0:DS]))
            v.wait_ge(P.sem("atprT"), 2)
            step(v.tensor_copy(out=dtile_rep[:, :], in_=ps[1][0:DS, 0:DS]))
            step(v.tensor_single_scalar(out=expcol[:, 0:1], in_=iota_p[:, :], scalar=31.5, op=ALU.is_gt))
            step(v.tensor_single_scalar(out=expcol[:, 1:2], in_=iota_p[:, :], scalar=63.5, op=ALU.is_gt))
            step(v.tensor_single_scalar(out=expcol[:, 2:3], in_=iota_p[:, :], scalar=95.5, op=ALU.is_gt))
            step(v.tensor_tensor(out=expcol[:, 0:1], in0=expcol[:, 0:1], in1=expcol[:, 1:2], op=ALU.add))
            step(v.tensor_tensor(out=expcol[:, 0:1], in0=expcol[:, 0:1], in1=expcol[:, 2:3], op=ALU.add))
            step(v.tensor_scalar(out=expcol[:, 3:4], in0=expcol[:, 0:1], scalar1=-32.0, scalar2=0.0, op0=ALU.mult))
            step(v.tensor_tensor(out=expcol[:, 3:4], in0=expcol[:, 3:4], in1=iota_p[:, :], op=ALU.add))
            step(v.tensor_scalar(out=expcol[:, 4:5], in0=expcol[:, 3:4], scalar1=-1.0, scalar2=16.0,
                                 op0=ALU.mult, op1=ALU.add))
            step(v.tensor_scalar(out=expcol[:, 5:6], in0=expcol[:, 3:4], scalar1=-1.0, scalar2=32.0,
                                 op0=ALU.mult, op1=ALU.add))
            step(v.tensor_scalar(out=expcol[:, 6:7], in0=iota_p[:, :], scalar1=-1.0, scalar2=64.0,
                                 op0=ALU.mult, op1=ALU.add))
            step(v.tensor_scalar(out=expcol[:, 7:8], in0=iota_p[:, :], scalar1=-1.0, scalar2=127.0,
                                 op0=ALU.mult, op1=ALU.add))
            for l, cc in enumerate((4, 5, 6, 7)):
                step(v.tensor_tensor(out=dcolx[:, l * DS:(l + 1) * DS], in0=lnd_rep[:, :],
                                     in1=expcol[:, cc:cc + 1].broadcast_to((TILE, DS)), op=ALU.mult))
            step(v.tensor_scalar(out=dcolx[:, :], in0=dcolx[:, :], scalar1=80.0, scalar2=0.0, op0=ALU.min))
            P.ti(v.engine_nop(), "atdc")
            v.wait_ge(P.sem("atpi"), 4)
            v.wait_ge(P.sem("atms"), 32)
            step(v.tensor_tensor(out=drowt[:, :], in0=exprow[:, :],
                                 in1=lnd_col[:, :].broadcast_to((DS, 4 * TILE)), op=ALU.mult))
            step(v.tensor_scalar(out=drowt[:, :], in0=drowt[:, :], scalar1=80.0, scalar2=0.0, op0=ALU.min))
            P.ti(v.engine_nop(), "atdc")
            v.wait_ge(P.sem("atpi"), 8)
            tA = xa[:, 0:TILE]
            tB = xa[:, TILE:2 * TILE]
            step(v.tensor_tensor(out=tA, in0=ttab[:, TILE:2 * TILE],
                                 in1=expcol[:, 0:1].broadcast_to((TILE, TILE)), op=ALU.is_equal))
            step(v.tensor_tensor(out=tB, in0=ttab[:, 0:TILE],
                                 in1=iota_p[:, :].broadcast_to((TILE, TILE)), op=ALU.is_ge))
            step(v.tensor_tensor(out=mask3[:, 0:TILE], in0=tA, in1=tB, op=ALU.mult))
            step(v.tensor_tensor(out=tA, in0=ttab[:, 2 * TILE:3 * TILE],
                                 in1=expcol[:, 1:2].broadcast_to((TILE, TILE)), op=ALU.is_equal))
            step(v.tensor_single_scalar(out=tB, in_=ttab[:, 3 * TILE:4 * TILE], scalar=31.5, op=ALU.is_gt))
            step(v.tensor_tensor(out=tA, in0=tA, in1=tB, op=ALU.mult))
            step(v.tensor_scalar(out=expcol[:, 3:4], in0=expcol[:, 1:2], scalar1=-64.0, scalar2=0.0, op0=ALU.mult))
            step(v.tensor_tensor(out=expcol[:, 3:4], in0=expcol[:, 3:4], in1=iota_p[:, :], op=ALU.add))
            step(v.tensor_single_scalar(out=expcol[:, 3:4], in_=expcol[:, 3:4], scalar=31.5, op=ALU.is_gt))
            step(v.tensor_scalar(out=expcol[:, 3:4], in0=expcol[:, 3:4], scalar1=-1.0, scalar2=1.0,
                                 op0=ALU.mult, op1=ALU.add))
            step(v.tensor_tensor(out=mask3[:, TILE:2 * TILE], in0=tA,
                                 in1=expcol[:, 3:4].broadcast_to((TILE, TILE)), op=ALU.mult))
            step(v.tensor_single_scalar(out=tA, in_=ttab[:, 0:TILE], scalar=63.5, op=ALU.is_gt))
            step(v.tensor_scalar(out=expcol[:, 4:5], in0=expcol[:, 1:2], scalar1=-1.0, scalar2=1.0,
                                 op0=ALU.mult, op1=ALU.add))
            step(v.tensor_tensor(out=mask3[:, 2 * TILE:3 * TILE], in0=tA,
                                 in1=expcol[:, 4:5].broadcast_to((TILE, TILE)), op=ALU.mult))
            step(v.memset(st_f[:, :], 0.0))
            step(v.memset(st_b[:, :], 0.0))
            P.ti(v.engine_nop(), "atsu")
            v.wait_ge(P.sem("attab"), 2)
            for t in range(NT + 2):
                for tq in ([0, 1] if t == 0 else [t + 1]):
                    if tq < NT:
                        v.wait_ge(P.sem("atev"), 3 * tq + 3)
                        if tq >= 3:
                            v.wait_ge(P.sem("atim"), tq - 2)
                        sl = (tq % 3) * TILE
                        vsl = (tq % 3) * 256
                        P.ti(v.tensor_tensor(
                            out=v4[:, vsl:vsl + 256].rearrange("p (l i) -> p l i", i=DS),
                            in0=kvt[:, sl + DS:sl + TILE].rearrange("p (o i) -> p o i", o=1)
                                .broadcast_to((TILE, 4, DS)),
                            in1=dcolx[:, :].rearrange("p (l i) -> p l i", i=DS),
                            op=ALU.mult), "atvv")
                if t < NT:
                    v.wait_ge(P.sem("atat"), t + 1)
                    if t >= 2:
                        v.wait_ge(P.sem("atim"), t - 1)
                    pt_ = ps[2 + t % 2]
                    msl = (t % 2) * 3 * TILE
                    for l in range(3):
                        P.ti(v.tensor_tensor(out=mav[:, msl + l * TILE:msl + (l + 1) * TILE],
                                             in0=pt_[:, 0:TILE], in1=mask3[:, l * TILE:(l + 1) * TILE],
                                             op=ALU.mult), "atmk")
                if 0 <= t - 2 < NT:
                    tau = t - 2
                    v.wait_ge(P.sem("atpjh"), 2 * tau + 1)
                    v.wait_ge(P.sem("atr2l"), (tau + 1) * 16)
                    if tau >= 2:
                        v.wait_ge(P.sem("atst"), (tau - 1) * 16)
                    P.ti(v.tensor_tensor(out=x2sb[:, (tau % 2) * D:(tau % 2) * D + 512],
                                         in0=ps[4 + tau % 2][:, :],
                                         in1=r2buf[:, (tau % 4) * D:(tau % 4) * D + 512],
                                         op=ALU.add), "atx2h")
                if 1 <= t <= NT - 1:
                    v.wait_ge(P.sem("atsa"), t)
                    if t == NTB_:
                        step(v.memset(st_f[:, :], 0.0))
                        step(v.memset(st_b[:, :], 0.0))
                    else:
                        v.wait_ge(P.sem("attt"), t)
                        ptm = ps[2 + (t - 1) % 2]
                        step(v.tensor_tensor(out=st_f[:, :], in0=st_f[:, :], in1=dtile_rep[:, :],
                                             op=ALU.mult))
                        step(v.tensor_tensor(out=st_f[:, :], in0=st_f[:, :],
                                             in1=ptm[0:DS, TILE:TILE + DS], op=ALU.add))
                        step(v.tensor_copy(out=st_b[:, :], in_=st_f[:, :]))
                    P.ti(v.engine_nop(), "atsu")
                if 0 <= t - 1 < NT:
                    tau = t - 1
                    v.wait_ge(P.sem("atim"), tau + 1)
                    v.wait_ge(P.sem("atsa"), tau + 1)
                    if tau >= 2:
                        v.wait_ge(P.sem("atpjh"), 2 * tau - 2)
                    pa_ = ps[0 + tau % 2]
                    step(v.tensor_tensor(out=tmpo[0:DS, 0:512], in0=pa_[0:DS, :],
                                         in1=drowt[0:DS, :], op=ALU.mult))
                    step(v.tensor_reduce(
                        out=obuf[0:DS, :].rearrange("p (x o) -> p x o", o=1),
                        in_=tmpo[0:DS, 0:512].rearrange("p (k x) -> p x k", k=4),
                        axis=AX.X, op=ALU.add))
                    P.ti(v.tensor_copy(out=obf[0:DS, (tau % 2) * TILE:(tau % 2 + 1) * TILE],
                                       in_=obuf[0:DS, :]), "atob")
                if 0 <= t - 2 < NT:
                    tau = t - 2
                    v.wait_ge(P.sem("atpjh"), 2 * tau + 2)
                    P.ti(v.tensor_tensor(out=x2sb[:, (tau % 2) * D + 512:(tau % 2) * D + 1024],
                                         in0=ps[4 + tau % 2][:, :],
                                         in1=r2buf[:, (tau % 4) * D + 512:(tau % 4) * D + 1024],
                                         op=ALU.add), "atx2")

        @block.tensor
        def _(te):
            te.wait_ge(P.sem("atwl"), 16 * 5)
            te.wait_ge(P.sem("atprA"), 3)
            P.ti(te.matmul(ps[0][:, 0:DS], ones1[0:1, 0:TILE], drow_sb[0:1, 2 * DS:3 * DS],
                           start=True, stop=True), "atprT")
            te.wait_ge(P.sem("atprA"), 4)
            P.ti(te.matmul(ps[1][0:DS, 0:DS], ones1[0:1, 0:DS], drow_sb[0:1, 3 * DS:4 * DS],
                           start=True, stop=True), "atprT")
            for t in range(NT + 2):
                if 0 <= t - 2 < NT:
                    tau = t - 2
                    pp = ps[4 + tau % 2]
                    te.wait_ge(P.sem("atob"), tau + 1)
                    if tau >= 2:
                        te.wait_ge(P.sem("atx2"), tau - 1)
                    osl = (tau % 2) * TILE
                    te.matmul(pp[:, :], obf[0:DS, osl:osl + TILE], woT_sb[0:DS, 0:512],
                              start=True, stop=False)
                    P.ti(te.matmul(pp[:, :], ones1[0:1, 0:TILE], bo_sb[0:1, 0:512],
                                   start=False, stop=True), "atpjh")
                for tq in ([0, 1] if t == 0 else [t + 1]):
                    if tq < NT:
                        g = tq // 4
                        te.wait_ge(P.sem("attp"), (g + 1) * 128)
                        if tq >= 2:
                            te.wait_ge(P.sem("atev"), 3 * tq)
                        pqq = ps[6 + tq % 2]
                        base = (g % 2) * 4096 + (tq % 4) * TILE
                        for c in range(8):
                            te.matmul(pqq[0:DS, 0:TILE], wqkv_sb[:, c * 192:c * 192 + DS],
                                      hlnT[:, base + c * 512:base + c * 512 + TILE],
                                      start=(c == 0), stop=False)
                        te.matmul(pqq[0:DS, 0:TILE], bqkv_sb[0:1, 0:DS], ones1[0:1, 0:TILE],
                                  start=False, stop=True)
                        for c in range(8):
                            te.matmul(pqq[0:DS, TILE:2 * TILE],
                                      wqkv_sb[:, c * 192 + DS:c * 192 + 2 * DS],
                                      hlnT[:, base + c * 512:base + c * 512 + TILE],
                                      start=(c == 0), stop=False)
                        te.matmul(pqq[0:DS, TILE:2 * TILE], bqkv_sb[0:1, DS:2 * DS],
                                  ones1[0:1, 0:TILE], start=False, stop=True)
                        for c in range(8):
                            te.matmul(pqq[:, 2 * TILE:3 * TILE],
                                      hlnT[:, base + c * 512:base + c * 512 + TILE],
                                      wqkv_sb[:, c * 192 + DS:c * 192 + 192],
                                      start=(c == 0), stop=False)
                        P.ti(te.matmul(pqq[:, 2 * TILE:3 * TILE], ones1[0:1, 0:TILE],
                                       bqkv_sb[0:1, DS:192], start=False, stop=True), "atqk")
                if t < NT:
                    te.wait_ge(P.sem("atev"), 3 * t + 3)
                    if t >= 2:
                        te.wait_ge(P.sem("atmk"), 3 * t - 3)
                        te.wait_ge(P.sem("atsu"), t)
                    pt_ = ps[2 + t % 2]
                    sl = (t % 3) * TILE
                    P.ti(te.matmul(pt_[:, 0:TILE], kT_bf[:, sl:sl + TILE],
                                   qT_bf[:, sl:sl + TILE], start=True, stop=True), "atat")
                    te.wait_ge(P.sem("atvv"), t + 1)
                    P.ti(te.matmul(pt_[0:DS, TILE:TILE + DS], kvt[:, sl:sl + DS],
                                   v4[:, (t % 3) * 256 + 192:(t % 3) * 256 + 256],
                                   start=True, stop=True), "attt")
                if 0 <= t - 1 < NT:
                    tau = t - 1
                    pa_ = ps[0 + tau % 2]
                    te.wait_ge(P.sem("atmk"), 3 * tau + 3)
                    if tau >= 2:
                        te.wait_ge(P.sem("atob"), tau - 1)
                    vsl = (tau % 3) * 256
                    msl = (tau % 2) * 3 * TILE
                    for l in range(3):
                        mi = te.matmul(pa_[0:DS, l * TILE:(l + 1) * TILE],
                                       v4[:, vsl + l * DS:vsl + (l + 1) * DS],
                                       mav[:, msl + l * TILE:msl + (l + 1) * TILE],
                                       start=True, stop=True)
                    P.ti(mi, "atim")
                    te.wait_ge(P.sem("atsu"), tau + 1)
                    sl1 = (tau % 3) * TILE
                    P.ti(te.matmul(pa_[0:DS, 3 * TILE:4 * TILE], st_b[:, :],
                                   qT_bf[:, sl1:sl1 + TILE], start=True, stop=True), "atsa")
                if 0 <= t - 2 < NT:
                    tau = t - 2
                    pp = ps[4 + tau % 2]
                    te.wait_ge(P.sem("atx2h"), tau + 1)
                    osl = (tau % 2) * TILE
                    te.matmul(pp[:, :], obf[0:DS, osl:osl + TILE], woT_sb[0:DS, 512:1024],
                              start=True, stop=False)
                    P.ti(te.matmul(pp[:, :], ones1[0:1, 0:TILE], bo_sb[0:1, 512:1024],
                                   start=False, stop=True), "atpjh")

    ln1_st = emit_ln(x_in, hln_d, "l1")
    if stn >= 1:
        emit_routing(ln1_st, "f1", route_pass)
    if stn >= 2:
        ex1 = emit_expert("f1", hln_d, expert_pass)
    if stn >= 3:
        sh1 = emit_shared("f1", "f1x", x_in, x1_d if stn >= 4 else y_out)
    if stn >= 4:
        phase_barrier()
        emit_ln(x1_d, hln_d, "l2", r2_dram=r2_d)
        phase_barrier()
        emit_attn()
    if stn >= 5:
        phase_barrier()
        ln3_st = emit_ln(x2_d, hln_d, "l1")
        emit_routing(ln3_st, "f1", route_pass)
        ex2 = emit_expert("f1", hln_d, expert_pass)
        sh2 = emit_shared("f1", "f1x", x2_d, y_out)

    @block.sync
    def _(sy):
        if "hln" in dbg:
            sy.wait_ge(P.sem(ln1_st), P.val(ln1_st))
            for t in range(0, NT, 8):
                P.ti(sy.dma_start(out=dbg["hln"][t * TILE:(t + 8) * TILE, :],
                                  in_=hln_d[t * TILE:(t + 8) * TILE, :]), "dbst", 16)
        if stn >= 1:
            if stn >= 3 and "x1" in dbg:
                sy.wait_ge(P.sem("f1hst"), P.val("f1hst"))
                for r in range(0, T, 4096):
                    P.ti(sy.dma_start(out=dbg["x1"][r:r + 4096, :], in_=x1_d[r:r + 4096, :]), "dbst", 16)
            if stn >= 2 and "xd" in dbg:
                sy.wait_ge(P.sem("f1xs"), P.val("f1xs"))
                for r in range(0, NSLOT, 4096):
                    P.ti(sy.dma_start(out=dbg["xd"][r:r + 4096, :], in_=xd_d[r:r + 4096, :]), "dbst", 16)
            if stn >= 2 and "yd" in dbg:
                sy.wait_ge(P.sem("f1xys"), P.val("f1xys"))
                for r in range(0, NSLOT, 4096):
                    P.ti(sy.dma_start(out=dbg["yd"][r:r + 4096, :], in_=yd_d[r:r + 4096, :]), "dbst", 16)
            if stn >= 4 and "x2" in dbg:
                sy.wait_ge(P.sem("atst"), P.val("atst"))
                for r in range(0, T, 4096):
                    P.ti(sy.dma_start(out=dbg["x2"][r:r + 4096, :], in_=x2_d[r:r + 4096, :]), "dbst", 16)
            if "inv" in dbg and P.val("f1scdone") > 0:
                sy.wait_ge(P.sem("f1scdone"), P.val("f1scdone"))
                for r in range(0, NSLOT, 4096):
                    P.ti(sy.dma_start(out=dbg["inv"][r:r + 4096, :],
                                      in_=inv_d[r:r + 4096, :]), "dbst", 16)

    if stn < 3:
        @block.gpsimd
        def _(gp):
            gp.wait_ge(P.sem(ln1_st), P.val(ln1_st))
            for r in range(0, n_out, 1024):
                P.ti(gp.dma_start(out=y_out[r:r + 1024, :],
                                  in_=hln_d[out_rows[0] + r:out_rows[0] + r + 1024, :]), "yst", 16)
    elif stn == 4:
        @block.gpsimd
        def _(gp):
            gp.wait_ge(P.sem("atst"), P.val("atst"))
            for r in range(0, n_out, 1024):
                P.ti(gp.dma_start(out=y_out[r:r + 1024, :],
                                  in_=x2_d[out_rows[0] + r:out_rows[0] + r + 1024, :]), "yst", 16)

    @block.sync
    def _(sy):
        for name in list(P.v.keys()):
            if name.endswith(("st", "sc", "scdone")) or name in ("wcast", "pz"):
                sy.wait_ge(P.sem(name), P.val(name))

    cs.close()
    return nc


def _make_in_map(inputs, stage):
    stn = STAGES[stage]
    m = {
        "x": np.ascontiguousarray(np.asarray(inputs["L0_attn_x"], np.float32).reshape(T, D)),
        "router_w": np.ascontiguousarray(np.asarray(inputs["L0_ffn_router_w"], np.float32)),
    }
    if stn >= 2:
        m["gate_w"] = np.ascontiguousarray(np.asarray(inputs["L0_ffn_gate_w"], np.float32))
        m["up_w"] = np.ascontiguousarray(np.asarray(inputs["L0_ffn_up_w"], np.float32))
        m["down_w"] = np.ascontiguousarray(np.asarray(inputs["L0_ffn_down_w"], np.float32))
    if stn >= 3:
        m["shared_gate_w"] = np.ascontiguousarray(np.asarray(inputs["L0_ffn_shared_gate_w"], np.float32))
        m["shared_up_w"] = np.ascontiguousarray(np.asarray(inputs["L0_ffn_shared_up_w"], np.float32))
        m["shared_down_w"] = np.ascontiguousarray(np.asarray(inputs["L0_ffn_shared_down_w"], np.float32))
    if stn >= 4:
        m["wq_t"] = np.ascontiguousarray(np.asarray(inputs["L0_attn_Wq"], np.float32).T)
        m["wk_t"] = np.ascontiguousarray(np.asarray(inputs["L0_attn_Wk"], np.float32).T)
        m["wv_t"] = np.ascontiguousarray(np.asarray(inputs["L0_attn_Wv"], np.float32).T)
        m["wo_t"] = np.ascontiguousarray(np.asarray(inputs["L0_attn_Wo"], np.float32).T)
        m["bqkv"] = np.ascontiguousarray(np.concatenate([
            np.asarray(inputs["L0_attn_bq"], np.float32),
            np.asarray(inputs["L0_attn_bk"], np.float32),
            np.asarray(inputs["L0_attn_bv"], np.float32)])[None, :])
        m["bo"] = np.ascontiguousarray(np.asarray(inputs["L0_attn_bo"], np.float32)[None, :])
        m["decay"] = np.ascontiguousarray(np.asarray(inputs["L0_attn_decay"], np.float32)[None, :])
    return m


def kernel(**inputs):
    stage = "full"
    nc = build_program(stage=stage)
    res = run_bass_kernel_spmd(nc, [_make_in_map(inputs, stage)], core_ids=[0])
    return res.results[0]["y"].reshape(NB, TB, D)
